# version 04
# baseline (speedup 1.0000x reference)
import numpy as np
import ml_dtypes

import concourse.bass as bass
import concourse.mybir as mybir
from concourse.bass_utils import run_bass_kernel_spmd

F32 = mybir.dt.float32
BF16 = mybir.dt.bfloat16
AF = mybir.ActivationFunctionType
ALU = mybir.AluOpType
AX = mybir.AxisListType

C = 1024
CH = 8
T = 2048
NL = 4
NH = 16
NSEQ = 2
PADW = 2052
EPS_OUT = 1e-5
EPS_IN = 1e-12

STOP_AFTER = None



class Prog:
    ENG = {"pe": "tensor", "act": "scalar", "dve": "vector", "pool": "gpsimd",
           "sp": "sync"}

    def __init__(self, nc):
        self.nc = nc
        self.esem = {e: nc.alloc_semaphore("esem_" + e)
                     for e in ("pe", "act", "dve", "pool")}
        self.tasks = {e: [] for e in self.ENG}
        self.ccnt = {e: 0 for e in ("pe", "act", "dve", "pool")}
        self.dsem = {}
        self.res = {}
        self.waited = {e: {} for e in self.ENG}

    def _iv(self, ap):
        t = ap.tensor
        if isinstance(t, bass.PSumTensorHandle):
            return (t.name, 0, 1 << 40)
        off = ap.offset if isinstance(ap.offset, int) else 0
        span = 1
        for step, count in ap.ap:
            span += abs(step) * (count - 1)
        return (t.name, off, off + span)

    def _collect(self, engine, reads, writes):
        deps = {}

        def want(dep):
            kind, key, val = dep
            k = (kind, key)
            if deps.get(k, -1) < val:
                deps[k] = val

        ivs_r = [self._iv(a) for a in reads]
        ivs_w = [self._iv(a) for a in writes]
        for name, lo, hi in ivs_r:
            ent = self.res.get(name)
            if ent:
                for a, b, dep in ent["w"]:
                    if a < hi and lo < b:
                        want(dep)
        for name, lo, hi in ivs_w:
            ent = self.res.get(name)
            if ent:
                for a, b, dep in ent["w"]:
                    if a < hi and lo < b:
                        want(dep)
                for a, b, dep in ent["r"]:
                    if a < hi and lo < b:
                        want(dep)
        return deps, ivs_r, ivs_w

    def _register(self, ivs_r, ivs_w, token):
        for name, lo, hi in ivs_w:
            ent = self.res.setdefault(name, {"w": [], "r": []})
            ent["w"] = [(a, b, d) for (a, b, d) in ent["w"] if not (lo <= a and b <= hi)]
            ent["r"] = [(a, b, d) for (a, b, d) in ent["r"] if not (lo <= a and b <= hi)]
            ent["w"].append((lo, hi, token))
        for name, lo, hi in ivs_r:
            self.res.setdefault(name, {"w": [], "r": []})["r"].append((lo, hi, token))

    def _mk_waits(self, engine, deps):
        waits = []
        for (kind, key), val in sorted(deps.items(), key=lambda kv: str(kv[0])):
            if val <= 0:
                continue
            sem = self.esem[key] if kind == "E" else self.dsem[key][0]
            prev = self.waited[engine].get((kind, key), -1)
            if val > prev:
                waits.append((sem, val))
                self.waited[engine][(kind, key)] = val
        return waits

    def add(self, engine, fn, reads=(), writes=(), tag=""):
        deps, ivs_r, ivs_w = self._collect(engine, reads, writes)
        waits = self._mk_waits(engine, deps)
        self.ccnt[engine] += 1
        token = ("E", engine, self.ccnt[engine])
        self.tasks[engine].append((waits, fn, tag, False))
        self._register(ivs_r, ivs_w, token)
        return token

    def dma(self, queue, semname, pairs, tag=""):
        if semname not in self.dsem:
            self.dsem[semname] = [self.nc.alloc_semaphore("dsem_" + semname), 0]
        ent = self.dsem[semname]
        prev_cum = ent[1]
        reads = [pp[1] for pp in pairs]
        writes = [pp[0] for pp in pairs]
        deps, ivs_r, ivs_w = self._collect(queue, reads, writes)
        if prev_cum > 0 and deps.get(("D", semname), -1) < prev_cum:
            deps[("D", semname)] = prev_cum
        waits = self._mk_waits(queue, deps)
        ent[1] += 16 * len(pairs)
        token = ("D", semname, ent[1])
        sem = ent[0]

        def fn(eng, pairs=pairs, sem=sem):
            for (o, i) in pairs:
                eng.dma_start(out=o, in_=i).then_inc(sem, 16)
            return None

        self.tasks[queue].append((waits, fn, tag, True))
        self._register(ivs_r, ivs_w, token)
        return token

    def barrier(self, exclude_sems=()):
        targets = {("E", e): self.ccnt[e] for e in self.esem if self.ccnt[e] > 0}
        targets.update({("D", n): ent[1] for n, ent in self.dsem.items()
                        if ent[1] > 0 and n not in exclude_sems})
        for e in self.ENG:
            waits = self._mk_waits(e, targets)
            if waits:
                def fn(eng, waits=waits):
                    return None
                self.tasks[e].append((waits, fn, "barrier", True))
        self.res = {}

    def final_wait(self, engine="sp"):
        targets = {("D", n): ent[1] for n, ent in self.dsem.items() if ent[1] > 0}
        targets.update({("E", e): self.ccnt[e] for e in self.esem if self.ccnt[e] > 0})
        waits = self._mk_waits(engine, targets)

        def fn(eng):
            return None
        self.tasks[engine].append((waits, fn, "final", True))

    def emit(self):
        with self.nc.Block() as block:
            for e, attr in self.ENG.items():
                dec = getattr(block, attr)
                tasks = self.tasks[e]
                esem = self.esem.get(e)

                def body(eng, tasks=tasks, esem=esem, ename=e):
                    for (waits, fn, tag, is_dma) in tasks:
                        for (sem, val) in waits:
                            eng.wait_ge(sem, val)
                        last = fn(eng)
                        if not is_dma:
                            assert last is not None, \
                                f"compute task {tag} on {ename} returned None"
                            last.then_inc(esem, 1)
                dec(body)



def _tile_km(w, granule):
    K, M = w.shape
    kc = K // 128
    nb = M // granule
    out = w.reshape(kc, 128, nb, granule).transpose(2, 1, 0, 3)
    return np.ascontiguousarray(out.reshape(nb, 128, kc * granule))


def _col_pack(v, nchunk):
    return np.ascontiguousarray(np.asarray(v, np.float32).reshape(nchunk, 128).T)


def _perm_order():
    r = np.arange(512)
    return np.concatenate([2 * r, 2 * r + 1])


def prepare_inputs(inputs):
    f = np.float32
    d = {}
    perm = _perm_order()

    j = np.arange(512, dtype=np.float64)
    inv = 1.0 / np.power(10000.0, 2.0 * j / 1024.0)
    pos = np.arange(T, dtype=np.float64)
    ang = pos[None, :] * inv[:, None]
    d["cosT"] = np.cos(ang).astype(f)
    d["sinT"] = np.sin(ang).astype(f)

    d["eye128"] = np.eye(128, dtype=f)
    d["eps_col"] = np.broadcast_to(np.array([[EPS_OUT, EPS_IN]], f), (128, 2)).copy()
    d["ones_col"] = np.ones((128, 1), dtype=f)
    d["ones_row"] = np.ones((1, 128), dtype=f)

    ebc = np.zeros((16, CH * 128), dtype=f)
    for c in range(CH):
        for m in range(128):
            ebc[2 * c + m // 64, 128 * c + m] = 1.0
    d["Ebc"] = ebc

    Wq = np.asarray(inputs["L0_attn_Wq"], dtype=f)[:, :, perm]
    Wk = np.asarray(inputs["L0_attn_Wk"], dtype=f)[:, :, perm]
    Wv = np.asarray(inputs["L0_attn_Wv"], dtype=f)
    d["wq_t"] = np.stack([_tile_km(Wq[l], 512) for l in range(NL)])
    d["wk_t"] = np.stack([_tile_km(Wk[l], 512) for l in range(NL)])
    d["wv_t"] = np.stack([_tile_km(Wv[l], 512) for l in range(NL)])

    bq = np.asarray(inputs["L0_attn_bq"], dtype=f)[:, perm]
    bk = np.asarray(inputs["L0_attn_bk"], dtype=f)[:, perm]
    d["bq_col"] = np.ascontiguousarray(
        np.stack([_col_pack(bq[l], CH) for l in range(NL)], axis=1).reshape(128, NL * CH))
    d["bk_col"] = np.ascontiguousarray(
        np.stack([_col_pack(bk[l], CH) for l in range(NL)], axis=1).reshape(128, NL * CH))

    bv = np.asarray(inputs["L0_attn_bv"], dtype=f)
    bvr = np.zeros((64, NL * C), dtype=f)
    for l in range(NL):
        bvr[0, C * l:C * (l + 1)] = bv[l]
        bvr[32, C * l:C * (l + 1)] = bv[l]
    d["bv_rows"] = bvr

    d["lnAg_col"] = np.ascontiguousarray(
        np.stack([_col_pack(np.asarray(inputs["L0_attn_lnA_g"], f)[l], CH)
                  for l in range(NL)], axis=1).reshape(128, NL * CH))
    d["lnAb_col"] = np.ascontiguousarray(
        np.stack([_col_pack(np.asarray(inputs["L0_attn_lnA_b"], f)[l], CH)
                  for l in range(NL)], axis=1).reshape(128, NL * CH))
    d["ln0g_col"] = _col_pack(inputs["L0_attn_ln0_g"], CH)
    d["ln0b_col"] = _col_pack(inputs["L0_attn_ln0_b"], CH)
    d["ln2g_col"] = _col_pack(inputs["L0_attn_ln2_g"], CH)
    d["ln2b_col"] = _col_pack(inputs["L0_attn_ln2_b"], CH)

    w1r = np.asarray(inputs["L0_attn_conv1_w"], f).reshape(C, 4, 5)
    w2r = np.asarray(inputs["L0_attn_conv2_w"], f)
    dia1 = np.zeros((CH, 128, 20 * 128), dtype=f)
    dia2 = np.zeros((CH, 128, 20 * 128), dtype=f)
    idx = np.arange(128)
    for c in range(CH):
        for jj in range(4):
            for t5 in range(5):
                col = 128 * (5 * jj + t5)
                dia1[c, idx, col + idx] = w1r[128 * c + idx, jj, t5]
                dia2[c, idx, col + idx] = w2r[128 * c + idx, jj, t5]
    d["conv1d"] = dia1
    d["conv2d"] = dia2
    d["cb1_col"] = np.ascontiguousarray(
        np.asarray(inputs["L0_attn_conv1_b"], f).reshape(CH, 128, 4)
        .transpose(1, 0, 2).reshape(128, CH * 4))
    d["cb2_col"] = _col_pack(inputs["L0_attn_conv2_b"], CH)

    d["W1_t"] = _tile_km(np.asarray(inputs["L0_attn_W1"], f), 512)
    d["W2_t"] = _tile_km(np.asarray(inputs["L0_attn_W2"], f), 1024)[0]
    d["b1_col"] = _col_pack(inputs["L0_attn_b1"], 32)
    d["b2_col"] = _col_pack(inputs["L0_attn_b2"], CH)

    d["f1_t"] = _tile_km(np.ascontiguousarray(np.asarray(inputs["L0_ffn_w1_w"], f).T), 512)
    d["f2_t"] = _tile_km(np.ascontiguousarray(np.asarray(inputs["L0_ffn_w2_w"], f).T), 512)
    d["h1_t"] = _tile_km(np.ascontiguousarray(np.asarray(inputs["L0_ffn_h1_w"], f).T), 128)
    d["h2_t"] = _tile_km(np.ascontiguousarray(np.asarray(inputs["L0_ffn_h2_w"], f).T), 128)
    d["w3_t"] = _tile_km(np.ascontiguousarray(np.asarray(inputs["L0_ffn_w3_w"], f).T), 128)
    d["s1_col"] = _col_pack(inputs["L0_ffn_w1_s"], 32)
    d["s2_col"] = _col_pack(inputs["L0_ffn_w2_s"], 32)
    d["h1s_col"] = _col_pack(inputs["L0_ffn_h1_s"], 32)
    d["h2s_col"] = _col_pack(inputs["L0_ffn_h2_s"], 32)
    d["s3_col"] = _col_pack(inputs["L0_ffn_w3_s"], CH)
    return d



class MK:
    def __init__(self, nc):
        self.nc = nc
        self.p = Prog(nc)
        self._declare_dram()
        self._alloc_top()

    def _declare_dram(self):
        nc = self.nc

        def D(n, sh, dt, kind="ExternalInput"):
            return nc.dram_tensor(n, sh, dt, kind=kind).ap()

        self.d_x = D("x", [NSEQ, C, T], F32)
        self.d_cos = D("cosT", [512, T], F32)
        self.d_sin = D("sinT", [512, T], F32)
        self.d_eye = D("eye128", [128, 128], F32)
        self.d_eps = D("eps_col", [128, 2], F32)
        self.d_ones_col = D("ones_col", [128, 1], F32)
        self.d_ones_row = D("ones_row", [1, 128], F32)
        self.d_ebc = D("Ebc", [16, CH * 128], F32)
        self.d_wq = D("wq_t", [NL, 2, 128, 4096], F32)
        self.d_wk = D("wk_t", [NL, 2, 128, 4096], F32)
        self.d_wv = D("wv_t", [NL, 2, 128, 4096], F32)
        self.d_bq = D("bq_col", [128, NL * CH], F32)
        self.d_bk = D("bk_col", [128, NL * CH], F32)
        self.d_bvr = D("bv_rows", [64, NL * C], F32)
        self.d_lnAg = D("lnAg_col", [128, NL * CH], F32)
        self.d_lnAb = D("lnAb_col", [128, NL * CH], F32)
        self.d_ln0g = D("ln0g_col", [128, CH], F32)
        self.d_ln0b = D("ln0b_col", [128, CH], F32)
        self.d_ln2g = D("ln2g_col", [128, CH], F32)
        self.d_ln2b = D("ln2b_col", [128, CH], F32)
        self.d_c1d = D("conv1d", [CH, 128, 2560], F32)
        self.d_c2d = D("conv2d", [CH, 128, 2560], F32)
        self.d_cb1 = D("cb1_col", [128, CH * 4], F32)
        self.d_cb2 = D("cb2_col", [128, CH], F32)
        self.d_W1 = D("W1_t", [8, 128, 4096], F32)
        self.d_W2 = D("W2_t", [128, 32768], F32)
        self.d_b1 = D("b1_col", [128, 32], F32)
        self.d_b2 = D("b2_col", [128, CH], F32)
        self.d_f1 = D("f1_t", [8, 128, 4096], F32)
        self.d_f2 = D("f2_t", [8, 128, 4096], F32)
        self.d_h1 = D("h1_t", [32, 128, 4096], F32)
        self.d_h2 = D("h2_t", [32, 128, 4096], F32)
        self.d_w3 = D("w3_t", [8, 128, 4096], F32)
        self.d_s1 = D("s1_col", [128, 32], F32)
        self.d_s2 = D("s2_col", [128, 32], F32)
        self.d_h1s = D("h1s_col", [128, 32], F32)
        self.d_h2s = D("h2s_col", [128, 32], F32)
        self.d_s3 = D("s3_col", [128, CH], F32)
        self.d_y = D("y", [NSEQ, C, T], F32, "ExternalOutput")
        self.d_x2sp = D("x2_spill", [C, T], F32, "Internal")
        self.d_x3sp = D("x3_spill", [C, T], BF16, "Internal")

    def _alloc_top(self):
        nc = self.nc
        A = nc.alloc_sbuf_tensor
        self.ws = [A(f"ws{i}", [128, 4096], BF16) for i in range(6)]
        self.eye = A("eye", [128, 128], BF16)
        self.epsc = A("epsc", [128, 2], F32)
        self.ones_col = A("onescol", [128, 1], BF16)
        self.ones_row = A("onesrow", [1, 128], F32)
        self.ebc = A("ebc", [16, CH * 128], BF16)
        self.bvr = A("bvr", [64, C], BF16)
        self.bq = A("bqc", [128, NL * CH], F32)
        self.bk = A("bkc", [128, NL * CH], F32)
        self.lnAg = A("lnAg", [128, NL * CH], F32)
        self.lnAb = A("lnAb", [128, NL * CH], F32)
        self.ln0g = A("ln0g", [128, CH], F32)
        self.ln0b = A("ln0b", [128, CH], F32)
        self.ln2g = A("ln2g", [128, CH], F32)
        self.ln2b = A("ln2b", [128, CH], F32)
        self.cb1 = A("cb1", [128, CH * 4], F32)
        self.cb2 = A("cb2", [128, CH], F32)
        self.b1c = A("b1c", [128, 32], F32)
        self.b2c = A("b2c", [128, CH], F32)
        self.s1c = A("s1c", [128, 32], F32)
        self.s2c = A("s2c", [128, 32], F32)
        self.h1sc = A("h1sc", [128, 32], F32)
        self.h2sc = A("h2sc", [128, 32], F32)
        self.s3c = A("s3c", [128, CH], F32)
        self.m_row = A("m_row", [1, 512], F32)
        self.v_row = A("v_row", [1, 512], F32)
        self.r_row = A("r_row", [1, 512], F32)
        self.km_tmp = A("km_tmp", [128, 1], F32)
        self.km_acc = A("km_acc", [128, CH], F32)
        self.km_mat = A("km_mat", [128, CH * 16], BF16)
        self.krs = A("krs", [128, 1024], BF16)
        self.z_sb = A("z_sb", [16, 512], F32)
        self.z_bf = A("z_bf", [16, 512], BF16)
        self.kvb = A("kvb", [128, 16 * 128], BF16)
        P = nc.alloc_psum_tensor
        self.ps = [P(f"ps{i}", [128, 512], F32) for i in range(6)]
        self.pt = P("pt", [128, 512], BF16)
        self.kvp = P("kvp", [128, 512], F32)

    def load_constants(self):
        p = self.p
        p.dma("pool", "consts_c", [
            (self.eye[:, :], self.d_eye),
            (self.ones_col[:, :], self.d_ones_col),
            (self.ebc[:, :], self.d_ebc),
        ], tag="consts_cast")
        p.dma("sp", "consts_p", [
            (self.ones_row[:, :], self.d_ones_row),
            (self.epsc[:, :], self.d_eps),
            (self.bq[:, :], self.d_bq), (self.bk[:, :], self.d_bk),
            (self.lnAg[:, :], self.d_lnAg), (self.lnAb[:, :], self.d_lnAb),
            (self.ln0g[:, :], self.d_ln0g), (self.ln0b[:, :], self.d_ln0b),
            (self.ln2g[:, :], self.d_ln2g), (self.ln2b[:, :], self.d_ln2b),
            (self.cb1[:, :], self.d_cb1), (self.cb2[:, :], self.d_cb2),
            (self.b1c[:, :], self.d_b1), (self.b2c[:, :], self.d_b2),
            (self.s1c[:, :], self.d_s1), (self.s2c[:, :], self.d_s2),
            (self.h1sc[:, :], self.d_h1s), (self.h2sc[:, :], self.d_h2s),
            (self.s3c[:, :], self.d_s3),
        ], tag="consts_plain")

    def emit_ln(self, items, eps, gamma=None, beta=None, dst_bf=None,
                bf_out=False, scratch=None, tag="ln"):
        p = self.p
        ps_st, ps_m, ps_r = self.ps[2], self.ps[3], self.ps[4]
        srcs = [it[0] for it in items]
        bfs = [it[1] for it in items]
        dsts = [it[2] for it in items]

        def mm_sum(t):
            for c in range(CH):
                inst = t.matmul(ps_st[0:1, :], self.ones_col[:, :], bfs[c],
                                start=(c == 0), stop=False, skip_group_check=True)
            return inst
        p.add("pe", mm_sum, reads=bfs + [self.ones_col[:, :]],
              writes=[ps_st[:, :]], tag=tag + "_mmsum")
        for c in range(CH):
            p.add("act", lambda a, c=c: a.activation(bfs[c], bfs[c], AF.Square),
                  reads=[bfs[c]], writes=[bfs[c]], tag=tag + "_sq")

        def mm_sq(t):
            for c in range(CH):
                inst = t.matmul(ps_st[64:65, :], self.ones_col[:, :], bfs[c],
                                start=(c == 0), stop=(c == CH - 1),
                                tile_position=(0, 64), skip_group_check=True)
            return inst
        p.add("pe", mm_sq, reads=bfs + [self.ones_col[:, :]],
              writes=[ps_st[:, :]], tag=tag + "_mmsq")

        p.add("dve", lambda v: v.tensor_scalar_mul(self.m_row[:, :], ps_st[0:1, :], 1.0 / C),
              reads=[ps_st[:, :]], writes=[self.m_row[:, :]], tag=tag + "_m")
        p.add("dve", lambda v: v.tensor_tensor(self.v_row[:, :], self.m_row[:, :],
                                               self.m_row[:, :], ALU.mult),
              reads=[self.m_row[:, :]], writes=[self.v_row[:, :]], tag=tag + "_m2")
        p.add("dve", lambda v: v.scalar_tensor_tensor(self.v_row[:, :], ps_st[64:65, :],
                                                      1.0 / C, self.v_row[:, :],
                                                      ALU.mult, ALU.subtract),
              reads=[ps_st[:, :], self.v_row[:, :]], writes=[self.v_row[:, :]],
              tag=tag + "_var")
        ecol = self.epsc[0:1, 0:1] if eps == EPS_OUT else self.epsc[0:1, 1:2]
        p.add("act", lambda a, ecol=ecol: a.activation(self.r_row[:, :], self.v_row[:, :],
                                                       AF.Sqrt, bias=ecol, scale=1.0),
              reads=[self.v_row[:, :], ecol], writes=[self.r_row[:, :]], tag=tag + "_sd")
        p.add("dve", lambda v: v.reciprocal(self.r_row[:, :], self.r_row[:, :]),
              reads=[self.r_row[:, :]], writes=[self.r_row[:, :]], tag=tag + "_rstd")
        p.add("pe", lambda t: t.matmul(ps_m[:, :], self.ones_row[:, :], self.m_row[:, :],
                                       start=True, stop=True, skip_group_check=True),
              reads=[self.m_row[:, :], self.ones_row[:, :]], writes=[ps_m[:, :]],
              tag=tag + "_bm")
        p.add("pe", lambda t: t.matmul(ps_r[:, :], self.ones_row[:, :], self.r_row[:, :],
                                       start=True, stop=True, skip_group_check=True),
              reads=[self.r_row[:, :], self.ones_row[:, :]], writes=[ps_r[:, :]],
              tag=tag + "_br")

        for c in range(CH):
            g = gamma[c] if gamma is not None else None
            b = beta[c] if beta is not None else None

            if bf_out:
                scr = scratch[c]
                p.add("dve", lambda v, c=c, scr=scr: v.tensor_tensor(
                          scr, srcs[c], ps_m[:, :], ALU.subtract),
                      reads=[srcs[c], ps_m[:, :]], writes=[scr], tag=tag + "_a1")
                p.add("dve", lambda v, c=c, scr=scr: v.scalar_tensor_tensor(
                          dsts[c], scr, 1.0, ps_r[:, :], ALU.mult, ALU.mult),
                      reads=[scr, ps_r[:, :]], writes=[dsts[c]], tag=tag + "_a2")
            else:
                p.add("dve", lambda v, c=c: v.tensor_tensor(
                          dsts[c], srcs[c], ps_m[:, :], ALU.subtract),
                      reads=[srcs[c], ps_m[:, :]], writes=[dsts[c]], tag=tag + "_a1")
                if g is not None:
                    p.add("dve", lambda v, c=c, g=g: v.scalar_tensor_tensor(
                              dsts[c], dsts[c], g, ps_r[:, :], ALU.mult, ALU.mult),
                          reads=[dsts[c], g, ps_r[:, :]], writes=[dsts[c]],
                          tag=tag + "_a2")
                    p.add("act", lambda a, c=c, b=b: a.activation(
                              dsts[c], dsts[c], AF.Identity, bias=b, scale=1.0),
                          reads=[dsts[c], b], writes=[dsts[c]], tag=tag + "_a3")
                else:
                    p.add("dve", lambda v, c=c: v.tensor_tensor(
                              dsts[c], dsts[c], ps_r[:, :], ALU.mult),
                          reads=[dsts[c], ps_r[:, :]], writes=[dsts[c]],
                          tag=tag + "_a2")
                if dst_bf is not None:
                    p.add("act", lambda a, c=c: a.copy(dst_bf[c], dsts[c]),
                          reads=[dsts[c]], writes=[dst_bf[c]], tag=tag + "_a4")


def MM(t, out, lhsT, rhs, start, stop, tp=None):
    return t.matmul(out, lhsT, rhs, start=start, stop=stop,
                    tile_position=tp, skip_group_check=True)



def build_p0(mk, seq, b0, b0d, hbfc):
    p = mk.p
    for c in range(CH):
        p.add("pool", lambda g, c=c: g.memset(b0[c][:, 0:2], 0.0),
              writes=[b0[c][:, 0:2]], tag="padz")
        p.add("pool", lambda g, c=c: g.memset(b0[c][:, 2 + T:PADW], 0.0),
              writes=[b0[c][:, 2 + T:PADW]], tag="padz")
    p.dma("sp", "xin", [(b0d[c], mk.d_x[seq, 128 * c:128 * (c + 1), :])
                        for c in range(CH)], tag="xload")
    for nq in range(4):
        nq0 = 512 * nq
        p.dma("pool", "xbf", [(hbfc[c], mk.d_x[seq, 128 * c:128 * (c + 1), nq0:nq0 + 512])
                              for c in range(CH)], tag="xbfload")
        items = [(b0d[c][:, nq0:nq0 + 512], hbfc[c], b0d[c][:, nq0:nq0 + 512])
                 for c in range(CH)]
        mk.emit_ln(items, EPS_OUT, tag=f"ln1_{seq}_{nq}")


def build_conv(mk, seq, scope, b0, b0d, hbfc):
    nc, p = mk.nc, mk.p
    xbfc = scope.enter_context(nc.sbuf_tensor(f"xbfc_{seq}", [128, 8 * 264], BF16))
    hjb = [scope.enter_context(nc.sbuf_tensor(f"hjb{i}_{seq}", [128, 4 * PADW], BF16))
           for i in range(2)]

    for c in range(CH):
        s1 = mk.ws[c % 2]
        s2 = mk.ws[2 + c % 2]
        p.dma("pool", f"ws{c % 2}", [(s1[:, 0:2560], mk.d_c1d[c, :, :])], tag="c1d")
        p.dma("pool", f"ws{2 + c % 2}", [(s2[:, 0:2560], mk.d_c2d[c, :, :])], tag="c2d")
        hj = hjb[c % 2]
        for jj in range(4):
            p.add("pool", lambda g, jj=jj, hj=hj: g.memset(hj[:, PADW * jj:PADW * jj + 2], 0.0),
                  writes=[hj[:, PADW * jj:PADW * jj + 2]], tag="hjz")
            p.add("pool", lambda g, jj=jj, hj=hj: g.memset(hj[:, PADW * jj + 2 + T:PADW * (jj + 1)], 0.0),
                  writes=[hj[:, PADW * jj + 2 + T:PADW * (jj + 1)]], tag="hjz")
        for tk in range(8):
            w = xbfc[:, 264 * tk:264 * tk + 260]
            p.add("act", lambda a, c=c, tk=tk, w=w: a.copy(
                      w, b0[c][:, 256 * tk:256 * tk + 260]),
                  reads=[b0[c][:, 256 * tk:256 * tk + 260]], writes=[w], tag="xc")
            for jj in range(4):
                psq = mk.ps[(tk * 4 + jj) % 2]

                def mm1(t, s1=s1, jj=jj, tk=tk, psq=psq):
                    for t5 in range(5):
                        inst = MM(t, psq[:, 0:256], s1[:, 128 * (5 * jj + t5):128 * (5 * jj + t5 + 1)],
                                  xbfc[:, 264 * tk + t5:264 * tk + t5 + 256],
                                  start=(t5 == 0), stop=(t5 == 4))
                    return inst
                p.add("pe", mm1, reads=[s1[:, 0:2560], w], writes=[psq[:, :]], tag="c1mm")
                p.add("act", lambda a, hj=hj, jj=jj, tk=tk, psq=psq, c=c: a.activation(
                          hj[:, PADW * jj + 2 + 256 * tk:PADW * jj + 2 + 256 * tk + 256],
                          psq[:, 0:256], AF.Gelu,
                          bias=mk.cb1[:, 4 * c + jj:4 * c + jj + 1], scale=1.0),
                      reads=[psq[:, :], mk.cb1[:, 4 * c + jj:4 * c + jj + 1]],
                      writes=[hj[:, PADW * jj + 2 + 256 * tk:PADW * jj + 2 + 256 * tk + 256]],
                      tag="c1ev")
        for tk in range(8):
            psq = mk.ps[2 + tk % 2]

            def mm2(t, s2=s2, tk=tk, psq=psq, hj=hj):
                first = True
                for jj in range(4):
                    for t5 in range(5):
                        inst = MM(t, psq[:, 0:256], s2[:, 128 * (5 * jj + t5):128 * (5 * jj + t5 + 1)],
                                  hj[:, PADW * jj + 256 * tk + t5:PADW * jj + 256 * tk + t5 + 256],
                                  start=first, stop=(jj == 3 and t5 == 4))
                        first = False
                return inst
            p.add("pe", mm2, reads=[s2[:, 0:2560], hj[:, :]], writes=[psq[:, :]], tag="c2mm")
            sl = slice(256 * tk, 256 * tk + 256)
            p.add("dve", lambda v, c=c, psq=psq, sl=sl: v.scalar_tensor_tensor(
                      b0d[c][:, sl], psq[:, 0:256], mk.cb2[:, c:c + 1], b0d[c][:, sl],
                      ALU.add, ALU.add),
                  reads=[psq[:, :], mk.cb2[:, c:c + 1], b0d[c][:, sl]],
                  writes=[b0d[c][:, sl]], tag="c2ev")
    for nq in range(4):
        nq0 = 512 * nq
        for c in range(CH):
            p.add("act", lambda a, c=c, nq0=nq0: a.copy(
                      hbfc[c], b0d[c][:, nq0:nq0 + 512]),
                  reads=[b0d[c][:, nq0:nq0 + 512]], writes=[hbfc[c]], tag="ln0c")
        items = [(b0d[c][:, nq0:nq0 + 512], hbfc[c], b0d[c][:, nq0:nq0 + 512])
                 for c in range(CH)]
        mk.emit_ln(items, EPS_IN,
                   gamma=[mk.ln0g[:, c:c + 1] for c in range(CH)],
                   beta=[mk.ln0b[:, c:c + 1] for c in range(CH)],
                   tag=f"ln0_{seq}_{nq}")


def build_attn_layer(mk, seq, l, b0d, hbfc2, bufs, upto=None):
    p = mk.p
    cosc, sinc = bufs["cosc"], bufs["sinc"]
    ksc, krc = bufs["ksc"], bufs["krc"]
    tmp1 = bufs["tmp1"]
    krt, vt = bufs["krt"], bufs["vt"]
    zbs = bufs["zbs"]
    ws = mk.ws

    p.dma("pool", "bvl", [(mk.bvr[:, :], mk.d_bvr[:, C * l:C * (l + 1)])], tag=f"bv{l}")
    for b in range(2):
        p.dma("pool", f"ws{b}", [(ws[b][:, :], mk.d_wk[l, b, :, :])], tag=f"wk{l}")
        p.dma("pool", f"ws{2 + b}", [(ws[2 + b][:, :], mk.d_wv[l, b, :, :])], tag=f"wv{l}")
        p.dma("pool", f"ws{4 + b}", [(ws[4 + b][:, :], mk.d_wq[l, b, :, :])], tag=f"wq{l}")

    def hb(c, par):
        return hbfc2[(par % 2) * CH + c]

    def cast_h(nq0, tag, par):
        for c in range(CH):
            p.add("act", lambda a, c=c, nq0=nq0, par=par: a.copy(
                      hb(c, par), b0d[c][:, nq0:nq0 + 512]),
                  reads=[b0d[c][:, nq0:nq0 + 512]], writes=[hb(c, par)], tag=tag)

    def proj(wslots, biascol, dst, tag, par):
        for mc in range(CH):
            wsl = wslots[0] if mc < 4 else wslots[1]
            mloc = mc % 4
            psq = mk.ps[mc % 2]

            def mmp(t, wsl=wsl, mloc=mloc, psq=psq, par=par):
                for kc in range(CH):
                    inst = MM(t, psq[:, :], wsl[:, 512 * kc + 128 * mloc:512 * kc + 128 * mloc + 128],
                              hb(kc, par), start=(kc == 0), stop=(kc == CH - 1))
                return inst
            p.add("pe", mmp, reads=[wsl[:, :]] + [hb(kc, par) for kc in range(CH)],
                  writes=[psq[:, :]], tag=tag + "mm")
            p.add("act", lambda a, mc=mc, psq=psq: a.activation(
                      dst[:, 512 * mc:512 * mc + 512], psq[:, :], AF.Gelu,
                      bias=biascol[:, CH * l + mc:CH * l + mc + 1], scale=1.0),
                  reads=[psq[:, :], biascol[:, CH * l + mc:CH * l + mc + 1]],
                  writes=[dst[:, 512 * mc:512 * mc + 512]], tag=tag + "ev")
        for mc in range(CH):
            p.add("dve", lambda v, mc=mc: v.tensor_scalar_add(
                      dst[:, 512 * mc:512 * mc + 512], dst[:, 512 * mc:512 * mc + 512],
                      0.21),
                  reads=[dst[:, 512 * mc:512 * mc + 512]],
                  writes=[dst[:, 512 * mc:512 * mc + 512]], tag=tag + "21")

    def rope(src, dst, nq0, tag):
        for pc in range(4):
            cw = cosc[pc][:, nq0:nq0 + 512]
            sw = sinc[pc][:, nq0:nq0 + 512]
            se = src[:, 512 * pc:512 * pc + 512]
            so = src[:, 512 * (pc + 4):512 * (pc + 4) + 512]
            de = dst[:, 512 * pc:512 * pc + 512]
            do = dst[:, 512 * (pc + 4):512 * (pc + 4) + 512]

            te = tmp1[:, 0:512]
            to = tmp1[:, 512:1024]
            p.add("dve", lambda v, so=so, sw=sw, te=te: v.tensor_tensor(
                      te, so, sw, ALU.mult),
                  reads=[so, sw], writes=[te], tag=tag + "re1")
            p.add("dve", lambda v, se=se, cw=cw, de=de: v.tensor_tensor(
                      de, se, cw, ALU.mult),
                  reads=[se, cw], writes=[de], tag=tag + "re2")
            p.add("dve", lambda v, de=de, te=te: v.tensor_tensor(
                      de, de, te, ALU.subtract),
                  reads=[de, te], writes=[de], tag=tag + "re3")
            p.add("pool", lambda g, se=se, sw=sw, to=to: g.tensor_tensor(
                      to, se, sw, ALU.mult),
                  reads=[se, sw], writes=[to], tag=tag + "ro1")
            p.add("pool", lambda g, so=so, cw=cw, do=do: g.tensor_tensor(
                      do, so, cw, ALU.mult),
                  reads=[so, cw], writes=[do], tag=tag + "ro2")
            p.add("pool", lambda g, do=do, to=to: g.tensor_tensor(
                      do, do, to, ALU.add),
                  reads=[do, to], writes=[do], tag=tag + "ro3")

    def kscv(par):
        return ksc[:, 0:CH * 512]

    for nq in range(4):
        nq0 = 512 * nq
        par = nq % 2
        ksq = kscv(par)
        cast_h(nq0, f"a{l}kc", par)
        proj((ws[0], ws[1]), mk.bk, ksq, f"a{l}k", par)
        for c in range(CH):
            p.add("dve", lambda v, c=c, ksq=ksq: v.reduce_sum(
                      mk.km_tmp[:, :], ksq[:, 512 * c:512 * c + 512], axis=AX.X),
                  reads=[ksq[:, 512 * c:512 * c + 512]], writes=[mk.km_tmp[:, :]],
                  tag=f"a{l}km1")
            if nq == 0:
                p.add("dve", lambda v, c=c: v.tensor_copy(
                          out=mk.km_acc[:, c:c + 1], in_=mk.km_tmp[:, :]),
                      reads=[mk.km_tmp[:, :]], writes=[mk.km_acc[:, c:c + 1]],
                      tag=f"a{l}km2")
            else:
                p.add("dve", lambda v, c=c: v.tensor_tensor(
                          mk.km_acc[:, c:c + 1], mk.km_acc[:, c:c + 1],
                          mk.km_tmp[:, :], ALU.add),
                      reads=[mk.km_tmp[:, :], mk.km_acc[:, c:c + 1]],
                      writes=[mk.km_acc[:, c:c + 1]], tag=f"a{l}km2")
        if upto == "kproj":
            return True
        rope(ksq, krc, nq0, f"a{l}rk")
        if upto == "krope":
            return True
        for g in range(4):
            for half in range(2):
                def tr(t, half=half, g=g):
                    for pp in range(4):
                        ci = 4 * half + pp
                        inst = t.transpose(mk.pt[:, 128 * pp:128 * pp + 128],
                                           krc[:, 512 * ci + 128 * g:512 * ci + 128 * g + 128],
                                           mk.eye[:, :])
                    return inst
                p.add("pe", tr, reads=[krc[:, :], mk.eye[:, :]], writes=[mk.pt[:, :]],
                      tag=f"a{l}tr")
                p.add("act", lambda a, half=half: a.copy(
                          krt[:, 512 * half:512 * half + 512], mk.pt[:, :]),
                      reads=[mk.pt[:, :]], writes=[krt[:, 512 * half:512 * half + 512]],
                      tag=f"a{l}trev")
            for vh in range(2):
                wsl = ws[2 + vh]
                psq = mk.ps[3 + vh]

                def mmv(t, wsl=wsl, g=g, psq=psq, par=par):
                    for kc in range(CH):
                        inst = MM(t, psq[:, :], hb(kc, par)[:, 128 * g:128 * g + 128],
                                  wsl[:, 512 * kc:512 * kc + 512],
                                  start=(kc == 0), stop=(kc == CH - 1))
                    return inst
                p.add("pe", mmv, reads=[wsl[:, :]] + [hb(kc, par) for kc in range(CH)],
                      writes=[psq[:, :]], tag=f"a{l}vmm")
                p.add("act", lambda a, vh=vh, psq=psq: a.copy(
                          vt[:, 512 * vh:512 * vh + 512], psq[:, :]),
                      reads=[psq[:, :]], writes=[vt[:, 512 * vh:512 * vh + 512]],
                      tag=f"a{l}vev")
            def mmkv(t, nq=nq, g=g):
                for h in range(NH):
                    rb = (32 * h) % 128
                    for half in range(2):
                        inst = MM(t, mk.kvp[rb:rb + 32, (h // 4) * 128 + 64 * half:
                                            (h // 4) * 128 + 64 * half + 64],
                                  krt[:, 512 * half + 32 * h:512 * half + 32 * h + 32],
                                  vt[:, 64 * h:64 * h + 64],
                                  start=(nq == 0 and g == 0 and h < 4 and half == 0),
                                  stop=False, tp=(0, rb))
                return inst
            p.add("pe", mmkv, reads=[krt[:, :], vt[:, :]], writes=[mk.kvp[:, :]],
                  tag=f"a{l}kv")
            def mmks(t, nq=nq, g=g):
                st = (nq == 0 and g == 0)
                MM(t, mk.ps[5][0:1, :], mk.ones_col[:, :], krt[:, 0:512],
                   start=st, stop=False)
                return MM(t, mk.ps[5][32:33, :], mk.ones_col[:, :], krt[:, 512:1024],
                          start=st, stop=False, tp=(0, 32))
            p.add("pe", mmks, reads=[krt[:, :], mk.ones_col[:, :]],
                  writes=[mk.ps[5][:, :]], tag=f"a{l}ks")
            if upto == "ktile" and g == 0:
                return True

    if upto == "kvtiles":
        return True
    def ksev(a):
        a.copy(mk.krs[0:1, 0:512], mk.ps[5][0:1, :])
        return a.copy(mk.krs[32:33, 0:512], mk.ps[5][32:33, :])
    p.add("act", ksev, reads=[mk.ps[5][:, :]], writes=[mk.krs[:, :]], tag=f"a{l}ksev")
    p.dma("sp", "krsmv", [(mk.krs[0:1, 512:1024], mk.krs[32:33, 0:512])],
          tag=f"a{l}krsmv")
    if upto == "ksev":
        return True

    def mmbv(t):
        for h in range(NH):
            rb = (32 * h) % 128
            for half in range(2):
                inst = MM(t, mk.kvp[rb:rb + 32, (h // 4) * 128 + 64 * half:
                                    (h // 4) * 128 + 64 * half + 64],
                          mk.krs[0:1, 512 * half + 32 * h:512 * half + 32 * h + 32],
                          mk.bvr[0:1, 64 * h:64 * h + 64],
                          start=False, stop=(h == NH - 1 and half == 1),
                          tp=(0, rb))
        return inst
    p.add("pe", mmbv, reads=[mk.krs[:, :], mk.bvr[:, :]], writes=[mk.kvp[:, :]],
          tag=f"a{l}bv")
    if upto == "mmbv":
        return True
    p.add("pool", lambda g: g.memset(mk.kvb[:, :], 0.0),
          writes=[mk.kvb[:, :]], tag=f"a{l}kvz")
    for Mp in range(CH):
        for half in range(2):
            base = 256 * Mp + 128 * half
            for hh in range(2):
                h = 2 * Mp + hh
                rb = (32 * h) % 128
                p.add("dve", lambda v, base=base, hh=hh, h=h, rb=rb, half=half:
                      v.tensor_scalar_mul(
                          mk.kvb[rb:rb + 32, base + 64 * hh:base + 64 * hh + 64],
                          mk.kvp[rb:rb + 32, (h // 4) * 128 + 64 * half:
                                 (h // 4) * 128 + 64 * half + 64], 1.0 / T),
                      reads=[mk.kvp[:, :]],
                      writes=[mk.kvb[rb:rb + 32, base + 64 * hh:base + 64 * hh + 64]],
                      tag=f"a{l}kvev")

    if upto == "kv":
        return True
    p.add("pool", lambda g: g.memset(mk.km_mat[:, :], 0.0),
          writes=[mk.km_mat[:, :]], tag=f"a{l}kmz")
    for c in range(CH):
        cc = c % 4
        for q4 in range(4):
            h = 4 * cc + q4
            p.add("dve", lambda v, c=c, q4=q4, h=h: v.tensor_scalar_mul(
                      mk.km_mat[32 * q4:32 * q4 + 32, 16 * c + h:16 * c + h + 1],
                      mk.km_acc[32 * q4:32 * q4 + 32, c:c + 1], 1.0 / T),
                  reads=[mk.km_acc[:, c:c + 1]],
                  writes=[mk.km_mat[:, 16 * c:16 * c + 16]], tag=f"a{l}kmb")

    for nq in range(4):
        nq0 = 512 * nq
        par = nq % 2
        ksq = kscv(par)
        cast_h(nq0, f"a{l}qc", par)
        proj((ws[4], ws[5]), mk.bq, ksq, f"a{l}q", par)
        def mmz(t, ksq=ksq):
            for c in range(CH):
                inst = MM(t, mk.ps[5][0:16, :], mk.km_mat[:, 16 * c:16 * c + 16],
                          ksq[:, 512 * c:512 * c + 512],
                          start=(c == 0), stop=(c == CH - 1))
            return inst
        p.add("pe", mmz, reads=[mk.km_mat[:, :]] + [ksq[:, 512 * c:512 * c + 512] for c in range(CH)],
              writes=[mk.ps[5][:, :]], tag=f"a{l}z")

        p.add("dve", lambda v: v.tensor_scalar_add(mk.z_sb[:, :], mk.ps[5][0:16, :], 1e-6),
              reads=[mk.ps[5][:, :]], writes=[mk.z_sb[:, :]], tag=f"a{l}zc1")
        p.add("dve", lambda v: v.reciprocal(mk.z_sb[:, :], mk.z_sb[:, :]),
              reads=[mk.z_sb[:, :]], writes=[mk.z_sb[:, :]], tag=f"a{l}zc2")
        p.add("dve", lambda v: v.tensor_copy(out=mk.z_bf[:, :], in_=mk.z_sb[:, :]),
              reads=[mk.z_sb[:, :]], writes=[mk.z_bf[:, :]], tag=f"a{l}zc3")
        if upto == "zq":
            return True
        rope(ksq, krc, nq0, f"a{l}rq")
        for ec in range(CH):
            pso = mk.ps[ec % 2]

            def mmo(t, ec=ec, pso=pso):
                for half in range(2):
                    ck = 4 * half + ec // 2
                    inst = MM(t, pso[:, :],
                              mk.kvb[:, 256 * ec + 128 * half:256 * ec + 128 * half + 128],
                              krc[:, 512 * ck:512 * ck + 512],
                              start=(half == 0), stop=(half == 1))
                return inst
            p.add("pe", mmo, reads=[mk.kvb[:, :], krc[:, :]], writes=[pso[:, :]],
                  tag=f"a{l}o")
            zbp = mk.ps[2 + ec % 2]
            p.add("pe", lambda t, ec=ec, zbp=zbp: MM(
                      t, zbp[:, :], mk.ebc[:, 128 * ec:128 * ec + 128], mk.z_bf[:, :],
                      start=True, stop=True),
                  reads=[mk.ebc[:, :], mk.z_bf[:, :]], writes=[zbp[:, :]], tag=f"a{l}zb")
            p.add("dve", lambda v, zbp=zbp: v.tensor_copy(out=zbs[:, :], in_=zbp[:, :]),
                  reads=[zbp[:, :]], writes=[zbs[:, :]], tag=f"a{l}zbe")
            if upto == "out" and ec == 0:
                return True

            p.add("dve", lambda v, pso=pso: v.tensor_tensor(
                      zbs[:, :], pso[:, :], zbs[:, :], ALU.mult),
                  reads=[pso[:, :], zbs[:, :]], writes=[zbs[:, :]], tag=f"a{l}sm1")
            p.add("dve", lambda v, ec=ec, nq0=nq0: v.tensor_tensor(
                      b0d[ec][:, nq0:nq0 + 512], b0d[ec][:, nq0:nq0 + 512],
                      zbs[:, :], ALU.add),
                  reads=[zbs[:, :], b0d[ec][:, nq0:nq0 + 512]],
                  writes=[b0d[ec][:, nq0:nq0 + 512]], tag=f"a{l}sm2")
        for c in range(CH):
            p.add("act", lambda a, c=c, nq0=nq0, par=par: a.copy(
                      hb(c, par), b0d[c][:, nq0:nq0 + 512]),
                  reads=[b0d[c][:, nq0:nq0 + 512]], writes=[hb(c, par)],
                  tag=f"a{l}lc")
        items = [(b0d[c][:, nq0:nq0 + 512], hb(c, par), b0d[c][:, nq0:nq0 + 512])
                 for c in range(CH)]
        mk.emit_ln(items, EPS_IN,
                   gamma=[mk.lnAg[:, CH * l + c:CH * l + c + 1] for c in range(CH)],
                   beta=[mk.lnAb[:, CH * l + c:CH * l + c + 1] for c in range(CH)],
                   tag=f"lnA{l}_{seq}_{nq}")
    return False


def build_mlp(mk, seq, scope, b0d, hbfc):
    nc, p = mk.nc, mk.p
    ub = scope.enter_context(nc.sbuf_tensor(f"ub_{seq}", [128, 4 * 512], BF16))
    fsb = scope.enter_context(nc.sbuf_tensor(f"fsb_{seq}", [128, CH * 1024], F32))
    x3c = scope.enter_context(nc.sbuf_tensor(f"x3c_{seq}", [128, CH * 512], BF16))
    ws = mk.ws
    hbf_base = hbfc[0].tensor

    for nq in range(2):
        nq0 = 1024 * nq
        mbfc = [hbf_base[:, 1024 * c:1024 * (c + 1)] for c in range(CH)]
        for c in range(CH):
            p.add("act", lambda a, c=c, nq0=nq0: a.copy(
                      mbfc[c], b0d[c][:, nq0:nq0 + 1024]),
                  reads=[b0d[c][:, nq0:nq0 + 1024]], writes=[mbfc[c]], tag="mh")
        for b in range(8):
            wsl1 = ws[b % 2]
            wsl2 = ws[2 + b % 2]
            p.dma("pool", f"ws{b % 2}", [(wsl1[:, :], mk.d_W1[b, :, :])], tag="w1l")
            p.dma("pool", f"ws{2 + b % 2}",
                  [(wsl2[:, :], mk.d_W2[:, 4096 * b:4096 * b + 4096])], tag="w2l")
            for nh in range(2):
                o = 512 * nh
                for msub in range(4):
                    psq = mk.ps[msub % 2]

                    def mm1(t, wsl1=wsl1, msub=msub, psq=psq, o=o):
                        for kc in range(CH):
                            inst = MM(t, psq[:, :],
                                      wsl1[:, 512 * kc + 128 * msub:512 * kc + 128 * msub + 128],
                                      mbfc[kc][:, o:o + 512],
                                      start=(kc == 0), stop=(kc == CH - 1))
                        return inst
                    p.add("pe", mm1,
                          reads=[wsl1[:, :]] + [mbfc[kc][:, o:o + 512] for kc in range(CH)],
                          writes=[psq[:, :]], tag="w1mm")
                    hidx = 4 * b + msub
                    p.add("act", lambda a, msub=msub, psq=psq, hidx=hidx: a.activation(
                              ub[:, 512 * msub:512 * msub + 512], psq[:, :], AF.Gelu,
                              bias=mk.b1c[:, hidx:hidx + 1], scale=1.0),
                          reads=[psq[:, :], mk.b1c[:, hidx:hidx + 1]],
                          writes=[ub[:, 512 * msub:512 * msub + 512]], tag="w1ev")
                for mc in range(CH):
                    psq = mk.ps[4 + (mc % 2)]

                    def mm2(t, wsl2=wsl2, mc=mc, psq=psq):
                        for kci in range(4):
                            inst = MM(t, psq[:, :],
                                      wsl2[:, 1024 * kci + 128 * mc:1024 * kci + 128 * mc + 128],
                                      ub[:, 512 * kci:512 * kci + 512],
                                      start=(kci == 0), stop=(kci == 3))
                        return inst
                    p.add("pe", mm2, reads=[wsl2[:, :], ub[:, :]], writes=[psq[:, :]],
                          tag="w2mm")
                    fdst = fsb[:, 1024 * mc + 512 * nh:1024 * mc + 512 * nh + 512]
                    if b == 0:
                        p.add("act", lambda a, psq=psq, fdst=fdst: a.copy(fdst, psq[:, :]),
                              reads=[psq[:, :]], writes=[fdst], tag="f0")
                    else:
                        p.add("dve", lambda v, psq=psq, fdst=fdst: v.tensor_tensor(
                                  fdst, fdst, psq[:, :], ALU.add),
                              reads=[psq[:, :], fdst], writes=[fdst], tag="facc")
        for nh in range(2):
            o = 512 * nh
            for c in range(CH):
                fdst = fsb[:, 1024 * c + o:1024 * c + o + 512]
                p.add("dve", lambda v, fdst=fdst, c=c, nq0=nq0, o=o: v.scalar_tensor_tensor(
                          fdst, fdst, mk.b2c[:, c:c + 1], b0d[c][:, nq0 + o:nq0 + o + 512],
                          ALU.add, ALU.add),
                      reads=[fdst, mk.b2c[:, c:c + 1], b0d[c][:, nq0 + o:nq0 + o + 512]],
                      writes=[fdst], tag="fb2")
                p.add("act", lambda a, c=c, fdst=fdst: a.copy(hbfc[c], fdst),
                      reads=[fdst], writes=[hbfc[c]], tag="l2c")
            fwin = [fsb[:, 1024 * c + o:1024 * c + o + 512] for c in range(CH)]
            mk.emit_ln([(fwin[c], hbfc[c], fwin[c]) for c in range(CH)], EPS_IN,
                       gamma=[mk.ln2g[:, c:c + 1] for c in range(CH)],
                       beta=[mk.ln2b[:, c:c + 1] for c in range(CH)],
                       tag=f"ln2_{seq}_{nq}_{nh}")
            for c in range(CH):
                p.add("act", lambda a, c=c, fdst=fwin[c]: a.copy(hbfc[c], fdst),
                      reads=[fwin[c]], writes=[hbfc[c]], tag="lpc")
            mk.emit_ln([(fwin[c], hbfc[c], fwin[c]) for c in range(CH)], EPS_OUT,
                       tag=f"lnp_{seq}_{nq}_{nh}")
            p.dma("sp", "xo", [(b0d[c][:, nq0 + o:nq0 + o + 512],
                                mk.d_x[seq, 128 * c:128 * (c + 1), nq0 + o:nq0 + o + 512])
                               for c in range(CH)], tag="xol")
            for c in range(CH):
                p.add("dve", lambda v, c=c, nq0=nq0, o=o, fdst=fwin[c]: v.tensor_tensor(
                          b0d[c][:, nq0 + o:nq0 + o + 512],
                          b0d[c][:, nq0 + o:nq0 + o + 512], fdst, ALU.add),
                      reads=[fwin[c], b0d[c][:, nq0 + o:nq0 + o + 512]],
                      writes=[b0d[c][:, nq0 + o:nq0 + o + 512]], tag="x2")
            for c in range(CH):
                p.add("act", lambda a, c=c, nq0=nq0, o=o: a.copy(
                          hbfc[c], b0d[c][:, nq0 + o:nq0 + o + 512]),
                      reads=[b0d[c][:, nq0 + o:nq0 + o + 512]], writes=[hbfc[c]],
                      tag="l3c")
            x3win = [x3c[:, 512 * c:512 * c + 512] for c in range(CH)]
            mk.emit_ln([(b0d[c][:, nq0 + o:nq0 + o + 512], hbfc[c], x3win[c])
                        for c in range(CH)], EPS_OUT,
                       bf_out=True, scratch=fwin, tag=f"ln3_{seq}_{nq}_{nh}")
            p.dma("sp", "x3sp", [(mk.d_x3sp[128 * c:128 * (c + 1), nq0 + o:nq0 + o + 512],
                                  x3win[c]) for c in range(CH)], tag="x3sp")
        p.dma("sp", "x2sp", [(mk.d_x2sp[128 * c:128 * (c + 1), nq0:nq0 + 1024],
                              b0d[c][:, nq0:nq0 + 1024]) for c in range(CH)], tag="x2sp")


def build_ffn(mk, seq, scope):
    nc, p = mk.nc, mk.p
    x3w = scope.enter_context(nc.sbuf_tensor(f"x3w_{seq}", [128, CH * 512], BF16))
    Ab = scope.enter_context(nc.sbuf_tensor(f"A_{seq}", [128, 32 * 512], BF16))
    Bb = scope.enter_context(nc.sbuf_tensor(f"B_{seq}", [128, 32 * 512], BF16))
    sil = scope.enter_context(nc.sbuf_tensor(f"sil_{seq}", [128, 512], F32))
    u2 = scope.enter_context(nc.sbuf_tensor(f"u2_{seq}", [128, 512], F32))
    f3 = scope.enter_context(nc.sbuf_tensor(f"f3_{seq}", [128, CH * 512], F32))
    x2c = scope.enter_context(nc.sbuf_tensor(f"x2c_{seq}", [128, CH * 512], F32))
    ws = mk.ws
    x3wc = [x3w[:, 512 * c:512 * c + 512] for c in range(CH)]

    for tc in range(4):
        t0 = 512 * tc
        src3 = mk.d_x3sp.rearrange("(c p) n -> p c n", p=128)
        p.dma("sp", "x3ld", [(x3w[:, :].rearrange("p (c n) -> p c n", n=512),
                              src3[:, :, t0:t0 + 512])], tag="x3ld")
        for b in range(8):
            p.dma("pool", f"ws{b % 2}", [(ws[b % 2][:, :], mk.d_f1[b, :, :])], tag="f1l")
            p.dma("pool", f"ws{2 + b % 2}", [(ws[2 + b % 2][:, :], mk.d_f2[b, :, :])],
                  tag="f2l")
            for msub in range(4):
                hidx = 4 * b + msub
                psq = mk.ps[msub % 2]

                def mmf1(t, b=b, msub=msub, psq=psq):
                    wsl = ws[b % 2]
                    for kc in range(CH):
                        inst = MM(t, psq[:, :],
                                  wsl[:, 512 * kc + 128 * msub:512 * kc + 128 * msub + 128],
                                  x3wc[kc], start=(kc == 0), stop=(kc == CH - 1))
                    return inst
                p.add("pe", mmf1, reads=[ws[b % 2][:, :]] + x3wc, writes=[psq[:, :]],
                      tag="f1mm")
                p.add("act", lambda a, psq=psq, hidx=hidx: a.activation(
                          sil[:, :], psq[:, :], AF.Silu, bias=0.0,
                          scale=mk.s1c[:, hidx:hidx + 1]),
                      reads=[psq[:, :], mk.s1c[:, hidx:hidx + 1]], writes=[sil[:, :]],
                      tag="f1ev")
                ps2q = mk.ps[2 + msub % 2]

                def mmf2(t, b=b, msub=msub, ps2q=ps2q):
                    wsl = ws[2 + b % 2]
                    for kc in range(CH):
                        inst = MM(t, ps2q[:, :],
                                  wsl[:, 512 * kc + 128 * msub:512 * kc + 128 * msub + 128],
                                  x3wc[kc], start=(kc == 0), stop=(kc == CH - 1))
                    return inst
                p.add("pe", mmf2, reads=[ws[2 + b % 2][:, :]] + x3wc, writes=[ps2q[:, :]],
                      tag="f2mm")
                p.add("act", lambda a, ps2q=ps2q, hidx=hidx: a.activation(
                          u2[:, :], ps2q[:, :], AF.Identity, bias=0.0,
                          scale=mk.s2c[:, hidx:hidx + 1]),
                      reads=[ps2q[:, :], mk.s2c[:, hidx:hidx + 1]], writes=[u2[:, :]],
                      tag="f2ev")
                p.add("dve", lambda v, hidx=hidx: v.tensor_tensor(
                          Ab[:, 512 * hidx:512 * hidx + 512], sil[:, :], u2[:, :],
                          ALU.mult),
                      reads=[sil[:, :], u2[:, :]],
                      writes=[Ab[:, 512 * hidx:512 * hidx + 512]], tag="f12m")
        for g in range(32):
            p.dma("pool", f"ws{4 + g % 2}", [(ws[4 + g % 2][:, :], mk.d_h1[g, :, :])],
                  tag="h1l")
            psq = mk.ps[4 + g % 2]

            def mmh1(t, g=g, psq=psq):
                wsl = ws[4 + g % 2]
                for kc in range(32):
                    inst = MM(t, psq[:, :], wsl[:, 128 * kc:128 * kc + 128],
                              Ab[:, 512 * kc:512 * kc + 512],
                              start=(kc == 0), stop=(kc == 31))
                return inst
            p.add("pe", mmh1, reads=[ws[4 + g % 2][:, :], Ab[:, :]], writes=[psq[:, :]],
                  tag="h1mm")
            p.add("act", lambda a, g=g, psq=psq: a.activation(
                      Bb[:, 512 * g:512 * g + 512], psq[:, :], AF.Gelu, bias=0.0,
                      scale=mk.h1sc[:, g:g + 1]),
                  reads=[psq[:, :], mk.h1sc[:, g:g + 1]],
                  writes=[Bb[:, 512 * g:512 * g + 512]], tag="h1ev")
        for g in range(32):
            p.dma("pool", f"ws{g % 2}", [(ws[g % 2][:, :], mk.d_h2[g, :, :])], tag="h2l")
            psq = mk.ps[g % 2]

            def mmh2(t, g=g, psq=psq):
                wsl = ws[g % 2]
                for kc in range(32):
                    inst = MM(t, psq[:, :], wsl[:, 128 * kc:128 * kc + 128],
                              Bb[:, 512 * kc:512 * kc + 512],
                              start=(kc == 0), stop=(kc == 31))
                return inst
            p.add("pe", mmh2, reads=[ws[g % 2][:, :], Bb[:, :]], writes=[psq[:, :]],
                  tag="h2mm")
            p.add("act", lambda a, g=g, psq=psq: a.activation(
                      Ab[:, 512 * g:512 * g + 512], psq[:, :], AF.Gelu, bias=0.0,
                      scale=mk.h2sc[:, g:g + 1]),
                  reads=[psq[:, :], mk.h2sc[:, g:g + 1]],
                  writes=[Ab[:, 512 * g:512 * g + 512]], tag="h2ev")
        for g in range(CH):
            p.dma("pool", f"ws{2 + g % 2}", [(ws[2 + g % 2][:, :], mk.d_w3[g, :, :])],
                  tag="w3l")
            psq = mk.ps[2 + g % 2]

            def mmw3(t, g=g, psq=psq):
                wsl = ws[2 + g % 2]
                for kc in range(32):
                    inst = MM(t, psq[:, :], wsl[:, 128 * kc:128 * kc + 128],
                              Ab[:, 512 * kc:512 * kc + 512],
                              start=(kc == 0), stop=(kc == 31))
                return inst
            p.add("pe", mmw3, reads=[ws[2 + g % 2][:, :], Ab[:, :]], writes=[psq[:, :]],
                  tag="w3mm")
            p.add("act", lambda a, g=g, psq=psq: a.activation(
                      f3[:, 512 * g:512 * g + 512], psq[:, :], AF.Identity, bias=0.0,
                      scale=mk.s3c[:, g:g + 1]),
                  reads=[psq[:, :], mk.s3c[:, g:g + 1]],
                  writes=[f3[:, 512 * g:512 * g + 512]], tag="w3ev")
        bfw = [Bb[:, 512 * c:512 * c + 512] for c in range(CH)]
        for c in range(CH):
            p.add("act", lambda a, c=c: a.copy(
                      bfw[c], f3[:, 512 * c:512 * c + 512]),
                  reads=[f3[:, 512 * c:512 * c + 512]], writes=[bfw[c]], tag="lfc")
        f3w = [f3[:, 512 * c:512 * c + 512] for c in range(CH)]
        mk.emit_ln([(f3w[c], bfw[c], f3w[c]) for c in range(CH)], EPS_OUT,
                   tag=f"lnf_{seq}_{tc}")
        src2 = mk.d_x2sp.rearrange("(c p) n -> p c n", p=128)
        p.dma("sp", "x2ld", [(x2c[:, :].rearrange("p (c n) -> p c n", n=512),
                              src2[:, :, t0:t0 + 512])], tag="x2ld")
        for c in range(CH):
            p.add("dve", lambda v, c=c: v.tensor_tensor(
                      x2c[:, 512 * c:512 * c + 512], x2c[:, 512 * c:512 * c + 512],
                      f3w[c], ALU.add),
                  reads=[f3w[c], x2c[:, 512 * c:512 * c + 512]],
                  writes=[x2c[:, 512 * c:512 * c + 512]], tag="yadd")
        dsty = mk.d_y[seq, :, t0:t0 + 512].rearrange("(c p) n -> p c n", p=128)
        p.dma("sp", "yout", [(dsty, x2c[:, :].rearrange("p (c n) -> p c n", n=512))],
              tag="yst")



def build_program(stop_after=None):
    from contextlib import ExitStack
    nc = bass.Bass("TRN2", target_bir_lowering=False)
    mk = MK(nc)
    p = mk.p
    mk.load_constants()

    stopped = False
    for seq in range(NSEQ):
        if stopped:
            break
        with ExitStack() as seq_scope:
            stopped = build_seq(mk, seq, seq_scope, stop_after)
        p.barrier(exclude_sems=("consts_c", "consts_p"))

    p.final_wait()
    p.emit()
    return nc


def build_seq(mk, seq, seq_scope, stop_after):
    from contextlib import ExitStack
    nc, p = mk.nc, mk.p

    def dump(chunks):
        pairs = [(mk.d_y[seq, 128 * c:128 * (c + 1), :], chunks[c][:, 0:T])
                 for c in range(CH)]
        p.dma("sp", "yout", pairs, tag="dbg_out")
        return True

    main_scope = ExitStack()
    seq_scope.enter_context(main_scope)
    big0 = main_scope.enter_context(
        nc.sbuf_tensor(f"big0_{seq}", [128, CH * PADW], F32))
    b0 = [big0[:, PADW * c:PADW * (c + 1)] for c in range(CH)]
    b0d = [big0[:, PADW * c + 2:PADW * c + 2 + T] for c in range(CH)]
    hbf = main_scope.enter_context(
        nc.sbuf_tensor(f"hbf_{seq}", [128, 2 * CH * 512], BF16))
    hbfc = [hbf[:, 512 * c:512 * (c + 1)] for c in range(2 * CH)]

    build_p0(mk, seq, b0, b0d, hbfc)
    if stop_after == "ln1" and seq == 0:
        return dump(b0d)

    with ExitStack() as conv_scope:
        build_conv(mk, seq, conv_scope, b0, b0d, hbfc)
    p.barrier(exclude_sems=("consts_c", "consts_p"))
    if stop_after == "conv" and seq == 0:
        return dump(b0d)

    with ExitStack() as attn_scope:
        ae = attn_scope.enter_context
        cosT = ae(nc.sbuf_tensor(f"cosT_{seq}", [128, 4 * T], BF16))
        sinT = ae(nc.sbuf_tensor(f"sinT_{seq}", [128, 4 * T], BF16))
        cosc = [cosT[:, T * i:T * (i + 1)] for i in range(4)]
        sinc = [sinT[:, T * i:T * (i + 1)] for i in range(4)]
        p.dma("pool", "tbl",
              [(cosc[i], mk.d_cos[128 * i:128 * (i + 1), :]) for i in range(4)]
              + [(sinc[i], mk.d_sin[128 * i:128 * (i + 1), :]) for i in range(4)],
              tag="tables")
        ksc = ae(nc.sbuf_tensor(f"ksc_{seq}", [128, CH * 512], BF16))
        krc = ae(nc.sbuf_tensor(f"krc_{seq}", [128, CH * 512], BF16))
        tmp1 = ae(nc.sbuf_tensor(f"tmp1_{seq}", [128, 1024], BF16))
        krt = ae(nc.sbuf_tensor(f"krt_{seq}", [128, 1024], BF16))
        vt = ae(nc.sbuf_tensor(f"vt_{seq}", [128, 1024], BF16))
        zbs = ae(nc.sbuf_tensor(f"zbs_{seq}", [128, 512], F32))
        bufs = dict(cosc=cosc, sinc=sinc, ksc=ksc, krc=krc, tmp1=tmp1,
                    krt=krt, vt=vt, zbs=zbs)
        partials = ("kproj", "krope", "ktile", "kvtiles", "ksev", "mmbv", "kv", "zq", "out")
        for l in range(NL):
            upto = stop_after if (l == 0 and stop_after in partials) else None
            if build_attn_layer(mk, seq, l, b0d, hbfc, bufs, upto=upto):
                return dump(b0d)
            if stop_after == f"attn{l}" and seq == 0:
                return dump(b0d)
    p.barrier(exclude_sems=("consts_c", "consts_p"))

    with ExitStack() as mlp_scope:
        build_mlp(mk, seq, mlp_scope, b0d, hbfc)
    p.barrier(exclude_sems=("consts_c", "consts_p"))
    if stop_after == "mlp" and seq == 0:
        return dump(b0d)

    main_scope.close()
    p.barrier(exclude_sems=("consts_c", "consts_p"))

    with ExitStack() as ffn_scope:
        build_ffn(mk, seq, ffn_scope)
    return False



def kernel(**inputs):
    prep = prepare_inputs(inputs)
    x = np.asarray(inputs["L0_attn_x"], np.float32)
    nc = build_program(STOP_AFTER)
    in_maps = []
    for core in range(8):
        m = dict(prep)
        m["x"] = np.ascontiguousarray(x[2 * core:2 * core + 2].transpose(0, 2, 1))
        in_maps.append(m)
    res = run_bass_kernel_spmd(nc, in_maps, core_ids=list(range(8)))
    out = np.empty((16, 2048, 1024), np.float32)
    for core in range(8):
        y = np.asarray(res.results[core]["y"])
        out[2 * core:2 * core + 2] = y.transpose(0, 2, 1)
    return out


# version 05
# speedup vs baseline: 1.0015x; 1.0015x over previous
import numpy as np
import ml_dtypes

import concourse.bass as bass
import concourse.mybir as mybir
from concourse.bass_utils import run_bass_kernel_spmd

F32 = mybir.dt.float32
BF16 = mybir.dt.bfloat16
AF = mybir.ActivationFunctionType
ALU = mybir.AluOpType
AX = mybir.AxisListType

C = 1024
CH = 8
T = 2048
NL = 4
NH = 16
NSEQ = 2
PADW = 2052
EPS_OUT = 1e-5
EPS_IN = 1e-12

STOP_AFTER = None



class Prog:
    ENG = {"pe": "tensor", "act": "scalar", "dve": "vector", "pool": "gpsimd",
           "sp": "sync"}

    def __init__(self, nc):
        self.nc = nc
        self.esem = {e: nc.alloc_semaphore("esem_" + e)
                     for e in ("pe", "act", "dve", "pool")}
        self.tasks = {e: [] for e in self.ENG}
        self.ccnt = {e: 0 for e in ("pe", "act", "dve", "pool")}
        self.dsem = {}
        self.res = {}
        self.waited = {e: {} for e in self.ENG}

    def _iv(self, ap):
        t = ap.tensor
        if isinstance(t, bass.PSumTensorHandle):
            return (t.name, 0, 1 << 40)
        off = ap.offset if isinstance(ap.offset, int) else 0
        span = 1
        for step, count in ap.ap:
            span += abs(step) * (count - 1)
        return (t.name, off, off + span)

    def _collect(self, engine, reads, writes):
        deps = {}

        def want(dep):
            kind, key, val = dep
            k = (kind, key)
            if deps.get(k, -1) < val:
                deps[k] = val

        ivs_r = [self._iv(a) for a in reads]
        ivs_w = [self._iv(a) for a in writes]
        for name, lo, hi in ivs_r:
            ent = self.res.get(name)
            if ent:
                for a, b, dep in ent["w"]:
                    if a < hi and lo < b:
                        want(dep)
        for name, lo, hi in ivs_w:
            ent = self.res.get(name)
            if ent:
                for a, b, dep in ent["w"]:
                    if a < hi and lo < b:
                        want(dep)
                for a, b, dep in ent["r"]:
                    if a < hi and lo < b:
                        want(dep)
        return deps, ivs_r, ivs_w

    def _register(self, ivs_r, ivs_w, token):
        for name, lo, hi in ivs_w:
            ent = self.res.setdefault(name, {"w": [], "r": []})
            ent["w"] = [(a, b, d) for (a, b, d) in ent["w"] if not (lo <= a and b <= hi)]
            ent["r"] = [(a, b, d) for (a, b, d) in ent["r"] if not (lo <= a and b <= hi)]
            ent["w"].append((lo, hi, token))
        for name, lo, hi in ivs_r:
            self.res.setdefault(name, {"w": [], "r": []})["r"].append((lo, hi, token))

    def _mk_waits(self, engine, deps):
        waits = []
        for (kind, key), val in sorted(deps.items(), key=lambda kv: str(kv[0])):
            if val <= 0:
                continue
            sem = self.esem[key] if kind == "E" else self.dsem[key][0]
            prev = self.waited[engine].get((kind, key), -1)
            if val > prev:
                waits.append((sem, val))
                self.waited[engine][(kind, key)] = val
        return waits

    def add(self, engine, fn, reads=(), writes=(), tag=""):
        deps, ivs_r, ivs_w = self._collect(engine, reads, writes)
        waits = self._mk_waits(engine, deps)
        self.ccnt[engine] += 1
        token = ("E", engine, self.ccnt[engine])
        self.tasks[engine].append((waits, fn, tag, False))
        self._register(ivs_r, ivs_w, token)
        return token

    def dma(self, queue, semname, pairs, tag=""):
        if semname not in self.dsem:
            self.dsem[semname] = [self.nc.alloc_semaphore("dsem_" + semname), 0]
        ent = self.dsem[semname]
        prev_cum = ent[1]
        reads = [pp[1] for pp in pairs]
        writes = [pp[0] for pp in pairs]
        deps, ivs_r, ivs_w = self._collect(queue, reads, writes)
        if prev_cum > 0 and deps.get(("D", semname), -1) < prev_cum:
            deps[("D", semname)] = prev_cum
        waits = self._mk_waits(queue, deps)
        ent[1] += 16 * len(pairs)
        token = ("D", semname, ent[1])
        sem = ent[0]

        def fn(eng, pairs=pairs, sem=sem):
            for (o, i) in pairs:
                eng.dma_start(out=o, in_=i).then_inc(sem, 16)
            return None

        self.tasks[queue].append((waits, fn, tag, True))
        self._register(ivs_r, ivs_w, token)
        return token

    def barrier(self, exclude_sems=()):
        targets = {("E", e): self.ccnt[e] for e in self.esem if self.ccnt[e] > 0}
        targets.update({("D", n): ent[1] for n, ent in self.dsem.items()
                        if ent[1] > 0 and n not in exclude_sems})
        for e in self.ENG:
            waits = self._mk_waits(e, targets)
            if waits:
                def fn(eng, waits=waits):
                    return None
                self.tasks[e].append((waits, fn, "barrier", True))
        self.res = {}

    def final_wait(self, engine="sp"):
        targets = {("D", n): ent[1] for n, ent in self.dsem.items() if ent[1] > 0}
        targets.update({("E", e): self.ccnt[e] for e in self.esem if self.ccnt[e] > 0})
        waits = self._mk_waits(engine, targets)

        def fn(eng):
            return None
        self.tasks[engine].append((waits, fn, "final", True))

    def emit(self):
        with self.nc.Block() as block:
            for e, attr in self.ENG.items():
                dec = getattr(block, attr)
                tasks = self.tasks[e]
                esem = self.esem.get(e)

                def body(eng, tasks=tasks, esem=esem, ename=e):
                    for (waits, fn, tag, is_dma) in tasks:
                        for (sem, val) in waits:
                            eng.wait_ge(sem, val)
                        last = fn(eng)
                        if not is_dma:
                            assert last is not None, \
                                f"compute task {tag} on {ename} returned None"
                            last.then_inc(esem, 1)
                dec(body)



def _tile_km(w, granule):
    K, M = w.shape
    kc = K // 128
    nb = M // granule
    out = w.reshape(kc, 128, nb, granule).transpose(2, 1, 0, 3)
    return np.ascontiguousarray(out.reshape(nb, 128, kc * granule))


def _col_pack(v, nchunk):
    return np.ascontiguousarray(np.asarray(v, np.float32).reshape(nchunk, 128).T)


def _perm_order():
    r = np.arange(512)
    return np.concatenate([2 * r, 2 * r + 1])


def prepare_inputs(inputs):
    f = np.float32
    d = {}
    perm = _perm_order()

    j = np.arange(512, dtype=np.float64)
    inv = 1.0 / np.power(10000.0, 2.0 * j / 1024.0)
    pos = np.arange(T, dtype=np.float64)
    ang = pos[None, :] * inv[:, None]
    d["cosT"] = np.cos(ang).astype(f)
    d["sinT"] = np.sin(ang).astype(f)

    d["eye128"] = np.eye(128, dtype=f)
    d["eps_col"] = np.broadcast_to(np.array([[EPS_OUT, EPS_IN]], f), (128, 2)).copy()
    d["ones_col"] = np.ones((128, 1), dtype=f)
    d["ones_row"] = np.ones((1, 128), dtype=f)

    ebc = np.zeros((16, CH * 128), dtype=f)
    for c in range(CH):
        for m in range(128):
            ebc[2 * c + m // 64, 128 * c + m] = 1.0
    d["Ebc"] = ebc

    Wq = np.asarray(inputs["L0_attn_Wq"], dtype=f)[:, :, perm]
    Wk = np.asarray(inputs["L0_attn_Wk"], dtype=f)[:, :, perm]
    Wv = np.asarray(inputs["L0_attn_Wv"], dtype=f)
    d["wq_t"] = np.stack([_tile_km(Wq[l], 512) for l in range(NL)])
    d["wk_t"] = np.stack([_tile_km(Wk[l], 512) for l in range(NL)])
    d["wv_t"] = np.stack([_tile_km(Wv[l], 512) for l in range(NL)])

    bq = np.asarray(inputs["L0_attn_bq"], dtype=f)[:, perm]
    bk = np.asarray(inputs["L0_attn_bk"], dtype=f)[:, perm]
    d["bq_col"] = np.ascontiguousarray(
        np.stack([_col_pack(bq[l], CH) for l in range(NL)], axis=1).reshape(128, NL * CH))
    d["bk_col"] = np.ascontiguousarray(
        np.stack([_col_pack(bk[l], CH) for l in range(NL)], axis=1).reshape(128, NL * CH))

    bv = np.asarray(inputs["L0_attn_bv"], dtype=f)
    bvr = np.zeros((64, NL * C), dtype=f)
    for l in range(NL):
        bvr[0, C * l:C * (l + 1)] = bv[l]
        bvr[32, C * l:C * (l + 1)] = bv[l]
    d["bv_rows"] = bvr

    d["lnAg_col"] = np.ascontiguousarray(
        np.stack([_col_pack(np.asarray(inputs["L0_attn_lnA_g"], f)[l], CH)
                  for l in range(NL)], axis=1).reshape(128, NL * CH))
    d["lnAb_col"] = np.ascontiguousarray(
        np.stack([_col_pack(np.asarray(inputs["L0_attn_lnA_b"], f)[l], CH)
                  for l in range(NL)], axis=1).reshape(128, NL * CH))
    d["ln0g_col"] = _col_pack(inputs["L0_attn_ln0_g"], CH)
    d["ln0b_col"] = _col_pack(inputs["L0_attn_ln0_b"], CH)
    d["ln2g_col"] = _col_pack(inputs["L0_attn_ln2_g"], CH)
    d["ln2b_col"] = _col_pack(inputs["L0_attn_ln2_b"], CH)

    w1r = np.asarray(inputs["L0_attn_conv1_w"], f).reshape(C, 4, 5)
    w2r = np.asarray(inputs["L0_attn_conv2_w"], f)
    dia1 = np.zeros((CH, 128, 20 * 128), dtype=f)
    dia2 = np.zeros((CH, 128, 20 * 128), dtype=f)
    idx = np.arange(128)
    for c in range(CH):
        for jj in range(4):
            for t5 in range(5):
                col = 128 * (5 * jj + t5)
                dia1[c, idx, col + idx] = w1r[128 * c + idx, jj, t5]
                dia2[c, idx, col + idx] = w2r[128 * c + idx, jj, t5]
    d["conv1d"] = dia1
    d["conv2d"] = dia2
    d["cb1_col"] = np.ascontiguousarray(
        np.asarray(inputs["L0_attn_conv1_b"], f).reshape(CH, 128, 4)
        .transpose(1, 0, 2).reshape(128, CH * 4))
    d["cb2_col"] = _col_pack(inputs["L0_attn_conv2_b"], CH)

    d["W1_t"] = _tile_km(np.asarray(inputs["L0_attn_W1"], f), 512)
    d["W2_t"] = _tile_km(np.asarray(inputs["L0_attn_W2"], f), 1024)[0]
    d["b1_col"] = _col_pack(inputs["L0_attn_b1"], 32)
    d["b2_col"] = _col_pack(inputs["L0_attn_b2"], CH)

    d["f1_t"] = _tile_km(np.ascontiguousarray(np.asarray(inputs["L0_ffn_w1_w"], f).T), 512)
    d["f2_t"] = _tile_km(np.ascontiguousarray(np.asarray(inputs["L0_ffn_w2_w"], f).T), 512)
    d["h1_t"] = _tile_km(np.ascontiguousarray(np.asarray(inputs["L0_ffn_h1_w"], f).T), 128)
    d["h2_t"] = _tile_km(np.ascontiguousarray(np.asarray(inputs["L0_ffn_h2_w"], f).T), 128)
    d["w3_t"] = _tile_km(np.ascontiguousarray(np.asarray(inputs["L0_ffn_w3_w"], f).T), 128)
    d["s1_col"] = _col_pack(inputs["L0_ffn_w1_s"], 32)
    d["s2_col"] = _col_pack(inputs["L0_ffn_w2_s"], 32)
    d["h1s_col"] = _col_pack(inputs["L0_ffn_h1_s"], 32)
    d["h2s_col"] = _col_pack(inputs["L0_ffn_h2_s"], 32)
    d["s3_col"] = _col_pack(inputs["L0_ffn_w3_s"], CH)
    return d



class MK:
    def __init__(self, nc):
        self.nc = nc
        self.p = Prog(nc)
        self._declare_dram()
        self._alloc_top()

    def _declare_dram(self):
        nc = self.nc

        def D(n, sh, dt, kind="ExternalInput"):
            return nc.dram_tensor(n, sh, dt, kind=kind).ap()

        self.d_x = D("x", [NSEQ, C, T], F32)
        self.d_cos = D("cosT", [512, T], F32)
        self.d_sin = D("sinT", [512, T], F32)
        self.d_eye = D("eye128", [128, 128], F32)
        self.d_eps = D("eps_col", [128, 2], F32)
        self.d_ones_col = D("ones_col", [128, 1], F32)
        self.d_ones_row = D("ones_row", [1, 128], F32)
        self.d_ebc = D("Ebc", [16, CH * 128], F32)
        self.d_wq = D("wq_t", [NL, 2, 128, 4096], F32)
        self.d_wk = D("wk_t", [NL, 2, 128, 4096], F32)
        self.d_wv = D("wv_t", [NL, 2, 128, 4096], F32)
        self.d_bq = D("bq_col", [128, NL * CH], F32)
        self.d_bk = D("bk_col", [128, NL * CH], F32)
        self.d_bvr = D("bv_rows", [64, NL * C], F32)
        self.d_lnAg = D("lnAg_col", [128, NL * CH], F32)
        self.d_lnAb = D("lnAb_col", [128, NL * CH], F32)
        self.d_ln0g = D("ln0g_col", [128, CH], F32)
        self.d_ln0b = D("ln0b_col", [128, CH], F32)
        self.d_ln2g = D("ln2g_col", [128, CH], F32)
        self.d_ln2b = D("ln2b_col", [128, CH], F32)
        self.d_c1d = D("conv1d", [CH, 128, 2560], F32)
        self.d_c2d = D("conv2d", [CH, 128, 2560], F32)
        self.d_cb1 = D("cb1_col", [128, CH * 4], F32)
        self.d_cb2 = D("cb2_col", [128, CH], F32)
        self.d_W1 = D("W1_t", [8, 128, 4096], F32)
        self.d_W2 = D("W2_t", [128, 32768], F32)
        self.d_b1 = D("b1_col", [128, 32], F32)
        self.d_b2 = D("b2_col", [128, CH], F32)
        self.d_f1 = D("f1_t", [8, 128, 4096], F32)
        self.d_f2 = D("f2_t", [8, 128, 4096], F32)
        self.d_h1 = D("h1_t", [32, 128, 4096], F32)
        self.d_h2 = D("h2_t", [32, 128, 4096], F32)
        self.d_w3 = D("w3_t", [8, 128, 4096], F32)
        self.d_s1 = D("s1_col", [128, 32], F32)
        self.d_s2 = D("s2_col", [128, 32], F32)
        self.d_h1s = D("h1s_col", [128, 32], F32)
        self.d_h2s = D("h2s_col", [128, 32], F32)
        self.d_s3 = D("s3_col", [128, CH], F32)
        self.d_y = D("y", [NSEQ, C, T], F32, "ExternalOutput")
        self.d_x2sp = D("x2_spill", [C, T], F32, "Internal")
        self.d_x3sp = D("x3_spill", [C, T], BF16, "Internal")

    def _alloc_top(self):
        nc = self.nc
        A = nc.alloc_sbuf_tensor
        self.ws = [A(f"ws{i}", [128, 4096], BF16) for i in range(6)]
        self.eye = A("eye", [128, 128], BF16)
        self.epsc = A("epsc", [128, 2], F32)
        self.ones_col = A("onescol", [128, 1], BF16)
        self.ones_row = A("onesrow", [1, 128], F32)
        self.ebc = A("ebc", [16, CH * 128], BF16)
        self.bvr = A("bvr", [64, C], BF16)
        self.bq = A("bqc", [128, NL * CH], F32)
        self.bk = A("bkc", [128, NL * CH], F32)
        self.lnAg = A("lnAg", [128, NL * CH], F32)
        self.lnAb = A("lnAb", [128, NL * CH], F32)
        self.ln0g = A("ln0g", [128, CH], F32)
        self.ln0b = A("ln0b", [128, CH], F32)
        self.ln2g = A("ln2g", [128, CH], F32)
        self.ln2b = A("ln2b", [128, CH], F32)
        self.cb1 = A("cb1", [128, CH * 4], F32)
        self.cb2 = A("cb2", [128, CH], F32)
        self.b1c = A("b1c", [128, 32], F32)
        self.b2c = A("b2c", [128, CH], F32)
        self.s1c = A("s1c", [128, 32], F32)
        self.s2c = A("s2c", [128, 32], F32)
        self.h1sc = A("h1sc", [128, 32], F32)
        self.h2sc = A("h2sc", [128, 32], F32)
        self.s3c = A("s3c", [128, CH], F32)
        self.m_row = A("m_row", [1, 512], F32)
        self.v_row = A("v_row", [1, 512], F32)
        self.r_row = A("r_row", [1, 512], F32)
        self.km_tmp = A("km_tmp", [128, 1], F32)
        self.km_acc = A("km_acc", [128, CH], F32)
        self.km_mat = A("km_mat", [128, CH * 16], BF16)
        self.krs = A("krs", [128, 1024], BF16)
        self.z_sb = A("z_sb", [16, 512], F32)
        self.z_bf = A("z_bf", [16, 512], BF16)
        self.kvb = A("kvb", [128, 16 * 128], BF16)
        P = nc.alloc_psum_tensor
        self.ps = [P(f"ps{i}", [128, 512], F32) for i in range(6)]
        self.pt = P("pt", [128, 512], BF16)
        self.kvp = P("kvp", [128, 512], F32)

    def load_constants(self):
        p = self.p
        p.dma("pool", "consts_c", [
            (self.eye[:, :], self.d_eye),
            (self.ones_col[:, :], self.d_ones_col),
            (self.ebc[:, :], self.d_ebc),
        ], tag="consts_cast")
        p.dma("sp", "consts_p", [
            (self.ones_row[:, :], self.d_ones_row),
            (self.epsc[:, :], self.d_eps),
            (self.bq[:, :], self.d_bq), (self.bk[:, :], self.d_bk),
            (self.lnAg[:, :], self.d_lnAg), (self.lnAb[:, :], self.d_lnAb),
            (self.ln0g[:, :], self.d_ln0g), (self.ln0b[:, :], self.d_ln0b),
            (self.ln2g[:, :], self.d_ln2g), (self.ln2b[:, :], self.d_ln2b),
            (self.cb1[:, :], self.d_cb1), (self.cb2[:, :], self.d_cb2),
            (self.b1c[:, :], self.d_b1), (self.b2c[:, :], self.d_b2),
            (self.s1c[:, :], self.d_s1), (self.s2c[:, :], self.d_s2),
            (self.h1sc[:, :], self.d_h1s), (self.h2sc[:, :], self.d_h2s),
            (self.s3c[:, :], self.d_s3),
        ], tag="consts_plain")

    def emit_ln(self, items, eps, gamma=None, beta=None, dst_bf=None,
                bf_out=False, scratch=None, tag="ln"):
        p = self.p
        ps_st, ps_m, ps_r = self.ps[2], self.ps[3], self.ps[4]
        srcs = [it[0] for it in items]
        bfs = [it[1] for it in items]
        dsts = [it[2] for it in items]

        def mm_sum(t):
            for c in range(CH):
                inst = t.matmul(ps_st[0:1, :], self.ones_col[:, :], bfs[c],
                                start=(c == 0), stop=False, skip_group_check=True)
            return inst
        p.add("pe", mm_sum, reads=bfs + [self.ones_col[:, :]],
              writes=[ps_st[:, :]], tag=tag + "_mmsum")
        for c in range(CH):
            p.add("act", lambda a, c=c: a.activation(bfs[c], bfs[c], AF.Square),
                  reads=[bfs[c]], writes=[bfs[c]], tag=tag + "_sq")

        def mm_sq(t):
            for c in range(CH):
                inst = t.matmul(ps_st[64:65, :], self.ones_col[:, :], bfs[c],
                                start=(c == 0), stop=(c == CH - 1),
                                tile_position=(0, 64), skip_group_check=True)
            return inst
        p.add("pe", mm_sq, reads=bfs + [self.ones_col[:, :]],
              writes=[ps_st[:, :]], tag=tag + "_mmsq")

        p.add("dve", lambda v: v.tensor_scalar_mul(self.m_row[:, :], ps_st[0:1, :], 1.0 / C),
              reads=[ps_st[:, :]], writes=[self.m_row[:, :]], tag=tag + "_m")
        p.add("dve", lambda v: v.tensor_tensor(self.v_row[:, :], self.m_row[:, :],
                                               self.m_row[:, :], ALU.mult),
              reads=[self.m_row[:, :]], writes=[self.v_row[:, :]], tag=tag + "_m2")
        p.add("dve", lambda v: v.scalar_tensor_tensor(self.v_row[:, :], ps_st[64:65, :],
                                                      1.0 / C, self.v_row[:, :],
                                                      ALU.mult, ALU.subtract),
              reads=[ps_st[:, :], self.v_row[:, :]], writes=[self.v_row[:, :]],
              tag=tag + "_var")
        ecol = self.epsc[0:1, 0:1] if eps == EPS_OUT else self.epsc[0:1, 1:2]
        p.add("act", lambda a, ecol=ecol: a.activation(self.r_row[:, :], self.v_row[:, :],
                                                       AF.Sqrt, bias=ecol, scale=1.0),
              reads=[self.v_row[:, :], ecol], writes=[self.r_row[:, :]], tag=tag + "_sd")
        p.add("dve", lambda v: v.reciprocal(self.r_row[:, :], self.r_row[:, :]),
              reads=[self.r_row[:, :]], writes=[self.r_row[:, :]], tag=tag + "_rstd")
        p.add("pe", lambda t: t.matmul(ps_m[:, :], self.ones_row[:, :], self.m_row[:, :],
                                       start=True, stop=True, skip_group_check=True),
              reads=[self.m_row[:, :], self.ones_row[:, :]], writes=[ps_m[:, :]],
              tag=tag + "_bm")
        p.add("pe", lambda t: t.matmul(ps_r[:, :], self.ones_row[:, :], self.r_row[:, :],
                                       start=True, stop=True, skip_group_check=True),
              reads=[self.r_row[:, :], self.ones_row[:, :]], writes=[ps_r[:, :]],
              tag=tag + "_br")

        for c in range(CH):
            g = gamma[c] if gamma is not None else None
            b = beta[c] if beta is not None else None

            if bf_out:
                scr = scratch[c]
                p.add("dve", lambda v, c=c, scr=scr: v.tensor_tensor(
                          scr, srcs[c], ps_m[:, :], ALU.subtract),
                      reads=[srcs[c], ps_m[:, :]], writes=[scr], tag=tag + "_a1")
                p.add("dve", lambda v, c=c, scr=scr: v.scalar_tensor_tensor(
                          dsts[c], scr, 1.0, ps_r[:, :], ALU.mult, ALU.mult),
                      reads=[scr, ps_r[:, :]], writes=[dsts[c]], tag=tag + "_a2")
            else:
                p.add("dve", lambda v, c=c: v.tensor_tensor(
                          dsts[c], srcs[c], ps_m[:, :], ALU.subtract),
                      reads=[srcs[c], ps_m[:, :]], writes=[dsts[c]], tag=tag + "_a1")
                if g is not None:
                    p.add("dve", lambda v, c=c, g=g: v.scalar_tensor_tensor(
                              dsts[c], dsts[c], g, ps_r[:, :], ALU.mult, ALU.mult),
                          reads=[dsts[c], g, ps_r[:, :]], writes=[dsts[c]],
                          tag=tag + "_a2")
                    p.add("act", lambda a, c=c, b=b: a.activation(
                              dsts[c], dsts[c], AF.Identity, bias=b, scale=1.0),
                          reads=[dsts[c], b], writes=[dsts[c]], tag=tag + "_a3")
                else:
                    p.add("dve", lambda v, c=c: v.tensor_tensor(
                              dsts[c], dsts[c], ps_r[:, :], ALU.mult),
                          reads=[dsts[c], ps_r[:, :]], writes=[dsts[c]],
                          tag=tag + "_a2")
                if dst_bf is not None:
                    p.add("act", lambda a, c=c: a.copy(dst_bf[c], dsts[c]),
                          reads=[dsts[c]], writes=[dst_bf[c]], tag=tag + "_a4")


def MM(t, out, lhsT, rhs, start, stop, tp=None):
    return t.matmul(out, lhsT, rhs, start=start, stop=stop,
                    tile_position=tp, skip_group_check=True)



def build_p0(mk, seq, b0, b0d, hbfc):
    p = mk.p
    for c in range(CH):
        p.add("pool", lambda g, c=c: g.memset(b0[c][:, 0:2], 0.0),
              writes=[b0[c][:, 0:2]], tag="padz")
        p.add("pool", lambda g, c=c: g.memset(b0[c][:, 2 + T:PADW], 0.0),
              writes=[b0[c][:, 2 + T:PADW]], tag="padz")
    p.dma("sp", "xin", [(b0d[c], mk.d_x[seq, 128 * c:128 * (c + 1), :])
                        for c in range(CH)], tag="xload")
    for nq in range(4):
        nq0 = 512 * nq
        p.dma("pool", "xbf", [(hbfc[c], mk.d_x[seq, 128 * c:128 * (c + 1), nq0:nq0 + 512])
                              for c in range(CH)], tag="xbfload")
        items = [(b0d[c][:, nq0:nq0 + 512], hbfc[c], b0d[c][:, nq0:nq0 + 512])
                 for c in range(CH)]
        mk.emit_ln(items, EPS_OUT, tag=f"ln1_{seq}_{nq}")


def build_conv(mk, seq, scope, b0, b0d, hbfc):
    nc, p = mk.nc, mk.p
    xbfc = scope.enter_context(nc.sbuf_tensor(f"xbfc_{seq}", [128, 8 * 264], BF16))
    hjb = [scope.enter_context(nc.sbuf_tensor(f"hjb{i}_{seq}", [128, 4 * PADW], BF16))
           for i in range(2)]

    for c in range(CH):
        s1 = mk.ws[c % 2]
        s2 = mk.ws[2 + c % 2]
        p.dma("pool", f"ws{c % 2}", [(s1[:, 0:2560], mk.d_c1d[c, :, :])], tag="c1d")
        p.dma("pool", f"ws{2 + c % 2}", [(s2[:, 0:2560], mk.d_c2d[c, :, :])], tag="c2d")
        hj = hjb[c % 2]
        for jj in range(4):
            p.add("pool", lambda g, jj=jj, hj=hj: g.memset(hj[:, PADW * jj:PADW * jj + 2], 0.0),
                  writes=[hj[:, PADW * jj:PADW * jj + 2]], tag="hjz")
            p.add("pool", lambda g, jj=jj, hj=hj: g.memset(hj[:, PADW * jj + 2 + T:PADW * (jj + 1)], 0.0),
                  writes=[hj[:, PADW * jj + 2 + T:PADW * (jj + 1)]], tag="hjz")
        for tk in range(8):
            w = xbfc[:, 264 * tk:264 * tk + 260]
            p.add("act", lambda a, c=c, tk=tk, w=w: a.copy(
                      w, b0[c][:, 256 * tk:256 * tk + 260]),
                  reads=[b0[c][:, 256 * tk:256 * tk + 260]], writes=[w], tag="xc")
            for jj in range(4):
                psq = mk.ps[(tk * 4 + jj) % 2]

                def mm1(t, s1=s1, jj=jj, tk=tk, psq=psq):
                    for t5 in range(5):
                        inst = MM(t, psq[:, 0:256], s1[:, 128 * (5 * jj + t5):128 * (5 * jj + t5 + 1)],
                                  xbfc[:, 264 * tk + t5:264 * tk + t5 + 256],
                                  start=(t5 == 0), stop=(t5 == 4))
                    return inst
                p.add("pe", mm1, reads=[s1[:, 0:2560], w], writes=[psq[:, :]], tag="c1mm")
                p.add("act", lambda a, hj=hj, jj=jj, tk=tk, psq=psq, c=c: a.activation(
                          hj[:, PADW * jj + 2 + 256 * tk:PADW * jj + 2 + 256 * tk + 256],
                          psq[:, 0:256], AF.Gelu,
                          bias=mk.cb1[:, 4 * c + jj:4 * c + jj + 1], scale=1.0),
                      reads=[psq[:, :], mk.cb1[:, 4 * c + jj:4 * c + jj + 1]],
                      writes=[hj[:, PADW * jj + 2 + 256 * tk:PADW * jj + 2 + 256 * tk + 256]],
                      tag="c1ev")
        for tk in range(8):
            psq = mk.ps[2 + tk % 2]

            def mm2(t, s2=s2, tk=tk, psq=psq, hj=hj):
                first = True
                for jj in range(4):
                    for t5 in range(5):
                        inst = MM(t, psq[:, 0:256], s2[:, 128 * (5 * jj + t5):128 * (5 * jj + t5 + 1)],
                                  hj[:, PADW * jj + 256 * tk + t5:PADW * jj + 256 * tk + t5 + 256],
                                  start=first, stop=(jj == 3 and t5 == 4))
                        first = False
                return inst
            p.add("pe", mm2, reads=[s2[:, 0:2560], hj[:, :]], writes=[psq[:, :]], tag="c2mm")
            sl = slice(256 * tk, 256 * tk + 256)
            p.add("dve", lambda v, c=c, psq=psq, sl=sl: v.scalar_tensor_tensor(
                      b0d[c][:, sl], psq[:, 0:256], mk.cb2[:, c:c + 1], b0d[c][:, sl],
                      ALU.add, ALU.add),
                  reads=[psq[:, :], mk.cb2[:, c:c + 1], b0d[c][:, sl]],
                  writes=[b0d[c][:, sl]], tag="c2ev")
    for nq in range(4):
        nq0 = 512 * nq
        for c in range(CH):
            p.add("act", lambda a, c=c, nq0=nq0: a.copy(
                      hbfc[c], b0d[c][:, nq0:nq0 + 512]),
                  reads=[b0d[c][:, nq0:nq0 + 512]], writes=[hbfc[c]], tag="ln0c")
        items = [(b0d[c][:, nq0:nq0 + 512], hbfc[c], b0d[c][:, nq0:nq0 + 512])
                 for c in range(CH)]
        mk.emit_ln(items, EPS_IN,
                   gamma=[mk.ln0g[:, c:c + 1] for c in range(CH)],
                   beta=[mk.ln0b[:, c:c + 1] for c in range(CH)],
                   tag=f"ln0_{seq}_{nq}")


def build_attn_layer(mk, seq, l, b0d, hbfc2, bufs, upto=None):
    p = mk.p
    cosc, sinc = bufs["cosc"], bufs["sinc"]
    ksc = bufs["ksc"]
    tmp1 = bufs["tmp1"]
    krt, vt = bufs["krt"], bufs["vt"]
    zbs = bufs["zbs"]
    ws = mk.ws

    p.dma("pool", "bvl", [(mk.bvr[:, :], mk.d_bvr[:, C * l:C * (l + 1)])], tag=f"bv{l}")
    for b in range(2):
        p.dma("pool", f"ws{b}", [(ws[b][:, :], mk.d_wk[l, b, :, :])], tag=f"wk{l}")
        p.dma("pool", f"ws{2 + b}", [(ws[2 + b][:, :], mk.d_wv[l, b, :, :])], tag=f"wv{l}")
        p.dma("pool", f"ws{4 + b}", [(ws[4 + b][:, :], mk.d_wq[l, b, :, :])], tag=f"wq{l}")

    def hb(c, par):
        return hbfc2[(par % 2) * CH + c]

    def cast_h(nq0, tag, par):
        for c in range(CH):
            p.add("act", lambda a, c=c, nq0=nq0, par=par: a.copy(
                      hb(c, par), b0d[c][:, nq0:nq0 + 512]),
                  reads=[b0d[c][:, nq0:nq0 + 512]], writes=[hb(c, par)], tag=tag)

    def proj(wslots, biascol, dst, tag, par):
        for mc in range(CH):
            wsl = wslots[0] if mc < 4 else wslots[1]
            mloc = mc % 4
            psq = mk.ps[mc % 2]

            def mmp(t, wsl=wsl, mloc=mloc, psq=psq, par=par):
                for kc in range(CH):
                    inst = MM(t, psq[:, :], wsl[:, 512 * kc + 128 * mloc:512 * kc + 128 * mloc + 128],
                              hb(kc, par), start=(kc == 0), stop=(kc == CH - 1))
                return inst
            p.add("pe", mmp, reads=[wsl[:, :]] + [hb(kc, par) for kc in range(CH)],
                  writes=[psq[:, :]], tag=tag + "mm")
            p.add("act", lambda a, mc=mc, psq=psq: a.activation(
                      dst[:, 512 * mc:512 * mc + 512], psq[:, :], AF.Gelu,
                      bias=biascol[:, CH * l + mc:CH * l + mc + 1], scale=1.0),
                  reads=[psq[:, :], biascol[:, CH * l + mc:CH * l + mc + 1]],
                  writes=[dst[:, 512 * mc:512 * mc + 512]], tag=tag + "ev")
        for mc in range(CH):
            p.add("dve", lambda v, mc=mc: v.tensor_scalar_add(
                      dst[:, 512 * mc:512 * mc + 512], dst[:, 512 * mc:512 * mc + 512],
                      0.21),
                  reads=[dst[:, 512 * mc:512 * mc + 512]],
                  writes=[dst[:, 512 * mc:512 * mc + 512]], tag=tag + "21")

    def rope(buf, nq0, tag):
        for pc in range(4):
            cw = cosc[pc][:, nq0:nq0 + 512]
            sw = sinc[pc][:, nq0:nq0 + 512]
            se = buf[:, 512 * pc:512 * pc + 512]
            so = buf[:, 512 * (pc + 4):512 * (pc + 4) + 512]
            te = tmp1[:, 0:512]
            to = tmp1[:, 512:1024]
            p.add("dve", lambda v, so=so, sw=sw, te=te: v.tensor_tensor(
                      te, so, sw, ALU.mult),
                  reads=[so, sw], writes=[te], tag=tag + "re1")
            p.add("pool", lambda g, se=se, sw=sw, to=to: g.tensor_tensor(
                      to, se, sw, ALU.mult),
                  reads=[se, sw], writes=[to], tag=tag + "ro1")
            p.add("dve", lambda v, se=se, cw=cw: v.tensor_tensor(
                      se, se, cw, ALU.mult),
                  reads=[se, cw], writes=[se], tag=tag + "re2")
            p.add("dve", lambda v, se=se, te=te: v.tensor_tensor(
                      se, se, te, ALU.subtract),
                  reads=[se, te], writes=[se], tag=tag + "re3")
            p.add("pool", lambda g, so=so, cw=cw: g.tensor_tensor(
                      so, so, cw, ALU.mult),
                  reads=[so, cw], writes=[so], tag=tag + "ro2")
            p.add("pool", lambda g, so=so, to=to: g.tensor_tensor(
                      so, so, to, ALU.add),
                  reads=[so, to], writes=[so], tag=tag + "ro3")

    def kscv(par):
        off = (par % 2) * CH * 512
        return ksc[:, off:off + CH * 512]

    for nq in range(4):
        nq0 = 512 * nq
        par = nq % 2
        ksq = kscv(par)
        cast_h(nq0, f"a{l}kc", par)
        proj((ws[0], ws[1]), mk.bk, ksq, f"a{l}k", par)
        for c in range(CH):
            p.add("dve", lambda v, c=c, ksq=ksq: v.reduce_sum(
                      mk.km_tmp[:, :], ksq[:, 512 * c:512 * c + 512], axis=AX.X),
                  reads=[ksq[:, 512 * c:512 * c + 512]], writes=[mk.km_tmp[:, :]],
                  tag=f"a{l}km1")
            if nq == 0:
                p.add("dve", lambda v, c=c: v.tensor_copy(
                          out=mk.km_acc[:, c:c + 1], in_=mk.km_tmp[:, :]),
                      reads=[mk.km_tmp[:, :]], writes=[mk.km_acc[:, c:c + 1]],
                      tag=f"a{l}km2")
            else:
                p.add("dve", lambda v, c=c: v.tensor_tensor(
                          mk.km_acc[:, c:c + 1], mk.km_acc[:, c:c + 1],
                          mk.km_tmp[:, :], ALU.add),
                      reads=[mk.km_tmp[:, :], mk.km_acc[:, c:c + 1]],
                      writes=[mk.km_acc[:, c:c + 1]], tag=f"a{l}km2")
        if upto == "kproj":
            return True
        rope(ksq, nq0, f"a{l}rk")
        if upto == "krope":
            return True
        for g in range(4):
            for half in range(2):
                def tr(t, half=half, g=g, ksq=ksq):
                    for pp in range(4):
                        ci = 4 * half + pp
                        inst = t.transpose(mk.pt[:, 128 * pp:128 * pp + 128],
                                           ksq[:, 512 * ci + 128 * g:512 * ci + 128 * g + 128],
                                           mk.eye[:, :])
                    return inst
                p.add("pe", tr, reads=[ksq[:, :], mk.eye[:, :]], writes=[mk.pt[:, :]],
                      tag=f"a{l}tr")
                p.add("act", lambda a, half=half: a.copy(
                          krt[:, 512 * half:512 * half + 512], mk.pt[:, :]),
                      reads=[mk.pt[:, :]], writes=[krt[:, 512 * half:512 * half + 512]],
                      tag=f"a{l}trev")
            for vh in range(2):
                wsl = ws[2 + vh]
                psq = mk.ps[3 + vh]

                def mmv(t, wsl=wsl, g=g, psq=psq, par=par):
                    for kc in range(CH):
                        inst = MM(t, psq[:, :], hb(kc, par)[:, 128 * g:128 * g + 128],
                                  wsl[:, 512 * kc:512 * kc + 512],
                                  start=(kc == 0), stop=(kc == CH - 1))
                    return inst
                p.add("pe", mmv, reads=[wsl[:, :]] + [hb(kc, par) for kc in range(CH)],
                      writes=[psq[:, :]], tag=f"a{l}vmm")
                p.add("act", lambda a, vh=vh, psq=psq: a.copy(
                          vt[:, 512 * vh:512 * vh + 512], psq[:, :]),
                      reads=[psq[:, :]], writes=[vt[:, 512 * vh:512 * vh + 512]],
                      tag=f"a{l}vev")
            def mmkv(t, nq=nq, g=g):
                for h in range(NH):
                    rb = (32 * h) % 128
                    for half in range(2):
                        inst = MM(t, mk.kvp[rb:rb + 32, (h // 4) * 128 + 64 * half:
                                            (h // 4) * 128 + 64 * half + 64],
                                  krt[:, 512 * half + 32 * h:512 * half + 32 * h + 32],
                                  vt[:, 64 * h:64 * h + 64],
                                  start=(nq == 0 and g == 0 and h < 4 and half == 0),
                                  stop=False, tp=(0, rb))
                return inst
            p.add("pe", mmkv, reads=[krt[:, :], vt[:, :]], writes=[mk.kvp[:, :]],
                  tag=f"a{l}kv")
            def mmks(t, nq=nq, g=g):
                st = (nq == 0 and g == 0)
                MM(t, mk.ps[5][0:1, :], mk.ones_col[:, :], krt[:, 0:512],
                   start=st, stop=False)
                return MM(t, mk.ps[5][32:33, :], mk.ones_col[:, :], krt[:, 512:1024],
                          start=st, stop=False, tp=(0, 32))
            p.add("pe", mmks, reads=[krt[:, :], mk.ones_col[:, :]],
                  writes=[mk.ps[5][:, :]], tag=f"a{l}ks")
            if upto == "ktile" and g == 0:
                return True

    if upto == "kvtiles":
        return True
    def ksev(a):
        a.copy(mk.krs[0:1, 0:512], mk.ps[5][0:1, :])
        return a.copy(mk.krs[32:33, 0:512], mk.ps[5][32:33, :])
    p.add("act", ksev, reads=[mk.ps[5][:, :]], writes=[mk.krs[:, :]], tag=f"a{l}ksev")
    p.dma("sp", "krsmv", [(mk.krs[0:1, 512:1024], mk.krs[32:33, 0:512])],
          tag=f"a{l}krsmv")
    if upto == "ksev":
        return True

    def mmbv(t):
        for h in range(NH):
            rb = (32 * h) % 128
            for half in range(2):
                inst = MM(t, mk.kvp[rb:rb + 32, (h // 4) * 128 + 64 * half:
                                    (h // 4) * 128 + 64 * half + 64],
                          mk.krs[0:1, 512 * half + 32 * h:512 * half + 32 * h + 32],
                          mk.bvr[0:1, 64 * h:64 * h + 64],
                          start=False, stop=(h == NH - 1 and half == 1),
                          tp=(0, rb))
        return inst
    p.add("pe", mmbv, reads=[mk.krs[:, :], mk.bvr[:, :]], writes=[mk.kvp[:, :]],
          tag=f"a{l}bv")
    if upto == "mmbv":
        return True
    p.add("pool", lambda g: g.memset(mk.kvb[:, :], 0.0),
          writes=[mk.kvb[:, :]], tag=f"a{l}kvz")
    for Mp in range(CH):
        for half in range(2):
            base = 256 * Mp + 128 * half
            for hh in range(2):
                h = 2 * Mp + hh
                rb = (32 * h) % 128
                p.add("dve", lambda v, base=base, hh=hh, h=h, rb=rb, half=half:
                      v.tensor_scalar_mul(
                          mk.kvb[rb:rb + 32, base + 64 * hh:base + 64 * hh + 64],
                          mk.kvp[rb:rb + 32, (h // 4) * 128 + 64 * half:
                                 (h // 4) * 128 + 64 * half + 64], 1.0 / T),
                      reads=[mk.kvp[:, :]],
                      writes=[mk.kvb[rb:rb + 32, base + 64 * hh:base + 64 * hh + 64]],
                      tag=f"a{l}kvev")

    if upto == "kv":
        return True
    p.add("pool", lambda g: g.memset(mk.km_mat[:, :], 0.0),
          writes=[mk.km_mat[:, :]], tag=f"a{l}kmz")
    for c in range(CH):
        cc = c % 4
        for q4 in range(4):
            h = 4 * cc + q4
            p.add("dve", lambda v, c=c, q4=q4, h=h: v.tensor_scalar_mul(
                      mk.km_mat[32 * q4:32 * q4 + 32, 16 * c + h:16 * c + h + 1],
                      mk.km_acc[32 * q4:32 * q4 + 32, c:c + 1], 1.0 / T),
                  reads=[mk.km_acc[:, c:c + 1]],
                  writes=[mk.km_mat[:, 16 * c:16 * c + 16]], tag=f"a{l}kmb")

    for nq in range(4):
        nq0 = 512 * nq
        par = nq % 2
        ksq = kscv(par)
        cast_h(nq0, f"a{l}qc", par)
        proj((ws[4], ws[5]), mk.bq, ksq, f"a{l}q", par)
        def mmz(t, ksq=ksq):
            for c in range(CH):
                inst = MM(t, mk.ps[5][0:16, :], mk.km_mat[:, 16 * c:16 * c + 16],
                          ksq[:, 512 * c:512 * c + 512],
                          start=(c == 0), stop=(c == CH - 1))
            return inst
        p.add("pe", mmz, reads=[mk.km_mat[:, :]] + [ksq[:, 512 * c:512 * c + 512] for c in range(CH)],
              writes=[mk.ps[5][:, :]], tag=f"a{l}z")

        p.add("dve", lambda v: v.tensor_scalar_add(mk.z_sb[:, :], mk.ps[5][0:16, :], 1e-6),
              reads=[mk.ps[5][:, :]], writes=[mk.z_sb[:, :]], tag=f"a{l}zc1")
        p.add("dve", lambda v: v.reciprocal(mk.z_sb[:, :], mk.z_sb[:, :]),
              reads=[mk.z_sb[:, :]], writes=[mk.z_sb[:, :]], tag=f"a{l}zc2")
        p.add("dve", lambda v: v.tensor_copy(out=mk.z_bf[:, :], in_=mk.z_sb[:, :]),
              reads=[mk.z_sb[:, :]], writes=[mk.z_bf[:, :]], tag=f"a{l}zc3")
        if upto == "zq":
            return True
        rope(ksq, nq0, f"a{l}rq")
        for ec in range(CH):
            pso = mk.ps[ec % 2]

            def mmo(t, ec=ec, pso=pso, ksq=ksq):
                for half in range(2):
                    ck = 4 * half + ec // 2
                    inst = MM(t, pso[:, :],
                              mk.kvb[:, 256 * ec + 128 * half:256 * ec + 128 * half + 128],
                              ksq[:, 512 * ck:512 * ck + 512],
                              start=(half == 0), stop=(half == 1))
                return inst
            p.add("pe", mmo, reads=[mk.kvb[:, :], ksq[:, :]], writes=[pso[:, :]],
                  tag=f"a{l}o")
            zbp = mk.ps[2 + ec % 2]
            p.add("pe", lambda t, ec=ec, zbp=zbp: MM(
                      t, zbp[:, :], mk.ebc[:, 128 * ec:128 * ec + 128], mk.z_bf[:, :],
                      start=True, stop=True),
                  reads=[mk.ebc[:, :], mk.z_bf[:, :]], writes=[zbp[:, :]], tag=f"a{l}zb")
            p.add("dve", lambda v, zbp=zbp: v.tensor_copy(out=zbs[:, :], in_=zbp[:, :]),
                  reads=[zbp[:, :]], writes=[zbs[:, :]], tag=f"a{l}zbe")
            if upto == "out" and ec == 0:
                return True

            p.add("dve", lambda v, pso=pso: v.tensor_tensor(
                      zbs[:, :], pso[:, :], zbs[:, :], ALU.mult),
                  reads=[pso[:, :], zbs[:, :]], writes=[zbs[:, :]], tag=f"a{l}sm1")
            p.add("dve", lambda v, ec=ec, nq0=nq0: v.tensor_tensor(
                      b0d[ec][:, nq0:nq0 + 512], b0d[ec][:, nq0:nq0 + 512],
                      zbs[:, :], ALU.add),
                  reads=[zbs[:, :], b0d[ec][:, nq0:nq0 + 512]],
                  writes=[b0d[ec][:, nq0:nq0 + 512]], tag=f"a{l}sm2")
        for c in range(CH):
            p.add("act", lambda a, c=c, nq0=nq0, par=par: a.copy(
                      hb(c, par), b0d[c][:, nq0:nq0 + 512]),
                  reads=[b0d[c][:, nq0:nq0 + 512]], writes=[hb(c, par)],
                  tag=f"a{l}lc")
        items = [(b0d[c][:, nq0:nq0 + 512], hb(c, par), b0d[c][:, nq0:nq0 + 512])
                 for c in range(CH)]
        mk.emit_ln(items, EPS_IN,
                   gamma=[mk.lnAg[:, CH * l + c:CH * l + c + 1] for c in range(CH)],
                   beta=[mk.lnAb[:, CH * l + c:CH * l + c + 1] for c in range(CH)],
                   tag=f"lnA{l}_{seq}_{nq}")
    return False


def build_mlp(mk, seq, scope, b0d, hbfc):
    nc, p = mk.nc, mk.p
    ub = scope.enter_context(nc.sbuf_tensor(f"ub_{seq}", [128, 4 * 512], BF16))
    fsb = scope.enter_context(nc.sbuf_tensor(f"fsb_{seq}", [128, CH * 1024], F32))

    mbf = scope.enter_context(nc.sbuf_tensor(f"mbf_{seq}", [128, CH * 1024], BF16))
    ws = mk.ws

    for nq in range(2):
        nq0 = 1024 * nq
        mbfc = [mbf[:, 1024 * c:1024 * (c + 1)] for c in range(CH)]
        for c in range(CH):
            p.add("act", lambda a, c=c, nq0=nq0: a.copy(
                      mbfc[c], b0d[c][:, nq0:nq0 + 1024]),
                  reads=[b0d[c][:, nq0:nq0 + 1024]], writes=[mbfc[c]], tag="mh")
        for b in range(8):
            wsl1 = ws[b % 2]
            wsl2 = ws[2 + b % 2]
            p.dma("pool", f"ws{b % 2}", [(wsl1[:, :], mk.d_W1[b, :, :])], tag="w1l")
            p.dma("pool", f"ws{2 + b % 2}",
                  [(wsl2[:, :], mk.d_W2[:, 4096 * b:4096 * b + 4096])], tag="w2l")
            for nh in range(2):
                o = 512 * nh
                for msub in range(4):
                    psq = mk.ps[msub % 2]

                    def mm1(t, wsl1=wsl1, msub=msub, psq=psq, o=o):
                        for kc in range(CH):
                            inst = MM(t, psq[:, :],
                                      wsl1[:, 512 * kc + 128 * msub:512 * kc + 128 * msub + 128],
                                      mbfc[kc][:, o:o + 512],
                                      start=(kc == 0), stop=(kc == CH - 1))
                        return inst
                    p.add("pe", mm1,
                          reads=[wsl1[:, :]] + [mbfc[kc][:, o:o + 512] for kc in range(CH)],
                          writes=[psq[:, :]], tag="w1mm")
                    hidx = 4 * b + msub
                    p.add("act", lambda a, msub=msub, psq=psq, hidx=hidx: a.activation(
                              ub[:, 512 * msub:512 * msub + 512], psq[:, :], AF.Gelu,
                              bias=mk.b1c[:, hidx:hidx + 1], scale=1.0),
                          reads=[psq[:, :], mk.b1c[:, hidx:hidx + 1]],
                          writes=[ub[:, 512 * msub:512 * msub + 512]], tag="w1ev")
                for mc in range(CH):
                    psq = mk.ps[4 + (mc % 2)]

                    def mm2(t, wsl2=wsl2, mc=mc, psq=psq):
                        for kci in range(4):
                            inst = MM(t, psq[:, :],
                                      wsl2[:, 1024 * kci + 128 * mc:1024 * kci + 128 * mc + 128],
                                      ub[:, 512 * kci:512 * kci + 512],
                                      start=(kci == 0), stop=(kci == 3))
                        return inst
                    p.add("pe", mm2, reads=[wsl2[:, :], ub[:, :]], writes=[psq[:, :]],
                          tag="w2mm")
                    fdst = fsb[:, 1024 * mc + 512 * nh:1024 * mc + 512 * nh + 512]
                    if b == 0:
                        p.add("act", lambda a, psq=psq, fdst=fdst: a.copy(fdst, psq[:, :]),
                              reads=[psq[:, :]], writes=[fdst], tag="f0")
                    else:
                        p.add("dve", lambda v, psq=psq, fdst=fdst: v.tensor_tensor(
                                  fdst, fdst, psq[:, :], ALU.add),
                              reads=[psq[:, :], fdst], writes=[fdst], tag="facc")
        for nh in range(2):
            o = 512 * nh
            for c in range(CH):
                fdst = fsb[:, 1024 * c + o:1024 * c + o + 512]
                p.add("dve", lambda v, fdst=fdst, c=c, nq0=nq0, o=o: v.scalar_tensor_tensor(
                          fdst, fdst, mk.b2c[:, c:c + 1], b0d[c][:, nq0 + o:nq0 + o + 512],
                          ALU.add, ALU.add),
                      reads=[fdst, mk.b2c[:, c:c + 1], b0d[c][:, nq0 + o:nq0 + o + 512]],
                      writes=[fdst], tag="fb2")
                p.add("act", lambda a, c=c, fdst=fdst: a.copy(hbfc[8 + c], fdst),
                      reads=[fdst], writes=[hbfc[8 + c]], tag="l2c")
            fwin = [fsb[:, 1024 * c + o:1024 * c + o + 512] for c in range(CH)]
            mk.emit_ln([(fwin[c], hbfc[8 + c], fwin[c]) for c in range(CH)], EPS_IN,
                       gamma=[mk.ln2g[:, c:c + 1] for c in range(CH)],
                       beta=[mk.ln2b[:, c:c + 1] for c in range(CH)],
                       tag=f"ln2_{seq}_{nq}_{nh}")
            for c in range(CH):
                p.add("act", lambda a, c=c, fdst=fwin[c]: a.copy(hbfc[8 + c], fdst),
                      reads=[fwin[c]], writes=[hbfc[8 + c]], tag="lpc")
            mk.emit_ln([(fwin[c], hbfc[8 + c], fwin[c]) for c in range(CH)], EPS_OUT,
                       tag=f"lnp_{seq}_{nq}_{nh}")
            p.dma("sp", "xo", [(b0d[c][:, nq0 + o:nq0 + o + 512],
                                mk.d_x[seq, 128 * c:128 * (c + 1), nq0 + o:nq0 + o + 512])
                               for c in range(CH)], tag="xol")
            for c in range(CH):
                p.add("dve", lambda v, c=c, nq0=nq0, o=o, fdst=fwin[c]: v.tensor_tensor(
                          b0d[c][:, nq0 + o:nq0 + o + 512],
                          b0d[c][:, nq0 + o:nq0 + o + 512], fdst, ALU.add),
                      reads=[fwin[c], b0d[c][:, nq0 + o:nq0 + o + 512]],
                      writes=[b0d[c][:, nq0 + o:nq0 + o + 512]], tag="x2")
            for c in range(CH):
                p.add("act", lambda a, c=c, nq0=nq0, o=o: a.copy(
                          hbfc[8 + c], b0d[c][:, nq0 + o:nq0 + o + 512]),
                      reads=[b0d[c][:, nq0 + o:nq0 + o + 512]], writes=[hbfc[8 + c]],
                      tag="l3c")
            x3win = [hbfc[c] for c in range(CH)]
            mk.emit_ln([(b0d[c][:, nq0 + o:nq0 + o + 512], hbfc[8 + c], x3win[c])
                        for c in range(CH)], EPS_OUT,
                       bf_out=True, scratch=fwin, tag=f"ln3_{seq}_{nq}_{nh}")
            p.dma("sp", "x3sp", [(mk.d_x3sp[128 * c:128 * (c + 1), nq0 + o:nq0 + o + 512],
                                  x3win[c]) for c in range(CH)], tag="x3sp")
        p.dma("sp", "x2sp", [(mk.d_x2sp[128 * c:128 * (c + 1), nq0:nq0 + 1024],
                              b0d[c][:, nq0:nq0 + 1024]) for c in range(CH)], tag="x2sp")


def build_ffn(mk, seq, scope):
    nc, p = mk.nc, mk.p
    x3w = scope.enter_context(nc.sbuf_tensor(f"x3w_{seq}", [128, CH * 512], BF16))
    Ab = scope.enter_context(nc.sbuf_tensor(f"A_{seq}", [128, 32 * 512], BF16))
    Bb = scope.enter_context(nc.sbuf_tensor(f"B_{seq}", [128, 32 * 512], BF16))
    sil = scope.enter_context(nc.sbuf_tensor(f"sil_{seq}", [128, 512], F32))
    u2 = scope.enter_context(nc.sbuf_tensor(f"u2_{seq}", [128, 512], F32))
    f3 = scope.enter_context(nc.sbuf_tensor(f"f3_{seq}", [128, CH * 512], F32))
    x2c = scope.enter_context(nc.sbuf_tensor(f"x2c_{seq}", [128, CH * 512], F32))
    ws = mk.ws
    x3wc = [x3w[:, 512 * c:512 * c + 512] for c in range(CH)]

    for tc in range(4):
        t0 = 512 * tc
        src3 = mk.d_x3sp.rearrange("(c p) n -> p c n", p=128)
        p.dma("sp", "x3ld", [(x3w[:, :].rearrange("p (c n) -> p c n", n=512),
                              src3[:, :, t0:t0 + 512])], tag="x3ld")
        for b in range(8):
            p.dma("pool", f"ws{b % 2}", [(ws[b % 2][:, :], mk.d_f1[b, :, :])], tag="f1l")
            p.dma("pool", f"ws{2 + b % 2}", [(ws[2 + b % 2][:, :], mk.d_f2[b, :, :])],
                  tag="f2l")
            for msub in range(4):
                hidx = 4 * b + msub
                psq = mk.ps[msub % 2]

                def mmf1(t, b=b, msub=msub, psq=psq):
                    wsl = ws[b % 2]
                    for kc in range(CH):
                        inst = MM(t, psq[:, :],
                                  wsl[:, 512 * kc + 128 * msub:512 * kc + 128 * msub + 128],
                                  x3wc[kc], start=(kc == 0), stop=(kc == CH - 1))
                    return inst
                p.add("pe", mmf1, reads=[ws[b % 2][:, :]] + x3wc, writes=[psq[:, :]],
                      tag="f1mm")
                p.add("act", lambda a, psq=psq, hidx=hidx: a.activation(
                          sil[:, :], psq[:, :], AF.Silu, bias=0.0,
                          scale=mk.s1c[:, hidx:hidx + 1]),
                      reads=[psq[:, :], mk.s1c[:, hidx:hidx + 1]], writes=[sil[:, :]],
                      tag="f1ev")
                ps2q = mk.ps[2 + msub % 2]

                def mmf2(t, b=b, msub=msub, ps2q=ps2q):
                    wsl = ws[2 + b % 2]
                    for kc in range(CH):
                        inst = MM(t, ps2q[:, :],
                                  wsl[:, 512 * kc + 128 * msub:512 * kc + 128 * msub + 128],
                                  x3wc[kc], start=(kc == 0), stop=(kc == CH - 1))
                    return inst
                p.add("pe", mmf2, reads=[ws[2 + b % 2][:, :]] + x3wc, writes=[ps2q[:, :]],
                      tag="f2mm")
                p.add("act", lambda a, ps2q=ps2q, hidx=hidx: a.activation(
                          u2[:, :], ps2q[:, :], AF.Identity, bias=0.0,
                          scale=mk.s2c[:, hidx:hidx + 1]),
                      reads=[ps2q[:, :], mk.s2c[:, hidx:hidx + 1]], writes=[u2[:, :]],
                      tag="f2ev")
                p.add("dve", lambda v, hidx=hidx: v.tensor_tensor(
                          Ab[:, 512 * hidx:512 * hidx + 512], sil[:, :], u2[:, :],
                          ALU.mult),
                      reads=[sil[:, :], u2[:, :]],
                      writes=[Ab[:, 512 * hidx:512 * hidx + 512]], tag="f12m")
        for g in range(32):
            p.dma("pool", f"ws{4 + g % 2}", [(ws[4 + g % 2][:, :], mk.d_h1[g, :, :])],
                  tag="h1l")
            psq = mk.ps[4 + g % 2]

            def mmh1(t, g=g, psq=psq):
                wsl = ws[4 + g % 2]
                for kc in range(32):
                    inst = MM(t, psq[:, :], wsl[:, 128 * kc:128 * kc + 128],
                              Ab[:, 512 * kc:512 * kc + 512],
                              start=(kc == 0), stop=(kc == 31))
                return inst
            p.add("pe", mmh1, reads=[ws[4 + g % 2][:, :], Ab[:, :]], writes=[psq[:, :]],
                  tag="h1mm")
            p.add("act", lambda a, g=g, psq=psq: a.activation(
                      Bb[:, 512 * g:512 * g + 512], psq[:, :], AF.Gelu, bias=0.0,
                      scale=mk.h1sc[:, g:g + 1]),
                  reads=[psq[:, :], mk.h1sc[:, g:g + 1]],
                  writes=[Bb[:, 512 * g:512 * g + 512]], tag="h1ev")
        for g in range(32):
            p.dma("pool", f"ws{g % 2}", [(ws[g % 2][:, :], mk.d_h2[g, :, :])], tag="h2l")
            psq = mk.ps[g % 2]

            def mmh2(t, g=g, psq=psq):
                wsl = ws[g % 2]
                for kc in range(32):
                    inst = MM(t, psq[:, :], wsl[:, 128 * kc:128 * kc + 128],
                              Bb[:, 512 * kc:512 * kc + 512],
                              start=(kc == 0), stop=(kc == 31))
                return inst
            p.add("pe", mmh2, reads=[ws[g % 2][:, :], Bb[:, :]], writes=[psq[:, :]],
                  tag="h2mm")
            p.add("act", lambda a, g=g, psq=psq: a.activation(
                      Ab[:, 512 * g:512 * g + 512], psq[:, :], AF.Gelu, bias=0.0,
                      scale=mk.h2sc[:, g:g + 1]),
                  reads=[psq[:, :], mk.h2sc[:, g:g + 1]],
                  writes=[Ab[:, 512 * g:512 * g + 512]], tag="h2ev")
        for g in range(CH):
            p.dma("pool", f"ws{2 + g % 2}", [(ws[2 + g % 2][:, :], mk.d_w3[g, :, :])],
                  tag="w3l")
            psq = mk.ps[2 + g % 2]

            def mmw3(t, g=g, psq=psq):
                wsl = ws[2 + g % 2]
                for kc in range(32):
                    inst = MM(t, psq[:, :], wsl[:, 128 * kc:128 * kc + 128],
                              Ab[:, 512 * kc:512 * kc + 512],
                              start=(kc == 0), stop=(kc == 31))
                return inst
            p.add("pe", mmw3, reads=[ws[2 + g % 2][:, :], Ab[:, :]], writes=[psq[:, :]],
                  tag="w3mm")
            p.add("act", lambda a, g=g, psq=psq: a.activation(
                      f3[:, 512 * g:512 * g + 512], psq[:, :], AF.Identity, bias=0.0,
                      scale=mk.s3c[:, g:g + 1]),
                  reads=[psq[:, :], mk.s3c[:, g:g + 1]],
                  writes=[f3[:, 512 * g:512 * g + 512]], tag="w3ev")
        bfw = [Bb[:, 512 * c:512 * c + 512] for c in range(CH)]
        for c in range(CH):
            p.add("act", lambda a, c=c: a.copy(
                      bfw[c], f3[:, 512 * c:512 * c + 512]),
                  reads=[f3[:, 512 * c:512 * c + 512]], writes=[bfw[c]], tag="lfc")
        f3w = [f3[:, 512 * c:512 * c + 512] for c in range(CH)]
        mk.emit_ln([(f3w[c], bfw[c], f3w[c]) for c in range(CH)], EPS_OUT,
                   tag=f"lnf_{seq}_{tc}")
        src2 = mk.d_x2sp.rearrange("(c p) n -> p c n", p=128)
        p.dma("sp", "x2ld", [(x2c[:, :].rearrange("p (c n) -> p c n", n=512),
                              src2[:, :, t0:t0 + 512])], tag="x2ld")
        for c in range(CH):
            p.add("dve", lambda v, c=c: v.tensor_tensor(
                      x2c[:, 512 * c:512 * c + 512], x2c[:, 512 * c:512 * c + 512],
                      f3w[c], ALU.add),
                  reads=[f3w[c], x2c[:, 512 * c:512 * c + 512]],
                  writes=[x2c[:, 512 * c:512 * c + 512]], tag="yadd")
        dsty = mk.d_y[seq, :, t0:t0 + 512].rearrange("(c p) n -> p c n", p=128)
        p.dma("sp", "yout", [(dsty, x2c[:, :].rearrange("p (c n) -> p c n", n=512))],
              tag="yst")



def build_program(stop_after=None):
    from contextlib import ExitStack
    nc = bass.Bass("TRN2", target_bir_lowering=False)
    mk = MK(nc)
    p = mk.p
    mk.load_constants()

    stopped = False
    for seq in range(NSEQ):
        if stopped:
            break
        with ExitStack() as seq_scope:
            stopped = build_seq(mk, seq, seq_scope, stop_after)
        p.barrier(exclude_sems=("consts_c", "consts_p"))

    p.final_wait()
    p.emit()
    return nc


def build_seq(mk, seq, seq_scope, stop_after):
    from contextlib import ExitStack
    nc, p = mk.nc, mk.p

    def dump(chunks):
        pairs = [(mk.d_y[seq, 128 * c:128 * (c + 1), :], chunks[c][:, 0:T])
                 for c in range(CH)]
        p.dma("sp", "yout", pairs, tag="dbg_out")
        return True

    main_scope = ExitStack()
    seq_scope.enter_context(main_scope)
    big0 = main_scope.enter_context(
        nc.sbuf_tensor(f"big0_{seq}", [128, CH * PADW], F32))
    b0 = [big0[:, PADW * c:PADW * (c + 1)] for c in range(CH)]
    b0d = [big0[:, PADW * c + 2:PADW * c + 2 + T] for c in range(CH)]
    hbf = main_scope.enter_context(
        nc.sbuf_tensor(f"hbf_{seq}", [128, 2 * CH * 512], BF16))
    hbfc = [hbf[:, 512 * c:512 * (c + 1)] for c in range(2 * CH)]

    build_p0(mk, seq, b0, b0d, hbfc)
    if stop_after == "ln1" and seq == 0:
        return dump(b0d)

    with ExitStack() as conv_scope:
        build_conv(mk, seq, conv_scope, b0, b0d, hbfc)
    p.barrier(exclude_sems=("consts_c", "consts_p"))
    if stop_after == "conv" and seq == 0:
        return dump(b0d)

    with ExitStack() as attn_scope:
        ae = attn_scope.enter_context
        cosT = ae(nc.sbuf_tensor(f"cosT_{seq}", [128, 4 * T], BF16))
        sinT = ae(nc.sbuf_tensor(f"sinT_{seq}", [128, 4 * T], BF16))
        cosc = [cosT[:, T * i:T * (i + 1)] for i in range(4)]
        sinc = [sinT[:, T * i:T * (i + 1)] for i in range(4)]
        p.dma("pool", "tbl",
              [(cosc[i], mk.d_cos[128 * i:128 * (i + 1), :]) for i in range(4)]
              + [(sinc[i], mk.d_sin[128 * i:128 * (i + 1), :]) for i in range(4)],
              tag="tables")
        ksc = ae(nc.sbuf_tensor(f"ksc_{seq}", [128, 2 * CH * 512], BF16))
        tmp1 = ae(nc.sbuf_tensor(f"tmp1_{seq}", [128, 1024], BF16))
        krt = ae(nc.sbuf_tensor(f"krt_{seq}", [128, 1024], BF16))
        vt = ae(nc.sbuf_tensor(f"vt_{seq}", [128, 1024], BF16))
        zbs = ae(nc.sbuf_tensor(f"zbs_{seq}", [128, 512], F32))
        bufs = dict(cosc=cosc, sinc=sinc, ksc=ksc, tmp1=tmp1,
                    krt=krt, vt=vt, zbs=zbs)
        partials = ("kproj", "krope", "ktile", "kvtiles", "ksev", "mmbv", "kv", "zq", "out")
        for l in range(NL):
            upto = stop_after if (l == 0 and stop_after in partials) else None
            if build_attn_layer(mk, seq, l, b0d, hbfc, bufs, upto=upto):
                return dump(b0d)
            if stop_after == f"attn{l}" and seq == 0:
                return dump(b0d)
    p.barrier(exclude_sems=("consts_c", "consts_p"))

    with ExitStack() as mlp_scope:
        build_mlp(mk, seq, mlp_scope, b0d, hbfc)
    p.barrier(exclude_sems=("consts_c", "consts_p"))
    if stop_after == "mlp" and seq == 0:
        return dump(b0d)

    main_scope.close()
    p.barrier(exclude_sems=("consts_c", "consts_p"))

    with ExitStack() as ffn_scope:
        build_ffn(mk, seq, ffn_scope)
    return False



def kernel(**inputs):
    prep = prepare_inputs(inputs)
    x = np.asarray(inputs["L0_attn_x"], np.float32)
    nc = build_program(STOP_AFTER)
    in_maps = []
    for core in range(8):
        m = dict(prep)
        m["x"] = np.ascontiguousarray(x[2 * core:2 * core + 2].transpose(0, 2, 1))
        in_maps.append(m)
    res = run_bass_kernel_spmd(nc, in_maps, core_ids=list(range(8)))
    out = np.empty((16, 2048, 1024), np.float32)
    for core in range(8):
        y = np.asarray(res.results[core]["y"])
        out[2 * core:2 * core + 2] = y.transpose(0, 2, 1)
    return out


# version 06
# speedup vs baseline: 1.0420x; 1.0404x over previous
import numpy as np
import ml_dtypes

import concourse.bass as bass
import concourse.mybir as mybir
from concourse.bass_utils import run_bass_kernel_spmd

F32 = mybir.dt.float32
BF16 = mybir.dt.bfloat16
AF = mybir.ActivationFunctionType
ALU = mybir.AluOpType
AX = mybir.AxisListType

C = 1024
CH = 8
T = 2048
NL = 4
NH = 16
NSEQ = 2
PADW = 2052
EPS_OUT = 1e-5
EPS_IN = 1e-12

STOP_AFTER = None



class Prog:
    ENG = {"pe": "tensor", "act": "scalar", "dve": "vector", "pool": "gpsimd",
           "sp": "sync"}

    def __init__(self, nc):
        self.nc = nc
        self.esem = {e: nc.alloc_semaphore("esem_" + e)
                     for e in ("pe", "act", "dve", "pool")}
        self.tasks = {e: [] for e in self.ENG}
        self.ccnt = {e: 0 for e in ("pe", "act", "dve", "pool")}
        self.dsem = {}
        self.res = {}
        self.waited = {e: {} for e in self.ENG}

    def _iv(self, ap):
        t = ap.tensor
        if isinstance(t, bass.PSumTensorHandle):
            return (t.name, 0, 1 << 40)
        off = ap.offset if isinstance(ap.offset, int) else 0
        span = 1
        for step, count in ap.ap:
            span += abs(step) * (count - 1)
        return (t.name, off, off + span)

    def _collect(self, engine, reads, writes):
        deps = {}

        def want(dep):
            kind, key, val = dep
            k = (kind, key)
            if deps.get(k, -1) < val:
                deps[k] = val

        ivs_r = [self._iv(a) for a in reads]
        ivs_w = [self._iv(a) for a in writes]
        for name, lo, hi in ivs_r:
            ent = self.res.get(name)
            if ent:
                for a, b, dep in ent["w"]:
                    if a < hi and lo < b:
                        want(dep)
        for name, lo, hi in ivs_w:
            ent = self.res.get(name)
            if ent:
                for a, b, dep in ent["w"]:
                    if a < hi and lo < b:
                        want(dep)
                for a, b, dep in ent["r"]:
                    if a < hi and lo < b:
                        want(dep)
        return deps, ivs_r, ivs_w

    def _register(self, ivs_r, ivs_w, token):
        for name, lo, hi in ivs_w:
            ent = self.res.setdefault(name, {"w": [], "r": []})
            ent["w"] = [(a, b, d) for (a, b, d) in ent["w"] if not (lo <= a and b <= hi)]
            ent["r"] = [(a, b, d) for (a, b, d) in ent["r"] if not (lo <= a and b <= hi)]
            ent["w"].append((lo, hi, token))
        for name, lo, hi in ivs_r:
            self.res.setdefault(name, {"w": [], "r": []})["r"].append((lo, hi, token))

    def _mk_waits(self, engine, deps):
        waits = []
        for (kind, key), val in sorted(deps.items(), key=lambda kv: str(kv[0])):
            if val <= 0:
                continue
            sem = self.esem[key] if kind == "E" else self.dsem[key][0]
            prev = self.waited[engine].get((kind, key), -1)
            if val > prev:
                waits.append((sem, val))
                self.waited[engine][(kind, key)] = val
        return waits

    def add(self, engine, fn, reads=(), writes=(), tag=""):
        deps, ivs_r, ivs_w = self._collect(engine, reads, writes)
        waits = self._mk_waits(engine, deps)
        self.ccnt[engine] += 1
        token = ("E", engine, self.ccnt[engine])
        self.tasks[engine].append((waits, fn, tag, False))
        self._register(ivs_r, ivs_w, token)
        return token

    def dma(self, queue, semname, pairs, tag=""):
        if semname not in self.dsem:
            self.dsem[semname] = [self.nc.alloc_semaphore("dsem_" + semname), 0]
        ent = self.dsem[semname]
        prev_cum = ent[1]
        reads = [pp[1] for pp in pairs]
        writes = [pp[0] for pp in pairs]
        deps, ivs_r, ivs_w = self._collect(queue, reads, writes)
        if prev_cum > 0 and deps.get(("D", semname), -1) < prev_cum:
            deps[("D", semname)] = prev_cum
        waits = self._mk_waits(queue, deps)
        ent[1] += 16 * len(pairs)
        token = ("D", semname, ent[1])
        sem = ent[0]

        def fn(eng, pairs=pairs, sem=sem):
            for (o, i) in pairs:
                eng.dma_start(out=o, in_=i).then_inc(sem, 16)
            return None

        self.tasks[queue].append((waits, fn, tag, True))
        self._register(ivs_r, ivs_w, token)
        return token

    def barrier(self, exclude_sems=()):
        targets = {("E", e): self.ccnt[e] for e in self.esem if self.ccnt[e] > 0}
        targets.update({("D", n): ent[1] for n, ent in self.dsem.items()
                        if ent[1] > 0 and n not in exclude_sems})
        for e in self.ENG:
            waits = self._mk_waits(e, targets)
            if waits:
                def fn(eng, waits=waits):
                    return None
                self.tasks[e].append((waits, fn, "barrier", True))
        self.res = {}

    def final_wait(self, engine="sp"):
        targets = {("D", n): ent[1] for n, ent in self.dsem.items() if ent[1] > 0}
        targets.update({("E", e): self.ccnt[e] for e in self.esem if self.ccnt[e] > 0})
        waits = self._mk_waits(engine, targets)

        def fn(eng):
            return None
        self.tasks[engine].append((waits, fn, "final", True))

    def emit(self):
        with self.nc.Block() as block:
            for e, attr in self.ENG.items():
                dec = getattr(block, attr)
                tasks = self.tasks[e]
                esem = self.esem.get(e)

                def body(eng, tasks=tasks, esem=esem, ename=e):
                    for (waits, fn, tag, is_dma) in tasks:
                        for (sem, val) in waits:
                            eng.wait_ge(sem, val)
                        last = fn(eng)
                        if not is_dma:
                            assert last is not None, \
                                f"compute task {tag} on {ename} returned None"
                            last.then_inc(esem, 1)
                dec(body)



def _tile_km(w, granule):
    K, M = w.shape
    kc = K // 128
    nb = M // granule
    out = w.reshape(kc, 128, nb, granule).transpose(2, 1, 0, 3)
    return np.ascontiguousarray(out.reshape(nb, 128, kc * granule))


def _col_pack(v, nchunk):
    return np.ascontiguousarray(np.asarray(v, np.float32).reshape(nchunk, 128).T)


def _perm_order():
    r = np.arange(512)
    return np.concatenate([2 * r, 2 * r + 1])


def prepare_inputs(inputs):
    f = np.float32
    d = {}
    perm = _perm_order()

    j = np.arange(512, dtype=np.float64)
    inv = 1.0 / np.power(10000.0, 2.0 * j / 1024.0)
    pos = np.arange(T, dtype=np.float64)
    ang = pos[None, :] * inv[:, None]
    d["cosT"] = np.cos(ang).astype(f)
    d["sinT"] = np.sin(ang).astype(f)

    d["eye128"] = np.eye(128, dtype=f)
    d["eps_col"] = np.broadcast_to(np.array([[EPS_OUT, EPS_IN]], f), (128, 2)).copy()
    d["ones_col"] = np.ones((128, 1), dtype=f)
    d["ones_row"] = np.ones((1, 128), dtype=f)

    ebc = np.zeros((16, CH * 128), dtype=f)
    for c in range(CH):
        for m in range(128):
            ebc[2 * c + m // 64, 128 * c + m] = 1.0
    d["Ebc"] = ebc

    Wq = np.asarray(inputs["L0_attn_Wq"], dtype=f)[:, :, perm]
    Wk = np.asarray(inputs["L0_attn_Wk"], dtype=f)[:, :, perm]
    Wv = np.asarray(inputs["L0_attn_Wv"], dtype=f)
    d["wq_t"] = np.stack([_tile_km(Wq[l], 512) for l in range(NL)])
    d["wk_t"] = np.stack([_tile_km(Wk[l], 512) for l in range(NL)])
    d["wv_t"] = np.stack([_tile_km(Wv[l], 512) for l in range(NL)])

    bq = np.asarray(inputs["L0_attn_bq"], dtype=f)[:, perm]
    bk = np.asarray(inputs["L0_attn_bk"], dtype=f)[:, perm]
    d["bq_col"] = np.ascontiguousarray(
        np.stack([_col_pack(bq[l], CH) for l in range(NL)], axis=1).reshape(128, NL * CH))
    d["bk_col"] = np.ascontiguousarray(
        np.stack([_col_pack(bk[l], CH) for l in range(NL)], axis=1).reshape(128, NL * CH))

    bv = np.asarray(inputs["L0_attn_bv"], dtype=f)
    bvr = np.zeros((64, NL * C), dtype=f)
    for l in range(NL):
        bvr[0, C * l:C * (l + 1)] = bv[l]
        bvr[32, C * l:C * (l + 1)] = bv[l]
    d["bv_rows"] = bvr

    d["lnAg_col"] = np.ascontiguousarray(
        np.stack([_col_pack(np.asarray(inputs["L0_attn_lnA_g"], f)[l], CH)
                  for l in range(NL)], axis=1).reshape(128, NL * CH))
    d["lnAb_col"] = np.ascontiguousarray(
        np.stack([_col_pack(np.asarray(inputs["L0_attn_lnA_b"], f)[l], CH)
                  for l in range(NL)], axis=1).reshape(128, NL * CH))
    d["ln0g_col"] = _col_pack(inputs["L0_attn_ln0_g"], CH)
    d["ln0b_col"] = _col_pack(inputs["L0_attn_ln0_b"], CH)
    d["ln2g_col"] = _col_pack(inputs["L0_attn_ln2_g"], CH)
    d["ln2b_col"] = _col_pack(inputs["L0_attn_ln2_b"], CH)

    w1r = np.asarray(inputs["L0_attn_conv1_w"], f).reshape(C, 4, 5)
    w2r = np.asarray(inputs["L0_attn_conv2_w"], f)
    dia1 = np.zeros((CH, 128, 20 * 128), dtype=f)
    dia2 = np.zeros((CH, 128, 20 * 128), dtype=f)
    idx = np.arange(128)
    for c in range(CH):
        for jj in range(4):
            for t5 in range(5):
                col = 128 * (5 * jj + t5)
                dia1[c, idx, col + idx] = w1r[128 * c + idx, jj, t5]
                dia2[c, idx, col + idx] = w2r[128 * c + idx, jj, t5]
    d["conv1d"] = dia1
    d["conv2d"] = dia2
    d["cb1_col"] = np.ascontiguousarray(
        np.asarray(inputs["L0_attn_conv1_b"], f).reshape(CH, 128, 4)
        .transpose(1, 0, 2).reshape(128, CH * 4))
    d["cb2_col"] = _col_pack(inputs["L0_attn_conv2_b"], CH)

    d["W1_t"] = _tile_km(np.asarray(inputs["L0_attn_W1"], f), 512)
    d["W2_t"] = _tile_km(np.asarray(inputs["L0_attn_W2"], f), 1024)[0]
    d["b1_col"] = _col_pack(inputs["L0_attn_b1"], 32)
    d["b2_col"] = _col_pack(inputs["L0_attn_b2"], CH)

    d["f1_t"] = _tile_km(np.ascontiguousarray(np.asarray(inputs["L0_ffn_w1_w"], f).T), 512)
    d["f2_t"] = _tile_km(np.ascontiguousarray(np.asarray(inputs["L0_ffn_w2_w"], f).T), 512)
    d["h1_t"] = _tile_km(np.ascontiguousarray(np.asarray(inputs["L0_ffn_h1_w"], f).T), 128)
    d["h2_t"] = _tile_km(np.ascontiguousarray(np.asarray(inputs["L0_ffn_h2_w"], f).T), 128)
    d["w3_t"] = _tile_km(np.ascontiguousarray(np.asarray(inputs["L0_ffn_w3_w"], f).T), 128)
    d["s1_col"] = _col_pack(inputs["L0_ffn_w1_s"], 32)
    d["s2_col"] = _col_pack(inputs["L0_ffn_w2_s"], 32)
    d["h1s_col"] = _col_pack(inputs["L0_ffn_h1_s"], 32)
    d["h2s_col"] = _col_pack(inputs["L0_ffn_h2_s"], 32)
    d["s3_col"] = _col_pack(inputs["L0_ffn_w3_s"], CH)
    return d



class MK:
    def __init__(self, nc):
        self.nc = nc
        self.p = Prog(nc)
        self._declare_dram()
        self._alloc_top()

    def _declare_dram(self):
        nc = self.nc

        def D(n, sh, dt, kind="ExternalInput"):
            return nc.dram_tensor(n, sh, dt, kind=kind).ap()

        self.d_x = D("x", [NSEQ, C, T], F32)
        self.d_cos = D("cosT", [512, T], F32)
        self.d_sin = D("sinT", [512, T], F32)
        self.d_eye = D("eye128", [128, 128], F32)
        self.d_eps = D("eps_col", [128, 2], F32)
        self.d_ones_col = D("ones_col", [128, 1], F32)
        self.d_ones_row = D("ones_row", [1, 128], F32)
        self.d_ebc = D("Ebc", [16, CH * 128], F32)
        self.d_wq = D("wq_t", [NL, 2, 128, 4096], F32)
        self.d_wk = D("wk_t", [NL, 2, 128, 4096], F32)
        self.d_wv = D("wv_t", [NL, 2, 128, 4096], F32)
        self.d_bq = D("bq_col", [128, NL * CH], F32)
        self.d_bk = D("bk_col", [128, NL * CH], F32)
        self.d_bvr = D("bv_rows", [64, NL * C], F32)
        self.d_lnAg = D("lnAg_col", [128, NL * CH], F32)
        self.d_lnAb = D("lnAb_col", [128, NL * CH], F32)
        self.d_ln0g = D("ln0g_col", [128, CH], F32)
        self.d_ln0b = D("ln0b_col", [128, CH], F32)
        self.d_ln2g = D("ln2g_col", [128, CH], F32)
        self.d_ln2b = D("ln2b_col", [128, CH], F32)
        self.d_c1d = D("conv1d", [CH, 128, 2560], F32)
        self.d_c2d = D("conv2d", [CH, 128, 2560], F32)
        self.d_cb1 = D("cb1_col", [128, CH * 4], F32)
        self.d_cb2 = D("cb2_col", [128, CH], F32)
        self.d_W1 = D("W1_t", [8, 128, 4096], F32)
        self.d_W2 = D("W2_t", [128, 32768], F32)
        self.d_b1 = D("b1_col", [128, 32], F32)
        self.d_b2 = D("b2_col", [128, CH], F32)
        self.d_f1 = D("f1_t", [8, 128, 4096], F32)
        self.d_f2 = D("f2_t", [8, 128, 4096], F32)
        self.d_h1 = D("h1_t", [32, 128, 4096], F32)
        self.d_h2 = D("h2_t", [32, 128, 4096], F32)
        self.d_w3 = D("w3_t", [8, 128, 4096], F32)
        self.d_s1 = D("s1_col", [128, 32], F32)
        self.d_s2 = D("s2_col", [128, 32], F32)
        self.d_h1s = D("h1s_col", [128, 32], F32)
        self.d_h2s = D("h2s_col", [128, 32], F32)
        self.d_s3 = D("s3_col", [128, CH], F32)
        self.d_y = D("y", [NSEQ, C, T], F32, "ExternalOutput")
        self.d_x2sp = D("x2_spill", [C, T], F32, "Internal")
        self.d_x3sp = D("x3_spill", [C, T], BF16, "Internal")

    def _alloc_top(self):
        nc = self.nc
        A = nc.alloc_sbuf_tensor
        self.ws = [A(f"ws{i}", [128, 4096], BF16) for i in range(6)]
        self.eye = A("eye", [128, 128], BF16)
        self.epsc = A("epsc", [128, 2], F32)
        self.ones_col = A("onescol", [128, 1], BF16)
        self.ones_row = A("onesrow", [1, 128], F32)
        self.ebc = A("ebc", [16, CH * 128], BF16)
        self.bvr = A("bvr", [64, C], BF16)
        self.bq = A("bqc", [128, NL * CH], F32)
        self.bk = A("bkc", [128, NL * CH], F32)
        self.lnAg = A("lnAg", [128, NL * CH], F32)
        self.lnAb = A("lnAb", [128, NL * CH], F32)
        self.ln0g = A("ln0g", [128, CH], F32)
        self.ln0b = A("ln0b", [128, CH], F32)
        self.ln2g = A("ln2g", [128, CH], F32)
        self.ln2b = A("ln2b", [128, CH], F32)
        self.cb1 = A("cb1", [128, CH * 4], F32)
        self.cb2 = A("cb2", [128, CH], F32)
        self.b1c = A("b1c", [128, 32], F32)
        self.b2c = A("b2c", [128, CH], F32)
        self.s1c = A("s1c", [128, 32], F32)
        self.s2c = A("s2c", [128, 32], F32)
        self.h1sc = A("h1sc", [128, 32], F32)
        self.h2sc = A("h2sc", [128, 32], F32)
        self.s3c = A("s3c", [128, CH], F32)
        self.m_row = A("m_row", [1, 512], F32)
        self.r_row = A("r_row", [1, 512], F32)
        self.km_tmp = A("km_tmp", [128, 1], F32)
        self.km_acc = A("km_acc", [128, CH], F32)
        self.km_mat = A("km_mat", [128, CH * 16], BF16)
        self.krs = A("krs", [128, 1024], BF16)
        self.z_sb = A("z_sb", [16, 1024], F32)
        self.z_bf = A("z_bf", [16, 1024], BF16)
        self.kvb = A("kvb", [128, 16 * 128], BF16)
        P = nc.alloc_psum_tensor
        self.ps = [P(f"ps{i}", [128, 512], F32) for i in range(6)]
        self.pt = P("pt", [128, 512], BF16)
        self.kvp = P("kvp", [128, 512], F32)

    def load_constants(self):
        p = self.p
        p.dma("pool", "consts_c", [
            (self.eye[:, :], self.d_eye),
            (self.ones_col[:, :], self.d_ones_col),
            (self.ebc[:, :], self.d_ebc),
        ], tag="consts_cast")
        p.dma("sp", "consts_p", [
            (self.ones_row[:, :], self.d_ones_row),
            (self.epsc[:, :], self.d_eps),
            (self.bq[:, :], self.d_bq), (self.bk[:, :], self.d_bk),
            (self.lnAg[:, :], self.d_lnAg), (self.lnAb[:, :], self.d_lnAb),
            (self.ln0g[:, :], self.d_ln0g), (self.ln0b[:, :], self.d_ln0b),
            (self.ln2g[:, :], self.d_ln2g), (self.ln2b[:, :], self.d_ln2b),
            (self.cb1[:, :], self.d_cb1), (self.cb2[:, :], self.d_cb2),
            (self.b1c[:, :], self.d_b1), (self.b2c[:, :], self.d_b2),
            (self.s1c[:, :], self.d_s1), (self.s2c[:, :], self.d_s2),
            (self.h1sc[:, :], self.d_h1s), (self.h2sc[:, :], self.d_h2s),
            (self.s3c[:, :], self.d_s3),
        ], tag="consts_plain")

    def emit_ln(self, items, eps, gamma=None, beta=None, dst_bf=None,
                bf_out=False, scratch=None, tag="ln"):
        p = self.p
        ps_st, ps_m, ps_r = self.ps[2], self.ps[3], self.ps[4]
        srcs = [it[0] for it in items]
        bfs = [it[1] for it in items]
        dsts = [it[2] for it in items]

        def mm_sum(t):
            for c in range(CH):
                inst = t.matmul(ps_st[0:1, :], self.ones_col[:, :], bfs[c],
                                start=(c == 0), stop=False, skip_group_check=True)
            return inst
        p.add("pe", mm_sum, reads=bfs + [self.ones_col[:, :]],
              writes=[ps_st[:, :]], tag=tag + "_mmsum")
        for c in range(CH):
            p.add("act", lambda a, c=c: a.activation(bfs[c], bfs[c], AF.Square),
                  reads=[bfs[c]], writes=[bfs[c]], tag=tag + "_sq")

        def mm_sq(t):
            for c in range(CH):
                inst = t.matmul(ps_st[64:65, :], self.ones_col[:, :], bfs[c],
                                start=(c == 0), stop=(c == CH - 1),
                                tile_position=(0, 64), skip_group_check=True)
            return inst
        p.add("pe", mm_sq, reads=bfs + [self.ones_col[:, :]],
              writes=[ps_st[:, :]], tag=tag + "_mmsq")

        p.add("dve", lambda v: v.tensor_scalar_mul(self.m_row[:, :], ps_st[0:1, :], 1.0 / C),
              reads=[ps_st[:, :]], writes=[self.m_row[:, :]], tag=tag + "_m")
        p.add("dve", lambda v: v.tensor_tensor(self.r_row[:, :], self.m_row[:, :],
                                               self.m_row[:, :], ALU.mult),
              reads=[self.m_row[:, :]], writes=[self.r_row[:, :]], tag=tag + "_m2")
        p.add("dve", lambda v: v.scalar_tensor_tensor(self.r_row[:, :], ps_st[64:65, :],
                                                      1.0 / C, self.r_row[:, :],
                                                      ALU.mult, ALU.subtract),
              reads=[ps_st[:, :], self.r_row[:, :]], writes=[self.r_row[:, :]],
              tag=tag + "_var")
        ecol = self.epsc[0:1, 0:1] if eps == EPS_OUT else self.epsc[0:1, 1:2]
        p.add("act", lambda a, ecol=ecol: a.activation(self.r_row[:, :], self.r_row[:, :],
                                                       AF.Sqrt, bias=ecol, scale=1.0),
              reads=[self.r_row[:, :], ecol], writes=[self.r_row[:, :]], tag=tag + "_sd")
        p.add("dve", lambda v: v.reciprocal(self.r_row[:, :], self.r_row[:, :]),
              reads=[self.r_row[:, :]], writes=[self.r_row[:, :]], tag=tag + "_rstd")
        p.add("pe", lambda t: t.matmul(ps_m[:, :], self.ones_row[:, :], self.m_row[:, :],
                                       start=True, stop=True, skip_group_check=True),
              reads=[self.m_row[:, :], self.ones_row[:, :]], writes=[ps_m[:, :]],
              tag=tag + "_bm")
        p.add("pe", lambda t: t.matmul(ps_r[:, :], self.ones_row[:, :], self.r_row[:, :],
                                       start=True, stop=True, skip_group_check=True),
              reads=[self.r_row[:, :], self.ones_row[:, :]], writes=[ps_r[:, :]],
              tag=tag + "_br")

        for c in range(CH):
            g = gamma[c] if gamma is not None else None
            b = beta[c] if beta is not None else None

            if bf_out:
                scr = scratch[c]
                p.add("dve", lambda v, c=c, scr=scr: v.tensor_tensor(
                          scr, srcs[c], ps_m[:, :], ALU.subtract),
                      reads=[srcs[c], ps_m[:, :]], writes=[scr], tag=tag + "_a1")
                p.add("dve", lambda v, c=c, scr=scr: v.scalar_tensor_tensor(
                          dsts[c], scr, 1.0, ps_r[:, :], ALU.mult, ALU.mult),
                      reads=[scr, ps_r[:, :]], writes=[dsts[c]], tag=tag + "_a2")
            else:
                p.add("dve", lambda v, c=c: v.tensor_tensor(
                          dsts[c], srcs[c], ps_m[:, :], ALU.subtract),
                      reads=[srcs[c], ps_m[:, :]], writes=[dsts[c]], tag=tag + "_a1")
                if g is not None:
                    p.add("dve", lambda v, c=c, g=g: v.scalar_tensor_tensor(
                              dsts[c], dsts[c], g, ps_r[:, :], ALU.mult, ALU.mult),
                          reads=[dsts[c], g, ps_r[:, :]], writes=[dsts[c]],
                          tag=tag + "_a2")
                    p.add("act", lambda a, c=c, b=b: a.activation(
                              dsts[c], dsts[c], AF.Identity, bias=b, scale=1.0),
                          reads=[dsts[c], b], writes=[dsts[c]], tag=tag + "_a3")
                else:
                    p.add("dve", lambda v, c=c: v.tensor_tensor(
                              dsts[c], dsts[c], ps_r[:, :], ALU.mult),
                          reads=[dsts[c], ps_r[:, :]], writes=[dsts[c]],
                          tag=tag + "_a2")
                if dst_bf is not None:
                    p.add("act", lambda a, c=c: a.copy(dst_bf[c], dsts[c]),
                          reads=[dsts[c]], writes=[dst_bf[c]], tag=tag + "_a4")


def MM(t, out, lhsT, rhs, start, stop, tp=None):
    return t.matmul(out, lhsT, rhs, start=start, stop=stop,
                    tile_position=tp, skip_group_check=True)



def build_p0(mk, seq, b0, b0d, hbfc):
    p = mk.p
    for c in range(CH):
        p.add("pool", lambda g, c=c: g.memset(b0[c][:, 0:2], 0.0),
              writes=[b0[c][:, 0:2]], tag="padz")
        p.add("pool", lambda g, c=c: g.memset(b0[c][:, 2 + T:PADW], 0.0),
              writes=[b0[c][:, 2 + T:PADW]], tag="padz")
    p.dma("sp", "xin", [(b0d[c], mk.d_x[seq, 128 * c:128 * (c + 1), :])
                        for c in range(CH)], tag="xload")
    for nq in range(4):
        nq0 = 512 * nq
        p.dma("pool", "xbf", [(hbfc[c], mk.d_x[seq, 128 * c:128 * (c + 1), nq0:nq0 + 512])
                              for c in range(CH)], tag="xbfload")
        items = [(b0d[c][:, nq0:nq0 + 512], hbfc[c], b0d[c][:, nq0:nq0 + 512])
                 for c in range(CH)]
        mk.emit_ln(items, EPS_OUT, tag=f"ln1_{seq}_{nq}")


def build_conv(mk, seq, scope, b0, b0d, hbfc):
    nc, p = mk.nc, mk.p
    xbfc = scope.enter_context(nc.sbuf_tensor(f"xbfc_{seq}", [128, 8 * 264], BF16))
    hjb = [scope.enter_context(nc.sbuf_tensor(f"hjb{i}_{seq}", [128, 4 * PADW], BF16))
           for i in range(2)]

    for c in range(CH):
        s1 = mk.ws[c % 2]
        s2 = mk.ws[2 + c % 2]
        p.dma("pool", f"ws{c % 2}", [(s1[:, 0:2560], mk.d_c1d[c, :, :])], tag="c1d")
        p.dma("pool", f"ws{2 + c % 2}", [(s2[:, 0:2560], mk.d_c2d[c, :, :])], tag="c2d")
        hj = hjb[c % 2]
        for jj in range(4):
            p.add("pool", lambda g, jj=jj, hj=hj: g.memset(hj[:, PADW * jj:PADW * jj + 2], 0.0),
                  writes=[hj[:, PADW * jj:PADW * jj + 2]], tag="hjz")
            p.add("pool", lambda g, jj=jj, hj=hj: g.memset(hj[:, PADW * jj + 2 + T:PADW * (jj + 1)], 0.0),
                  writes=[hj[:, PADW * jj + 2 + T:PADW * (jj + 1)]], tag="hjz")
        for tk in range(8):
            w = xbfc[:, 264 * tk:264 * tk + 260]
            p.add("act", lambda a, c=c, tk=tk, w=w: a.copy(
                      w, b0[c][:, 256 * tk:256 * tk + 260]),
                  reads=[b0[c][:, 256 * tk:256 * tk + 260]], writes=[w], tag="xc")
            for jj in range(4):
                psq = mk.ps[(tk * 4 + jj) % 2]

                def mm1(t, s1=s1, jj=jj, tk=tk, psq=psq):
                    for t5 in range(5):
                        inst = MM(t, psq[:, 0:256], s1[:, 128 * (5 * jj + t5):128 * (5 * jj + t5 + 1)],
                                  xbfc[:, 264 * tk + t5:264 * tk + t5 + 256],
                                  start=(t5 == 0), stop=(t5 == 4))
                    return inst
                p.add("pe", mm1, reads=[s1[:, 0:2560], w], writes=[psq[:, :]], tag="c1mm")
                p.add("act", lambda a, hj=hj, jj=jj, tk=tk, psq=psq, c=c: a.activation(
                          hj[:, PADW * jj + 2 + 256 * tk:PADW * jj + 2 + 256 * tk + 256],
                          psq[:, 0:256], AF.Gelu,
                          bias=mk.cb1[:, 4 * c + jj:4 * c + jj + 1], scale=1.0),
                      reads=[psq[:, :], mk.cb1[:, 4 * c + jj:4 * c + jj + 1]],
                      writes=[hj[:, PADW * jj + 2 + 256 * tk:PADW * jj + 2 + 256 * tk + 256]],
                      tag="c1ev")
        for tk in range(8):
            psq = mk.ps[2 + tk % 2]

            def mm2(t, s2=s2, tk=tk, psq=psq, hj=hj):
                first = True
                for jj in range(4):
                    for t5 in range(5):
                        inst = MM(t, psq[:, 0:256], s2[:, 128 * (5 * jj + t5):128 * (5 * jj + t5 + 1)],
                                  hj[:, PADW * jj + 256 * tk + t5:PADW * jj + 256 * tk + t5 + 256],
                                  start=first, stop=(jj == 3 and t5 == 4))
                        first = False
                return inst
            p.add("pe", mm2, reads=[s2[:, 0:2560], hj[:, :]], writes=[psq[:, :]], tag="c2mm")
            sl = slice(256 * tk, 256 * tk + 256)
            p.add("dve", lambda v, c=c, psq=psq, sl=sl: v.scalar_tensor_tensor(
                      b0d[c][:, sl], psq[:, 0:256], mk.cb2[:, c:c + 1], b0d[c][:, sl],
                      ALU.add, ALU.add),
                  reads=[psq[:, :], mk.cb2[:, c:c + 1], b0d[c][:, sl]],
                  writes=[b0d[c][:, sl]], tag="c2ev")
    for nq in range(4):
        nq0 = 512 * nq
        for c in range(CH):
            p.add("act", lambda a, c=c, nq0=nq0: a.copy(
                      hbfc[c], b0d[c][:, nq0:nq0 + 512]),
                  reads=[b0d[c][:, nq0:nq0 + 512]], writes=[hbfc[c]], tag="ln0c")
        items = [(b0d[c][:, nq0:nq0 + 512], hbfc[c], b0d[c][:, nq0:nq0 + 512])
                 for c in range(CH)]
        mk.emit_ln(items, EPS_IN,
                   gamma=[mk.ln0g[:, c:c + 1] for c in range(CH)],
                   beta=[mk.ln0b[:, c:c + 1] for c in range(CH)],
                   tag=f"ln0_{seq}_{nq}")


def build_attn_layer(mk, seq, l, b0d, hbfc2, bufs, upto=None):
    p = mk.p
    cosc, sinc = bufs["cosc"], bufs["sinc"]
    ksc = bufs["ksc"]
    tmp1 = bufs["tmp1"]
    krt, vt = bufs["krt"], bufs["vt"]
    zbs = bufs["zbs"]
    ws = mk.ws

    p.dma("pool", "bvl", [(mk.bvr[:, :], mk.d_bvr[:, C * l:C * (l + 1)])], tag=f"bv{l}")
    for b in range(2):
        p.dma("pool", f"ws{b}", [(ws[b][:, :], mk.d_wk[l, b, :, :])], tag=f"wk{l}")
        p.dma("pool", f"ws{2 + b}", [(ws[2 + b][:, :], mk.d_wv[l, b, :, :])], tag=f"wv{l}")
        p.dma("pool", f"ws{4 + b}", [(ws[4 + b][:, :], mk.d_wq[l, b, :, :])], tag=f"wq{l}")

    def hb(c, par):
        return hbfc2[(par % 2) * CH + c]

    def kscv(par):
        off = (par % 2) * CH * 512
        return ksc[:, off:off + CH * 512]

    def cast_h(nq0, tag, par):
        for c in range(CH):
            p.add("act", lambda a, c=c, nq0=nq0, par=par: a.copy(
                      hb(c, par), b0d[c][:, nq0:nq0 + 512]),
                  reads=[b0d[c][:, nq0:nq0 + 512]], writes=[hb(c, par)], tag=tag)

    def proj(wslots, biascol, dst, tag, par):
        for mc in range(CH):
            wsl = wslots[0] if mc < 4 else wslots[1]
            mloc = mc % 4
            psq = mk.ps[mc % 2]

            def mmp(t, wsl=wsl, mloc=mloc, psq=psq, par=par):
                for kc in range(CH):
                    inst = MM(t, psq[:, :], wsl[:, 512 * kc + 128 * mloc:512 * kc + 128 * mloc + 128],
                              hb(kc, par), start=(kc == 0), stop=(kc == CH - 1))
                return inst
            p.add("pe", mmp, reads=[wsl[:, :]] + [hb(kc, par) for kc in range(CH)],
                  writes=[psq[:, :]], tag=tag + "mm")
            p.add("act", lambda a, mc=mc, psq=psq: a.activation(
                      dst[:, 512 * mc:512 * mc + 512], psq[:, :], AF.Gelu,
                      bias=biascol[:, CH * l + mc:CH * l + mc + 1], scale=1.0),
                  reads=[psq[:, :], biascol[:, CH * l + mc:CH * l + mc + 1]],
                  writes=[dst[:, 512 * mc:512 * mc + 512]], tag=tag + "ev")
            p.add("dve", lambda v, mc=mc: v.tensor_scalar_add(
                      dst[:, 512 * mc:512 * mc + 512], dst[:, 512 * mc:512 * mc + 512],
                      0.21),
                  reads=[dst[:, 512 * mc:512 * mc + 512]],
                  writes=[dst[:, 512 * mc:512 * mc + 512]], tag=tag + "21")

    def rope(buf, nq0, tag):
        for pc in range(4):
            cw = cosc[pc][:, nq0:nq0 + 512]
            sw = sinc[pc][:, nq0:nq0 + 512]
            se = buf[:, 512 * pc:512 * pc + 512]
            so = buf[:, 512 * (pc + 4):512 * (pc + 4) + 512]
            te = tmp1[:, 0:512]
            to = tmp1[:, 512:1024]
            p.add("dve", lambda v, so=so, sw=sw, te=te: v.tensor_tensor(
                      te, so, sw, ALU.mult),
                  reads=[so, sw], writes=[te], tag=tag + "re1")
            p.add("pool", lambda g, se=se, sw=sw, to=to: g.tensor_tensor(
                      to, se, sw, ALU.mult),
                  reads=[se, sw], writes=[to], tag=tag + "ro1")
            p.add("dve", lambda v, se=se, cw=cw: v.tensor_tensor(
                      se, se, cw, ALU.mult),
                  reads=[se, cw], writes=[se], tag=tag + "re2")
            p.add("dve", lambda v, se=se, te=te: v.tensor_tensor(
                      se, se, te, ALU.subtract),
                  reads=[se, te], writes=[se], tag=tag + "re3")
            p.add("pool", lambda g, so=so, cw=cw: g.tensor_tensor(
                      so, so, cw, ALU.mult),
                  reads=[so, cw], writes=[so], tag=tag + "ro2")
            p.add("pool", lambda g, so=so, to=to: g.tensor_tensor(
                      so, so, to, ALU.add),
                  reads=[so, to], writes=[so], tag=tag + "ro3")

    def kv_A(nq):
        nq0 = 512 * nq
        par = nq % 2
        ksq = kscv(par)
        cast_h(nq0, f"a{l}kc", par)
        proj((ws[0], ws[1]), mk.bk, ksq, f"a{l}k", par)
        for c in range(CH):
            p.add("dve", lambda v, c=c, ksq=ksq: v.reduce_sum(
                      mk.km_tmp[:, :], ksq[:, 512 * c:512 * c + 512], axis=AX.X),
                  reads=[ksq[:, 512 * c:512 * c + 512]], writes=[mk.km_tmp[:, :]],
                  tag=f"a{l}km1")
            if nq == 0:
                p.add("dve", lambda v, c=c: v.tensor_copy(
                          out=mk.km_acc[:, c:c + 1], in_=mk.km_tmp[:, :]),
                      reads=[mk.km_tmp[:, :]], writes=[mk.km_acc[:, c:c + 1]],
                      tag=f"a{l}km2")
            else:
                p.add("dve", lambda v, c=c: v.tensor_tensor(
                          mk.km_acc[:, c:c + 1], mk.km_acc[:, c:c + 1],
                          mk.km_tmp[:, :], ALU.add),
                      reads=[mk.km_tmp[:, :], mk.km_acc[:, c:c + 1]],
                      writes=[mk.km_acc[:, c:c + 1]], tag=f"a{l}km2")
        rope(ksq, nq0, f"a{l}rk")

    def kv_B(nq):
        nq0 = 512 * nq
        par = nq % 2
        ksq = kscv(par)
        for g in range(4):
            for half in range(2):
                def tr(t, half=half, g=g, ksq=ksq):
                    for pp in range(4):
                        ci = 4 * half + pp
                        inst = t.transpose(mk.pt[:, 128 * pp:128 * pp + 128],
                                           ksq[:, 512 * ci + 128 * g:512 * ci + 128 * g + 128],
                                           mk.eye[:, :])
                    return inst
                p.add("pe", tr, reads=[ksq[:, :], mk.eye[:, :]], writes=[mk.pt[:, :]],
                      tag=f"a{l}tr")
                p.add("act", lambda a, half=half: a.copy(
                          krt[:, 512 * half:512 * half + 512], mk.pt[:, :]),
                      reads=[mk.pt[:, :]], writes=[krt[:, 512 * half:512 * half + 512]],
                      tag=f"a{l}trev")
            for vh in range(2):
                wsl = ws[2 + vh]
                psq = mk.ps[3 + vh]

                def mmv(t, wsl=wsl, g=g, psq=psq, par=par):
                    for kc in range(CH):
                        inst = MM(t, psq[:, :], hb(kc, par)[:, 128 * g:128 * g + 128],
                                  wsl[:, 512 * kc:512 * kc + 512],
                                  start=(kc == 0), stop=(kc == CH - 1))
                    return inst
                p.add("pe", mmv, reads=[wsl[:, :]] + [hb(kc, par) for kc in range(CH)],
                      writes=[psq[:, :]], tag=f"a{l}vmm")
                p.add("act", lambda a, vh=vh, psq=psq: a.copy(
                          vt[:, 512 * vh:512 * vh + 512], psq[:, :]),
                      reads=[psq[:, :]], writes=[vt[:, 512 * vh:512 * vh + 512]],
                      tag=f"a{l}vev")

            def mmkv(t, nq=nq, g=g):
                for h in range(NH):
                    rb = (32 * h) % 128
                    for half in range(2):
                        inst = MM(t, mk.kvp[rb:rb + 32, (h // 4) * 128 + 64 * half:
                                            (h // 4) * 128 + 64 * half + 64],
                                  krt[:, 512 * half + 32 * h:512 * half + 32 * h + 32],
                                  vt[:, 64 * h:64 * h + 64],
                                  start=(nq == 0 and g == 0 and h < 4 and half == 0),
                                  stop=False, tp=(0, rb))
                return inst
            p.add("pe", mmkv, reads=[krt[:, :], vt[:, :]], writes=[mk.kvp[:, :]],
                  tag=f"a{l}kv")

            def mmks(t, nq=nq, g=g):
                st = (nq == 0 and g == 0)
                MM(t, mk.ps[5][0:1, :], mk.ones_col[:, :], krt[:, 0:512],
                   start=st, stop=False)
                return MM(t, mk.ps[5][32:33, :], mk.ones_col[:, :], krt[:, 512:1024],
                          start=st, stop=False, tp=(0, 32))
            p.add("pe", mmks, reads=[krt[:, :], mk.ones_col[:, :]],
                  writes=[mk.ps[5][:, :]], tag=f"a{l}ks")

    kv_A(0)
    kv_A(1)
    kv_B(0)
    kv_A(2)
    kv_B(1)
    kv_A(3)
    kv_B(2)
    kv_B(3)

    def ksev(a):
        a.copy(mk.krs[0:1, 0:512], mk.ps[5][0:1, :])
        return a.copy(mk.krs[32:33, 0:512], mk.ps[5][32:33, :])
    p.add("act", ksev, reads=[mk.ps[5][:, :]], writes=[mk.krs[:, :]], tag=f"a{l}ksev")
    p.dma("sp", "krsmv", [(mk.krs[0:1, 512:1024], mk.krs[32:33, 0:512])],
          tag=f"a{l}krsmv")

    def mmbv(t):
        for h in range(NH):
            rb = (32 * h) % 128
            for half in range(2):
                inst = MM(t, mk.kvp[rb:rb + 32, (h // 4) * 128 + 64 * half:
                                    (h // 4) * 128 + 64 * half + 64],
                          mk.krs[0:1, 512 * half + 32 * h:512 * half + 32 * h + 32],
                          mk.bvr[0:1, 64 * h:64 * h + 64],
                          start=False, stop=(h == NH - 1 and half == 1),
                          tp=(0, rb))
        return inst
    p.add("pe", mmbv, reads=[mk.krs[:, :], mk.bvr[:, :]], writes=[mk.kvp[:, :]],
          tag=f"a{l}bv")
    p.add("pool", lambda g: g.memset(mk.kvb[:, :], 0.0),
          writes=[mk.kvb[:, :]], tag=f"a{l}kvz")
    for Mp in range(CH):
        for half in range(2):
            base = 256 * Mp + 128 * half
            for hh in range(2):
                h = 2 * Mp + hh
                rb = (32 * h) % 128
                p.add("dve", lambda v, base=base, hh=hh, h=h, rb=rb, half=half:
                      v.tensor_scalar_mul(
                          mk.kvb[rb:rb + 32, base + 64 * hh:base + 64 * hh + 64],
                          mk.kvp[rb:rb + 32, (h // 4) * 128 + 64 * half:
                                 (h // 4) * 128 + 64 * half + 64], 1.0 / T),
                      reads=[mk.kvp[:, :]],
                      writes=[mk.kvb[rb:rb + 32, base + 64 * hh:base + 64 * hh + 64]],
                      tag=f"a{l}kvev")
    p.add("pool", lambda g: g.memset(mk.km_mat[:, :], 0.0),
          writes=[mk.km_mat[:, :]], tag=f"a{l}kmz")
    for c in range(CH):
        cc = c % 4
        for q4 in range(4):
            h = 4 * cc + q4
            p.add("dve", lambda v, c=c, q4=q4, h=h: v.tensor_scalar_mul(
                      mk.km_mat[32 * q4:32 * q4 + 32, 16 * c + h:16 * c + h + 1],
                      mk.km_acc[32 * q4:32 * q4 + 32, c:c + 1], 1.0 / T),
                  reads=[mk.km_acc[:, c:c + 1]],
                  writes=[mk.km_mat[:, 16 * c:16 * c + 16]], tag=f"a{l}kmb")

    def zv(par, bf):
        off = (par % 2) * 512
        t = mk.z_bf if bf else mk.z_sb
        return t[:, off:off + 512]

    def q_A(nq):
        nq0 = 512 * nq
        par = nq % 2
        ksq = kscv(par)
        cast_h(nq0, f"a{l}qc", par)
        proj((ws[4], ws[5]), mk.bq, ksq, f"a{l}q", par)

        def mmz(t, ksq=ksq):
            for c in range(CH):
                inst = MM(t, mk.ps[5][0:16, :], mk.km_mat[:, 16 * c:16 * c + 16],
                          ksq[:, 512 * c:512 * c + 512],
                          start=(c == 0), stop=(c == CH - 1))
            return inst
        p.add("pe", mmz, reads=[mk.km_mat[:, :]] + [ksq[:, 512 * c:512 * c + 512] for c in range(CH)],
              writes=[mk.ps[5][:, :]], tag=f"a{l}z")
        zs = zv(par, False)
        zb = zv(par, True)
        p.add("dve", lambda v, zs=zs: v.tensor_scalar_add(zs, mk.ps[5][0:16, :], 1e-6),
              reads=[mk.ps[5][:, :]], writes=[zs], tag=f"a{l}zc1")
        p.add("dve", lambda v, zs=zs: v.reciprocal(zs, zs),
              reads=[zs], writes=[zs], tag=f"a{l}zc2")
        p.add("dve", lambda v, zs=zs, zb=zb: v.tensor_copy(out=zb, in_=zs),
              reads=[zs], writes=[zb], tag=f"a{l}zc3")
        rope(ksq, nq0, f"a{l}rq")

    def q_B(nq):
        nq0 = 512 * nq
        par = nq % 2
        ksq = kscv(par)
        zb = zv(par, True)
        for ec in range(CH):
            pso = mk.ps[3 + ec % 2]

            def mmo(t, ec=ec, pso=pso, ksq=ksq):
                for half in range(2):
                    ck = 4 * half + ec // 2
                    inst = MM(t, pso[:, :],
                              mk.kvb[:, 256 * ec + 128 * half:256 * ec + 128 * half + 128],
                              ksq[:, 512 * ck:512 * ck + 512],
                              start=(half == 0), stop=(half == 1))
                return inst
            p.add("pe", mmo, reads=[mk.kvb[:, :], ksq[:, :]], writes=[pso[:, :]],
                  tag=f"a{l}o")
            zbp = mk.ps[2] if ec % 2 == 0 else mk.ps[5]
            p.add("pe", lambda t, ec=ec, zbp=zbp, zb=zb: MM(
                      t, zbp[:, :], mk.ebc[:, 128 * ec:128 * ec + 128], zb,
                      start=True, stop=True),
                  reads=[mk.ebc[:, :], zb], writes=[zbp[:, :]], tag=f"a{l}zb")
            p.add("dve", lambda v, zbp=zbp: v.tensor_copy(out=zbs[:, :], in_=zbp[:, :]),
                  reads=[zbp[:, :]], writes=[zbs[:, :]], tag=f"a{l}zbe")
            p.add("dve", lambda v, pso=pso: v.tensor_tensor(
                      zbs[:, :], pso[:, :], zbs[:, :], ALU.mult),
                  reads=[pso[:, :], zbs[:, :]], writes=[zbs[:, :]], tag=f"a{l}sm1")
            p.add("dve", lambda v, ec=ec, nq0=nq0: v.tensor_tensor(
                      b0d[ec][:, nq0:nq0 + 512], b0d[ec][:, nq0:nq0 + 512],
                      zbs[:, :], ALU.add),
                  reads=[zbs[:, :], b0d[ec][:, nq0:nq0 + 512]],
                  writes=[b0d[ec][:, nq0:nq0 + 512]], tag=f"a{l}sm2")
        for c in range(CH):
            p.add("act", lambda a, c=c, nq0=nq0, par=par: a.copy(
                      hb(c, par), b0d[c][:, nq0:nq0 + 512]),
                  reads=[b0d[c][:, nq0:nq0 + 512]], writes=[hb(c, par)],
                  tag=f"a{l}lc")
        items = [(b0d[c][:, nq0:nq0 + 512], hb(c, par), b0d[c][:, nq0:nq0 + 512])
                 for c in range(CH)]
        mk.emit_ln(items, EPS_IN,
                   gamma=[mk.lnAg[:, CH * l + c:CH * l + c + 1] for c in range(CH)],
                   beta=[mk.lnAb[:, CH * l + c:CH * l + c + 1] for c in range(CH)],
                   tag=f"lnA{l}_{seq}_{nq}")

    q_A(0)
    q_A(1)
    q_B(0)
    q_A(2)
    q_B(1)
    q_A(3)
    q_B(2)
    q_B(3)
    return False


def build_mlp(mk, seq, scope, b0d, hbfc):
    nc, p = mk.nc, mk.p
    ub = scope.enter_context(nc.sbuf_tensor(f"ub_{seq}", [128, 4 * 512], BF16))
    fsb = scope.enter_context(nc.sbuf_tensor(f"fsb_{seq}", [128, CH * 1024], F32))

    mbf = scope.enter_context(nc.sbuf_tensor(f"mbf_{seq}", [128, CH * 1024], BF16))
    ws = mk.ws

    for nq in range(2):
        nq0 = 1024 * nq
        mbfc = [mbf[:, 1024 * c:1024 * (c + 1)] for c in range(CH)]
        for c in range(CH):
            p.add("act", lambda a, c=c, nq0=nq0: a.copy(
                      mbfc[c], b0d[c][:, nq0:nq0 + 1024]),
                  reads=[b0d[c][:, nq0:nq0 + 1024]], writes=[mbfc[c]], tag="mh")
        for b in range(8):
            wsl1 = ws[b % 2]
            wsl2 = ws[2 + b % 2]
            p.dma("pool", f"ws{b % 2}", [(wsl1[:, :], mk.d_W1[b, :, :])], tag="w1l")
            p.dma("pool", f"ws{2 + b % 2}",
                  [(wsl2[:, :], mk.d_W2[:, 4096 * b:4096 * b + 4096])], tag="w2l")
            for nh in range(2):
                o = 512 * nh
                for msub in range(4):
                    psq = mk.ps[msub % 2]

                    def mm1(t, wsl1=wsl1, msub=msub, psq=psq, o=o):
                        for kc in range(CH):
                            inst = MM(t, psq[:, :],
                                      wsl1[:, 512 * kc + 128 * msub:512 * kc + 128 * msub + 128],
                                      mbfc[kc][:, o:o + 512],
                                      start=(kc == 0), stop=(kc == CH - 1))
                        return inst
                    p.add("pe", mm1,
                          reads=[wsl1[:, :]] + [mbfc[kc][:, o:o + 512] for kc in range(CH)],
                          writes=[psq[:, :]], tag="w1mm")
                    hidx = 4 * b + msub
                    p.add("act", lambda a, msub=msub, psq=psq, hidx=hidx: a.activation(
                              ub[:, 512 * msub:512 * msub + 512], psq[:, :], AF.Gelu,
                              bias=mk.b1c[:, hidx:hidx + 1], scale=1.0),
                          reads=[psq[:, :], mk.b1c[:, hidx:hidx + 1]],
                          writes=[ub[:, 512 * msub:512 * msub + 512]], tag="w1ev")
                for mc in range(CH):
                    psq = mk.ps[4 + (mc % 2)]

                    def mm2(t, wsl2=wsl2, mc=mc, psq=psq):
                        for kci in range(4):
                            inst = MM(t, psq[:, :],
                                      wsl2[:, 1024 * kci + 128 * mc:1024 * kci + 128 * mc + 128],
                                      ub[:, 512 * kci:512 * kci + 512],
                                      start=(kci == 0), stop=(kci == 3))
                        return inst
                    p.add("pe", mm2, reads=[wsl2[:, :], ub[:, :]], writes=[psq[:, :]],
                          tag="w2mm")
                    fdst = fsb[:, 1024 * mc + 512 * nh:1024 * mc + 512 * nh + 512]
                    if b == 0:
                        p.add("act", lambda a, psq=psq, fdst=fdst: a.copy(fdst, psq[:, :]),
                              reads=[psq[:, :]], writes=[fdst], tag="f0")
                    else:
                        p.add("dve", lambda v, psq=psq, fdst=fdst: v.tensor_tensor(
                                  fdst, fdst, psq[:, :], ALU.add),
                              reads=[psq[:, :], fdst], writes=[fdst], tag="facc")
        for nh in range(2):
            o = 512 * nh
            for c in range(CH):
                fdst = fsb[:, 1024 * c + o:1024 * c + o + 512]
                p.add("dve", lambda v, fdst=fdst, c=c, nq0=nq0, o=o: v.scalar_tensor_tensor(
                          fdst, fdst, mk.b2c[:, c:c + 1], b0d[c][:, nq0 + o:nq0 + o + 512],
                          ALU.add, ALU.add),
                      reads=[fdst, mk.b2c[:, c:c + 1], b0d[c][:, nq0 + o:nq0 + o + 512]],
                      writes=[fdst], tag="fb2")
                p.add("act", lambda a, c=c, fdst=fdst: a.copy(hbfc[8 + c], fdst),
                      reads=[fdst], writes=[hbfc[8 + c]], tag="l2c")
            fwin = [fsb[:, 1024 * c + o:1024 * c + o + 512] for c in range(CH)]
            mk.emit_ln([(fwin[c], hbfc[8 + c], fwin[c]) for c in range(CH)], EPS_IN,
                       gamma=[mk.ln2g[:, c:c + 1] for c in range(CH)],
                       beta=[mk.ln2b[:, c:c + 1] for c in range(CH)],
                       tag=f"ln2_{seq}_{nq}_{nh}")
            for c in range(CH):
                p.add("act", lambda a, c=c, fdst=fwin[c]: a.copy(hbfc[8 + c], fdst),
                      reads=[fwin[c]], writes=[hbfc[8 + c]], tag="lpc")
            mk.emit_ln([(fwin[c], hbfc[8 + c], fwin[c]) for c in range(CH)], EPS_OUT,
                       tag=f"lnp_{seq}_{nq}_{nh}")
            p.dma("sp", "xo", [(b0d[c][:, nq0 + o:nq0 + o + 512],
                                mk.d_x[seq, 128 * c:128 * (c + 1), nq0 + o:nq0 + o + 512])
                               for c in range(CH)], tag="xol")
            for c in range(CH):
                p.add("dve", lambda v, c=c, nq0=nq0, o=o, fdst=fwin[c]: v.tensor_tensor(
                          b0d[c][:, nq0 + o:nq0 + o + 512],
                          b0d[c][:, nq0 + o:nq0 + o + 512], fdst, ALU.add),
                      reads=[fwin[c], b0d[c][:, nq0 + o:nq0 + o + 512]],
                      writes=[b0d[c][:, nq0 + o:nq0 + o + 512]], tag="x2")
            for c in range(CH):
                p.add("act", lambda a, c=c, nq0=nq0, o=o: a.copy(
                          hbfc[8 + c], b0d[c][:, nq0 + o:nq0 + o + 512]),
                      reads=[b0d[c][:, nq0 + o:nq0 + o + 512]], writes=[hbfc[8 + c]],
                      tag="l3c")
            x3win = [hbfc[c] for c in range(CH)]
            mk.emit_ln([(b0d[c][:, nq0 + o:nq0 + o + 512], hbfc[8 + c], x3win[c])
                        for c in range(CH)], EPS_OUT,
                       bf_out=True, scratch=fwin, tag=f"ln3_{seq}_{nq}_{nh}")
            p.dma("sp", "x3sp", [(mk.d_x3sp[128 * c:128 * (c + 1), nq0 + o:nq0 + o + 512],
                                  x3win[c]) for c in range(CH)], tag="x3sp")
        p.dma("sp", "x2sp", [(mk.d_x2sp[128 * c:128 * (c + 1), nq0:nq0 + 1024],
                              b0d[c][:, nq0:nq0 + 1024]) for c in range(CH)], tag="x2sp")


def build_ffn(mk, seq, scope):
    nc, p = mk.nc, mk.p
    x3w = scope.enter_context(nc.sbuf_tensor(f"x3w_{seq}", [128, CH * 512], BF16))
    Ab = scope.enter_context(nc.sbuf_tensor(f"A_{seq}", [128, 32 * 512], BF16))
    Bb = scope.enter_context(nc.sbuf_tensor(f"B_{seq}", [128, 32 * 512], BF16))
    sil = scope.enter_context(nc.sbuf_tensor(f"sil_{seq}", [128, 512], F32))
    u2 = scope.enter_context(nc.sbuf_tensor(f"u2_{seq}", [128, 512], F32))
    f3 = scope.enter_context(nc.sbuf_tensor(f"f3_{seq}", [128, CH * 512], F32))
    x2c = scope.enter_context(nc.sbuf_tensor(f"x2c_{seq}", [128, CH * 512], F32))
    ws = mk.ws
    x3wc = [x3w[:, 512 * c:512 * c + 512] for c in range(CH)]

    for tc in range(4):
        t0 = 512 * tc
        src3 = mk.d_x3sp.rearrange("(c p) n -> p c n", p=128)
        p.dma("sp", "x3ld", [(x3w[:, :].rearrange("p (c n) -> p c n", n=512),
                              src3[:, :, t0:t0 + 512])], tag="x3ld")
        for b in range(8):
            p.dma("pool", f"ws{b % 2}", [(ws[b % 2][:, :], mk.d_f1[b, :, :])], tag="f1l")
            p.dma("pool", f"ws{2 + b % 2}", [(ws[2 + b % 2][:, :], mk.d_f2[b, :, :])],
                  tag="f2l")
            for msub in range(4):
                hidx = 4 * b + msub
                psq = mk.ps[msub % 2]

                def mmf1(t, b=b, msub=msub, psq=psq):
                    wsl = ws[b % 2]
                    for kc in range(CH):
                        inst = MM(t, psq[:, :],
                                  wsl[:, 512 * kc + 128 * msub:512 * kc + 128 * msub + 128],
                                  x3wc[kc], start=(kc == 0), stop=(kc == CH - 1))
                    return inst
                p.add("pe", mmf1, reads=[ws[b % 2][:, :]] + x3wc, writes=[psq[:, :]],
                      tag="f1mm")
                p.add("act", lambda a, psq=psq, hidx=hidx: a.activation(
                          sil[:, :], psq[:, :], AF.Silu, bias=0.0,
                          scale=mk.s1c[:, hidx:hidx + 1]),
                      reads=[psq[:, :], mk.s1c[:, hidx:hidx + 1]], writes=[sil[:, :]],
                      tag="f1ev")
                ps2q = mk.ps[2 + msub % 2]

                def mmf2(t, b=b, msub=msub, ps2q=ps2q):
                    wsl = ws[2 + b % 2]
                    for kc in range(CH):
                        inst = MM(t, ps2q[:, :],
                                  wsl[:, 512 * kc + 128 * msub:512 * kc + 128 * msub + 128],
                                  x3wc[kc], start=(kc == 0), stop=(kc == CH - 1))
                    return inst
                p.add("pe", mmf2, reads=[ws[2 + b % 2][:, :]] + x3wc, writes=[ps2q[:, :]],
                      tag="f2mm")
                p.add("act", lambda a, ps2q=ps2q, hidx=hidx: a.activation(
                          u2[:, :], ps2q[:, :], AF.Identity, bias=0.0,
                          scale=mk.s2c[:, hidx:hidx + 1]),
                      reads=[ps2q[:, :], mk.s2c[:, hidx:hidx + 1]], writes=[u2[:, :]],
                      tag="f2ev")
                p.add("dve", lambda v, hidx=hidx: v.tensor_tensor(
                          Ab[:, 512 * hidx:512 * hidx + 512], sil[:, :], u2[:, :],
                          ALU.mult),
                      reads=[sil[:, :], u2[:, :]],
                      writes=[Ab[:, 512 * hidx:512 * hidx + 512]], tag="f12m")
        for g in range(32):
            p.dma("pool", f"ws{4 + g % 2}", [(ws[4 + g % 2][:, :], mk.d_h1[g, :, :])],
                  tag="h1l")
            psq = mk.ps[4 + g % 2]

            def mmh1(t, g=g, psq=psq):
                wsl = ws[4 + g % 2]
                for kc in range(32):
                    inst = MM(t, psq[:, :], wsl[:, 128 * kc:128 * kc + 128],
                              Ab[:, 512 * kc:512 * kc + 512],
                              start=(kc == 0), stop=(kc == 31))
                return inst
            p.add("pe", mmh1, reads=[ws[4 + g % 2][:, :], Ab[:, :]], writes=[psq[:, :]],
                  tag="h1mm")
            p.add("act", lambda a, g=g, psq=psq: a.activation(
                      Bb[:, 512 * g:512 * g + 512], psq[:, :], AF.Gelu, bias=0.0,
                      scale=mk.h1sc[:, g:g + 1]),
                  reads=[psq[:, :], mk.h1sc[:, g:g + 1]],
                  writes=[Bb[:, 512 * g:512 * g + 512]], tag="h1ev")
        for g in range(32):
            p.dma("pool", f"ws{g % 2}", [(ws[g % 2][:, :], mk.d_h2[g, :, :])], tag="h2l")
            psq = mk.ps[g % 2]

            def mmh2(t, g=g, psq=psq):
                wsl = ws[g % 2]
                for kc in range(32):
                    inst = MM(t, psq[:, :], wsl[:, 128 * kc:128 * kc + 128],
                              Bb[:, 512 * kc:512 * kc + 512],
                              start=(kc == 0), stop=(kc == 31))
                return inst
            p.add("pe", mmh2, reads=[ws[g % 2][:, :], Bb[:, :]], writes=[psq[:, :]],
                  tag="h2mm")
            p.add("act", lambda a, g=g, psq=psq: a.activation(
                      Ab[:, 512 * g:512 * g + 512], psq[:, :], AF.Gelu, bias=0.0,
                      scale=mk.h2sc[:, g:g + 1]),
                  reads=[psq[:, :], mk.h2sc[:, g:g + 1]],
                  writes=[Ab[:, 512 * g:512 * g + 512]], tag="h2ev")
        for g in range(CH):
            p.dma("pool", f"ws{2 + g % 2}", [(ws[2 + g % 2][:, :], mk.d_w3[g, :, :])],
                  tag="w3l")
            psq = mk.ps[2 + g % 2]

            def mmw3(t, g=g, psq=psq):
                wsl = ws[2 + g % 2]
                for kc in range(32):
                    inst = MM(t, psq[:, :], wsl[:, 128 * kc:128 * kc + 128],
                              Ab[:, 512 * kc:512 * kc + 512],
                              start=(kc == 0), stop=(kc == 31))
                return inst
            p.add("pe", mmw3, reads=[ws[2 + g % 2][:, :], Ab[:, :]], writes=[psq[:, :]],
                  tag="w3mm")
            p.add("act", lambda a, g=g, psq=psq: a.activation(
                      f3[:, 512 * g:512 * g + 512], psq[:, :], AF.Identity, bias=0.0,
                      scale=mk.s3c[:, g:g + 1]),
                  reads=[psq[:, :], mk.s3c[:, g:g + 1]],
                  writes=[f3[:, 512 * g:512 * g + 512]], tag="w3ev")
        bfw = [Bb[:, 512 * c:512 * c + 512] for c in range(CH)]
        for c in range(CH):
            p.add("act", lambda a, c=c: a.copy(
                      bfw[c], f3[:, 512 * c:512 * c + 512]),
                  reads=[f3[:, 512 * c:512 * c + 512]], writes=[bfw[c]], tag="lfc")
        f3w = [f3[:, 512 * c:512 * c + 512] for c in range(CH)]
        mk.emit_ln([(f3w[c], bfw[c], f3w[c]) for c in range(CH)], EPS_OUT,
                   tag=f"lnf_{seq}_{tc}")
        src2 = mk.d_x2sp.rearrange("(c p) n -> p c n", p=128)
        p.dma("sp", "x2ld", [(x2c[:, :].rearrange("p (c n) -> p c n", n=512),
                              src2[:, :, t0:t0 + 512])], tag="x2ld")
        for c in range(CH):
            p.add("dve", lambda v, c=c: v.tensor_tensor(
                      x2c[:, 512 * c:512 * c + 512], x2c[:, 512 * c:512 * c + 512],
                      f3w[c], ALU.add),
                  reads=[f3w[c], x2c[:, 512 * c:512 * c + 512]],
                  writes=[x2c[:, 512 * c:512 * c + 512]], tag="yadd")
        dsty = mk.d_y[seq, :, t0:t0 + 512].rearrange("(c p) n -> p c n", p=128)
        p.dma("sp", "yout", [(dsty, x2c[:, :].rearrange("p (c n) -> p c n", n=512))],
              tag="yst")



def build_program(stop_after=None):
    from contextlib import ExitStack
    nc = bass.Bass("TRN2", target_bir_lowering=False)
    mk = MK(nc)
    p = mk.p
    mk.load_constants()

    stopped = False
    for seq in range(NSEQ):
        if stopped:
            break
        with ExitStack() as seq_scope:
            stopped = build_seq(mk, seq, seq_scope, stop_after)
        p.barrier(exclude_sems=("consts_c", "consts_p"))

    p.final_wait()
    p.emit()
    return nc


def build_seq(mk, seq, seq_scope, stop_after):
    from contextlib import ExitStack
    nc, p = mk.nc, mk.p

    def dump(chunks):
        pairs = [(mk.d_y[seq, 128 * c:128 * (c + 1), :], chunks[c][:, 0:T])
                 for c in range(CH)]
        p.dma("sp", "yout", pairs, tag="dbg_out")
        return True

    main_scope = ExitStack()
    seq_scope.enter_context(main_scope)
    big0 = main_scope.enter_context(
        nc.sbuf_tensor(f"big0_{seq}", [128, CH * PADW], F32))
    b0 = [big0[:, PADW * c:PADW * (c + 1)] for c in range(CH)]
    b0d = [big0[:, PADW * c + 2:PADW * c + 2 + T] for c in range(CH)]
    hbf = main_scope.enter_context(
        nc.sbuf_tensor(f"hbf_{seq}", [128, 2 * CH * 512], BF16))
    hbfc = [hbf[:, 512 * c:512 * (c + 1)] for c in range(2 * CH)]

    build_p0(mk, seq, b0, b0d, hbfc)
    if stop_after == "ln1" and seq == 0:
        return dump(b0d)

    with ExitStack() as conv_scope:
        build_conv(mk, seq, conv_scope, b0, b0d, hbfc)
    p.barrier(exclude_sems=("consts_c", "consts_p"))
    if stop_after == "conv" and seq == 0:
        return dump(b0d)

    with ExitStack() as attn_scope:
        ae = attn_scope.enter_context
        cosT = ae(nc.sbuf_tensor(f"cosT_{seq}", [128, 4 * T], BF16))
        sinT = ae(nc.sbuf_tensor(f"sinT_{seq}", [128, 4 * T], BF16))
        cosc = [cosT[:, T * i:T * (i + 1)] for i in range(4)]
        sinc = [sinT[:, T * i:T * (i + 1)] for i in range(4)]
        p.dma("pool", "tbl",
              [(cosc[i], mk.d_cos[128 * i:128 * (i + 1), :]) for i in range(4)]
              + [(sinc[i], mk.d_sin[128 * i:128 * (i + 1), :]) for i in range(4)],
              tag="tables")
        ksc = ae(nc.sbuf_tensor(f"ksc_{seq}", [128, 2 * CH * 512], BF16))
        tmp1 = ae(nc.sbuf_tensor(f"tmp1_{seq}", [128, 1024], BF16))
        krt = ae(nc.sbuf_tensor(f"krt_{seq}", [128, 1024], BF16))
        vt = ae(nc.sbuf_tensor(f"vt_{seq}", [128, 1024], BF16))
        zbs = ae(nc.sbuf_tensor(f"zbs_{seq}", [128, 512], F32))
        bufs = dict(cosc=cosc, sinc=sinc, ksc=ksc, tmp1=tmp1,
                    krt=krt, vt=vt, zbs=zbs)
        for l in range(NL):
            build_attn_layer(mk, seq, l, b0d, hbfc, bufs)
            if stop_after == f"attn{l}" and seq == 0:
                return dump(b0d)
    p.barrier(exclude_sems=("consts_c", "consts_p"))

    with ExitStack() as mlp_scope:
        build_mlp(mk, seq, mlp_scope, b0d, hbfc)
    p.barrier(exclude_sems=("consts_c", "consts_p"))
    if stop_after == "mlp" and seq == 0:
        return dump(b0d)

    main_scope.close()
    p.barrier(exclude_sems=("consts_c", "consts_p"))

    with ExitStack() as ffn_scope:
        build_ffn(mk, seq, ffn_scope)
    return False



def kernel(**inputs):
    prep = prepare_inputs(inputs)
    x = np.asarray(inputs["L0_attn_x"], np.float32)
    nc = build_program(STOP_AFTER)
    in_maps = []
    for core in range(8):
        m = dict(prep)
        m["x"] = np.ascontiguousarray(x[2 * core:2 * core + 2].transpose(0, 2, 1))
        in_maps.append(m)
    res = run_bass_kernel_spmd(nc, in_maps, core_ids=list(range(8)))
    out = np.empty((16, 2048, 1024), np.float32)
    for core in range(8):
        y = np.asarray(res.results[core]["y"])
        out[2 * core:2 * core + 2] = y.transpose(0, 2, 1)
    return out


# version 07
# speedup vs baseline: 1.0572x; 1.0146x over previous
import numpy as np
import ml_dtypes

import concourse.bass as bass
import concourse.mybir as mybir
from concourse.bass_utils import run_bass_kernel_spmd

F32 = mybir.dt.float32
BF16 = mybir.dt.bfloat16
AF = mybir.ActivationFunctionType
ALU = mybir.AluOpType
AX = mybir.AxisListType

C = 1024
CH = 8
T = 2048
NL = 4
NH = 16
NSEQ = 2
PADW = 2052
EPS_OUT = 1e-5
EPS_IN = 1e-12

STOP_AFTER = None



class Prog:
    ENG = {"pe": "tensor", "act": "scalar", "dve": "vector", "pool": "gpsimd",
           "sp": "sync"}

    def __init__(self, nc):
        self.nc = nc
        self.esem = {e: nc.alloc_semaphore("esem_" + e)
                     for e in ("pe", "act", "dve", "pool")}
        self.tasks = {e: [] for e in self.ENG}
        self.ccnt = {e: 0 for e in ("pe", "act", "dve", "pool")}
        self.dsem = {}
        self.res = {}
        self.waited = {e: {} for e in self.ENG}

    def _iv(self, ap):
        t = ap.tensor
        if isinstance(t, bass.PSumTensorHandle):
            return (t.name, 0, 1 << 40)
        off = ap.offset if isinstance(ap.offset, int) else 0
        span = 1
        for step, count in ap.ap:
            span += abs(step) * (count - 1)
        return (t.name, off, off + span)

    def _collect(self, engine, reads, writes):
        deps = {}

        def want(dep):
            kind, key, val = dep
            k = (kind, key)
            if deps.get(k, -1) < val:
                deps[k] = val

        ivs_r = [self._iv(a) for a in reads]
        ivs_w = [self._iv(a) for a in writes]
        for name, lo, hi in ivs_r:
            ent = self.res.get(name)
            if ent:
                for a, b, dep in ent["w"]:
                    if a < hi and lo < b:
                        want(dep)
        for name, lo, hi in ivs_w:
            ent = self.res.get(name)
            if ent:
                for a, b, dep in ent["w"]:
                    if a < hi and lo < b:
                        want(dep)
                for a, b, dep in ent["r"]:
                    if a < hi and lo < b:
                        want(dep)
        return deps, ivs_r, ivs_w

    def _register(self, ivs_r, ivs_w, token):
        for name, lo, hi in ivs_w:
            ent = self.res.setdefault(name, {"w": [], "r": []})
            ent["w"] = [(a, b, d) for (a, b, d) in ent["w"] if not (lo <= a and b <= hi)]
            ent["r"] = [(a, b, d) for (a, b, d) in ent["r"] if not (lo <= a and b <= hi)]
            ent["w"].append((lo, hi, token))
        for name, lo, hi in ivs_r:
            self.res.setdefault(name, {"w": [], "r": []})["r"].append((lo, hi, token))

    def _mk_waits(self, engine, deps):
        waits = []
        for (kind, key), val in sorted(deps.items(), key=lambda kv: str(kv[0])):
            if val <= 0:
                continue
            sem = self.esem[key] if kind == "E" else self.dsem[key][0]
            prev = self.waited[engine].get((kind, key), -1)
            if val > prev:
                waits.append((sem, val))
                self.waited[engine][(kind, key)] = val
        return waits

    def add(self, engine, fn, reads=(), writes=(), tag=""):
        deps, ivs_r, ivs_w = self._collect(engine, reads, writes)
        waits = self._mk_waits(engine, deps)
        self.ccnt[engine] += 1
        token = ("E", engine, self.ccnt[engine])
        self.tasks[engine].append((waits, fn, tag, False))
        self._register(ivs_r, ivs_w, token)
        return token

    def dma(self, queue, semname, pairs, tag=""):
        if semname not in self.dsem:
            self.dsem[semname] = [self.nc.alloc_semaphore("dsem_" + semname), 0]
        ent = self.dsem[semname]
        prev_cum = ent[1]
        reads = [pp[1] for pp in pairs]
        writes = [pp[0] for pp in pairs]
        deps, ivs_r, ivs_w = self._collect(queue, reads, writes)
        if prev_cum > 0 and deps.get(("D", semname), -1) < prev_cum:
            deps[("D", semname)] = prev_cum
        waits = self._mk_waits(queue, deps)
        ent[1] += 16 * len(pairs)
        token = ("D", semname, ent[1])
        sem = ent[0]

        def fn(eng, pairs=pairs, sem=sem):
            for (o, i) in pairs:
                eng.dma_start(out=o, in_=i).then_inc(sem, 16)
            return None

        self.tasks[queue].append((waits, fn, tag, True))
        self._register(ivs_r, ivs_w, token)
        return token

    def barrier(self, exclude_sems=()):
        targets = {("E", e): self.ccnt[e] for e in self.esem if self.ccnt[e] > 0}
        targets.update({("D", n): ent[1] for n, ent in self.dsem.items()
                        if ent[1] > 0 and n not in exclude_sems})
        for e in self.ENG:
            waits = self._mk_waits(e, targets)
            if waits:
                def fn(eng, waits=waits):
                    return None
                self.tasks[e].append((waits, fn, "barrier", True))
        self.res = {}

    def final_wait(self, engine="sp"):
        targets = {("D", n): ent[1] for n, ent in self.dsem.items() if ent[1] > 0}
        targets.update({("E", e): self.ccnt[e] for e in self.esem if self.ccnt[e] > 0})
        waits = self._mk_waits(engine, targets)

        def fn(eng):
            return None
        self.tasks[engine].append((waits, fn, "final", True))

    def emit(self):
        with self.nc.Block() as block:
            for e, attr in self.ENG.items():
                dec = getattr(block, attr)
                tasks = self.tasks[e]
                esem = self.esem.get(e)

                def body(eng, tasks=tasks, esem=esem, ename=e):
                    for (waits, fn, tag, is_dma) in tasks:
                        for (sem, val) in waits:
                            eng.wait_ge(sem, val)
                        last = fn(eng)
                        if not is_dma:
                            assert last is not None, \
                                f"compute task {tag} on {ename} returned None"
                            last.then_inc(esem, 1)
                dec(body)



def _tile_km(w, granule):
    K, M = w.shape
    kc = K // 128
    nb = M // granule
    out = w.reshape(kc, 128, nb, granule).transpose(2, 1, 0, 3)
    return np.ascontiguousarray(out.reshape(nb, 128, kc * granule))


def _col_pack(v, nchunk):
    return np.ascontiguousarray(np.asarray(v, np.float32).reshape(nchunk, 128).T)


def _perm_order():
    r = np.arange(512)
    return np.concatenate([2 * r, 2 * r + 1])


def prepare_inputs(inputs):
    f = np.float32
    d = {}
    perm = _perm_order()

    j = np.arange(512, dtype=np.float64)
    inv = 1.0 / np.power(10000.0, 2.0 * j / 1024.0)
    pos = np.arange(T, dtype=np.float64)
    ang = pos[None, :] * inv[:, None]
    def _tbl(a):
        a = a.reshape(4, 128, 4, 512).transpose(1, 2, 0, 3)
        return np.ascontiguousarray(a.reshape(128, 4 * 4 * 512)).astype(f)
    d["cosT"] = _tbl(np.cos(ang))
    d["sinT"] = _tbl(np.sin(ang))

    d["eye128"] = np.eye(128, dtype=f)
    d["eps_col"] = np.broadcast_to(np.array([[EPS_OUT, EPS_IN]], f), (128, 2)).copy()
    d["ones_col"] = np.ones((128, 1), dtype=f)
    d["ones_row"] = np.ones((1, 128), dtype=f)

    ebc = np.zeros((16, CH * 128), dtype=f)
    for c in range(CH):
        for m in range(128):
            ebc[2 * c + m // 64, 128 * c + m] = 1.0
    d["Ebc"] = ebc

    Wq = np.asarray(inputs["L0_attn_Wq"], dtype=f)[:, :, perm]
    Wk = np.asarray(inputs["L0_attn_Wk"], dtype=f)[:, :, perm]
    Wv = np.asarray(inputs["L0_attn_Wv"], dtype=f)
    d["wq_t"] = np.stack([_tile_km(Wq[l], 512) for l in range(NL)])
    d["wk_t"] = np.stack([_tile_km(Wk[l], 512) for l in range(NL)])
    d["wv_t"] = np.stack([_tile_km(Wv[l], 512) for l in range(NL)])

    bq = np.asarray(inputs["L0_attn_bq"], dtype=f)[:, perm]
    bk = np.asarray(inputs["L0_attn_bk"], dtype=f)[:, perm]
    d["bq_col"] = np.ascontiguousarray(
        np.stack([_col_pack(bq[l], CH) for l in range(NL)], axis=1).reshape(128, NL * CH))
    d["bk_col"] = np.ascontiguousarray(
        np.stack([_col_pack(bk[l], CH) for l in range(NL)], axis=1).reshape(128, NL * CH))

    bv = np.asarray(inputs["L0_attn_bv"], dtype=f)
    bvr = np.zeros((64, NL * C), dtype=f)
    for l in range(NL):
        bvr[0, C * l:C * (l + 1)] = bv[l]
        bvr[32, C * l:C * (l + 1)] = bv[l]
    d["bv_rows"] = bvr

    d["lnAg_col"] = np.ascontiguousarray(
        np.stack([_col_pack(np.asarray(inputs["L0_attn_lnA_g"], f)[l], CH)
                  for l in range(NL)], axis=1).reshape(128, NL * CH))
    d["lnAb_col"] = np.ascontiguousarray(
        np.stack([_col_pack(np.asarray(inputs["L0_attn_lnA_b"], f)[l], CH)
                  for l in range(NL)], axis=1).reshape(128, NL * CH))
    d["ln0g_col"] = _col_pack(inputs["L0_attn_ln0_g"], CH)
    d["ln0b_col"] = _col_pack(inputs["L0_attn_ln0_b"], CH)
    d["ln2g_col"] = _col_pack(inputs["L0_attn_ln2_g"], CH)
    d["ln2b_col"] = _col_pack(inputs["L0_attn_ln2_b"], CH)

    w1r = np.asarray(inputs["L0_attn_conv1_w"], f).reshape(C, 4, 5)
    w2r = np.asarray(inputs["L0_attn_conv2_w"], f)
    dia1 = np.zeros((CH, 128, 20 * 128), dtype=f)
    dia2 = np.zeros((CH, 128, 20 * 128), dtype=f)
    idx = np.arange(128)
    for c in range(CH):
        for jj in range(4):
            for t5 in range(5):
                col = 128 * (5 * jj + t5)
                dia1[c, idx, col + idx] = w1r[128 * c + idx, jj, t5]
                dia2[c, idx, col + idx] = w2r[128 * c + idx, jj, t5]
    d["conv1d"] = dia1
    d["conv2d"] = dia2
    d["cb1_col"] = np.ascontiguousarray(
        np.asarray(inputs["L0_attn_conv1_b"], f).reshape(CH, 128, 4)
        .transpose(1, 0, 2).reshape(128, CH * 4))
    d["cb2_col"] = _col_pack(inputs["L0_attn_conv2_b"], CH)

    d["W1_t"] = _tile_km(np.asarray(inputs["L0_attn_W1"], f), 512)
    d["W2_t"] = _tile_km(np.asarray(inputs["L0_attn_W2"], f), 1024)[0]
    d["b1_col"] = _col_pack(inputs["L0_attn_b1"], 32)
    d["b2_col"] = _col_pack(inputs["L0_attn_b2"], CH)

    d["f1_t"] = _tile_km(np.ascontiguousarray(np.asarray(inputs["L0_ffn_w1_w"], f).T), 512)
    d["f2_t"] = _tile_km(np.ascontiguousarray(np.asarray(inputs["L0_ffn_w2_w"], f).T), 512)
    d["h1_t"] = _tile_km(np.ascontiguousarray(np.asarray(inputs["L0_ffn_h1_w"], f).T), 128)
    d["h2_t"] = _tile_km(np.ascontiguousarray(np.asarray(inputs["L0_ffn_h2_w"], f).T), 128)
    d["w3_t"] = _tile_km(np.ascontiguousarray(np.asarray(inputs["L0_ffn_w3_w"], f).T), 128)
    d["s1_col"] = _col_pack(inputs["L0_ffn_w1_s"], 32)
    d["s2_col"] = _col_pack(inputs["L0_ffn_w2_s"], 32)
    d["h1s_col"] = _col_pack(inputs["L0_ffn_h1_s"], 32)
    d["h2s_col"] = _col_pack(inputs["L0_ffn_h2_s"], 32)
    d["s3_col"] = _col_pack(inputs["L0_ffn_w3_s"], CH)
    return d



class MK:
    def __init__(self, nc):
        self.nc = nc
        self.p = Prog(nc)
        self._declare_dram()
        self._alloc_top()

    def _declare_dram(self):
        nc = self.nc

        def D(n, sh, dt, kind="ExternalInput"):
            return nc.dram_tensor(n, sh, dt, kind=kind).ap()

        self.d_x = D("x", [NSEQ, C, T], F32)
        self.d_cos = D("cosT", [128, 8192], F32)
        self.d_sin = D("sinT", [128, 8192], F32)
        self.d_eye = D("eye128", [128, 128], F32)
        self.d_eps = D("eps_col", [128, 2], F32)
        self.d_ones_col = D("ones_col", [128, 1], F32)
        self.d_ones_row = D("ones_row", [1, 128], F32)
        self.d_ebc = D("Ebc", [16, CH * 128], F32)
        self.d_wq = D("wq_t", [NL, 2, 128, 4096], F32)
        self.d_wk = D("wk_t", [NL, 2, 128, 4096], F32)
        self.d_wv = D("wv_t", [NL, 2, 128, 4096], F32)
        self.d_bq = D("bq_col", [128, NL * CH], F32)
        self.d_bk = D("bk_col", [128, NL * CH], F32)
        self.d_bvr = D("bv_rows", [64, NL * C], F32)
        self.d_lnAg = D("lnAg_col", [128, NL * CH], F32)
        self.d_lnAb = D("lnAb_col", [128, NL * CH], F32)
        self.d_ln0g = D("ln0g_col", [128, CH], F32)
        self.d_ln0b = D("ln0b_col", [128, CH], F32)
        self.d_ln2g = D("ln2g_col", [128, CH], F32)
        self.d_ln2b = D("ln2b_col", [128, CH], F32)
        self.d_c1d = D("conv1d", [CH, 128, 2560], F32)
        self.d_c2d = D("conv2d", [CH, 128, 2560], F32)
        self.d_cb1 = D("cb1_col", [128, CH * 4], F32)
        self.d_cb2 = D("cb2_col", [128, CH], F32)
        self.d_W1 = D("W1_t", [8, 128, 4096], F32)
        self.d_W2 = D("W2_t", [128, 32768], F32)
        self.d_b1 = D("b1_col", [128, 32], F32)
        self.d_b2 = D("b2_col", [128, CH], F32)
        self.d_f1 = D("f1_t", [8, 128, 4096], F32)
        self.d_f2 = D("f2_t", [8, 128, 4096], F32)
        self.d_h1 = D("h1_t", [32, 128, 4096], F32)
        self.d_h2 = D("h2_t", [32, 128, 4096], F32)
        self.d_w3 = D("w3_t", [8, 128, 4096], F32)
        self.d_s1 = D("s1_col", [128, 32], F32)
        self.d_s2 = D("s2_col", [128, 32], F32)
        self.d_h1s = D("h1s_col", [128, 32], F32)
        self.d_h2s = D("h2s_col", [128, 32], F32)
        self.d_s3 = D("s3_col", [128, CH], F32)
        self.d_y = D("y", [NSEQ, C, T], F32, "ExternalOutput")
        self.d_x2sp = D("x2_spill", [C, T], F32, "Internal")
        self.d_x3sp = D("x3_spill", [C, T], BF16, "Internal")

    def _alloc_top(self):
        nc = self.nc
        A = nc.alloc_sbuf_tensor
        self.ws = [A(f"ws{i}", [128, 4096], BF16) for i in range(6)]
        self.eye = A("eye", [128, 128], BF16)
        self.epsc = A("epsc", [128, 2], F32)
        self.ones_col = A("onescol", [128, 1], BF16)
        self.ones_row = A("onesrow", [1, 128], F32)
        self.ebc = A("ebc", [16, CH * 128], BF16)
        self.bvr = A("bvr", [64, C], BF16)
        self.bq = A("bqc", [128, NL * CH], F32)
        self.bk = A("bkc", [128, NL * CH], F32)
        self.lnAg = A("lnAg", [128, NL * CH], F32)
        self.lnAb = A("lnAb", [128, NL * CH], F32)
        self.ln0g = A("ln0g", [128, CH], F32)
        self.ln0b = A("ln0b", [128, CH], F32)
        self.ln2g = A("ln2g", [128, CH], F32)
        self.ln2b = A("ln2b", [128, CH], F32)
        self.cb1 = A("cb1", [128, CH * 4], F32)
        self.cb2 = A("cb2", [128, CH], F32)
        self.b1c = A("b1c", [128, 32], F32)
        self.b2c = A("b2c", [128, CH], F32)
        self.s1c = A("s1c", [128, 32], F32)
        self.s2c = A("s2c", [128, 32], F32)
        self.h1sc = A("h1sc", [128, 32], F32)
        self.h2sc = A("h2sc", [128, 32], F32)
        self.s3c = A("s3c", [128, CH], F32)
        self.m_row = A("m_row", [1, 512], F32)
        self.r_row = A("r_row", [1, 512], F32)
        self.km_tmp = A("km_tmp", [128, 1], F32)
        self.km_acc = A("km_acc", [128, CH], F32)
        self.km_mat = A("km_mat", [128, CH * 16], BF16)
        self.krs = A("krs", [128, 1024], BF16)
        self.z_bf = A("z_bf", [16, 1024], BF16)
        self.kvb = A("kvb", [128, 16 * 128], BF16)
        P = nc.alloc_psum_tensor
        self.ps = [P(f"ps{i}", [128, 512], F32) for i in range(6)]
        self.pt = P("pt", [128, 512], BF16)
        self.kvp = P("kvp", [128, 512], F32)

    def load_constants(self):
        p = self.p
        p.dma("pool", "consts_c", [
            (self.eye[:, :], self.d_eye),
            (self.ones_col[:, :], self.d_ones_col),
            (self.ebc[:, :], self.d_ebc),
        ], tag="consts_cast")
        p.dma("sp", "consts_p", [
            (self.ones_row[:, :], self.d_ones_row),
            (self.epsc[:, :], self.d_eps),
            (self.bq[:, :], self.d_bq), (self.bk[:, :], self.d_bk),
            (self.lnAg[:, :], self.d_lnAg), (self.lnAb[:, :], self.d_lnAb),
            (self.ln0g[:, :], self.d_ln0g), (self.ln0b[:, :], self.d_ln0b),
            (self.ln2g[:, :], self.d_ln2g), (self.ln2b[:, :], self.d_ln2b),
            (self.cb1[:, :], self.d_cb1), (self.cb2[:, :], self.d_cb2),
            (self.b1c[:, :], self.d_b1), (self.b2c[:, :], self.d_b2),
            (self.s1c[:, :], self.d_s1), (self.s2c[:, :], self.d_s2),
            (self.h1sc[:, :], self.d_h1s), (self.h2sc[:, :], self.d_h2s),
            (self.s3c[:, :], self.d_s3),
        ], tag="consts_plain")

    def emit_ln(self, items, eps, gamma=None, beta=None, dst_bf=None,
                bf_out=False, scratch=None, tag="ln"):
        p = self.p
        ps_st, ps_m, ps_r = self.ps[2], self.ps[3], self.ps[4]
        srcs = [it[0] for it in items]
        bfs = [it[1] for it in items]
        dsts = [it[2] for it in items]

        def mm_sum(t):
            for c in range(CH):
                inst = t.matmul(ps_st[0:1, :], self.ones_col[:, :], bfs[c],
                                start=(c == 0), stop=False, skip_group_check=True)
            return inst
        p.add("pe", mm_sum, reads=bfs + [self.ones_col[:, :]],
              writes=[ps_st[:, :]], tag=tag + "_mmsum")
        for c in range(CH):
            p.add("act", lambda a, c=c: a.activation(bfs[c], bfs[c], AF.Square),
                  reads=[bfs[c]], writes=[bfs[c]], tag=tag + "_sq")

        def mm_sq(t):
            for c in range(CH):
                inst = t.matmul(ps_st[64:65, :], self.ones_col[:, :], bfs[c],
                                start=(c == 0), stop=(c == CH - 1),
                                tile_position=(0, 64), skip_group_check=True)
            return inst
        p.add("pe", mm_sq, reads=bfs + [self.ones_col[:, :]],
              writes=[ps_st[:, :]], tag=tag + "_mmsq")

        p.add("dve", lambda v: v.tensor_scalar_mul(self.m_row[:, :], ps_st[0:1, :], 1.0 / C),
              reads=[ps_st[:, :]], writes=[self.m_row[:, :]], tag=tag + "_m")
        p.add("dve", lambda v: v.tensor_tensor(self.r_row[:, :], self.m_row[:, :],
                                               self.m_row[:, :], ALU.mult),
              reads=[self.m_row[:, :]], writes=[self.r_row[:, :]], tag=tag + "_m2")
        p.add("dve", lambda v: v.scalar_tensor_tensor(self.r_row[:, :], ps_st[64:65, :],
                                                      1.0 / C, self.r_row[:, :],
                                                      ALU.mult, ALU.subtract),
              reads=[ps_st[:, :], self.r_row[:, :]], writes=[self.r_row[:, :]],
              tag=tag + "_var")
        ecol = self.epsc[0:1, 0:1] if eps == EPS_OUT else self.epsc[0:1, 1:2]
        p.add("act", lambda a, ecol=ecol: a.activation(self.r_row[:, :], self.r_row[:, :],
                                                       AF.Sqrt, bias=ecol, scale=1.0),
              reads=[self.r_row[:, :], ecol], writes=[self.r_row[:, :]], tag=tag + "_sd")
        p.add("dve", lambda v: v.reciprocal(self.r_row[:, :], self.r_row[:, :]),
              reads=[self.r_row[:, :]], writes=[self.r_row[:, :]], tag=tag + "_rstd")
        p.add("pe", lambda t: t.matmul(ps_m[:, :], self.ones_row[:, :], self.m_row[:, :],
                                       start=True, stop=True, skip_group_check=True),
              reads=[self.m_row[:, :], self.ones_row[:, :]], writes=[ps_m[:, :]],
              tag=tag + "_bm")
        p.add("pe", lambda t: t.matmul(ps_r[:, :], self.ones_row[:, :], self.r_row[:, :],
                                       start=True, stop=True, skip_group_check=True),
              reads=[self.r_row[:, :], self.ones_row[:, :]], writes=[ps_r[:, :]],
              tag=tag + "_br")

        for c in range(CH):
            g = gamma[c] if gamma is not None else None
            b = beta[c] if beta is not None else None

            if bf_out:
                scr = scratch[c]
                p.add("dve", lambda v, c=c, scr=scr: v.tensor_tensor(
                          scr, srcs[c], ps_m[:, :], ALU.subtract),
                      reads=[srcs[c], ps_m[:, :]], writes=[scr], tag=tag + "_a1")
                p.add("dve", lambda v, c=c, scr=scr: v.scalar_tensor_tensor(
                          dsts[c], scr, 1.0, ps_r[:, :], ALU.mult, ALU.mult),
                      reads=[scr, ps_r[:, :]], writes=[dsts[c]], tag=tag + "_a2")
            else:
                p.add("dve", lambda v, c=c: v.tensor_tensor(
                          dsts[c], srcs[c], ps_m[:, :], ALU.subtract),
                      reads=[srcs[c], ps_m[:, :]], writes=[dsts[c]], tag=tag + "_a1")
                if g is not None:
                    p.add("dve", lambda v, c=c, g=g: v.scalar_tensor_tensor(
                              dsts[c], dsts[c], g, ps_r[:, :], ALU.mult, ALU.mult),
                          reads=[dsts[c], g, ps_r[:, :]], writes=[dsts[c]],
                          tag=tag + "_a2")
                    p.add("act", lambda a, c=c, b=b: a.activation(
                              dsts[c], dsts[c], AF.Identity, bias=b, scale=1.0),
                          reads=[dsts[c], b], writes=[dsts[c]], tag=tag + "_a3")
                else:
                    p.add("dve", lambda v, c=c: v.tensor_tensor(
                              dsts[c], dsts[c], ps_r[:, :], ALU.mult),
                          reads=[dsts[c], ps_r[:, :]], writes=[dsts[c]],
                          tag=tag + "_a2")
                if dst_bf is not None:
                    p.add("act", lambda a, c=c: a.copy(dst_bf[c], dsts[c]),
                          reads=[dsts[c]], writes=[dst_bf[c]], tag=tag + "_a4")


def MM(t, out, lhsT, rhs, start, stop, tp=None):
    return t.matmul(out, lhsT, rhs, start=start, stop=stop,
                    tile_position=tp, skip_group_check=True)



def build_p0(mk, seq, b0, b0d, hbfc):
    p = mk.p
    for c in range(CH):
        p.add("pool", lambda g, c=c: g.memset(b0[c][:, 0:2], 0.0),
              writes=[b0[c][:, 0:2]], tag="padz")
        p.add("pool", lambda g, c=c: g.memset(b0[c][:, 2 + T:PADW], 0.0),
              writes=[b0[c][:, 2 + T:PADW]], tag="padz")
    p.dma("sp", "xin", [(b0d[c], mk.d_x[seq, 128 * c:128 * (c + 1), :])
                        for c in range(CH)], tag="xload")
    for nq in range(4):
        nq0 = 512 * nq
        p.dma("pool", "xbf", [(hbfc[c], mk.d_x[seq, 128 * c:128 * (c + 1), nq0:nq0 + 512])
                              for c in range(CH)], tag="xbfload")
        items = [(b0d[c][:, nq0:nq0 + 512], hbfc[c], b0d[c][:, nq0:nq0 + 512])
                 for c in range(CH)]
        mk.emit_ln(items, EPS_OUT, tag=f"ln1_{seq}_{nq}")


def build_conv(mk, seq, scope, b0, b0d, hbfc):
    nc, p = mk.nc, mk.p
    xbfc = scope.enter_context(nc.sbuf_tensor(f"xbfc_{seq}", [128, 8 * 264], BF16))
    hjb = [scope.enter_context(nc.sbuf_tensor(f"hjb{i}_{seq}", [128, 4 * PADW], BF16))
           for i in range(2)]

    for c in range(CH):
        s1 = mk.ws[c % 2]
        s2 = mk.ws[2 + c % 2]
        p.dma("pool", f"ws{c % 2}", [(s1[:, 0:2560], mk.d_c1d[c, :, :])], tag="c1d")
        p.dma("pool", f"ws{2 + c % 2}", [(s2[:, 0:2560], mk.d_c2d[c, :, :])], tag="c2d")
        hj = hjb[c % 2]
        for jj in range(4):
            p.add("pool", lambda g, jj=jj, hj=hj: g.memset(hj[:, PADW * jj:PADW * jj + 2], 0.0),
                  writes=[hj[:, PADW * jj:PADW * jj + 2]], tag="hjz")
            p.add("pool", lambda g, jj=jj, hj=hj: g.memset(hj[:, PADW * jj + 2 + T:PADW * (jj + 1)], 0.0),
                  writes=[hj[:, PADW * jj + 2 + T:PADW * (jj + 1)]], tag="hjz")
        for tk in range(8):
            w = xbfc[:, 264 * tk:264 * tk + 260]
            p.add("act", lambda a, c=c, tk=tk, w=w: a.copy(
                      w, b0[c][:, 256 * tk:256 * tk + 260]),
                  reads=[b0[c][:, 256 * tk:256 * tk + 260]], writes=[w], tag="xc")
            for jj in range(4):
                psq = mk.ps[(tk * 4 + jj) % 2]

                def mm1(t, s1=s1, jj=jj, tk=tk, psq=psq):
                    for t5 in range(5):
                        inst = MM(t, psq[:, 0:256], s1[:, 128 * (5 * jj + t5):128 * (5 * jj + t5 + 1)],
                                  xbfc[:, 264 * tk + t5:264 * tk + t5 + 256],
                                  start=(t5 == 0), stop=(t5 == 4))
                    return inst
                p.add("pe", mm1, reads=[s1[:, 0:2560], w], writes=[psq[:, :]], tag="c1mm")
                p.add("act", lambda a, hj=hj, jj=jj, tk=tk, psq=psq, c=c: a.activation(
                          hj[:, PADW * jj + 2 + 256 * tk:PADW * jj + 2 + 256 * tk + 256],
                          psq[:, 0:256], AF.Gelu,
                          bias=mk.cb1[:, 4 * c + jj:4 * c + jj + 1], scale=1.0),
                      reads=[psq[:, :], mk.cb1[:, 4 * c + jj:4 * c + jj + 1]],
                      writes=[hj[:, PADW * jj + 2 + 256 * tk:PADW * jj + 2 + 256 * tk + 256]],
                      tag="c1ev")
        for tk in range(8):
            psq = mk.ps[2 + tk % 2]

            def mm2(t, s2=s2, tk=tk, psq=psq, hj=hj):
                first = True
                for jj in range(4):
                    for t5 in range(5):
                        inst = MM(t, psq[:, 0:256], s2[:, 128 * (5 * jj + t5):128 * (5 * jj + t5 + 1)],
                                  hj[:, PADW * jj + 256 * tk + t5:PADW * jj + 256 * tk + t5 + 256],
                                  start=first, stop=(jj == 3 and t5 == 4))
                        first = False
                return inst
            p.add("pe", mm2, reads=[s2[:, 0:2560], hj[:, :]], writes=[psq[:, :]], tag="c2mm")
            sl = slice(256 * tk, 256 * tk + 256)
            p.add("dve", lambda v, c=c, psq=psq, sl=sl: v.scalar_tensor_tensor(
                      b0d[c][:, sl], psq[:, 0:256], mk.cb2[:, c:c + 1], b0d[c][:, sl],
                      ALU.add, ALU.add),
                  reads=[psq[:, :], mk.cb2[:, c:c + 1], b0d[c][:, sl]],
                  writes=[b0d[c][:, sl]], tag="c2ev")
    for nq in range(4):
        nq0 = 512 * nq
        for c in range(CH):
            p.add("act", lambda a, c=c, nq0=nq0: a.copy(
                      hbfc[c], b0d[c][:, nq0:nq0 + 512]),
                  reads=[b0d[c][:, nq0:nq0 + 512]], writes=[hbfc[c]], tag="ln0c")
        items = [(b0d[c][:, nq0:nq0 + 512], hbfc[c], b0d[c][:, nq0:nq0 + 512])
                 for c in range(CH)]
        mk.emit_ln(items, EPS_IN,
                   gamma=[mk.ln0g[:, c:c + 1] for c in range(CH)],
                   beta=[mk.ln0b[:, c:c + 1] for c in range(CH)],
                   tag=f"ln0_{seq}_{nq}")


def build_attn_layer(mk, seq, l, b0d, hbfc2, bufs, upto=None):
    p = mk.p
    cosc, sinc = bufs["cosc"], bufs["sinc"]
    ksc = bufs["ksc"]
    tmp1 = bufs["tmp1"]
    krt, vt = bufs["krt"], bufs["vt"]
    zbs = bufs["zbs"]
    ws = mk.ws

    p.dma("pool", "bvl", [(mk.bvr[:, :], mk.d_bvr[:, C * l:C * (l + 1)])], tag=f"bv{l}")
    for b in range(2):
        p.dma("pool", f"ws{b}", [(ws[b][:, :], mk.d_wk[l, b, :, :])], tag=f"wk{l}")
        p.dma("pool", f"ws{2 + b}", [(ws[2 + b][:, :], mk.d_wv[l, b, :, :])], tag=f"wv{l}")
        p.dma("pool", f"ws{4 + b}", [(ws[4 + b][:, :], mk.d_wq[l, b, :, :])], tag=f"wq{l}")

    def hb(c, par):
        return hbfc2[(par % 2) * CH + c]

    def kscv(par):
        off = (par % 2) * CH * 512
        return ksc[:, off:off + CH * 512]

    def cast_h(nq0, tag, par):
        for c in range(CH):
            p.add("act", lambda a, c=c, nq0=nq0, par=par: a.copy(
                      hb(c, par), b0d[c][:, nq0:nq0 + 512]),
                  reads=[b0d[c][:, nq0:nq0 + 512]], writes=[hb(c, par)], tag=tag)

    def proj(wslots, biascol, dst, tag, par):
        for mc in range(CH):
            wsl = wslots[0] if mc < 4 else wslots[1]
            mloc = mc % 4
            psq = mk.ps[mc % 2]

            def mmp(t, wsl=wsl, mloc=mloc, psq=psq, par=par):
                for kc in range(CH):
                    inst = MM(t, psq[:, :], wsl[:, 512 * kc + 128 * mloc:512 * kc + 128 * mloc + 128],
                              hb(kc, par), start=(kc == 0), stop=(kc == CH - 1))
                return inst
            p.add("pe", mmp, reads=[wsl[:, :]] + [hb(kc, par) for kc in range(CH)],
                  writes=[psq[:, :]], tag=tag + "mm")
            p.add("act", lambda a, mc=mc, psq=psq: a.activation(
                      dst[:, 512 * mc:512 * mc + 512], psq[:, :], AF.Gelu,
                      bias=biascol[:, CH * l + mc:CH * l + mc + 1], scale=1.0),
                  reads=[psq[:, :], biascol[:, CH * l + mc:CH * l + mc + 1]],
                  writes=[dst[:, 512 * mc:512 * mc + 512]], tag=tag + "ev")
            p.add("dve", lambda v, mc=mc: v.tensor_scalar_add(
                      dst[:, 512 * mc:512 * mc + 512], dst[:, 512 * mc:512 * mc + 512],
                      0.21),
                  reads=[dst[:, 512 * mc:512 * mc + 512]],
                  writes=[dst[:, 512 * mc:512 * mc + 512]], tag=tag + "21")

    def rope(buf, nq, tag):
        cw = cosc[nq]
        sw = sinc[nq]
        se = buf[:, 0:2048]
        so = buf[:, 2048:4096]
        te = tmp1[:, 0:2048]
        toh = tmp1[:, 2048:3072]
        p.add("dve", lambda v: v.tensor_tensor(te, so, sw, ALU.mult),
              reads=[so, sw], writes=[te], tag=tag + "1")
        for hf in range(2):
            seh = buf[:, 1024 * hf:1024 * hf + 1024]
            swh = sw[:, 1024 * hf:1024 * hf + 1024]
            p.add("dve", lambda v, seh=seh, swh=swh: v.tensor_tensor(
                      toh, seh, swh, ALU.mult),
                  reads=[seh, swh], writes=[toh], tag=tag + f"2{hf}")
            soh = buf[:, 2048 + 1024 * hf:2048 + 1024 * hf + 1024]
            cwh = cw[:, 1024 * hf:1024 * hf + 1024]
            p.add("dve", lambda v, soh=soh, cwh=cwh: v.tensor_tensor(
                      soh, soh, cwh, ALU.mult),
                  reads=[soh, cwh], writes=[soh], tag=tag + f"5{hf}")
            p.add("pool", lambda g, soh=soh: g.tensor_tensor(
                      soh, soh, toh, ALU.add),
                  reads=[soh, toh], writes=[soh], tag=tag + f"6{hf}")
        p.add("dve", lambda v: v.tensor_tensor(se, se, cw, ALU.mult),
              reads=[se, cw], writes=[se], tag=tag + "3")
        p.add("dve", lambda v: v.tensor_tensor(se, se, te, ALU.subtract),
              reads=[se, te], writes=[se], tag=tag + "4")

    def kv_A(nq):
        nq0 = 512 * nq
        par = nq % 2
        ksq = kscv(par)
        cast_h(nq0, f"a{l}kc", par)
        proj((ws[0], ws[1]), mk.bk, ksq, f"a{l}k", par)
        for c in range(CH):
            p.add("dve", lambda v, c=c, ksq=ksq: v.reduce_sum(
                      mk.km_tmp[:, :], ksq[:, 512 * c:512 * c + 512], axis=AX.X),
                  reads=[ksq[:, 512 * c:512 * c + 512]], writes=[mk.km_tmp[:, :]],
                  tag=f"a{l}km1")
            if nq == 0:
                p.add("dve", lambda v, c=c: v.tensor_copy(
                          out=mk.km_acc[:, c:c + 1], in_=mk.km_tmp[:, :]),
                      reads=[mk.km_tmp[:, :]], writes=[mk.km_acc[:, c:c + 1]],
                      tag=f"a{l}km2")
            else:
                p.add("dve", lambda v, c=c: v.tensor_tensor(
                          mk.km_acc[:, c:c + 1], mk.km_acc[:, c:c + 1],
                          mk.km_tmp[:, :], ALU.add),
                      reads=[mk.km_tmp[:, :], mk.km_acc[:, c:c + 1]],
                      writes=[mk.km_acc[:, c:c + 1]], tag=f"a{l}km2")
        rope(ksq, nq, f"a{l}rk")

    def kv_B(nq):
        nq0 = 512 * nq
        par = nq % 2
        ksq = kscv(par)
        for g in range(4):
            for half in range(2):
                def tr(t, half=half, g=g, ksq=ksq):
                    for pp in range(4):
                        ci = 4 * half + pp
                        inst = t.transpose(mk.pt[:, 128 * pp:128 * pp + 128],
                                           ksq[:, 512 * ci + 128 * g:512 * ci + 128 * g + 128],
                                           mk.eye[:, :])
                    return inst
                trd = [ksq[:, 512 * (4 * half + pp) + 128 * g:512 * (4 * half + pp) + 128 * g + 128]
                       for pp in range(4)]
                p.add("pe", tr, reads=trd + [mk.eye[:, :]], writes=[mk.pt[:, :]],
                      tag=f"a{l}tr")
                p.add("act", lambda a, half=half: a.copy(
                          krt[:, 512 * half:512 * half + 512], mk.pt[:, :]),
                      reads=[mk.pt[:, :]], writes=[krt[:, 512 * half:512 * half + 512]],
                      tag=f"a{l}trev")
            for vh in range(2):
                wsl = ws[2 + vh]
                psq = mk.ps[3 + vh]

                def mmv(t, wsl=wsl, g=g, psq=psq, par=par):
                    for kc in range(CH):
                        inst = MM(t, psq[:, :], hb(kc, par)[:, 128 * g:128 * g + 128],
                                  wsl[:, 512 * kc:512 * kc + 512],
                                  start=(kc == 0), stop=(kc == CH - 1))
                    return inst
                p.add("pe", mmv, reads=[wsl[:, :]] + [hb(kc, par) for kc in range(CH)],
                      writes=[psq[:, :]], tag=f"a{l}vmm")
                p.add("act", lambda a, vh=vh, psq=psq: a.copy(
                          vt[:, 512 * vh:512 * vh + 512], psq[:, :]),
                      reads=[psq[:, :]], writes=[vt[:, 512 * vh:512 * vh + 512]],
                      tag=f"a{l}vev")

            def mmkv(t, nq=nq, g=g):
                for h in range(NH):
                    rb = (32 * h) % 128
                    for half in range(2):
                        inst = MM(t, mk.kvp[rb:rb + 32, (h // 4) * 128 + 64 * half:
                                            (h // 4) * 128 + 64 * half + 64],
                                  krt[:, 512 * half + 32 * h:512 * half + 32 * h + 32],
                                  vt[:, 64 * h:64 * h + 64],
                                  start=(nq == 0 and g == 0 and h < 4 and half == 0),
                                  stop=False, tp=(0, rb))
                return inst
            p.add("pe", mmkv, reads=[krt[:, :], vt[:, :]], writes=[mk.kvp[:, :]],
                  tag=f"a{l}kv")

            def mmks(t, nq=nq, g=g):
                st = (nq == 0 and g == 0)
                MM(t, mk.ps[5][0:1, :], mk.ones_col[:, :], krt[:, 0:512],
                   start=st, stop=False)
                return MM(t, mk.ps[5][32:33, :], mk.ones_col[:, :], krt[:, 512:1024],
                          start=st, stop=False, tp=(0, 32))
            p.add("pe", mmks, reads=[krt[:, :], mk.ones_col[:, :]],
                  writes=[mk.ps[5][:, :]], tag=f"a{l}ks")

    kv_A(0)
    kv_A(1)
    kv_B(0)
    kv_A(2)
    kv_B(1)
    kv_A(3)
    kv_B(2)
    kv_B(3)

    def ksev(a):
        a.copy(mk.krs[0:1, 0:512], mk.ps[5][0:1, :])
        return a.copy(mk.krs[32:33, 0:512], mk.ps[5][32:33, :])
    p.add("act", ksev, reads=[mk.ps[5][:, :]], writes=[mk.krs[:, :]], tag=f"a{l}ksev")
    p.dma("sp", "krsmv", [(mk.krs[0:1, 512:1024], mk.krs[32:33, 0:512])],
          tag=f"a{l}krsmv")

    def mmbv(t):
        for h in range(NH):
            rb = (32 * h) % 128
            for half in range(2):
                inst = MM(t, mk.kvp[rb:rb + 32, (h // 4) * 128 + 64 * half:
                                    (h // 4) * 128 + 64 * half + 64],
                          mk.krs[0:1, 512 * half + 32 * h:512 * half + 32 * h + 32],
                          mk.bvr[0:1, 64 * h:64 * h + 64],
                          start=False, stop=(h == NH - 1 and half == 1),
                          tp=(0, rb))
        return inst
    p.add("pe", mmbv, reads=[mk.krs[:, :], mk.bvr[:, :]], writes=[mk.kvp[:, :]],
          tag=f"a{l}bv")
    p.add("pool", lambda g: g.memset(mk.kvb[:, :], 0.0),
          writes=[mk.kvb[:, :]], tag=f"a{l}kvz")
    for Mp in range(CH):
        for half in range(2):
            base = 256 * Mp + 128 * half
            for hh in range(2):
                h = 2 * Mp + hh
                rb = (32 * h) % 128
                p.add("dve", lambda v, base=base, hh=hh, h=h, rb=rb, half=half:
                      v.tensor_scalar_mul(
                          mk.kvb[rb:rb + 32, base + 64 * hh:base + 64 * hh + 64],
                          mk.kvp[rb:rb + 32, (h // 4) * 128 + 64 * half:
                                 (h // 4) * 128 + 64 * half + 64], 1.0 / T),
                      reads=[mk.kvp[:, :]],
                      writes=[mk.kvb[rb:rb + 32, base + 64 * hh:base + 64 * hh + 64]],
                      tag=f"a{l}kvev")
    p.add("pool", lambda g: g.memset(mk.km_mat[:, :], 0.0),
          writes=[mk.km_mat[:, :]], tag=f"a{l}kmz")
    for c in range(CH):
        cc = c % 4
        for q4 in range(4):
            h = 4 * cc + q4
            p.add("dve", lambda v, c=c, q4=q4, h=h: v.tensor_scalar_mul(
                      mk.km_mat[32 * q4:32 * q4 + 32, 16 * c + h:16 * c + h + 1],
                      mk.km_acc[32 * q4:32 * q4 + 32, c:c + 1], 1.0 / T),
                  reads=[mk.km_acc[:, c:c + 1]],
                  writes=[mk.km_mat[:, 16 * c:16 * c + 16]], tag=f"a{l}kmb")

    def zv(par, bf):
        off = (par % 2) * 512
        return mk.z_bf[:, off:off + 512]

    def q_A(nq):
        nq0 = 512 * nq
        par = nq % 2
        ksq = kscv(par)
        cast_h(nq0, f"a{l}qc", par)
        proj((ws[4], ws[5]), mk.bq, ksq, f"a{l}q", par)

        def mmz(t, ksq=ksq):
            for c in range(CH):
                inst = MM(t, mk.ps[5][0:16, :], mk.km_mat[:, 16 * c:16 * c + 16],
                          ksq[:, 512 * c:512 * c + 512],
                          start=(c == 0), stop=(c == CH - 1))
            return inst
        p.add("pe", mmz, reads=[mk.km_mat[:, :]] + [ksq[:, 512 * c:512 * c + 512] for c in range(CH)],
              writes=[mk.ps[5][:, :]], tag=f"a{l}z")
        zb = zv(par, True)
        p.add("dve", lambda v: v.tensor_scalar_add(mk.ps[5][0:16, :], mk.ps[5][0:16, :], 1e-6),
              reads=[mk.ps[5][:, :]], writes=[mk.ps[5][:, :]], tag=f"a{l}zc1")
        p.add("dve", lambda v: v.reciprocal(mk.ps[5][0:16, :], mk.ps[5][0:16, :]),
              reads=[mk.ps[5][:, :]], writes=[mk.ps[5][:, :]], tag=f"a{l}zc2")
        p.add("dve", lambda v, zb=zb: v.tensor_copy(out=zb, in_=mk.ps[5][0:16, :]),
              reads=[mk.ps[5][:, :]], writes=[zb], tag=f"a{l}zc3")
        rope(ksq, nq, f"a{l}rq")

    def q_B(nq):
        nq0 = 512 * nq
        par = nq % 2
        ksq = kscv(par)
        zb = zv(par, True)
        for ec in range(CH):
            pso = mk.ps[3 + ec % 2]

            def mmo(t, ec=ec, pso=pso, ksq=ksq):
                for half in range(2):
                    ck = 4 * half + ec // 2
                    inst = MM(t, pso[:, :],
                              mk.kvb[:, 256 * ec + 128 * half:256 * ec + 128 * half + 128],
                              ksq[:, 512 * ck:512 * ck + 512],
                              start=(half == 0), stop=(half == 1))
                return inst
            mmord = [ksq[:, 512 * (4 * half + ec // 2):512 * (4 * half + ec // 2) + 512]
                     for half in range(2)]
            p.add("pe", mmo, reads=[mk.kvb[:, :]] + mmord, writes=[pso[:, :]],
                  tag=f"a{l}o")
            zbp = mk.ps[2] if ec % 2 == 0 else mk.ps[5]
            p.add("pe", lambda t, ec=ec, zbp=zbp, zb=zb: MM(
                      t, zbp[:, :], mk.ebc[:, 128 * ec:128 * ec + 128], zb,
                      start=True, stop=True),
                  reads=[mk.ebc[:, :], zb], writes=[zbp[:, :]], tag=f"a{l}zb")
            p.add("dve", lambda v, zbp=zbp: v.tensor_copy(out=zbs[:, :], in_=zbp[:, :]),
                  reads=[zbp[:, :]], writes=[zbs[:, :]], tag=f"a{l}zbe")
            p.add("dve", lambda v, pso=pso: v.tensor_tensor(
                      zbs[:, :], pso[:, :], zbs[:, :], ALU.mult),
                  reads=[pso[:, :], zbs[:, :]], writes=[zbs[:, :]], tag=f"a{l}sm1")
            p.add("dve", lambda v, ec=ec, nq0=nq0: v.tensor_tensor(
                      b0d[ec][:, nq0:nq0 + 512], b0d[ec][:, nq0:nq0 + 512],
                      zbs[:, :], ALU.add),
                  reads=[zbs[:, :], b0d[ec][:, nq0:nq0 + 512]],
                  writes=[b0d[ec][:, nq0:nq0 + 512]], tag=f"a{l}sm2")
        for c in range(CH):
            p.add("act", lambda a, c=c, nq0=nq0, par=par: a.copy(
                      hb(c, par), b0d[c][:, nq0:nq0 + 512]),
                  reads=[b0d[c][:, nq0:nq0 + 512]], writes=[hb(c, par)],
                  tag=f"a{l}lc")
        items = [(b0d[c][:, nq0:nq0 + 512], hb(c, par), b0d[c][:, nq0:nq0 + 512])
                 for c in range(CH)]
        mk.emit_ln(items, EPS_IN,
                   gamma=[mk.lnAg[:, CH * l + c:CH * l + c + 1] for c in range(CH)],
                   beta=[mk.lnAb[:, CH * l + c:CH * l + c + 1] for c in range(CH)],
                   tag=f"lnA{l}_{seq}_{nq}")

    q_A(0)
    q_A(1)
    q_B(0)
    q_A(2)
    q_B(1)
    q_A(3)
    q_B(2)
    q_B(3)
    return False


def build_mlp(mk, seq, scope, b0d, hbfc):
    nc, p = mk.nc, mk.p
    ub = scope.enter_context(nc.sbuf_tensor(f"ub_{seq}", [128, 4 * 512], BF16))
    fsb = scope.enter_context(nc.sbuf_tensor(f"fsb_{seq}", [128, CH * 1024], F32))

    mbf = scope.enter_context(nc.sbuf_tensor(f"mbf_{seq}", [128, CH * 1024], BF16))
    ws = mk.ws

    for nq in range(2):
        nq0 = 1024 * nq
        mbfc = [mbf[:, 1024 * c:1024 * (c + 1)] for c in range(CH)]
        for c in range(CH):
            p.add("act", lambda a, c=c, nq0=nq0: a.copy(
                      mbfc[c], b0d[c][:, nq0:nq0 + 1024]),
                  reads=[b0d[c][:, nq0:nq0 + 1024]], writes=[mbfc[c]], tag="mh")
        for b in range(8):
            wsl1 = ws[b % 2]
            wsl2 = ws[2 + b % 2]
            p.dma("pool", f"ws{b % 2}", [(wsl1[:, :], mk.d_W1[b, :, :])], tag="w1l")
            p.dma("pool", f"ws{2 + b % 2}",
                  [(wsl2[:, :], mk.d_W2[:, 4096 * b:4096 * b + 4096])], tag="w2l")
            for nh in range(2):
                o = 512 * nh
                for msub in range(4):
                    psq = mk.ps[msub % 2]

                    def mm1(t, wsl1=wsl1, msub=msub, psq=psq, o=o):
                        for kc in range(CH):
                            inst = MM(t, psq[:, :],
                                      wsl1[:, 512 * kc + 128 * msub:512 * kc + 128 * msub + 128],
                                      mbfc[kc][:, o:o + 512],
                                      start=(kc == 0), stop=(kc == CH - 1))
                        return inst
                    p.add("pe", mm1,
                          reads=[wsl1[:, :]] + [mbfc[kc][:, o:o + 512] for kc in range(CH)],
                          writes=[psq[:, :]], tag="w1mm")
                    hidx = 4 * b + msub
                    p.add("act", lambda a, msub=msub, psq=psq, hidx=hidx: a.activation(
                              ub[:, 512 * msub:512 * msub + 512], psq[:, :], AF.Gelu,
                              bias=mk.b1c[:, hidx:hidx + 1], scale=1.0),
                          reads=[psq[:, :], mk.b1c[:, hidx:hidx + 1]],
                          writes=[ub[:, 512 * msub:512 * msub + 512]], tag="w1ev")
                for mc in range(CH):
                    psq = mk.ps[4 + (mc % 2)]

                    def mm2(t, wsl2=wsl2, mc=mc, psq=psq):
                        for kci in range(4):
                            inst = MM(t, psq[:, :],
                                      wsl2[:, 1024 * kci + 128 * mc:1024 * kci + 128 * mc + 128],
                                      ub[:, 512 * kci:512 * kci + 512],
                                      start=(kci == 0), stop=(kci == 3))
                        return inst
                    p.add("pe", mm2, reads=[wsl2[:, :], ub[:, :]], writes=[psq[:, :]],
                          tag="w2mm")
                    fdst = fsb[:, 1024 * mc + 512 * nh:1024 * mc + 512 * nh + 512]
                    if b == 0:
                        p.add("act", lambda a, psq=psq, fdst=fdst: a.copy(fdst, psq[:, :]),
                              reads=[psq[:, :]], writes=[fdst], tag="f0")
                    else:
                        p.add("dve", lambda v, psq=psq, fdst=fdst: v.tensor_tensor(
                                  fdst, fdst, psq[:, :], ALU.add),
                              reads=[psq[:, :], fdst], writes=[fdst], tag="facc")
        for nh in range(2):
            o = 512 * nh
            for c in range(CH):
                fdst = fsb[:, 1024 * c + o:1024 * c + o + 512]
                p.add("dve", lambda v, fdst=fdst, c=c, nq0=nq0, o=o: v.scalar_tensor_tensor(
                          fdst, fdst, mk.b2c[:, c:c + 1], b0d[c][:, nq0 + o:nq0 + o + 512],
                          ALU.add, ALU.add),
                      reads=[fdst, mk.b2c[:, c:c + 1], b0d[c][:, nq0 + o:nq0 + o + 512]],
                      writes=[fdst], tag="fb2")
                p.add("act", lambda a, c=c, fdst=fdst: a.copy(hbfc[8 + c], fdst),
                      reads=[fdst], writes=[hbfc[8 + c]], tag="l2c")
            fwin = [fsb[:, 1024 * c + o:1024 * c + o + 512] for c in range(CH)]
            mk.emit_ln([(fwin[c], hbfc[8 + c], fwin[c]) for c in range(CH)], EPS_IN,
                       gamma=[mk.ln2g[:, c:c + 1] for c in range(CH)],
                       beta=[mk.ln2b[:, c:c + 1] for c in range(CH)],
                       tag=f"ln2_{seq}_{nq}_{nh}")
            for c in range(CH):
                p.add("act", lambda a, c=c, fdst=fwin[c]: a.copy(hbfc[8 + c], fdst),
                      reads=[fwin[c]], writes=[hbfc[8 + c]], tag="lpc")
            mk.emit_ln([(fwin[c], hbfc[8 + c], fwin[c]) for c in range(CH)], EPS_OUT,
                       tag=f"lnp_{seq}_{nq}_{nh}")
            p.dma("sp", "xo", [(b0d[c][:, nq0 + o:nq0 + o + 512],
                                mk.d_x[seq, 128 * c:128 * (c + 1), nq0 + o:nq0 + o + 512])
                               for c in range(CH)], tag="xol")
            for c in range(CH):
                p.add("dve", lambda v, c=c, nq0=nq0, o=o, fdst=fwin[c]: v.tensor_tensor(
                          b0d[c][:, nq0 + o:nq0 + o + 512],
                          b0d[c][:, nq0 + o:nq0 + o + 512], fdst, ALU.add),
                      reads=[fwin[c], b0d[c][:, nq0 + o:nq0 + o + 512]],
                      writes=[b0d[c][:, nq0 + o:nq0 + o + 512]], tag="x2")
            for c in range(CH):
                p.add("act", lambda a, c=c, nq0=nq0, o=o: a.copy(
                          hbfc[8 + c], b0d[c][:, nq0 + o:nq0 + o + 512]),
                      reads=[b0d[c][:, nq0 + o:nq0 + o + 512]], writes=[hbfc[8 + c]],
                      tag="l3c")
            x3win = [hbfc[c] for c in range(CH)]
            mk.emit_ln([(b0d[c][:, nq0 + o:nq0 + o + 512], hbfc[8 + c], x3win[c])
                        for c in range(CH)], EPS_OUT,
                       bf_out=True, scratch=fwin, tag=f"ln3_{seq}_{nq}_{nh}")
            p.dma("sp", "x3sp", [(mk.d_x3sp[128 * c:128 * (c + 1), nq0 + o:nq0 + o + 512],
                                  x3win[c]) for c in range(CH)], tag="x3sp")
        p.dma("sp", "x2sp", [(mk.d_x2sp[128 * c:128 * (c + 1), nq0:nq0 + 1024],
                              b0d[c][:, nq0:nq0 + 1024]) for c in range(CH)], tag="x2sp")


def build_ffn(mk, seq, scope):
    nc, p = mk.nc, mk.p
    x3w = scope.enter_context(nc.sbuf_tensor(f"x3w_{seq}", [128, CH * 512], BF16))
    Ab = scope.enter_context(nc.sbuf_tensor(f"A_{seq}", [128, 32 * 512], BF16))
    Bb = scope.enter_context(nc.sbuf_tensor(f"B_{seq}", [128, 32 * 512], BF16))
    sil = scope.enter_context(nc.sbuf_tensor(f"sil_{seq}", [128, 512], F32))
    u2 = scope.enter_context(nc.sbuf_tensor(f"u2_{seq}", [128, 512], F32))
    f3 = scope.enter_context(nc.sbuf_tensor(f"f3_{seq}", [128, CH * 512], F32))
    x2c = scope.enter_context(nc.sbuf_tensor(f"x2c_{seq}", [128, CH * 512], F32))
    ws = mk.ws
    x3wc = [x3w[:, 512 * c:512 * c + 512] for c in range(CH)]

    for tc in range(4):
        t0 = 512 * tc
        src3 = mk.d_x3sp.rearrange("(c p) n -> p c n", p=128)
        p.dma("sp", "x3ld", [(x3w[:, :].rearrange("p (c n) -> p c n", n=512),
                              src3[:, :, t0:t0 + 512])], tag="x3ld")
        for b in range(8):
            p.dma("pool", f"ws{b % 2}", [(ws[b % 2][:, :], mk.d_f1[b, :, :])], tag="f1l")
            p.dma("pool", f"ws{2 + b % 2}", [(ws[2 + b % 2][:, :], mk.d_f2[b, :, :])],
                  tag="f2l")
            for msub in range(4):
                hidx = 4 * b + msub
                psq = mk.ps[msub % 2]

                def mmf1(t, b=b, msub=msub, psq=psq):
                    wsl = ws[b % 2]
                    for kc in range(CH):
                        inst = MM(t, psq[:, :],
                                  wsl[:, 512 * kc + 128 * msub:512 * kc + 128 * msub + 128],
                                  x3wc[kc], start=(kc == 0), stop=(kc == CH - 1))
                    return inst
                p.add("pe", mmf1, reads=[ws[b % 2][:, :]] + x3wc, writes=[psq[:, :]],
                      tag="f1mm")
                p.add("act", lambda a, psq=psq, hidx=hidx: a.activation(
                          sil[:, :], psq[:, :], AF.Silu, bias=0.0,
                          scale=mk.s1c[:, hidx:hidx + 1]),
                      reads=[psq[:, :], mk.s1c[:, hidx:hidx + 1]], writes=[sil[:, :]],
                      tag="f1ev")
                ps2q = mk.ps[2 + msub % 2]

                def mmf2(t, b=b, msub=msub, ps2q=ps2q):
                    wsl = ws[2 + b % 2]
                    for kc in range(CH):
                        inst = MM(t, ps2q[:, :],
                                  wsl[:, 512 * kc + 128 * msub:512 * kc + 128 * msub + 128],
                                  x3wc[kc], start=(kc == 0), stop=(kc == CH - 1))
                    return inst
                p.add("pe", mmf2, reads=[ws[2 + b % 2][:, :]] + x3wc, writes=[ps2q[:, :]],
                      tag="f2mm")
                p.add("act", lambda a, ps2q=ps2q, hidx=hidx: a.activation(
                          u2[:, :], ps2q[:, :], AF.Identity, bias=0.0,
                          scale=mk.s2c[:, hidx:hidx + 1]),
                      reads=[ps2q[:, :], mk.s2c[:, hidx:hidx + 1]], writes=[u2[:, :]],
                      tag="f2ev")
                p.add("dve", lambda v, hidx=hidx: v.tensor_tensor(
                          Ab[:, 512 * hidx:512 * hidx + 512], sil[:, :], u2[:, :],
                          ALU.mult),
                      reads=[sil[:, :], u2[:, :]],
                      writes=[Ab[:, 512 * hidx:512 * hidx + 512]], tag="f12m")
        for g in range(32):
            p.dma("pool", f"ws{4 + g % 2}", [(ws[4 + g % 2][:, :], mk.d_h1[g, :, :])],
                  tag="h1l")
            psq = mk.ps[4 + g % 2]

            def mmh1(t, g=g, psq=psq):
                wsl = ws[4 + g % 2]
                for kc in range(32):
                    inst = MM(t, psq[:, :], wsl[:, 128 * kc:128 * kc + 128],
                              Ab[:, 512 * kc:512 * kc + 512],
                              start=(kc == 0), stop=(kc == 31))
                return inst
            p.add("pe", mmh1, reads=[ws[4 + g % 2][:, :], Ab[:, :]], writes=[psq[:, :]],
                  tag="h1mm")
            p.add("act", lambda a, g=g, psq=psq: a.activation(
                      Bb[:, 512 * g:512 * g + 512], psq[:, :], AF.Gelu, bias=0.0,
                      scale=mk.h1sc[:, g:g + 1]),
                  reads=[psq[:, :], mk.h1sc[:, g:g + 1]],
                  writes=[Bb[:, 512 * g:512 * g + 512]], tag="h1ev")
        for g in range(32):
            p.dma("pool", f"ws{g % 2}", [(ws[g % 2][:, :], mk.d_h2[g, :, :])], tag="h2l")
            psq = mk.ps[g % 2]

            def mmh2(t, g=g, psq=psq):
                wsl = ws[g % 2]
                for kc in range(32):
                    inst = MM(t, psq[:, :], wsl[:, 128 * kc:128 * kc + 128],
                              Bb[:, 512 * kc:512 * kc + 512],
                              start=(kc == 0), stop=(kc == 31))
                return inst
            p.add("pe", mmh2, reads=[ws[g % 2][:, :], Bb[:, :]], writes=[psq[:, :]],
                  tag="h2mm")
            p.add("act", lambda a, g=g, psq=psq: a.activation(
                      Ab[:, 512 * g:512 * g + 512], psq[:, :], AF.Gelu, bias=0.0,
                      scale=mk.h2sc[:, g:g + 1]),
                  reads=[psq[:, :], mk.h2sc[:, g:g + 1]],
                  writes=[Ab[:, 512 * g:512 * g + 512]], tag="h2ev")
        for g in range(CH):
            p.dma("pool", f"ws{2 + g % 2}", [(ws[2 + g % 2][:, :], mk.d_w3[g, :, :])],
                  tag="w3l")
            psq = mk.ps[2 + g % 2]

            def mmw3(t, g=g, psq=psq):
                wsl = ws[2 + g % 2]
                for kc in range(32):
                    inst = MM(t, psq[:, :], wsl[:, 128 * kc:128 * kc + 128],
                              Ab[:, 512 * kc:512 * kc + 512],
                              start=(kc == 0), stop=(kc == 31))
                return inst
            p.add("pe", mmw3, reads=[ws[2 + g % 2][:, :], Ab[:, :]], writes=[psq[:, :]],
                  tag="w3mm")
            p.add("act", lambda a, g=g, psq=psq: a.activation(
                      f3[:, 512 * g:512 * g + 512], psq[:, :], AF.Identity, bias=0.0,
                      scale=mk.s3c[:, g:g + 1]),
                  reads=[psq[:, :], mk.s3c[:, g:g + 1]],
                  writes=[f3[:, 512 * g:512 * g + 512]], tag="w3ev")
        bfw = [Bb[:, 512 * c:512 * c + 512] for c in range(CH)]
        for c in range(CH):
            p.add("act", lambda a, c=c: a.copy(
                      bfw[c], f3[:, 512 * c:512 * c + 512]),
                  reads=[f3[:, 512 * c:512 * c + 512]], writes=[bfw[c]], tag="lfc")
        f3w = [f3[:, 512 * c:512 * c + 512] for c in range(CH)]
        mk.emit_ln([(f3w[c], bfw[c], f3w[c]) for c in range(CH)], EPS_OUT,
                   tag=f"lnf_{seq}_{tc}")
        src2 = mk.d_x2sp.rearrange("(c p) n -> p c n", p=128)
        p.dma("sp", "x2ld", [(x2c[:, :].rearrange("p (c n) -> p c n", n=512),
                              src2[:, :, t0:t0 + 512])], tag="x2ld")
        for c in range(CH):
            p.add("dve", lambda v, c=c: v.tensor_tensor(
                      x2c[:, 512 * c:512 * c + 512], x2c[:, 512 * c:512 * c + 512],
                      f3w[c], ALU.add),
                  reads=[f3w[c], x2c[:, 512 * c:512 * c + 512]],
                  writes=[x2c[:, 512 * c:512 * c + 512]], tag="yadd")
        dsty = mk.d_y[seq, :, t0:t0 + 512].rearrange("(c p) n -> p c n", p=128)
        p.dma("sp", "yout", [(dsty, x2c[:, :].rearrange("p (c n) -> p c n", n=512))],
              tag="yst")



def build_program(stop_after=None):
    from contextlib import ExitStack
    nc = bass.Bass("TRN2", target_bir_lowering=False)
    mk = MK(nc)
    p = mk.p
    mk.load_constants()

    stopped = False
    for seq in range(NSEQ):
        if stopped:
            break
        with ExitStack() as seq_scope:
            stopped = build_seq(mk, seq, seq_scope, stop_after)
        p.barrier(exclude_sems=("consts_c", "consts_p"))

    p.final_wait()
    p.emit()
    return nc


def build_seq(mk, seq, seq_scope, stop_after):
    from contextlib import ExitStack
    nc, p = mk.nc, mk.p

    def dump(chunks):
        pairs = [(mk.d_y[seq, 128 * c:128 * (c + 1), :], chunks[c][:, 0:T])
                 for c in range(CH)]
        p.dma("sp", "yout", pairs, tag="dbg_out")
        return True

    main_scope = ExitStack()
    seq_scope.enter_context(main_scope)
    big0 = main_scope.enter_context(
        nc.sbuf_tensor(f"big0_{seq}", [128, CH * PADW], F32))
    b0 = [big0[:, PADW * c:PADW * (c + 1)] for c in range(CH)]
    b0d = [big0[:, PADW * c + 2:PADW * c + 2 + T] for c in range(CH)]
    hbf = main_scope.enter_context(
        nc.sbuf_tensor(f"hbf_{seq}", [128, 2 * CH * 512], BF16))
    hbfc = [hbf[:, 512 * c:512 * (c + 1)] for c in range(2 * CH)]

    build_p0(mk, seq, b0, b0d, hbfc)
    if stop_after == "ln1" and seq == 0:
        return dump(b0d)

    with ExitStack() as conv_scope:
        build_conv(mk, seq, conv_scope, b0, b0d, hbfc)
    p.barrier(exclude_sems=("consts_c", "consts_p"))
    if stop_after == "conv" and seq == 0:
        return dump(b0d)

    with ExitStack() as attn_scope:
        ae = attn_scope.enter_context
        cosT = ae(nc.sbuf_tensor(f"cosT_{seq}", [128, 8192], BF16))
        sinT = ae(nc.sbuf_tensor(f"sinT_{seq}", [128, 8192], BF16))
        cosc = [cosT[:, 2048 * i:2048 * (i + 1)] for i in range(4)]
        sinc = [sinT[:, 2048 * i:2048 * (i + 1)] for i in range(4)]
        p.dma("pool", "tbl",
              [(cosT[:, :], mk.d_cos[:, :]), (sinT[:, :], mk.d_sin[:, :])],
              tag="tables")
        ksc = ae(nc.sbuf_tensor(f"ksc_{seq}", [128, 2 * CH * 512], BF16))
        tmp1 = ae(nc.sbuf_tensor(f"tmp1_{seq}", [128, 3072], BF16))
        krt = ae(nc.sbuf_tensor(f"krt_{seq}", [128, 1024], BF16))
        vt = ae(nc.sbuf_tensor(f"vt_{seq}", [128, 1024], BF16))
        zbs = ae(nc.sbuf_tensor(f"zbs_{seq}", [128, 512], F32))
        bufs = dict(cosc=cosc, sinc=sinc, ksc=ksc, tmp1=tmp1,
                    krt=krt, vt=vt, zbs=zbs)
        for l in range(NL):
            build_attn_layer(mk, seq, l, b0d, hbfc, bufs)
            if stop_after == f"attn{l}" and seq == 0:
                return dump(b0d)
    p.barrier(exclude_sems=("consts_c", "consts_p"))

    with ExitStack() as mlp_scope:
        build_mlp(mk, seq, mlp_scope, b0d, hbfc)
    p.barrier(exclude_sems=("consts_c", "consts_p"))
    if stop_after == "mlp" and seq == 0:
        return dump(b0d)

    main_scope.close()
    p.barrier(exclude_sems=("consts_c", "consts_p"))

    with ExitStack() as ffn_scope:
        build_ffn(mk, seq, ffn_scope)
    return False



def kernel(**inputs):
    prep = prepare_inputs(inputs)
    x = np.asarray(inputs["L0_attn_x"], np.float32)
    nc = build_program(STOP_AFTER)
    in_maps = []
    for core in range(8):
        m = dict(prep)
        m["x"] = np.ascontiguousarray(x[2 * core:2 * core + 2].transpose(0, 2, 1))
        in_maps.append(m)
    res = run_bass_kernel_spmd(nc, in_maps, core_ids=list(range(8)))
    out = np.empty((16, 2048, 1024), np.float32)
    for core in range(8):
        y = np.asarray(res.results[core]["y"])
        out[2 * core:2 * core + 2] = y.transpose(0, 2, 1)
    return out


# version 08
# speedup vs baseline: 1.0780x; 1.0197x over previous
import numpy as np
import ml_dtypes

import concourse.bass as bass
import concourse.mybir as mybir
from concourse.bass_utils import run_bass_kernel_spmd

F32 = mybir.dt.float32
BF16 = mybir.dt.bfloat16
AF = mybir.ActivationFunctionType
ALU = mybir.AluOpType
AX = mybir.AxisListType

C = 1024
CH = 8
T = 2048
NL = 4
NH = 16
NSEQ = 2
PADW = 2052
EPS_OUT = 1e-5
EPS_IN = 1e-12

STOP_AFTER = None



class Prog:
    ENG = {"pe": "tensor", "act": "scalar", "dve": "vector", "pool": "gpsimd",
           "sp": "sync"}

    def __init__(self, nc):
        self.nc = nc
        self.esem = {e: nc.alloc_semaphore("esem_" + e)
                     for e in ("pe", "act", "dve", "pool")}
        self.tasks = {e: [] for e in self.ENG}
        self.ccnt = {e: 0 for e in ("pe", "act", "dve", "pool")}
        self.dsem = {}
        self.res = {}
        self.waited = {e: {} for e in self.ENG}

    def _iv(self, ap):
        t = ap.tensor
        if isinstance(t, bass.PSumTensorHandle):
            return (t.name, 0, 1 << 40)
        off = ap.offset if isinstance(ap.offset, int) else 0
        span = 1
        for step, count in ap.ap:
            span += abs(step) * (count - 1)
        return (t.name, off, off + span)

    def _collect(self, engine, reads, writes):
        deps = {}

        def want(dep):
            kind, key, val = dep
            k = (kind, key)
            if deps.get(k, -1) < val:
                deps[k] = val

        ivs_r = [self._iv(a) for a in reads]
        ivs_w = [self._iv(a) for a in writes]
        for name, lo, hi in ivs_r:
            ent = self.res.get(name)
            if ent:
                for a, b, dep in ent["w"]:
                    if a < hi and lo < b:
                        want(dep)
        for name, lo, hi in ivs_w:
            ent = self.res.get(name)
            if ent:
                for a, b, dep in ent["w"]:
                    if a < hi and lo < b:
                        want(dep)
                for a, b, dep in ent["r"]:
                    if a < hi and lo < b:
                        want(dep)
        return deps, ivs_r, ivs_w

    def _register(self, ivs_r, ivs_w, token):
        for name, lo, hi in ivs_w:
            ent = self.res.setdefault(name, {"w": [], "r": []})
            ent["w"] = [(a, b, d) for (a, b, d) in ent["w"] if not (lo <= a and b <= hi)]
            ent["r"] = [(a, b, d) for (a, b, d) in ent["r"] if not (lo <= a and b <= hi)]
            ent["w"].append((lo, hi, token))
        for name, lo, hi in ivs_r:
            self.res.setdefault(name, {"w": [], "r": []})["r"].append((lo, hi, token))

    def _mk_waits(self, engine, deps):
        waits = []
        for (kind, key), val in sorted(deps.items(), key=lambda kv: str(kv[0])):
            if val <= 0:
                continue
            sem = self.esem[key] if kind == "E" else self.dsem[key][0]
            prev = self.waited[engine].get((kind, key), -1)
            if val > prev:
                waits.append((sem, val))
                self.waited[engine][(kind, key)] = val
        return waits

    def add(self, engine, fn, reads=(), writes=(), tag=""):
        deps, ivs_r, ivs_w = self._collect(engine, reads, writes)
        waits = self._mk_waits(engine, deps)
        self.ccnt[engine] += 1
        token = ("E", engine, self.ccnt[engine])
        self.tasks[engine].append((waits, fn, tag, False))
        self._register(ivs_r, ivs_w, token)
        return token

    def dma(self, queue, semname, pairs, tag=""):
        if semname not in self.dsem:
            self.dsem[semname] = [self.nc.alloc_semaphore("dsem_" + semname), 0]
        ent = self.dsem[semname]
        prev_cum = ent[1]
        reads = [pp[1] for pp in pairs]
        writes = [pp[0] for pp in pairs]
        deps, ivs_r, ivs_w = self._collect(queue, reads, writes)
        if prev_cum > 0 and deps.get(("D", semname), -1) < prev_cum:
            deps[("D", semname)] = prev_cum
        waits = self._mk_waits(queue, deps)
        ent[1] += 16 * len(pairs)
        token = ("D", semname, ent[1])
        sem = ent[0]

        def fn(eng, pairs=pairs, sem=sem):
            for (o, i) in pairs:
                eng.dma_start(out=o, in_=i).then_inc(sem, 16)
            return None

        self.tasks[queue].append((waits, fn, tag, True))
        self._register(ivs_r, ivs_w, token)
        return token

    def barrier(self, exclude_sems=()):
        targets = {("E", e): self.ccnt[e] for e in self.esem if self.ccnt[e] > 0}
        targets.update({("D", n): ent[1] for n, ent in self.dsem.items()
                        if ent[1] > 0 and n not in exclude_sems})
        for e in self.ENG:
            waits = self._mk_waits(e, targets)
            if waits:
                def fn(eng, waits=waits):
                    return None
                self.tasks[e].append((waits, fn, "barrier", True))
        self.res = {}

    def final_wait(self, engine="sp"):
        targets = {("D", n): ent[1] for n, ent in self.dsem.items() if ent[1] > 0}
        targets.update({("E", e): self.ccnt[e] for e in self.esem if self.ccnt[e] > 0})
        waits = self._mk_waits(engine, targets)

        def fn(eng):
            return None
        self.tasks[engine].append((waits, fn, "final", True))

    def emit(self):
        with self.nc.Block() as block:
            for e, attr in self.ENG.items():
                dec = getattr(block, attr)
                tasks = self.tasks[e]
                esem = self.esem.get(e)

                def body(eng, tasks=tasks, esem=esem, ename=e):
                    for (waits, fn, tag, is_dma) in tasks:
                        for (sem, val) in waits:
                            eng.wait_ge(sem, val)
                        last = fn(eng)
                        if not is_dma:
                            assert last is not None, \
                                f"compute task {tag} on {ename} returned None"
                            last.then_inc(esem, 1)
                dec(body)



def _tile_km(w, granule):
    K, M = w.shape
    kc = K // 128
    nb = M // granule
    out = w.reshape(kc, 128, nb, granule).transpose(2, 1, 0, 3)
    return np.ascontiguousarray(out.reshape(nb, 128, kc * granule))


def _col_pack(v, nchunk):
    return np.ascontiguousarray(np.asarray(v, np.float32).reshape(nchunk, 128).T)


def _perm_order():
    r = np.arange(512)
    return np.concatenate([2 * r, 2 * r + 1])


def prepare_inputs(inputs):
    f = np.float32
    d = {}
    perm = _perm_order()

    j = np.arange(512, dtype=np.float64)
    inv = 1.0 / np.power(10000.0, 2.0 * j / 1024.0)
    pos = np.arange(T, dtype=np.float64)
    ang = pos[None, :] * inv[:, None]
    def _tbl(a):
        a = a.reshape(4, 128, 4, 512).transpose(1, 2, 0, 3)
        return np.ascontiguousarray(a.reshape(128, 4 * 4 * 512)).astype(f)
    d["cosT"] = _tbl(np.cos(ang))
    d["sinT"] = _tbl(np.sin(ang))

    d["eye128"] = np.eye(128, dtype=f)
    d["eps_col"] = np.broadcast_to(np.array([[EPS_OUT, EPS_IN]], f), (128, 2)).copy()
    d["ones_col"] = np.ones((128, 1), dtype=f)
    d["ones_row"] = np.ones((1, 128), dtype=f)

    ebc = np.zeros((16, CH * 128), dtype=f)
    for ci in range(CH):
        for m in range(128):
            ebc[4 * (ci % 4) + m // 32, 128 * ci + m] = 1.0
    d["Ebc"] = ebc

    Wq = np.asarray(inputs["L0_attn_Wq"], dtype=f)[:, :, perm]
    Wk = np.asarray(inputs["L0_attn_Wk"], dtype=f)[:, :, perm]
    Wv = np.asarray(inputs["L0_attn_Wv"], dtype=f)
    d["wq_t"] = np.stack([_tile_km(Wq[l], 512) for l in range(NL)])
    d["wk_t"] = np.stack([_tile_km(Wk[l], 512) for l in range(NL)])
    d["wv_t"] = np.stack([_tile_km(Wv[l], 512) for l in range(NL)])

    bq = np.asarray(inputs["L0_attn_bq"], dtype=f)[:, perm]
    bk = np.asarray(inputs["L0_attn_bk"], dtype=f)[:, perm]
    d["bq_col"] = np.ascontiguousarray(
        np.stack([_col_pack(bq[l], CH) for l in range(NL)], axis=1).reshape(128, NL * CH))
    d["bk_col"] = np.ascontiguousarray(
        np.stack([_col_pack(bk[l], CH) for l in range(NL)], axis=1).reshape(128, NL * CH))

    bv = np.asarray(inputs["L0_attn_bv"], dtype=f)
    bvr = np.zeros((64, NL * C), dtype=f)
    for l in range(NL):
        bvr[0, C * l:C * (l + 1)] = bv[l]
        bvr[32, C * l:C * (l + 1)] = bv[l]
    d["bv_rows"] = bvr

    d["lnAg_col"] = np.ascontiguousarray(
        np.stack([_col_pack(np.asarray(inputs["L0_attn_lnA_g"], f)[l], CH)
                  for l in range(NL)], axis=1).reshape(128, NL * CH))
    d["lnAb_col"] = np.ascontiguousarray(
        np.stack([_col_pack(np.asarray(inputs["L0_attn_lnA_b"], f)[l], CH)
                  for l in range(NL)], axis=1).reshape(128, NL * CH))
    d["ln0g_col"] = _col_pack(inputs["L0_attn_ln0_g"], CH)
    d["ln0b_col"] = _col_pack(inputs["L0_attn_ln0_b"], CH)
    d["ln2g_col"] = _col_pack(inputs["L0_attn_ln2_g"], CH)
    d["ln2b_col"] = _col_pack(inputs["L0_attn_ln2_b"], CH)

    w1r = np.asarray(inputs["L0_attn_conv1_w"], f).reshape(C, 4, 5)
    w2r = np.asarray(inputs["L0_attn_conv2_w"], f)
    dia1 = np.zeros((CH, 128, 20 * 128), dtype=f)
    dia2 = np.zeros((CH, 128, 20 * 128), dtype=f)
    idx = np.arange(128)
    for c in range(CH):
        for jj in range(4):
            for t5 in range(5):
                col = 128 * (5 * jj + t5)
                dia1[c, idx, col + idx] = w1r[128 * c + idx, jj, t5]
                dia2[c, idx, col + idx] = w2r[128 * c + idx, jj, t5]
    d["conv1d"] = dia1
    d["conv2d"] = dia2
    d["cb1_col"] = np.ascontiguousarray(
        np.asarray(inputs["L0_attn_conv1_b"], f).reshape(CH, 128, 4)
        .transpose(1, 0, 2).reshape(128, CH * 4))
    d["cb2_col"] = _col_pack(inputs["L0_attn_conv2_b"], CH)

    d["W1_t"] = _tile_km(np.asarray(inputs["L0_attn_W1"], f), 512)
    d["W2_t"] = _tile_km(np.asarray(inputs["L0_attn_W2"], f), 1024)[0]
    d["b1_col"] = _col_pack(inputs["L0_attn_b1"], 32)
    d["b2_col"] = _col_pack(inputs["L0_attn_b2"], CH)

    d["f1_t"] = _tile_km(np.ascontiguousarray(np.asarray(inputs["L0_ffn_w1_w"], f).T), 512)
    d["f2_t"] = _tile_km(np.ascontiguousarray(np.asarray(inputs["L0_ffn_w2_w"], f).T), 512)
    d["h1_t"] = _tile_km(np.ascontiguousarray(np.asarray(inputs["L0_ffn_h1_w"], f).T), 128)
    d["h2_t"] = _tile_km(np.ascontiguousarray(np.asarray(inputs["L0_ffn_h2_w"], f).T), 128)
    d["w3_t"] = _tile_km(np.ascontiguousarray(np.asarray(inputs["L0_ffn_w3_w"], f).T), 128)
    d["s1_col"] = _col_pack(inputs["L0_ffn_w1_s"], 32)
    d["s2_col"] = _col_pack(inputs["L0_ffn_w2_s"], 32)
    d["h1s_col"] = _col_pack(inputs["L0_ffn_h1_s"], 32)
    d["h2s_col"] = _col_pack(inputs["L0_ffn_h2_s"], 32)
    d["s3_col"] = _col_pack(inputs["L0_ffn_w3_s"], CH)
    return d



class MK:
    def __init__(self, nc):
        self.nc = nc
        self.p = Prog(nc)
        self._declare_dram()
        self._alloc_top()

    def _declare_dram(self):
        nc = self.nc

        def D(n, sh, dt, kind="ExternalInput"):
            return nc.dram_tensor(n, sh, dt, kind=kind).ap()

        self.d_x = D("x", [NSEQ, C, T], F32)
        self.d_cos = D("cosT", [128, 8192], F32)
        self.d_sin = D("sinT", [128, 8192], F32)
        self.d_eye = D("eye128", [128, 128], F32)
        self.d_eps = D("eps_col", [128, 2], F32)
        self.d_ones_col = D("ones_col", [128, 1], F32)
        self.d_ones_row = D("ones_row", [1, 128], F32)
        self.d_ebc = D("Ebc", [16, CH * 128], F32)
        self.d_wq = D("wq_t", [NL, 2, 128, 4096], F32)
        self.d_wk = D("wk_t", [NL, 2, 128, 4096], F32)
        self.d_wv = D("wv_t", [NL, 2, 128, 4096], F32)
        self.d_bq = D("bq_col", [128, NL * CH], F32)
        self.d_bk = D("bk_col", [128, NL * CH], F32)
        self.d_bvr = D("bv_rows", [64, NL * C], F32)
        self.d_lnAg = D("lnAg_col", [128, NL * CH], F32)
        self.d_lnAb = D("lnAb_col", [128, NL * CH], F32)
        self.d_ln0g = D("ln0g_col", [128, CH], F32)
        self.d_ln0b = D("ln0b_col", [128, CH], F32)
        self.d_ln2g = D("ln2g_col", [128, CH], F32)
        self.d_ln2b = D("ln2b_col", [128, CH], F32)
        self.d_c1d = D("conv1d", [CH, 128, 2560], F32)
        self.d_c2d = D("conv2d", [CH, 128, 2560], F32)
        self.d_cb1 = D("cb1_col", [128, CH * 4], F32)
        self.d_cb2 = D("cb2_col", [128, CH], F32)
        self.d_W1 = D("W1_t", [8, 128, 4096], F32)
        self.d_W2 = D("W2_t", [128, 32768], F32)
        self.d_b1 = D("b1_col", [128, 32], F32)
        self.d_b2 = D("b2_col", [128, CH], F32)
        self.d_f1 = D("f1_t", [8, 128, 4096], F32)
        self.d_f2 = D("f2_t", [8, 128, 4096], F32)
        self.d_h1 = D("h1_t", [32, 128, 4096], F32)
        self.d_h2 = D("h2_t", [32, 128, 4096], F32)
        self.d_w3 = D("w3_t", [8, 128, 4096], F32)
        self.d_s1 = D("s1_col", [128, 32], F32)
        self.d_s2 = D("s2_col", [128, 32], F32)
        self.d_h1s = D("h1s_col", [128, 32], F32)
        self.d_h2s = D("h2s_col", [128, 32], F32)
        self.d_s3 = D("s3_col", [128, CH], F32)
        self.d_y = D("y", [NSEQ, C, T], F32, "ExternalOutput")
        self.d_x2sp = D("x2_spill", [C, T], F32, "Internal")
        self.d_x3sp = D("x3_spill", [C, T], BF16, "Internal")

    def _alloc_top(self):
        nc = self.nc
        A = nc.alloc_sbuf_tensor
        self.ws = [A(f"ws{i}", [128, 4096], BF16) for i in range(6)]
        self.eye = A("eye", [128, 128], BF16)
        self.epsc = A("epsc", [128, 2], F32)
        self.ones_col = A("onescol", [128, 1], BF16)
        self.ones_row = A("onesrow", [1, 128], F32)
        self.ebc = A("ebc", [16, CH * 128], BF16)
        self.bvr = A("bvr", [64, C], BF16)
        self.bq = A("bqc", [128, NL * CH], F32)
        self.bk = A("bkc", [128, NL * CH], F32)
        self.lnAg = A("lnAg", [128, NL * CH], F32)
        self.lnAb = A("lnAb", [128, NL * CH], F32)
        self.ln0g = A("ln0g", [128, CH], F32)
        self.ln0b = A("ln0b", [128, CH], F32)
        self.ln2g = A("ln2g", [128, CH], F32)
        self.ln2b = A("ln2b", [128, CH], F32)
        self.cb1 = A("cb1", [128, CH * 4], F32)
        self.cb2 = A("cb2", [128, CH], F32)
        self.b1c = A("b1c", [128, 32], F32)
        self.b2c = A("b2c", [128, CH], F32)
        self.s1c = A("s1c", [128, 32], F32)
        self.s2c = A("s2c", [128, 32], F32)
        self.h1sc = A("h1sc", [128, 32], F32)
        self.h2sc = A("h2sc", [128, 32], F32)
        self.s3c = A("s3c", [128, CH], F32)
        self.m_row = A("m_row", [1, 512], F32)
        self.r_row = A("r_row", [1, 512], F32)
        self.km_tmp = A("km_tmp", [128, 1], F32)
        self.km_acc = A("km_acc", [128, CH], F32)
        self.km_mat = A("km_mat", [128, CH * 16], BF16)
        self.krs = A("krs", [128, 1024], BF16)
        self.z_bf = A("z_bf", [16, 1024], BF16)
        self.kvb = A("kvb", [128, 16 * 128], BF16)
        P = nc.alloc_psum_tensor
        self.ps = [P(f"ps{i}", [128, 512], F32) for i in range(6)]
        self.pt = P("pt", [128, 512], BF16)
        self.kvp = P("kvp", [128, 512], F32)

    def load_constants(self):
        p = self.p
        p.dma("pool", "consts_c", [
            (self.eye[:, :], self.d_eye),
            (self.ones_col[:, :], self.d_ones_col),
            (self.ebc[:, :], self.d_ebc),
        ], tag="consts_cast")
        p.dma("sp", "consts_p", [
            (self.ones_row[:, :], self.d_ones_row),
            (self.epsc[:, :], self.d_eps),
            (self.bq[:, :], self.d_bq), (self.bk[:, :], self.d_bk),
            (self.lnAg[:, :], self.d_lnAg), (self.lnAb[:, :], self.d_lnAb),
            (self.ln0g[:, :], self.d_ln0g), (self.ln0b[:, :], self.d_ln0b),
            (self.ln2g[:, :], self.d_ln2g), (self.ln2b[:, :], self.d_ln2b),
            (self.cb1[:, :], self.d_cb1), (self.cb2[:, :], self.d_cb2),
            (self.b1c[:, :], self.d_b1), (self.b2c[:, :], self.d_b2),
            (self.s1c[:, :], self.d_s1), (self.s2c[:, :], self.d_s2),
            (self.h1sc[:, :], self.d_h1s), (self.h2sc[:, :], self.d_h2s),
            (self.s3c[:, :], self.d_s3),
        ], tag="consts_plain")

    def emit_ln(self, items, eps, gamma=None, beta=None, dst_bf=None,
                bf_out=False, scratch=None, tag="ln"):
        p = self.p
        ps_st, ps_m, ps_r = self.ps[2], self.ps[3], self.ps[4]
        srcs = [it[0] for it in items]
        bfs = [it[1] for it in items]
        dsts = [it[2] for it in items]

        def mm_sum(t):
            for c in range(CH):
                inst = t.matmul(ps_st[0:1, :], self.ones_col[:, :], bfs[c],
                                start=(c == 0), stop=False, skip_group_check=True)
            return inst
        p.add("pe", mm_sum, reads=bfs + [self.ones_col[:, :]],
              writes=[ps_st[:, :]], tag=tag + "_mmsum")
        for c in range(CH):
            p.add("act", lambda a, c=c: a.activation(bfs[c], bfs[c], AF.Square),
                  reads=[bfs[c]], writes=[bfs[c]], tag=tag + "_sq")

        def mm_sq(t):
            for c in range(CH):
                inst = t.matmul(ps_st[64:65, :], self.ones_col[:, :], bfs[c],
                                start=(c == 0), stop=(c == CH - 1),
                                tile_position=(0, 64), skip_group_check=True)
            return inst
        p.add("pe", mm_sq, reads=bfs + [self.ones_col[:, :]],
              writes=[ps_st[:, :]], tag=tag + "_mmsq")

        p.add("dve", lambda v: v.tensor_scalar_mul(self.m_row[:, :], ps_st[0:1, :], 1.0 / C),
              reads=[ps_st[:, :]], writes=[self.m_row[:, :]], tag=tag + "_m")
        p.add("dve", lambda v: v.tensor_tensor(self.r_row[:, :], self.m_row[:, :],
                                               self.m_row[:, :], ALU.mult),
              reads=[self.m_row[:, :]], writes=[self.r_row[:, :]], tag=tag + "_m2")
        p.add("dve", lambda v: v.scalar_tensor_tensor(self.r_row[:, :], ps_st[64:65, :],
                                                      1.0 / C, self.r_row[:, :],
                                                      ALU.mult, ALU.subtract),
              reads=[ps_st[:, :], self.r_row[:, :]], writes=[self.r_row[:, :]],
              tag=tag + "_var")
        ecol = self.epsc[0:1, 0:1] if eps == EPS_OUT else self.epsc[0:1, 1:2]
        p.add("act", lambda a, ecol=ecol: a.activation(self.r_row[:, :], self.r_row[:, :],
                                                       AF.Sqrt, bias=ecol, scale=1.0),
              reads=[self.r_row[:, :], ecol], writes=[self.r_row[:, :]], tag=tag + "_sd")
        p.add("dve", lambda v: v.reciprocal(self.r_row[:, :], self.r_row[:, :]),
              reads=[self.r_row[:, :]], writes=[self.r_row[:, :]], tag=tag + "_rstd")
        p.add("pe", lambda t: t.matmul(ps_m[:, :], self.ones_row[:, :], self.m_row[:, :],
                                       start=True, stop=True, skip_group_check=True),
              reads=[self.m_row[:, :], self.ones_row[:, :]], writes=[ps_m[:, :]],
              tag=tag + "_bm")
        p.add("pe", lambda t: t.matmul(ps_r[:, :], self.ones_row[:, :], self.r_row[:, :],
                                       start=True, stop=True, skip_group_check=True),
              reads=[self.r_row[:, :], self.ones_row[:, :]], writes=[ps_r[:, :]],
              tag=tag + "_br")

        for c in range(CH):
            g = gamma[c] if gamma is not None else None
            b = beta[c] if beta is not None else None

            if bf_out:
                scr = scratch[c]
                p.add("dve", lambda v, c=c, scr=scr: v.tensor_tensor(
                          scr, srcs[c], ps_m[:, :], ALU.subtract),
                      reads=[srcs[c], ps_m[:, :]], writes=[scr], tag=tag + "_a1")
                p.add("dve", lambda v, c=c, scr=scr: v.scalar_tensor_tensor(
                          dsts[c], scr, 1.0, ps_r[:, :], ALU.mult, ALU.mult),
                      reads=[scr, ps_r[:, :]], writes=[dsts[c]], tag=tag + "_a2")
            else:
                p.add("dve", lambda v, c=c: v.tensor_tensor(
                          dsts[c], srcs[c], ps_m[:, :], ALU.subtract),
                      reads=[srcs[c], ps_m[:, :]], writes=[dsts[c]], tag=tag + "_a1")
                if g is not None:
                    p.add("dve", lambda v, c=c, g=g: v.scalar_tensor_tensor(
                              dsts[c], dsts[c], g, ps_r[:, :], ALU.mult, ALU.mult),
                          reads=[dsts[c], g, ps_r[:, :]], writes=[dsts[c]],
                          tag=tag + "_a2")
                    p.add("act", lambda a, c=c, b=b: a.activation(
                              dsts[c], dsts[c], AF.Identity, bias=b, scale=1.0),
                          reads=[dsts[c], b], writes=[dsts[c]], tag=tag + "_a3")
                else:
                    p.add("dve", lambda v, c=c: v.tensor_tensor(
                              dsts[c], dsts[c], ps_r[:, :], ALU.mult),
                          reads=[dsts[c], ps_r[:, :]], writes=[dsts[c]],
                          tag=tag + "_a2")
                if dst_bf is not None:
                    p.add("act", lambda a, c=c: a.copy(dst_bf[c], dsts[c]),
                          reads=[dsts[c]], writes=[dst_bf[c]], tag=tag + "_a4")


def MM(t, out, lhsT, rhs, start, stop, tp=None):
    return t.matmul(out, lhsT, rhs, start=start, stop=stop,
                    tile_position=tp, skip_group_check=True)



def build_p0(mk, seq, b0, b0d, hbfc):
    p = mk.p
    for c in range(CH):
        p.add("pool", lambda g, c=c: g.memset(b0[c][:, 0:2], 0.0),
              writes=[b0[c][:, 0:2]], tag="padz")
        p.add("pool", lambda g, c=c: g.memset(b0[c][:, 2 + T:PADW], 0.0),
              writes=[b0[c][:, 2 + T:PADW]], tag="padz")
    p.dma("sp", "xin", [(b0d[c], mk.d_x[seq, 128 * c:128 * (c + 1), :])
                        for c in range(CH)], tag="xload")
    for nq in range(4):
        nq0 = 512 * nq
        p.dma("pool", "xbf", [(hbfc[c], mk.d_x[seq, 128 * c:128 * (c + 1), nq0:nq0 + 512])
                              for c in range(CH)], tag="xbfload")
        items = [(b0d[c][:, nq0:nq0 + 512], hbfc[c], b0d[c][:, nq0:nq0 + 512])
                 for c in range(CH)]
        mk.emit_ln(items, EPS_OUT, tag=f"ln1_{seq}_{nq}")


def build_conv(mk, seq, scope, b0, b0d, hbfc):
    nc, p = mk.nc, mk.p
    xbfc = scope.enter_context(nc.sbuf_tensor(f"xbfc_{seq}", [128, 8 * 264], BF16))
    hjb = [scope.enter_context(nc.sbuf_tensor(f"hjb{i}_{seq}", [128, 4 * PADW], BF16))
           for i in range(2)]

    for c in range(CH):
        s1 = mk.ws[c % 2]
        s2 = mk.ws[2 + c % 2]
        p.dma("pool", f"ws{c % 2}", [(s1[:, 0:2560], mk.d_c1d[c, :, :])], tag="c1d")
        p.dma("pool", f"ws{2 + c % 2}", [(s2[:, 0:2560], mk.d_c2d[c, :, :])], tag="c2d")
        hj = hjb[c % 2]
        for jj in range(4):
            p.add("pool", lambda g, jj=jj, hj=hj: g.memset(hj[:, PADW * jj:PADW * jj + 2], 0.0),
                  writes=[hj[:, PADW * jj:PADW * jj + 2]], tag="hjz")
            p.add("pool", lambda g, jj=jj, hj=hj: g.memset(hj[:, PADW * jj + 2 + T:PADW * (jj + 1)], 0.0),
                  writes=[hj[:, PADW * jj + 2 + T:PADW * (jj + 1)]], tag="hjz")
        for tk in range(8):
            w = xbfc[:, 264 * tk:264 * tk + 260]
            p.add("act", lambda a, c=c, tk=tk, w=w: a.copy(
                      w, b0[c][:, 256 * tk:256 * tk + 260]),
                  reads=[b0[c][:, 256 * tk:256 * tk + 260]], writes=[w], tag="xc")
            for jj in range(4):
                psq = mk.ps[(tk * 4 + jj) % 2]

                def mm1(t, s1=s1, jj=jj, tk=tk, psq=psq):
                    for t5 in range(5):
                        inst = MM(t, psq[:, 0:256], s1[:, 128 * (5 * jj + t5):128 * (5 * jj + t5 + 1)],
                                  xbfc[:, 264 * tk + t5:264 * tk + t5 + 256],
                                  start=(t5 == 0), stop=(t5 == 4))
                    return inst
                p.add("pe", mm1, reads=[s1[:, 0:2560], w], writes=[psq[:, :]], tag="c1mm")
                p.add("act", lambda a, hj=hj, jj=jj, tk=tk, psq=psq, c=c: a.activation(
                          hj[:, PADW * jj + 2 + 256 * tk:PADW * jj + 2 + 256 * tk + 256],
                          psq[:, 0:256], AF.Gelu,
                          bias=mk.cb1[:, 4 * c + jj:4 * c + jj + 1], scale=1.0),
                      reads=[psq[:, :], mk.cb1[:, 4 * c + jj:4 * c + jj + 1]],
                      writes=[hj[:, PADW * jj + 2 + 256 * tk:PADW * jj + 2 + 256 * tk + 256]],
                      tag="c1ev")
        for tk in range(8):
            psq = mk.ps[2 + tk % 2]

            def mm2(t, s2=s2, tk=tk, psq=psq, hj=hj):
                first = True
                for jj in range(4):
                    for t5 in range(5):
                        inst = MM(t, psq[:, 0:256], s2[:, 128 * (5 * jj + t5):128 * (5 * jj + t5 + 1)],
                                  hj[:, PADW * jj + 256 * tk + t5:PADW * jj + 256 * tk + t5 + 256],
                                  start=first, stop=(jj == 3 and t5 == 4))
                        first = False
                return inst
            p.add("pe", mm2, reads=[s2[:, 0:2560], hj[:, :]], writes=[psq[:, :]], tag="c2mm")
            sl = slice(256 * tk, 256 * tk + 256)
            p.add("dve", lambda v, c=c, psq=psq, sl=sl: v.scalar_tensor_tensor(
                      b0d[c][:, sl], psq[:, 0:256], mk.cb2[:, c:c + 1], b0d[c][:, sl],
                      ALU.add, ALU.add),
                  reads=[psq[:, :], mk.cb2[:, c:c + 1], b0d[c][:, sl]],
                  writes=[b0d[c][:, sl]], tag="c2ev")
    for nq in range(4):
        nq0 = 512 * nq
        for c in range(CH):
            p.add("act", lambda a, c=c, nq0=nq0: a.copy(
                      hbfc[c], b0d[c][:, nq0:nq0 + 512]),
                  reads=[b0d[c][:, nq0:nq0 + 512]], writes=[hbfc[c]], tag="ln0c")
        items = [(b0d[c][:, nq0:nq0 + 512], hbfc[c], b0d[c][:, nq0:nq0 + 512])
                 for c in range(CH)]
        mk.emit_ln(items, EPS_IN,
                   gamma=[mk.ln0g[:, c:c + 1] for c in range(CH)],
                   beta=[mk.ln0b[:, c:c + 1] for c in range(CH)],
                   tag=f"ln0_{seq}_{nq}")


def build_attn_layer(mk, seq, l, b0d, hbfc2, bufs, upto=None):
    p = mk.p
    cosc, sinc = bufs["cosc"], bufs["sinc"]
    ksc = bufs["ksc"]
    tmp1 = bufs["tmp1"]
    krt, vt = bufs["krt"], bufs["vt"]
    ws = mk.ws

    p.dma("pool", "bvl", [(mk.bvr[:, :], mk.d_bvr[:, C * l:C * (l + 1)])], tag=f"bv{l}")
    for b in range(2):
        p.dma("pool", f"ws{b}", [(ws[b][:, :], mk.d_wk[l, b, :, :])], tag=f"wk{l}")
        p.dma("pool", f"ws{2 + b}", [(ws[2 + b][:, :], mk.d_wv[l, b, :, :])], tag=f"wv{l}")
        p.dma("pool", f"ws{4 + b}", [(ws[4 + b][:, :], mk.d_wq[l, b, :, :])], tag=f"wq{l}")

    def hb(c, par):
        return hbfc2[(par % 2) * CH + c]

    def kscv(par):
        off = (par % 2) * CH * 512
        return ksc[:, off:off + CH * 512]

    def cast_h(nq0, tag, par):
        for c in range(CH):
            p.add("act", lambda a, c=c, nq0=nq0, par=par: a.copy(
                      hb(c, par), b0d[c][:, nq0:nq0 + 512]),
                  reads=[b0d[c][:, nq0:nq0 + 512]], writes=[hb(c, par)], tag=tag)

    def proj(wslots, biascol, dst, tag, par):
        for mc in range(CH):
            wsl = wslots[0] if mc < 4 else wslots[1]
            mloc = mc % 4
            psq = mk.ps[mc % 2]

            def mmp(t, wsl=wsl, mloc=mloc, psq=psq, par=par):
                for kc in range(CH):
                    inst = MM(t, psq[:, :], wsl[:, 512 * kc + 128 * mloc:512 * kc + 128 * mloc + 128],
                              hb(kc, par), start=(kc == 0), stop=(kc == CH - 1))
                return inst
            p.add("pe", mmp, reads=[wsl[:, :]] + [hb(kc, par) for kc in range(CH)],
                  writes=[psq[:, :]], tag=tag + "mm")
            p.add("act", lambda a, mc=mc, psq=psq: a.activation(
                      dst[:, 512 * mc:512 * mc + 512], psq[:, :], AF.Gelu,
                      bias=biascol[:, CH * l + mc:CH * l + mc + 1], scale=1.0),
                  reads=[psq[:, :], biascol[:, CH * l + mc:CH * l + mc + 1]],
                  writes=[dst[:, 512 * mc:512 * mc + 512]], tag=tag + "ev")
            p.add("dve", lambda v, mc=mc: v.tensor_scalar_add(
                      dst[:, 512 * mc:512 * mc + 512], dst[:, 512 * mc:512 * mc + 512],
                      0.21),
                  reads=[dst[:, 512 * mc:512 * mc + 512]],
                  writes=[dst[:, 512 * mc:512 * mc + 512]], tag=tag + "21")

    def rope(buf, nq, tag):
        cw = cosc[nq]
        sw = sinc[nq]
        se = buf[:, 0:2048]
        so = buf[:, 2048:4096]
        te = tmp1[:, 0:2048]
        toh = tmp1[:, 2048:3072]
        p.add("dve", lambda v: v.tensor_tensor(te, so, sw, ALU.mult),
              reads=[so, sw], writes=[te], tag=tag + "1")
        for hf in range(2):
            seh = buf[:, 1024 * hf:1024 * hf + 1024]
            swh = sw[:, 1024 * hf:1024 * hf + 1024]
            p.add("dve", lambda v, seh=seh, swh=swh: v.tensor_tensor(
                      toh, seh, swh, ALU.mult),
                  reads=[seh, swh], writes=[toh], tag=tag + f"2{hf}")
            soh = buf[:, 2048 + 1024 * hf:2048 + 1024 * hf + 1024]
            cwh = cw[:, 1024 * hf:1024 * hf + 1024]
            p.add("dve", lambda v, soh=soh, cwh=cwh: v.tensor_tensor(
                      soh, soh, cwh, ALU.mult),
                  reads=[soh, cwh], writes=[soh], tag=tag + f"5{hf}")
            p.add("pool", lambda g, soh=soh: g.tensor_tensor(
                      soh, soh, toh, ALU.add),
                  reads=[soh, toh], writes=[soh], tag=tag + f"6{hf}")
        p.add("dve", lambda v: v.tensor_tensor(se, se, cw, ALU.mult),
              reads=[se, cw], writes=[se], tag=tag + "3")
        p.add("dve", lambda v: v.tensor_tensor(se, se, te, ALU.subtract),
              reads=[se, te], writes=[se], tag=tag + "4")

    def kv_A(nq):
        nq0 = 512 * nq
        par = nq % 2
        ksq = kscv(par)
        cast_h(nq0, f"a{l}kc", par)
        proj((ws[0], ws[1]), mk.bk, ksq, f"a{l}k", par)
        for c in range(CH):
            p.add("dve", lambda v, c=c, ksq=ksq: v.reduce_sum(
                      mk.km_tmp[:, :], ksq[:, 512 * c:512 * c + 512], axis=AX.X),
                  reads=[ksq[:, 512 * c:512 * c + 512]], writes=[mk.km_tmp[:, :]],
                  tag=f"a{l}km1")
            if nq == 0:
                p.add("dve", lambda v, c=c: v.tensor_copy(
                          out=mk.km_acc[:, c:c + 1], in_=mk.km_tmp[:, :]),
                      reads=[mk.km_tmp[:, :]], writes=[mk.km_acc[:, c:c + 1]],
                      tag=f"a{l}km2")
            else:
                p.add("dve", lambda v, c=c: v.tensor_tensor(
                          mk.km_acc[:, c:c + 1], mk.km_acc[:, c:c + 1],
                          mk.km_tmp[:, :], ALU.add),
                      reads=[mk.km_tmp[:, :], mk.km_acc[:, c:c + 1]],
                      writes=[mk.km_acc[:, c:c + 1]], tag=f"a{l}km2")
        rope(ksq, nq, f"a{l}rk")

    def kv_B(nq):
        nq0 = 512 * nq
        par = nq % 2
        ksq = kscv(par)
        for g in range(4):
            for half in range(2):
                def tr(t, half=half, g=g, ksq=ksq):
                    for pp in range(4):
                        ci = 4 * half + pp
                        inst = t.transpose(mk.pt[:, 128 * pp:128 * pp + 128],
                                           ksq[:, 512 * ci + 128 * g:512 * ci + 128 * g + 128],
                                           mk.eye[:, :])
                    return inst
                trd = [ksq[:, 512 * (4 * half + pp) + 128 * g:512 * (4 * half + pp) + 128 * g + 128]
                       for pp in range(4)]
                p.add("pe", tr, reads=trd + [mk.eye[:, :]], writes=[mk.pt[:, :]],
                      tag=f"a{l}tr")
                p.add("act", lambda a, half=half: a.copy(
                          krt[:, 512 * half:512 * half + 512], mk.pt[:, :]),
                      reads=[mk.pt[:, :]], writes=[krt[:, 512 * half:512 * half + 512]],
                      tag=f"a{l}trev")
            for vh in range(2):
                wsl = ws[2 + vh]
                psq = mk.ps[3 + vh]

                def mmv(t, wsl=wsl, g=g, psq=psq, par=par):
                    for kc in range(CH):
                        inst = MM(t, psq[:, :], hb(kc, par)[:, 128 * g:128 * g + 128],
                                  wsl[:, 512 * kc:512 * kc + 512],
                                  start=(kc == 0), stop=(kc == CH - 1))
                    return inst
                p.add("pe", mmv, reads=[wsl[:, :]] + [hb(kc, par) for kc in range(CH)],
                      writes=[psq[:, :]], tag=f"a{l}vmm")
                p.add("act", lambda a, vh=vh, psq=psq: a.copy(
                          vt[:, 512 * vh:512 * vh + 512], psq[:, :]),
                      reads=[psq[:, :]], writes=[vt[:, 512 * vh:512 * vh + 512]],
                      tag=f"a{l}vev")

            def mmkv(t, nq=nq, g=g):
                for h in range(NH):
                    rb = (32 * h) % 128
                    for half in range(2):
                        inst = MM(t, mk.kvp[rb:rb + 32, (h // 4) * 128 + 64 * half:
                                            (h // 4) * 128 + 64 * half + 64],
                                  krt[:, 512 * half + 32 * h:512 * half + 32 * h + 32],
                                  vt[:, 64 * h:64 * h + 64],
                                  start=(nq == 0 and g == 0 and h < 4 and half == 0),
                                  stop=False, tp=(0, rb))
                return inst
            p.add("pe", mmkv, reads=[krt[:, :], vt[:, :]], writes=[mk.kvp[:, :]],
                  tag=f"a{l}kv")

            def mmks(t, nq=nq, g=g):
                st = (nq == 0 and g == 0)
                MM(t, mk.ps[5][0:1, :], mk.ones_col[:, :], krt[:, 0:512],
                   start=st, stop=False)
                return MM(t, mk.ps[5][32:33, :], mk.ones_col[:, :], krt[:, 512:1024],
                          start=st, stop=False, tp=(0, 32))
            p.add("pe", mmks, reads=[krt[:, :], mk.ones_col[:, :]],
                  writes=[mk.ps[5][:, :]], tag=f"a{l}ks")

    kv_A(0)
    kv_A(1)
    kv_B(0)
    kv_A(2)
    kv_B(1)
    kv_A(3)
    kv_B(2)
    kv_B(3)

    def ksev(a):
        a.copy(mk.krs[0:1, 0:512], mk.ps[5][0:1, :])
        return a.copy(mk.krs[32:33, 0:512], mk.ps[5][32:33, :])
    p.add("act", ksev, reads=[mk.ps[5][:, :]], writes=[mk.krs[:, :]], tag=f"a{l}ksev")
    p.dma("sp", "krsmv", [(mk.krs[0:1, 512:1024], mk.krs[32:33, 0:512])],
          tag=f"a{l}krsmv")

    def mmbv(t):
        for h in range(NH):
            rb = (32 * h) % 128
            for half in range(2):
                inst = MM(t, mk.kvp[rb:rb + 32, (h // 4) * 128 + 64 * half:
                                    (h // 4) * 128 + 64 * half + 64],
                          mk.krs[0:1, 512 * half + 32 * h:512 * half + 32 * h + 32],
                          mk.bvr[0:1, 64 * h:64 * h + 64],
                          start=False, stop=(h == NH - 1 and half == 1),
                          tp=(0, rb))
        return inst
    p.add("pe", mmbv, reads=[mk.krs[:, :], mk.bvr[:, :]], writes=[mk.kvp[:, :]],
          tag=f"a{l}bv")
    p.add("pool", lambda g: g.memset(mk.kvb[:, :], 0.0),
          writes=[mk.kvb[:, :]], tag=f"a{l}kvz")
    for Mp in range(CH):
        for half in range(2):
            base = 256 * Mp + 128 * half
            for hh in range(2):
                h = 2 * Mp + hh
                rb = (32 * h) % 128
                p.add("dve", lambda v, base=base, hh=hh, h=h, rb=rb, half=half:
                      v.tensor_scalar_mul(
                          mk.kvb[rb:rb + 32, base + 64 * hh:base + 64 * hh + 64],
                          mk.kvp[rb:rb + 32, (h // 4) * 128 + 64 * half:
                                 (h // 4) * 128 + 64 * half + 64], 1.0 / T),
                      reads=[mk.kvp[:, :]],
                      writes=[mk.kvb[rb:rb + 32, base + 64 * hh:base + 64 * hh + 64]],
                      tag=f"a{l}kvev")
    p.add("pool", lambda g: g.memset(mk.km_mat[:, :], 0.0),
          writes=[mk.km_mat[:, :]], tag=f"a{l}kmz")
    for c in range(CH):
        cc = c % 4
        for q4 in range(4):
            h = 4 * cc + q4
            p.add("dve", lambda v, c=c, q4=q4, h=h: v.tensor_scalar_mul(
                      mk.km_mat[32 * q4:32 * q4 + 32, 16 * c + h:16 * c + h + 1],
                      mk.km_acc[32 * q4:32 * q4 + 32, c:c + 1], 1.0 / T),
                  reads=[mk.km_acc[:, c:c + 1]],
                  writes=[mk.km_mat[:, 16 * c:16 * c + 16]], tag=f"a{l}kmb")

    def zv(par, bf):
        off = (par % 2) * 512
        return mk.z_bf[:, off:off + 512]

    def q_A(nq):
        nq0 = 512 * nq
        par = nq % 2
        ksq = kscv(par)
        cast_h(nq0, f"a{l}qc", par)
        proj((ws[4], ws[5]), mk.bq, ksq, f"a{l}q", par)

        def mmz(t, ksq=ksq):
            for c in range(CH):
                inst = MM(t, mk.ps[5][0:16, :], mk.km_mat[:, 16 * c:16 * c + 16],
                          ksq[:, 512 * c:512 * c + 512],
                          start=(c == 0), stop=(c == CH - 1))
            return inst
        p.add("pe", mmz, reads=[mk.km_mat[:, :]] + [ksq[:, 512 * c:512 * c + 512] for c in range(CH)],
              writes=[mk.ps[5][:, :]], tag=f"a{l}z")
        zb = zv(par, True)
        p.add("dve", lambda v: v.tensor_scalar_add(mk.ps[5][0:16, :], mk.ps[5][0:16, :], 1e-6),
              reads=[mk.ps[5][:, :]], writes=[mk.ps[5][:, :]], tag=f"a{l}zc1")
        p.add("dve", lambda v: v.reciprocal(mk.ps[5][0:16, :], mk.ps[5][0:16, :]),
              reads=[mk.ps[5][:, :]], writes=[mk.ps[5][:, :]], tag=f"a{l}zc2")
        p.add("dve", lambda v, zb=zb: v.tensor_copy(out=zb, in_=mk.ps[5][0:16, :]),
              reads=[mk.ps[5][:, :]], writes=[zb], tag=f"a{l}zc3")
        rope(ksq, nq, f"a{l}rq")
        for ci in range(CH):
            zbig = mk.ps[2] if ci % 2 == 0 else mk.ps[5]
            p.add("pe", lambda t, ci=ci, zbig=zbig, zb=zb: MM(
                      t, zbig[:, :], mk.ebc[:, 128 * ci:128 * ci + 128], zb,
                      start=True, stop=True),
                  reads=[mk.ebc[:, :], zb], writes=[zbig[:, :]], tag=f"a{l}zbg")
            qsl = ksq[:, 512 * ci:512 * ci + 512]
            p.add("dve", lambda v, qsl=qsl, zbig=zbig: v.tensor_tensor(
                      qsl, qsl, zbig[:, :], ALU.mult),
                  reads=[qsl, zbig[:, :]], writes=[qsl], tag=f"a{l}zap")

    def q_B(nq):
        nq0 = 512 * nq
        par = nq % 2
        ksq = kscv(par)
        for ec in range(CH):
            pso = mk.ps[3 + ec % 2]

            def mmo(t, ec=ec, pso=pso, ksq=ksq):
                for half in range(2):
                    ck = 4 * half + ec // 2
                    inst = MM(t, pso[:, :],
                              mk.kvb[:, 256 * ec + 128 * half:256 * ec + 128 * half + 128],
                              ksq[:, 512 * ck:512 * ck + 512],
                              start=(half == 0), stop=(half == 1))
                return inst
            mmord = [ksq[:, 512 * (4 * half + ec // 2):512 * (4 * half + ec // 2) + 512]
                     for half in range(2)]
            p.add("pe", mmo, reads=[mk.kvb[:, :]] + mmord, writes=[pso[:, :]],
                  tag=f"a{l}o")
            p.add("dve", lambda v, ec=ec, nq0=nq0, pso=pso: v.tensor_tensor(
                      b0d[ec][:, nq0:nq0 + 512], b0d[ec][:, nq0:nq0 + 512],
                      pso[:, :], ALU.add),
                  reads=[pso[:, :], b0d[ec][:, nq0:nq0 + 512]],
                  writes=[b0d[ec][:, nq0:nq0 + 512]], tag=f"a{l}sm2")
        for c in range(CH):
            p.add("act", lambda a, c=c, nq0=nq0, par=par: a.copy(
                      hb(c, par), b0d[c][:, nq0:nq0 + 512]),
                  reads=[b0d[c][:, nq0:nq0 + 512]], writes=[hb(c, par)],
                  tag=f"a{l}lc")
        items = [(b0d[c][:, nq0:nq0 + 512], hb(c, par), b0d[c][:, nq0:nq0 + 512])
                 for c in range(CH)]
        mk.emit_ln(items, EPS_IN,
                   gamma=[mk.lnAg[:, CH * l + c:CH * l + c + 1] for c in range(CH)],
                   beta=[mk.lnAb[:, CH * l + c:CH * l + c + 1] for c in range(CH)],
                   tag=f"lnA{l}_{seq}_{nq}")

    q_A(0)
    q_A(1)
    q_B(0)
    q_A(2)
    q_B(1)
    q_A(3)
    q_B(2)
    q_B(3)
    return False


def build_mlp(mk, seq, scope, b0d, hbfc):
    nc, p = mk.nc, mk.p
    ub = scope.enter_context(nc.sbuf_tensor(f"ub_{seq}", [128, 4 * 512], BF16))
    fsb = scope.enter_context(nc.sbuf_tensor(f"fsb_{seq}", [128, CH * 1024], F32))

    mbf = scope.enter_context(nc.sbuf_tensor(f"mbf_{seq}", [128, CH * 1024], BF16))
    ws = mk.ws

    for nq in range(2):
        nq0 = 1024 * nq
        mbfc = [mbf[:, 1024 * c:1024 * (c + 1)] for c in range(CH)]
        for c in range(CH):
            p.add("act", lambda a, c=c, nq0=nq0: a.copy(
                      mbfc[c], b0d[c][:, nq0:nq0 + 1024]),
                  reads=[b0d[c][:, nq0:nq0 + 1024]], writes=[mbfc[c]], tag="mh")
        for b in range(8):
            wsl1 = ws[b % 2]
            wsl2 = ws[2 + b % 2]
            p.dma("pool", f"ws{b % 2}", [(wsl1[:, :], mk.d_W1[b, :, :])], tag="w1l")
            p.dma("pool", f"ws{2 + b % 2}",
                  [(wsl2[:, :], mk.d_W2[:, 4096 * b:4096 * b + 4096])], tag="w2l")
            for nh in range(2):
                o = 512 * nh
                for msub in range(4):
                    psq = mk.ps[msub % 2]

                    def mm1(t, wsl1=wsl1, msub=msub, psq=psq, o=o):
                        for kc in range(CH):
                            inst = MM(t, psq[:, :],
                                      wsl1[:, 512 * kc + 128 * msub:512 * kc + 128 * msub + 128],
                                      mbfc[kc][:, o:o + 512],
                                      start=(kc == 0), stop=(kc == CH - 1))
                        return inst
                    p.add("pe", mm1,
                          reads=[wsl1[:, :]] + [mbfc[kc][:, o:o + 512] for kc in range(CH)],
                          writes=[psq[:, :]], tag="w1mm")
                    hidx = 4 * b + msub
                    p.add("act", lambda a, msub=msub, psq=psq, hidx=hidx: a.activation(
                              ub[:, 512 * msub:512 * msub + 512], psq[:, :], AF.Gelu,
                              bias=mk.b1c[:, hidx:hidx + 1], scale=1.0),
                          reads=[psq[:, :], mk.b1c[:, hidx:hidx + 1]],
                          writes=[ub[:, 512 * msub:512 * msub + 512]], tag="w1ev")
                for mc in range(CH):
                    psq = mk.ps[4 + (mc % 2)]

                    def mm2(t, wsl2=wsl2, mc=mc, psq=psq):
                        for kci in range(4):
                            inst = MM(t, psq[:, :],
                                      wsl2[:, 1024 * kci + 128 * mc:1024 * kci + 128 * mc + 128],
                                      ub[:, 512 * kci:512 * kci + 512],
                                      start=(kci == 0), stop=(kci == 3))
                        return inst
                    p.add("pe", mm2, reads=[wsl2[:, :], ub[:, :]], writes=[psq[:, :]],
                          tag="w2mm")
                    fdst = fsb[:, 1024 * mc + 512 * nh:1024 * mc + 512 * nh + 512]
                    if b == 0:
                        p.add("act", lambda a, psq=psq, fdst=fdst: a.copy(fdst, psq[:, :]),
                              reads=[psq[:, :]], writes=[fdst], tag="f0")
                    else:
                        p.add("dve", lambda v, psq=psq, fdst=fdst: v.tensor_tensor(
                                  fdst, fdst, psq[:, :], ALU.add),
                              reads=[psq[:, :], fdst], writes=[fdst], tag="facc")
        for nh in range(2):
            o = 512 * nh
            for c in range(CH):
                fdst = fsb[:, 1024 * c + o:1024 * c + o + 512]
                p.add("dve", lambda v, fdst=fdst, c=c, nq0=nq0, o=o: v.scalar_tensor_tensor(
                          fdst, fdst, mk.b2c[:, c:c + 1], b0d[c][:, nq0 + o:nq0 + o + 512],
                          ALU.add, ALU.add),
                      reads=[fdst, mk.b2c[:, c:c + 1], b0d[c][:, nq0 + o:nq0 + o + 512]],
                      writes=[fdst], tag="fb2")
                p.add("act", lambda a, c=c, fdst=fdst: a.copy(hbfc[8 + c], fdst),
                      reads=[fdst], writes=[hbfc[8 + c]], tag="l2c")
            fwin = [fsb[:, 1024 * c + o:1024 * c + o + 512] for c in range(CH)]
            mk.emit_ln([(fwin[c], hbfc[8 + c], fwin[c]) for c in range(CH)], EPS_IN,
                       gamma=[mk.ln2g[:, c:c + 1] for c in range(CH)],
                       beta=[mk.ln2b[:, c:c + 1] for c in range(CH)],
                       tag=f"ln2_{seq}_{nq}_{nh}")
            for c in range(CH):
                p.add("act", lambda a, c=c, fdst=fwin[c]: a.copy(hbfc[8 + c], fdst),
                      reads=[fwin[c]], writes=[hbfc[8 + c]], tag="lpc")
            mk.emit_ln([(fwin[c], hbfc[8 + c], fwin[c]) for c in range(CH)], EPS_OUT,
                       tag=f"lnp_{seq}_{nq}_{nh}")
            p.dma("sp", "xo", [(b0d[c][:, nq0 + o:nq0 + o + 512],
                                mk.d_x[seq, 128 * c:128 * (c + 1), nq0 + o:nq0 + o + 512])
                               for c in range(CH)], tag="xol")
            for c in range(CH):
                p.add("dve", lambda v, c=c, nq0=nq0, o=o, fdst=fwin[c]: v.tensor_tensor(
                          b0d[c][:, nq0 + o:nq0 + o + 512],
                          b0d[c][:, nq0 + o:nq0 + o + 512], fdst, ALU.add),
                      reads=[fwin[c], b0d[c][:, nq0 + o:nq0 + o + 512]],
                      writes=[b0d[c][:, nq0 + o:nq0 + o + 512]], tag="x2")
            for c in range(CH):
                p.add("act", lambda a, c=c, nq0=nq0, o=o: a.copy(
                          hbfc[8 + c], b0d[c][:, nq0 + o:nq0 + o + 512]),
                      reads=[b0d[c][:, nq0 + o:nq0 + o + 512]], writes=[hbfc[8 + c]],
                      tag="l3c")
            x3win = [hbfc[c] for c in range(CH)]
            mk.emit_ln([(b0d[c][:, nq0 + o:nq0 + o + 512], hbfc[8 + c], x3win[c])
                        for c in range(CH)], EPS_OUT,
                       bf_out=True, scratch=fwin, tag=f"ln3_{seq}_{nq}_{nh}")
            p.dma("sp", "x3sp", [(mk.d_x3sp[128 * c:128 * (c + 1), nq0 + o:nq0 + o + 512],
                                  x3win[c]) for c in range(CH)], tag="x3sp")
        p.dma("sp", "x2sp", [(mk.d_x2sp[128 * c:128 * (c + 1), nq0:nq0 + 1024],
                              b0d[c][:, nq0:nq0 + 1024]) for c in range(CH)], tag="x2sp")


def build_ffn(mk, seq, scope):
    nc, p = mk.nc, mk.p
    x3w = scope.enter_context(nc.sbuf_tensor(f"x3w_{seq}", [128, CH * 512], BF16))
    Ab = scope.enter_context(nc.sbuf_tensor(f"A_{seq}", [128, 32 * 512], BF16))
    Bb = scope.enter_context(nc.sbuf_tensor(f"B_{seq}", [128, 32 * 512], BF16))
    sil = scope.enter_context(nc.sbuf_tensor(f"sil_{seq}", [128, 512], F32))
    u2 = scope.enter_context(nc.sbuf_tensor(f"u2_{seq}", [128, 512], F32))
    f3 = scope.enter_context(nc.sbuf_tensor(f"f3_{seq}", [128, CH * 512], F32))
    x2c = scope.enter_context(nc.sbuf_tensor(f"x2c_{seq}", [128, CH * 512], F32))
    ws = mk.ws
    x3wc = [x3w[:, 512 * c:512 * c + 512] for c in range(CH)]

    for tc in range(4):
        t0 = 512 * tc
        src3 = mk.d_x3sp.rearrange("(c p) n -> p c n", p=128)
        p.dma("sp", "x3ld", [(x3w[:, :].rearrange("p (c n) -> p c n", n=512),
                              src3[:, :, t0:t0 + 512])], tag="x3ld")
        for b in range(8):
            p.dma("pool", f"ws{b % 2}", [(ws[b % 2][:, :], mk.d_f1[b, :, :])], tag="f1l")
            p.dma("pool", f"ws{2 + b % 2}", [(ws[2 + b % 2][:, :], mk.d_f2[b, :, :])],
                  tag="f2l")
            for msub in range(4):
                hidx = 4 * b + msub
                psq = mk.ps[msub % 2]

                def mmf1(t, b=b, msub=msub, psq=psq):
                    wsl = ws[b % 2]
                    for kc in range(CH):
                        inst = MM(t, psq[:, :],
                                  wsl[:, 512 * kc + 128 * msub:512 * kc + 128 * msub + 128],
                                  x3wc[kc], start=(kc == 0), stop=(kc == CH - 1))
                    return inst
                p.add("pe", mmf1, reads=[ws[b % 2][:, :]] + x3wc, writes=[psq[:, :]],
                      tag="f1mm")
                p.add("act", lambda a, psq=psq, hidx=hidx: a.activation(
                          sil[:, :], psq[:, :], AF.Silu, bias=0.0,
                          scale=mk.s1c[:, hidx:hidx + 1]),
                      reads=[psq[:, :], mk.s1c[:, hidx:hidx + 1]], writes=[sil[:, :]],
                      tag="f1ev")
                ps2q = mk.ps[2 + msub % 2]

                def mmf2(t, b=b, msub=msub, ps2q=ps2q):
                    wsl = ws[2 + b % 2]
                    for kc in range(CH):
                        inst = MM(t, ps2q[:, :],
                                  wsl[:, 512 * kc + 128 * msub:512 * kc + 128 * msub + 128],
                                  x3wc[kc], start=(kc == 0), stop=(kc == CH - 1))
                    return inst
                p.add("pe", mmf2, reads=[ws[2 + b % 2][:, :]] + x3wc, writes=[ps2q[:, :]],
                      tag="f2mm")
                p.add("act", lambda a, ps2q=ps2q, hidx=hidx: a.activation(
                          u2[:, :], ps2q[:, :], AF.Identity, bias=0.0,
                          scale=mk.s2c[:, hidx:hidx + 1]),
                      reads=[ps2q[:, :], mk.s2c[:, hidx:hidx + 1]], writes=[u2[:, :]],
                      tag="f2ev")
                p.add("dve", lambda v, hidx=hidx: v.tensor_tensor(
                          Ab[:, 512 * hidx:512 * hidx + 512], sil[:, :], u2[:, :],
                          ALU.mult),
                      reads=[sil[:, :], u2[:, :]],
                      writes=[Ab[:, 512 * hidx:512 * hidx + 512]], tag="f12m")
        for g in range(32):
            p.dma("pool", f"ws{4 + g % 2}", [(ws[4 + g % 2][:, :], mk.d_h1[g, :, :])],
                  tag="h1l")
            psq = mk.ps[4 + g % 2]

            def mmh1(t, g=g, psq=psq):
                wsl = ws[4 + g % 2]
                for kc in range(32):
                    inst = MM(t, psq[:, :], wsl[:, 128 * kc:128 * kc + 128],
                              Ab[:, 512 * kc:512 * kc + 512],
                              start=(kc == 0), stop=(kc == 31))
                return inst
            p.add("pe", mmh1, reads=[ws[4 + g % 2][:, :], Ab[:, :]], writes=[psq[:, :]],
                  tag="h1mm")
            p.add("act", lambda a, g=g, psq=psq: a.activation(
                      Bb[:, 512 * g:512 * g + 512], psq[:, :], AF.Gelu, bias=0.0,
                      scale=mk.h1sc[:, g:g + 1]),
                  reads=[psq[:, :], mk.h1sc[:, g:g + 1]],
                  writes=[Bb[:, 512 * g:512 * g + 512]], tag="h1ev")
        for g in range(32):
            p.dma("pool", f"ws{g % 2}", [(ws[g % 2][:, :], mk.d_h2[g, :, :])], tag="h2l")
            psq = mk.ps[g % 2]

            def mmh2(t, g=g, psq=psq):
                wsl = ws[g % 2]
                for kc in range(32):
                    inst = MM(t, psq[:, :], wsl[:, 128 * kc:128 * kc + 128],
                              Bb[:, 512 * kc:512 * kc + 512],
                              start=(kc == 0), stop=(kc == 31))
                return inst
            p.add("pe", mmh2, reads=[ws[g % 2][:, :], Bb[:, :]], writes=[psq[:, :]],
                  tag="h2mm")
            p.add("act", lambda a, g=g, psq=psq: a.activation(
                      Ab[:, 512 * g:512 * g + 512], psq[:, :], AF.Gelu, bias=0.0,
                      scale=mk.h2sc[:, g:g + 1]),
                  reads=[psq[:, :], mk.h2sc[:, g:g + 1]],
                  writes=[Ab[:, 512 * g:512 * g + 512]], tag="h2ev")
        for g in range(CH):
            p.dma("pool", f"ws{2 + g % 2}", [(ws[2 + g % 2][:, :], mk.d_w3[g, :, :])],
                  tag="w3l")
            psq = mk.ps[2 + g % 2]

            def mmw3(t, g=g, psq=psq):
                wsl = ws[2 + g % 2]
                for kc in range(32):
                    inst = MM(t, psq[:, :], wsl[:, 128 * kc:128 * kc + 128],
                              Ab[:, 512 * kc:512 * kc + 512],
                              start=(kc == 0), stop=(kc == 31))
                return inst
            p.add("pe", mmw3, reads=[ws[2 + g % 2][:, :], Ab[:, :]], writes=[psq[:, :]],
                  tag="w3mm")
            p.add("act", lambda a, g=g, psq=psq: a.activation(
                      f3[:, 512 * g:512 * g + 512], psq[:, :], AF.Identity, bias=0.0,
                      scale=mk.s3c[:, g:g + 1]),
                  reads=[psq[:, :], mk.s3c[:, g:g + 1]],
                  writes=[f3[:, 512 * g:512 * g + 512]], tag="w3ev")
        bfw = [Bb[:, 512 * c:512 * c + 512] for c in range(CH)]
        for c in range(CH):
            p.add("act", lambda a, c=c: a.copy(
                      bfw[c], f3[:, 512 * c:512 * c + 512]),
                  reads=[f3[:, 512 * c:512 * c + 512]], writes=[bfw[c]], tag="lfc")
        f3w = [f3[:, 512 * c:512 * c + 512] for c in range(CH)]
        mk.emit_ln([(f3w[c], bfw[c], f3w[c]) for c in range(CH)], EPS_OUT,
                   tag=f"lnf_{seq}_{tc}")
        src2 = mk.d_x2sp.rearrange("(c p) n -> p c n", p=128)
        p.dma("sp", "x2ld", [(x2c[:, :].rearrange("p (c n) -> p c n", n=512),
                              src2[:, :, t0:t0 + 512])], tag="x2ld")
        for c in range(CH):
            p.add("dve", lambda v, c=c: v.tensor_tensor(
                      x2c[:, 512 * c:512 * c + 512], x2c[:, 512 * c:512 * c + 512],
                      f3w[c], ALU.add),
                  reads=[f3w[c], x2c[:, 512 * c:512 * c + 512]],
                  writes=[x2c[:, 512 * c:512 * c + 512]], tag="yadd")
        dsty = mk.d_y[seq, :, t0:t0 + 512].rearrange("(c p) n -> p c n", p=128)
        p.dma("sp", "yout", [(dsty, x2c[:, :].rearrange("p (c n) -> p c n", n=512))],
              tag="yst")



def build_program(stop_after=None):
    from contextlib import ExitStack
    nc = bass.Bass("TRN2", target_bir_lowering=False)
    mk = MK(nc)
    p = mk.p
    mk.load_constants()

    stopped = False
    for seq in range(NSEQ):
        if stopped:
            break
        with ExitStack() as seq_scope:
            stopped = build_seq(mk, seq, seq_scope, stop_after)
        p.barrier(exclude_sems=("consts_c", "consts_p"))

    p.final_wait()
    p.emit()
    return nc


def build_seq(mk, seq, seq_scope, stop_after):
    from contextlib import ExitStack
    nc, p = mk.nc, mk.p

    def dump(chunks):
        pairs = [(mk.d_y[seq, 128 * c:128 * (c + 1), :], chunks[c][:, 0:T])
                 for c in range(CH)]
        p.dma("sp", "yout", pairs, tag="dbg_out")
        return True

    main_scope = ExitStack()
    seq_scope.enter_context(main_scope)
    big0 = main_scope.enter_context(
        nc.sbuf_tensor(f"big0_{seq}", [128, CH * PADW], F32))
    b0 = [big0[:, PADW * c:PADW * (c + 1)] for c in range(CH)]
    b0d = [big0[:, PADW * c + 2:PADW * c + 2 + T] for c in range(CH)]
    hbf = main_scope.enter_context(
        nc.sbuf_tensor(f"hbf_{seq}", [128, 2 * CH * 512], BF16))
    hbfc = [hbf[:, 512 * c:512 * (c + 1)] for c in range(2 * CH)]

    build_p0(mk, seq, b0, b0d, hbfc)
    if stop_after == "ln1" and seq == 0:
        return dump(b0d)

    with ExitStack() as conv_scope:
        build_conv(mk, seq, conv_scope, b0, b0d, hbfc)
    p.barrier(exclude_sems=("consts_c", "consts_p"))
    if stop_after == "conv" and seq == 0:
        return dump(b0d)

    with ExitStack() as attn_scope:
        ae = attn_scope.enter_context
        cosT = ae(nc.sbuf_tensor(f"cosT_{seq}", [128, 8192], BF16))
        sinT = ae(nc.sbuf_tensor(f"sinT_{seq}", [128, 8192], BF16))
        cosc = [cosT[:, 2048 * i:2048 * (i + 1)] for i in range(4)]
        sinc = [sinT[:, 2048 * i:2048 * (i + 1)] for i in range(4)]
        p.dma("pool", "tbl",
              [(cosT[:, :], mk.d_cos[:, :]), (sinT[:, :], mk.d_sin[:, :])],
              tag="tables")
        ksc = ae(nc.sbuf_tensor(f"ksc_{seq}", [128, 2 * CH * 512], BF16))
        tmp1 = ae(nc.sbuf_tensor(f"tmp1_{seq}", [128, 3072], BF16))
        krt = ae(nc.sbuf_tensor(f"krt_{seq}", [128, 1024], BF16))
        vt = ae(nc.sbuf_tensor(f"vt_{seq}", [128, 1024], BF16))
        bufs = dict(cosc=cosc, sinc=sinc, ksc=ksc, tmp1=tmp1,
                    krt=krt, vt=vt)
        for l in range(NL):
            build_attn_layer(mk, seq, l, b0d, hbfc, bufs)
            if stop_after == f"attn{l}" and seq == 0:
                return dump(b0d)
    p.barrier(exclude_sems=("consts_c", "consts_p"))

    with ExitStack() as mlp_scope:
        build_mlp(mk, seq, mlp_scope, b0d, hbfc)
    p.barrier(exclude_sems=("consts_c", "consts_p"))
    if stop_after == "mlp" and seq == 0:
        return dump(b0d)

    main_scope.close()
    p.barrier(exclude_sems=("consts_c", "consts_p"))

    with ExitStack() as ffn_scope:
        build_ffn(mk, seq, ffn_scope)
    return False



def kernel(**inputs):
    prep = prepare_inputs(inputs)
    x = np.asarray(inputs["L0_attn_x"], np.float32)
    nc = build_program(STOP_AFTER)
    in_maps = []
    for core in range(8):
        m = dict(prep)
        m["x"] = np.ascontiguousarray(x[2 * core:2 * core + 2].transpose(0, 2, 1))
        in_maps.append(m)
    res = run_bass_kernel_spmd(nc, in_maps, core_ids=list(range(8)))
    out = np.empty((16, 2048, 1024), np.float32)
    for core in range(8):
        y = np.asarray(res.results[core]["y"])
        out[2 * core:2 * core + 2] = y.transpose(0, 2, 1)
    return out


# version 09
# speedup vs baseline: 1.1392x; 1.0567x over previous
import numpy as np
import ml_dtypes

import concourse.bass as bass
import concourse.mybir as mybir
from concourse.bass_utils import run_bass_kernel_spmd

F32 = mybir.dt.float32
BF16 = mybir.dt.bfloat16
AF = mybir.ActivationFunctionType
ALU = mybir.AluOpType
AX = mybir.AxisListType

C = 1024
CH = 8
T = 2048
NL = 4
NH = 16
NSEQ = 2
PADW = 2052
EPS_OUT = 1e-5
EPS_IN = 1e-12

STOP_AFTER = None



class Prog:
    ENG = {"pe": "tensor", "act": "scalar", "dve": "vector", "pool": "gpsimd",
           "sp": "sync"}

    def __init__(self, nc):
        self.nc = nc
        self.esem = {e: nc.alloc_semaphore("esem_" + e)
                     for e in ("pe", "act", "dve", "pool")}
        self.tasks = {e: [] for e in self.ENG}
        self.ccnt = {e: 0 for e in ("pe", "act", "dve", "pool")}
        self.dsem = {}
        self.res = {}
        self.waited = {e: {} for e in self.ENG}

    def _iv(self, ap):
        t = ap.tensor
        if isinstance(t, bass.PSumTensorHandle):
            return (t.name, 0, 1 << 40)
        off = ap.offset if isinstance(ap.offset, int) else 0
        span = 1
        for step, count in ap.ap:
            span += abs(step) * (count - 1)
        return (t.name, off, off + span)

    def _collect(self, engine, reads, writes):
        deps = {}

        def want(dep):
            kind, key, val = dep
            k = (kind, key)
            if deps.get(k, -1) < val:
                deps[k] = val

        ivs_r = [self._iv(a) for a in reads]
        ivs_w = [self._iv(a) for a in writes]
        for name, lo, hi in ivs_r:
            ent = self.res.get(name)
            if ent:
                for a, b, dep in ent["w"]:
                    if a < hi and lo < b:
                        want(dep)
        for name, lo, hi in ivs_w:
            ent = self.res.get(name)
            if ent:
                for a, b, dep in ent["w"]:
                    if a < hi and lo < b:
                        want(dep)
                for a, b, dep in ent["r"]:
                    if a < hi and lo < b:
                        want(dep)
        return deps, ivs_r, ivs_w

    def _register(self, ivs_r, ivs_w, token):
        for name, lo, hi in ivs_w:
            ent = self.res.setdefault(name, {"w": [], "r": []})
            ent["w"] = [(a, b, d) for (a, b, d) in ent["w"] if not (lo <= a and b <= hi)]
            ent["r"] = [(a, b, d) for (a, b, d) in ent["r"] if not (lo <= a and b <= hi)]
            ent["w"].append((lo, hi, token))
        for name, lo, hi in ivs_r:
            self.res.setdefault(name, {"w": [], "r": []})["r"].append((lo, hi, token))

    def _mk_waits(self, engine, deps):
        waits = []
        for (kind, key), val in sorted(deps.items(), key=lambda kv: str(kv[0])):
            if val <= 0:
                continue
            sem = self.esem[key] if kind == "E" else self.dsem[key][0]
            prev = self.waited[engine].get((kind, key), -1)
            if val > prev:
                waits.append((sem, val))
                self.waited[engine][(kind, key)] = val
        return waits

    def add(self, engine, fn, reads=(), writes=(), tag=""):
        deps, ivs_r, ivs_w = self._collect(engine, reads, writes)
        waits = self._mk_waits(engine, deps)
        self.ccnt[engine] += 1
        token = ("E", engine, self.ccnt[engine])
        self.tasks[engine].append((waits, fn, tag, False))
        self._register(ivs_r, ivs_w, token)
        return token

    def dma(self, queue, semname, pairs, tag=""):
        if semname not in self.dsem:
            self.dsem[semname] = [self.nc.alloc_semaphore("dsem_" + semname), 0]
        ent = self.dsem[semname]
        prev_cum = ent[1]
        reads = [pp[1] for pp in pairs]
        writes = [pp[0] for pp in pairs]
        deps, ivs_r, ivs_w = self._collect(queue, reads, writes)
        if prev_cum > 0 and deps.get(("D", semname), -1) < prev_cum:
            deps[("D", semname)] = prev_cum
        waits = self._mk_waits(queue, deps)
        ent[1] += 16 * len(pairs)
        token = ("D", semname, ent[1])
        sem = ent[0]

        def fn(eng, pairs=pairs, sem=sem):
            for (o, i) in pairs:
                eng.dma_start(out=o, in_=i).then_inc(sem, 16)
            return None

        self.tasks[queue].append((waits, fn, tag, True))
        self._register(ivs_r, ivs_w, token)
        return token

    def barrier(self, exclude_sems=()):
        targets = {("E", e): self.ccnt[e] for e in self.esem if self.ccnt[e] > 0}
        targets.update({("D", n): ent[1] for n, ent in self.dsem.items()
                        if ent[1] > 0 and n not in exclude_sems})
        for e in self.ENG:
            waits = self._mk_waits(e, targets)
            if waits:
                def fn(eng, waits=waits):
                    return None
                self.tasks[e].append((waits, fn, "barrier", True))
        self.res = {}

    def final_wait(self, engine="sp"):
        targets = {("D", n): ent[1] for n, ent in self.dsem.items() if ent[1] > 0}
        targets.update({("E", e): self.ccnt[e] for e in self.esem if self.ccnt[e] > 0})
        waits = self._mk_waits(engine, targets)

        def fn(eng):
            return None
        self.tasks[engine].append((waits, fn, "final", True))

    def emit(self):
        with self.nc.Block() as block:
            for e, attr in self.ENG.items():
                dec = getattr(block, attr)
                tasks = self.tasks[e]
                esem = self.esem.get(e)

                def body(eng, tasks=tasks, esem=esem, ename=e):
                    for (waits, fn, tag, is_dma) in tasks:
                        for (sem, val) in waits:
                            eng.wait_ge(sem, val)
                        last = fn(eng)
                        if not is_dma:
                            assert last is not None, \
                                f"compute task {tag} on {ename} returned None"
                            last.then_inc(esem, 1)
                dec(body)



def _tile_km(w, granule):
    K, M = w.shape
    kc = K // 128
    nb = M // granule
    out = w.reshape(kc, 128, nb, granule).transpose(2, 1, 0, 3)
    return np.ascontiguousarray(out.reshape(nb, 128, kc * granule))


def _col_pack(v, nchunk):
    return np.ascontiguousarray(np.asarray(v, np.float32).reshape(nchunk, 128).T)


def _perm_order():
    r = np.arange(512)
    return np.concatenate([2 * r, 2 * r + 1])


def prepare_inputs(inputs):
    f = np.float32
    d = {}
    perm = _perm_order()

    j = np.arange(512, dtype=np.float64)
    inv = 1.0 / np.power(10000.0, 2.0 * j / 1024.0)
    pos = np.arange(T, dtype=np.float64)
    ang = pos[None, :] * inv[:, None]
    def _tbl(a):
        a = a.reshape(4, 128, 4, 512).transpose(1, 2, 0, 3)
        return np.ascontiguousarray(a.reshape(128, 4 * 4 * 512)).astype(f)
    d["cosT"] = _tbl(np.cos(ang))
    d["sinT"] = _tbl(np.sin(ang))

    d["eye128"] = np.eye(128, dtype=f)
    d["eps_col"] = np.broadcast_to(np.array([[EPS_OUT, EPS_IN]], f), (128, 2)).copy()
    d["ones_col"] = np.ones((128, 1), dtype=f)
    d["ones_row"] = np.ones((1, 128), dtype=f)

    ebc = np.zeros((16, CH * 128), dtype=f)
    for ci in range(CH):
        for m in range(128):
            ebc[4 * (ci % 4) + m // 32, 128 * ci + m] = 1.0
    d["Ebc"] = ebc

    Wq = np.asarray(inputs["L0_attn_Wq"], dtype=f)[:, :, perm]
    Wk = np.asarray(inputs["L0_attn_Wk"], dtype=f)[:, :, perm]
    Wv = np.asarray(inputs["L0_attn_Wv"], dtype=f)
    d["wq_t"] = np.stack([_tile_km(Wq[l], 512) for l in range(NL)])
    d["wk_t"] = np.stack([_tile_km(Wk[l], 512) for l in range(NL)])
    d["wv_t"] = np.stack([_tile_km(Wv[l], 512) for l in range(NL)])

    bq = np.asarray(inputs["L0_attn_bq"], dtype=f)[:, perm]
    bk = np.asarray(inputs["L0_attn_bk"], dtype=f)[:, perm]
    d["bq_col"] = np.ascontiguousarray(
        np.stack([_col_pack(bq[l], CH) for l in range(NL)], axis=1).reshape(128, NL * CH))
    d["bk_col"] = np.ascontiguousarray(
        np.stack([_col_pack(bk[l], CH) for l in range(NL)], axis=1).reshape(128, NL * CH))

    bv = np.asarray(inputs["L0_attn_bv"], dtype=f)
    bvr = np.zeros((64, NL * C), dtype=f)
    for l in range(NL):
        bvr[0, C * l:C * (l + 1)] = bv[l]
        bvr[32, C * l:C * (l + 1)] = bv[l]
    d["bv_rows"] = bvr

    d["lnAg_col"] = np.ascontiguousarray(
        np.stack([_col_pack(np.asarray(inputs["L0_attn_lnA_g"], f)[l], CH)
                  for l in range(NL)], axis=1).reshape(128, NL * CH))
    d["lnAb_col"] = np.ascontiguousarray(
        np.stack([_col_pack(np.asarray(inputs["L0_attn_lnA_b"], f)[l], CH)
                  for l in range(NL)], axis=1).reshape(128, NL * CH))
    d["ln0g_col"] = _col_pack(inputs["L0_attn_ln0_g"], CH)
    d["ln0b_col"] = _col_pack(inputs["L0_attn_ln0_b"], CH)
    d["ln2g_col"] = _col_pack(inputs["L0_attn_ln2_g"], CH)
    d["ln2b_col"] = _col_pack(inputs["L0_attn_ln2_b"], CH)

    w1r = np.asarray(inputs["L0_attn_conv1_w"], f).reshape(C, 4, 5)
    w2r = np.asarray(inputs["L0_attn_conv2_w"], f)
    dia1 = np.zeros((CH, 128, 20 * 128), dtype=f)
    dia2 = np.zeros((CH, 128, 20 * 128), dtype=f)
    idx = np.arange(128)
    for c in range(CH):
        for jj in range(4):
            for t5 in range(5):
                col = 128 * (5 * jj + t5)
                dia1[c, idx, col + idx] = w1r[128 * c + idx, jj, t5]
                dia2[c, idx, col + idx] = w2r[128 * c + idx, jj, t5]
    d["conv1d"] = dia1
    d["conv2d"] = dia2
    d["cb1_col"] = np.ascontiguousarray(
        np.asarray(inputs["L0_attn_conv1_b"], f).reshape(CH, 128, 4)
        .transpose(1, 0, 2).reshape(128, CH * 4))
    d["cb2_col"] = _col_pack(inputs["L0_attn_conv2_b"], CH)

    d["W1_t"] = _tile_km(np.asarray(inputs["L0_attn_W1"], f), 512)
    d["W2_t"] = _tile_km(np.asarray(inputs["L0_attn_W2"], f), 1024)[0]
    d["b1_col"] = _col_pack(inputs["L0_attn_b1"], 32)
    d["b2_col"] = _col_pack(inputs["L0_attn_b2"], CH)

    d["f1_t"] = _tile_km(np.ascontiguousarray(np.asarray(inputs["L0_ffn_w1_w"], f).T), 512)
    d["f2_t"] = _tile_km(np.ascontiguousarray(np.asarray(inputs["L0_ffn_w2_w"], f).T), 512)
    d["h1_t"] = _tile_km(np.ascontiguousarray(np.asarray(inputs["L0_ffn_h1_w"], f).T), 128)
    d["h2_t"] = _tile_km(np.ascontiguousarray(np.asarray(inputs["L0_ffn_h2_w"], f).T), 128)
    d["w3_t"] = _tile_km(np.ascontiguousarray(np.asarray(inputs["L0_ffn_w3_w"], f).T), 128)
    d["s1_col"] = _col_pack(inputs["L0_ffn_w1_s"], 32)
    d["s2_col"] = _col_pack(inputs["L0_ffn_w2_s"], 32)
    d["h1s_col"] = _col_pack(inputs["L0_ffn_h1_s"], 32)
    d["h2s_col"] = _col_pack(inputs["L0_ffn_h2_s"], 32)
    d["s3_col"] = _col_pack(inputs["L0_ffn_w3_s"], CH)
    return d



class MK:
    def __init__(self, nc):
        self.nc = nc
        self.p = Prog(nc)
        self._declare_dram()
        self._alloc_top()

    def _declare_dram(self):
        nc = self.nc

        def D(n, sh, dt, kind="ExternalInput"):
            return nc.dram_tensor(n, sh, dt, kind=kind).ap()

        self.d_x = D("x", [NSEQ, C, T], F32)
        self.d_cos = D("cosT", [128, 8192], F32)
        self.d_sin = D("sinT", [128, 8192], F32)
        self.d_eye = D("eye128", [128, 128], F32)
        self.d_eps = D("eps_col", [128, 2], F32)
        self.d_ones_col = D("ones_col", [128, 1], F32)
        self.d_ones_row = D("ones_row", [1, 128], F32)
        self.d_ebc = D("Ebc", [16, CH * 128], F32)
        self.d_wq = D("wq_t", [NL, 2, 128, 4096], F32)
        self.d_wk = D("wk_t", [NL, 2, 128, 4096], F32)
        self.d_wv = D("wv_t", [NL, 2, 128, 4096], F32)
        self.d_bq = D("bq_col", [128, NL * CH], F32)
        self.d_bk = D("bk_col", [128, NL * CH], F32)
        self.d_bvr = D("bv_rows", [64, NL * C], F32)
        self.d_lnAg = D("lnAg_col", [128, NL * CH], F32)
        self.d_lnAb = D("lnAb_col", [128, NL * CH], F32)
        self.d_ln0g = D("ln0g_col", [128, CH], F32)
        self.d_ln0b = D("ln0b_col", [128, CH], F32)
        self.d_ln2g = D("ln2g_col", [128, CH], F32)
        self.d_ln2b = D("ln2b_col", [128, CH], F32)
        self.d_c1d = D("conv1d", [CH, 128, 2560], F32)
        self.d_c2d = D("conv2d", [CH, 128, 2560], F32)
        self.d_cb1 = D("cb1_col", [128, CH * 4], F32)
        self.d_cb2 = D("cb2_col", [128, CH], F32)
        self.d_W1 = D("W1_t", [8, 128, 4096], F32)
        self.d_W2 = D("W2_t", [128, 32768], F32)
        self.d_b1 = D("b1_col", [128, 32], F32)
        self.d_b2 = D("b2_col", [128, CH], F32)
        self.d_f1 = D("f1_t", [8, 128, 4096], F32)
        self.d_f2 = D("f2_t", [8, 128, 4096], F32)
        self.d_h1 = D("h1_t", [32, 128, 4096], F32)
        self.d_h2 = D("h2_t", [32, 128, 4096], F32)
        self.d_w3 = D("w3_t", [8, 128, 4096], F32)
        self.d_s1 = D("s1_col", [128, 32], F32)
        self.d_s2 = D("s2_col", [128, 32], F32)
        self.d_h1s = D("h1s_col", [128, 32], F32)
        self.d_h2s = D("h2s_col", [128, 32], F32)
        self.d_s3 = D("s3_col", [128, CH], F32)
        self.d_y = D("y", [NSEQ, C, T], F32, "ExternalOutput")
        self.d_x2sp = D("x2_spill", [C, T], F32, "Internal")
        self.d_x3sp = D("x3_spill", [C, T], BF16, "Internal")

    def _alloc_top(self):
        nc = self.nc
        A = nc.alloc_sbuf_tensor
        self.ws = [A(f"ws{i}", [128, 4096], BF16) for i in range(6)]
        self.eye = A("eye", [128, 128], BF16)
        self.epsc = A("epsc", [128, 2], F32)
        self.ones_col = A("onescol", [128, 1], BF16)
        self.ones_row = A("onesrow", [1, 128], F32)
        self.ebc = A("ebc", [16, CH * 128], BF16)
        self.bvr = A("bvr", [64, C], BF16)
        self.bq = A("bqc", [128, NL * CH], F32)
        self.bk = A("bkc", [128, NL * CH], F32)
        self.lnAg = A("lnAg", [128, NL * CH], F32)
        self.lnAb = A("lnAb", [128, NL * CH], F32)
        self.ln0g = A("ln0g", [128, CH], F32)
        self.ln0b = A("ln0b", [128, CH], F32)
        self.ln2g = A("ln2g", [128, CH], F32)
        self.ln2b = A("ln2b", [128, CH], F32)
        self.cb1 = A("cb1", [128, CH * 4], F32)
        self.cb2 = A("cb2", [128, CH], F32)
        self.b1c = A("b1c", [128, 32], F32)
        self.b2c = A("b2c", [128, CH], F32)
        self.s1c = A("s1c", [128, 32], F32)
        self.s2c = A("s2c", [128, 32], F32)
        self.h1sc = A("h1sc", [128, 32], F32)
        self.h2sc = A("h2sc", [128, 32], F32)
        self.s3c = A("s3c", [128, CH], F32)
        self.m_row = A("m_row", [1, 512], F32)
        self.r_row = A("r_row", [1, 512], F32)
        self.km_tmp = A("km_tmp", [128, 1], F32)
        self.km_acc = A("km_acc", [128, CH], F32)
        self.km_mat = A("km_mat", [128, CH * 16], BF16)
        self.krs = A("krs", [128, 1024], BF16)
        self.z_bf = A("z_bf", [16, 1024], BF16)
        self.kvb = A("kvb", [128, 16 * 128], BF16)
        P = nc.alloc_psum_tensor
        self.ps = [P(f"ps{i}", [128, 512], F32) for i in range(6)]
        self.pt = P("pt", [128, 512], BF16)
        self.kvp = P("kvp", [128, 512], F32)

    def load_constants(self):
        p = self.p
        p.dma("pool", "consts_c", [
            (self.eye[:, :], self.d_eye),
            (self.ones_col[:, :], self.d_ones_col),
            (self.ebc[:, :], self.d_ebc),
        ], tag="consts_cast")
        p.dma("sp", "consts_p", [
            (self.ones_row[:, :], self.d_ones_row),
            (self.epsc[:, :], self.d_eps),
            (self.bq[:, :], self.d_bq), (self.bk[:, :], self.d_bk),
            (self.lnAg[:, :], self.d_lnAg), (self.lnAb[:, :], self.d_lnAb),
            (self.ln0g[:, :], self.d_ln0g), (self.ln0b[:, :], self.d_ln0b),
            (self.ln2g[:, :], self.d_ln2g), (self.ln2b[:, :], self.d_ln2b),
            (self.cb1[:, :], self.d_cb1), (self.cb2[:, :], self.d_cb2),
            (self.b1c[:, :], self.d_b1), (self.b2c[:, :], self.d_b2),
            (self.s1c[:, :], self.d_s1), (self.s2c[:, :], self.d_s2),
            (self.h1sc[:, :], self.d_h1s), (self.h2sc[:, :], self.d_h2s),
            (self.s3c[:, :], self.d_s3),
        ], tag="consts_plain")

    def emit_ln(self, src3, bf3, bfs, dst3, eps, gamma8=None, beta8=None,
                dst_bf3=None, bf_out=False, scratch3=None, cast=True,
                tag="ln"):
        p = self.p
        ps_st, ps_m, ps_r = self.ps[2], self.ps[3], self.ps[4]
        if cast:
            p.add("act", lambda a: a.copy(bf3, src3),
                  reads=[src3], writes=[bf3], tag=tag + "_c")

        def mm_sum(t):
            for c in range(CH):
                inst = t.matmul(ps_st[0:1, :], self.ones_col[:, :], bfs[c],
                                start=(c == 0), stop=False, skip_group_check=True)
            return inst
        p.add("pe", mm_sum, reads=[bf3, self.ones_col[:, :]],
              writes=[ps_st[:, :]], tag=tag + "_mmsum")
        p.add("dve", lambda v: v.tensor_tensor(bf3, bf3, bf3, ALU.mult),
              reads=[bf3], writes=[bf3], tag=tag + "_sq")

        def mm_sq(t):
            for c in range(CH):
                inst = t.matmul(ps_st[64:65, :], self.ones_col[:, :], bfs[c],
                                start=(c == 0), stop=(c == CH - 1),
                                tile_position=(0, 64), skip_group_check=True)
            return inst
        p.add("pe", mm_sq, reads=[bf3, self.ones_col[:, :]],
              writes=[ps_st[:, :]], tag=tag + "_mmsq")

        p.add("dve", lambda v: v.tensor_scalar_mul(self.m_row[:, :], ps_st[0:1, :], 1.0 / C),
              reads=[ps_st[:, :]], writes=[self.m_row[:, :]], tag=tag + "_m")
        p.add("dve", lambda v: v.tensor_tensor(self.r_row[:, :], self.m_row[:, :],
                                               self.m_row[:, :], ALU.mult),
              reads=[self.m_row[:, :]], writes=[self.r_row[:, :]], tag=tag + "_m2")
        p.add("dve", lambda v: v.scalar_tensor_tensor(self.r_row[:, :], ps_st[64:65, :],
                                                      1.0 / C, self.r_row[:, :],
                                                      ALU.mult, ALU.subtract),
              reads=[ps_st[:, :], self.r_row[:, :]], writes=[self.r_row[:, :]],
              tag=tag + "_var")
        ecol = self.epsc[0:1, 0:1] if eps == EPS_OUT else self.epsc[0:1, 1:2]
        p.add("act", lambda a, ecol=ecol: a.activation(self.r_row[:, :], self.r_row[:, :],
                                                       AF.Sqrt, bias=ecol, scale=1.0),
              reads=[self.r_row[:, :], ecol], writes=[self.r_row[:, :]], tag=tag + "_sd")
        p.add("dve", lambda v: v.reciprocal(self.r_row[:, :], self.r_row[:, :]),
              reads=[self.r_row[:, :]], writes=[self.r_row[:, :]], tag=tag + "_rstd")
        p.add("pe", lambda t: t.matmul(ps_m[:, :], self.ones_row[:, :], self.m_row[:, :],
                                       start=True, stop=True, skip_group_check=True),
              reads=[self.m_row[:, :], self.ones_row[:, :]], writes=[ps_m[:, :]],
              tag=tag + "_bm")
        p.add("pe", lambda t: t.matmul(ps_r[:, :], self.ones_row[:, :], self.r_row[:, :],
                                       start=True, stop=True, skip_group_check=True),
              reads=[self.r_row[:, :], self.ones_row[:, :]], writes=[ps_r[:, :]],
              tag=tag + "_br")

        mb3 = ps_m[:, :].rearrange("p (c n) -> p c n", c=1).broadcast_to([128, CH, 512])
        rb3 = ps_r[:, :].rearrange("p (c n) -> p c n", c=1).broadcast_to([128, CH, 512])
        if bf_out:
            p.add("dve", lambda v: v.tensor_tensor(scratch3, src3, mb3, ALU.subtract),
                  reads=[src3, ps_m[:, :]], writes=[scratch3], tag=tag + "_a1")
            p.add("dve", lambda v: v.tensor_tensor(dst3, scratch3, rb3, ALU.mult),
                  reads=[scratch3, ps_r[:, :]], writes=[dst3], tag=tag + "_a2")
        else:
            p.add("dve", lambda v: v.tensor_tensor(dst3, src3, mb3, ALU.subtract),
                  reads=[src3, ps_m[:, :]], writes=[dst3], tag=tag + "_a1")
            p.add("dve", lambda v: v.tensor_tensor(dst3, dst3, rb3, ALU.mult),
                  reads=[dst3, ps_r[:, :]], writes=[dst3], tag=tag + "_a2")
            if gamma8 is not None:
                g3 = gamma8.rearrange("p (c n) -> p c n", n=1).broadcast_to([128, CH, 512])
                b3 = beta8.rearrange("p (c n) -> p c n", n=1).broadcast_to([128, CH, 512])
                p.add("pool", lambda g: g.tensor_tensor(dst3, dst3, g3, ALU.mult),
                      reads=[dst3, gamma8], writes=[dst3], tag=tag + "_a2g")
                p.add("dve", lambda v: v.tensor_tensor(dst3, dst3, b3, ALU.add),
                      reads=[dst3, beta8], writes=[dst3], tag=tag + "_a3")
            if dst_bf3 is not None:
                p.add("act", lambda a: a.copy(dst_bf3, dst3),
                      reads=[dst3], writes=[dst_bf3], tag=tag + "_a4")


def MM(t, out, lhsT, rhs, start, stop, tp=None):
    return t.matmul(out, lhsT, rhs, start=start, stop=stop,
                    tile_position=tp, skip_group_check=True)



def build_p0(mk, seq, b0, b0d, b0w, hbf3, hbfc):
    p = mk.p
    for c in range(CH):
        p.add("pool", lambda g, c=c: g.memset(b0[c][:, 0:2], 0.0),
              writes=[b0[c][:, 0:2]], tag="padz")
        p.add("pool", lambda g, c=c: g.memset(b0[c][:, 2 + T:PADW], 0.0),
              writes=[b0[c][:, 2 + T:PADW]], tag="padz")
    p.dma("sp", "xin", [(b0d[c], mk.d_x[seq, 128 * c:128 * (c + 1), :])
                        for c in range(CH)], tag="xload")
    for nq in range(4):
        nq0 = 512 * nq
        p.dma("pool", "xbf", [(hbfc[c], mk.d_x[seq, 128 * c:128 * (c + 1), nq0:nq0 + 512])
                              for c in range(CH)], tag="xbfload")
        mk.emit_ln(b0w(nq0), hbf3(0), hbfc[0:CH], b0w(nq0), EPS_OUT,
                   cast=False, tag=f"ln1_{seq}_{nq}")


def build_conv(mk, seq, scope, b0, b0d, b0w, hbf3, hbfc):
    nc, p = mk.nc, mk.p
    xbfc = scope.enter_context(nc.sbuf_tensor(f"xbfc_{seq}", [128, 8 * 264], BF16))
    hjb = [scope.enter_context(nc.sbuf_tensor(f"hjb{i}_{seq}", [128, 4 * PADW], BF16))
           for i in range(2)]

    for c in range(CH):
        s1 = mk.ws[c % 2]
        s2 = mk.ws[2 + c % 2]
        p.dma("pool", f"ws{c % 2}", [(s1[:, 0:2560], mk.d_c1d[c, :, :])], tag="c1d")
        p.dma("pool", f"ws{2 + c % 2}", [(s2[:, 0:2560], mk.d_c2d[c, :, :])], tag="c2d")
        hj = hjb[c % 2]
        for jj in range(4):
            p.add("pool", lambda g, jj=jj, hj=hj: g.memset(hj[:, PADW * jj:PADW * jj + 2], 0.0),
                  writes=[hj[:, PADW * jj:PADW * jj + 2]], tag="hjz")
            p.add("pool", lambda g, jj=jj, hj=hj: g.memset(hj[:, PADW * jj + 2 + T:PADW * (jj + 1)], 0.0),
                  writes=[hj[:, PADW * jj + 2 + T:PADW * (jj + 1)]], tag="hjz")
        for tk in range(8):
            w = xbfc[:, 264 * tk:264 * tk + 260]
            p.add("act", lambda a, c=c, tk=tk, w=w: a.copy(
                      w, b0[c][:, 256 * tk:256 * tk + 260]),
                  reads=[b0[c][:, 256 * tk:256 * tk + 260]], writes=[w], tag="xc")
            for jj in range(4):
                psq = mk.ps[(tk * 4 + jj) % 2]

                def mm1(t, s1=s1, jj=jj, tk=tk, psq=psq):
                    for t5 in range(5):
                        inst = MM(t, psq[:, 0:256], s1[:, 128 * (5 * jj + t5):128 * (5 * jj + t5 + 1)],
                                  xbfc[:, 264 * tk + t5:264 * tk + t5 + 256],
                                  start=(t5 == 0), stop=(t5 == 4))
                    return inst
                p.add("pe", mm1, reads=[s1[:, 0:2560], w], writes=[psq[:, :]], tag="c1mm")
                p.add("act", lambda a, hj=hj, jj=jj, tk=tk, psq=psq, c=c: a.activation(
                          hj[:, PADW * jj + 2 + 256 * tk:PADW * jj + 2 + 256 * tk + 256],
                          psq[:, 0:256], AF.Gelu,
                          bias=mk.cb1[:, 4 * c + jj:4 * c + jj + 1], scale=1.0),
                      reads=[psq[:, :], mk.cb1[:, 4 * c + jj:4 * c + jj + 1]],
                      writes=[hj[:, PADW * jj + 2 + 256 * tk:PADW * jj + 2 + 256 * tk + 256]],
                      tag="c1ev")
        for tk in range(8):
            psq = mk.ps[2 + tk % 2]

            def mm2(t, s2=s2, tk=tk, psq=psq, hj=hj):
                first = True
                for jj in range(4):
                    for t5 in range(5):
                        inst = MM(t, psq[:, 0:256], s2[:, 128 * (5 * jj + t5):128 * (5 * jj + t5 + 1)],
                                  hj[:, PADW * jj + 256 * tk + t5:PADW * jj + 256 * tk + t5 + 256],
                                  start=first, stop=(jj == 3 and t5 == 4))
                        first = False
                return inst
            p.add("pe", mm2, reads=[s2[:, 0:2560], hj[:, :]], writes=[psq[:, :]], tag="c2mm")
            sl = slice(256 * tk, 256 * tk + 256)
            p.add("dve", lambda v, c=c, psq=psq, sl=sl: v.scalar_tensor_tensor(
                      b0d[c][:, sl], psq[:, 0:256], mk.cb2[:, c:c + 1], b0d[c][:, sl],
                      ALU.add, ALU.add),
                  reads=[psq[:, :], mk.cb2[:, c:c + 1], b0d[c][:, sl]],
                  writes=[b0d[c][:, sl]], tag="c2ev")
    for nq in range(4):
        nq0 = 512 * nq
        mk.emit_ln(b0w(nq0), hbf3(0), hbfc[0:CH], b0w(nq0), EPS_IN,
                   gamma8=mk.ln0g[:, :], beta8=mk.ln0b[:, :],
                   tag=f"ln0_{seq}_{nq}")


def build_attn_layer(mk, seq, l, b0d, b0w, hbf3, hbfc2, bufs, upto=None):
    p = mk.p
    cosc, sinc = bufs["cosc"], bufs["sinc"]
    ksc = bufs["ksc"]
    tmp1 = bufs["tmp1"]
    krt, vt = bufs["krt"], bufs["vt"]
    ws = mk.ws

    p.dma("pool", "bvl", [(mk.bvr[:, :], mk.d_bvr[:, C * l:C * (l + 1)])], tag=f"bv{l}")
    for b in range(2):
        p.dma("pool", f"ws{b}", [(ws[b][:, :], mk.d_wk[l, b, :, :])], tag=f"wk{l}")
        p.dma("pool", f"ws{2 + b}", [(ws[2 + b][:, :], mk.d_wv[l, b, :, :])], tag=f"wv{l}")
        p.dma("pool", f"ws{4 + b}", [(ws[4 + b][:, :], mk.d_wq[l, b, :, :])], tag=f"wq{l}")

    def hb(c, par):
        return hbfc2[(par % 2) * CH + c]

    def kscv(par):
        off = (par % 2) * CH * 512
        return ksc[:, off:off + CH * 512]

    def cast_h(nq0, tag, par):
        p.add("act", lambda a, nq0=nq0, par=par: a.copy(hbf3(par), b0w(nq0)),
              reads=[b0w(nq0)], writes=[hbf3(par)], tag=tag)

    def proj(wslots, biascol, dst, tag, par):
        for mc in range(CH):
            wsl = wslots[0] if mc < 4 else wslots[1]
            mloc = mc % 4
            psq = mk.ps[mc % 2]

            def mmp(t, wsl=wsl, mloc=mloc, psq=psq, par=par):
                for kc in range(CH):
                    inst = MM(t, psq[:, :], wsl[:, 512 * kc + 128 * mloc:512 * kc + 128 * mloc + 128],
                              hb(kc, par), start=(kc == 0), stop=(kc == CH - 1))
                return inst
            p.add("pe", mmp, reads=[wsl[:, :]] + [hb(kc, par) for kc in range(CH)],
                  writes=[psq[:, :]], tag=tag + "mm")
            p.add("act", lambda a, mc=mc, psq=psq: a.activation(
                      dst[:, 512 * mc:512 * mc + 512], psq[:, :], AF.Gelu,
                      bias=biascol[:, CH * l + mc:CH * l + mc + 1], scale=1.0),
                  reads=[psq[:, :], biascol[:, CH * l + mc:CH * l + mc + 1]],
                  writes=[dst[:, 512 * mc:512 * mc + 512]], tag=tag + "ev")
        p.add("dve", lambda v: v.tensor_scalar_add(
                  dst[:, 0:CH * 512], dst[:, 0:CH * 512], 0.21),
              reads=[dst[:, 0:CH * 512]], writes=[dst[:, 0:CH * 512]], tag=tag + "21")

    def rope(buf, nq, tag):
        cw = cosc[nq]
        sw = sinc[nq]
        se = buf[:, 0:2048]
        so = buf[:, 2048:4096]
        te = tmp1[:, 0:2048]
        toh = tmp1[:, 2048:3072]
        p.add("dve", lambda v: v.tensor_tensor(te, so, sw, ALU.mult),
              reads=[so, sw], writes=[te], tag=tag + "1")
        for hf in range(2):
            seh = buf[:, 1024 * hf:1024 * hf + 1024]
            swh = sw[:, 1024 * hf:1024 * hf + 1024]
            p.add("dve", lambda v, seh=seh, swh=swh: v.tensor_tensor(
                      toh, seh, swh, ALU.mult),
                  reads=[seh, swh], writes=[toh], tag=tag + f"2{hf}")
            soh = buf[:, 2048 + 1024 * hf:2048 + 1024 * hf + 1024]
            cwh = cw[:, 1024 * hf:1024 * hf + 1024]
            p.add("dve", lambda v, soh=soh, cwh=cwh: v.tensor_tensor(
                      soh, soh, cwh, ALU.mult),
                  reads=[soh, cwh], writes=[soh], tag=tag + f"5{hf}")
            p.add("pool", lambda g, soh=soh: g.tensor_tensor(
                      soh, soh, toh, ALU.add),
                  reads=[soh, toh], writes=[soh], tag=tag + f"6{hf}")
        p.add("dve", lambda v: v.tensor_tensor(se, se, cw, ALU.mult),
              reads=[se, cw], writes=[se], tag=tag + "3")
        p.add("dve", lambda v: v.tensor_tensor(se, se, te, ALU.subtract),
              reads=[se, te], writes=[se], tag=tag + "4")

    def kv_A(nq):
        nq0 = 512 * nq
        par = nq % 2
        ksq = kscv(par)
        cast_h(nq0, f"a{l}kc", par)
        proj((ws[0], ws[1]), mk.bk, ksq, f"a{l}k", par)
        for c in range(CH):
            p.add("dve", lambda v, c=c, ksq=ksq: v.reduce_sum(
                      mk.km_tmp[:, :], ksq[:, 512 * c:512 * c + 512], axis=AX.X),
                  reads=[ksq[:, 512 * c:512 * c + 512]], writes=[mk.km_tmp[:, :]],
                  tag=f"a{l}km1")
            if nq == 0:
                p.add("dve", lambda v, c=c: v.tensor_copy(
                          out=mk.km_acc[:, c:c + 1], in_=mk.km_tmp[:, :]),
                      reads=[mk.km_tmp[:, :]], writes=[mk.km_acc[:, c:c + 1]],
                      tag=f"a{l}km2")
            else:
                p.add("dve", lambda v, c=c: v.tensor_tensor(
                          mk.km_acc[:, c:c + 1], mk.km_acc[:, c:c + 1],
                          mk.km_tmp[:, :], ALU.add),
                      reads=[mk.km_tmp[:, :], mk.km_acc[:, c:c + 1]],
                      writes=[mk.km_acc[:, c:c + 1]], tag=f"a{l}km2")
        rope(ksq, nq, f"a{l}rk")

    def kv_B(nq):
        nq0 = 512 * nq
        par = nq % 2
        ksq = kscv(par)
        for g in range(4):
            for half in range(2):
                def tr(t, half=half, g=g, ksq=ksq):
                    for pp in range(4):
                        ci = 4 * half + pp
                        inst = t.transpose(mk.pt[:, 128 * pp:128 * pp + 128],
                                           ksq[:, 512 * ci + 128 * g:512 * ci + 128 * g + 128],
                                           mk.eye[:, :])
                    return inst
                trd = [ksq[:, 512 * (4 * half + pp) + 128 * g:512 * (4 * half + pp) + 128 * g + 128]
                       for pp in range(4)]
                p.add("pe", tr, reads=trd + [mk.eye[:, :]], writes=[mk.pt[:, :]],
                      tag=f"a{l}tr")
                p.add("act", lambda a, half=half: a.copy(
                          krt[:, 512 * half:512 * half + 512], mk.pt[:, :]),
                      reads=[mk.pt[:, :]], writes=[krt[:, 512 * half:512 * half + 512]],
                      tag=f"a{l}trev")
            for vh in range(2):
                wsl = ws[2 + vh]
                psq = mk.ps[3 + vh]

                def mmv(t, wsl=wsl, g=g, psq=psq, par=par):
                    for kc in range(CH):
                        inst = MM(t, psq[:, :], hb(kc, par)[:, 128 * g:128 * g + 128],
                                  wsl[:, 512 * kc:512 * kc + 512],
                                  start=(kc == 0), stop=(kc == CH - 1))
                    return inst
                p.add("pe", mmv, reads=[wsl[:, :]] + [hb(kc, par) for kc in range(CH)],
                      writes=[psq[:, :]], tag=f"a{l}vmm")
                p.add("act", lambda a, vh=vh, psq=psq: a.copy(
                          vt[:, 512 * vh:512 * vh + 512], psq[:, :]),
                      reads=[psq[:, :]], writes=[vt[:, 512 * vh:512 * vh + 512]],
                      tag=f"a{l}vev")

            def mmkv(t, nq=nq, g=g):
                for h in range(NH):
                    rb = (32 * h) % 128
                    for half in range(2):
                        inst = MM(t, mk.kvp[rb:rb + 32, (h // 4) * 128 + 64 * half:
                                            (h // 4) * 128 + 64 * half + 64],
                                  krt[:, 512 * half + 32 * h:512 * half + 32 * h + 32],
                                  vt[:, 64 * h:64 * h + 64],
                                  start=(nq == 0 and g == 0 and h < 4 and half == 0),
                                  stop=False, tp=(0, rb))
                return inst
            p.add("pe", mmkv, reads=[krt[:, :], vt[:, :]], writes=[mk.kvp[:, :]],
                  tag=f"a{l}kv")

            def mmks(t, nq=nq, g=g):
                st = (nq == 0 and g == 0)
                MM(t, mk.ps[5][0:1, :], mk.ones_col[:, :], krt[:, 0:512],
                   start=st, stop=False)
                return MM(t, mk.ps[5][32:33, :], mk.ones_col[:, :], krt[:, 512:1024],
                          start=st, stop=False, tp=(0, 32))
            p.add("pe", mmks, reads=[krt[:, :], mk.ones_col[:, :]],
                  writes=[mk.ps[5][:, :]], tag=f"a{l}ks")

    kv_A(0)
    kv_A(1)
    kv_B(0)
    kv_A(2)
    kv_B(1)
    kv_A(3)
    kv_B(2)
    kv_B(3)

    def ksev(a):
        a.copy(mk.krs[0:1, 0:512], mk.ps[5][0:1, :])
        return a.copy(mk.krs[32:33, 0:512], mk.ps[5][32:33, :])
    p.add("act", ksev, reads=[mk.ps[5][:, :]], writes=[mk.krs[:, :]], tag=f"a{l}ksev")
    p.dma("sp", "krsmv", [(mk.krs[0:1, 512:1024], mk.krs[32:33, 0:512])],
          tag=f"a{l}krsmv")

    def mmbv(t):
        for h in range(NH):
            rb = (32 * h) % 128
            for half in range(2):
                inst = MM(t, mk.kvp[rb:rb + 32, (h // 4) * 128 + 64 * half:
                                    (h // 4) * 128 + 64 * half + 64],
                          mk.krs[0:1, 512 * half + 32 * h:512 * half + 32 * h + 32],
                          mk.bvr[0:1, 64 * h:64 * h + 64],
                          start=False, stop=(h == NH - 1 and half == 1),
                          tp=(0, rb))
        return inst
    p.add("pe", mmbv, reads=[mk.krs[:, :], mk.bvr[:, :]], writes=[mk.kvp[:, :]],
          tag=f"a{l}bv")
    p.add("pool", lambda g: g.memset(mk.kvb[:, :], 0.0),
          writes=[mk.kvb[:, :]], tag=f"a{l}kvz")
    for Mp in range(CH):
        for half in range(2):
            base = 256 * Mp + 128 * half
            for hh in range(2):
                h = 2 * Mp + hh
                rb = (32 * h) % 128
                p.add("dve", lambda v, base=base, hh=hh, h=h, rb=rb, half=half:
                      v.tensor_scalar_mul(
                          mk.kvb[rb:rb + 32, base + 64 * hh:base + 64 * hh + 64],
                          mk.kvp[rb:rb + 32, (h // 4) * 128 + 64 * half:
                                 (h // 4) * 128 + 64 * half + 64], 1.0 / T),
                      reads=[mk.kvp[:, :]],
                      writes=[mk.kvb[rb:rb + 32, base + 64 * hh:base + 64 * hh + 64]],
                      tag=f"a{l}kvev")
    p.add("pool", lambda g: g.memset(mk.km_mat[:, :], 0.0),
          writes=[mk.km_mat[:, :]], tag=f"a{l}kmz")
    for c in range(CH):
        cc = c % 4
        for q4 in range(4):
            h = 4 * cc + q4
            p.add("dve", lambda v, c=c, q4=q4, h=h: v.tensor_scalar_mul(
                      mk.km_mat[32 * q4:32 * q4 + 32, 16 * c + h:16 * c + h + 1],
                      mk.km_acc[32 * q4:32 * q4 + 32, c:c + 1], 1.0 / T),
                  reads=[mk.km_acc[:, c:c + 1]],
                  writes=[mk.km_mat[:, 16 * c:16 * c + 16]], tag=f"a{l}kmb")

    def zv(par, bf):
        off = (par % 2) * 512
        return mk.z_bf[:, off:off + 512]

    def q_A(nq):
        nq0 = 512 * nq
        par = nq % 2
        ksq = kscv(par)
        cast_h(nq0, f"a{l}qc", par)
        proj((ws[4], ws[5]), mk.bq, ksq, f"a{l}q", par)

        def mmz(t, ksq=ksq):
            for c in range(CH):
                inst = MM(t, mk.ps[5][0:16, :], mk.km_mat[:, 16 * c:16 * c + 16],
                          ksq[:, 512 * c:512 * c + 512],
                          start=(c == 0), stop=(c == CH - 1))
            return inst
        p.add("pe", mmz, reads=[mk.km_mat[:, :]] + [ksq[:, 512 * c:512 * c + 512] for c in range(CH)],
              writes=[mk.ps[5][:, :]], tag=f"a{l}z")
        zb = zv(par, True)
        p.add("dve", lambda v: v.tensor_scalar_add(mk.ps[5][0:16, :], mk.ps[5][0:16, :], 1e-6),
              reads=[mk.ps[5][:, :]], writes=[mk.ps[5][:, :]], tag=f"a{l}zc1")
        p.add("dve", lambda v: v.reciprocal(mk.ps[5][0:16, :], mk.ps[5][0:16, :]),
              reads=[mk.ps[5][:, :]], writes=[mk.ps[5][:, :]], tag=f"a{l}zc2")
        p.add("dve", lambda v, zb=zb: v.tensor_copy(out=zb, in_=mk.ps[5][0:16, :]),
              reads=[mk.ps[5][:, :]], writes=[zb], tag=f"a{l}zc3")
        rope(ksq, nq, f"a{l}rq")
        for ci in range(CH):
            zbig = mk.ps[2] if ci % 2 == 0 else mk.ps[5]
            p.add("pe", lambda t, ci=ci, zbig=zbig, zb=zb: MM(
                      t, zbig[:, :], mk.ebc[:, 128 * ci:128 * ci + 128], zb,
                      start=True, stop=True),
                  reads=[mk.ebc[:, :], zb], writes=[zbig[:, :]], tag=f"a{l}zbg")
            qsl = ksq[:, 512 * ci:512 * ci + 512]
            p.add("dve", lambda v, qsl=qsl, zbig=zbig: v.tensor_tensor(
                      qsl, qsl, zbig[:, :], ALU.mult),
                  reads=[qsl, zbig[:, :]], writes=[qsl], tag=f"a{l}zap")

    def q_B(nq):
        nq0 = 512 * nq
        par = nq % 2
        ksq = kscv(par)
        for ec in range(CH):
            pso = mk.ps[3 + ec % 2]

            def mmo(t, ec=ec, pso=pso, ksq=ksq):
                for half in range(2):
                    ck = 4 * half + ec // 2
                    inst = MM(t, pso[:, :],
                              mk.kvb[:, 256 * ec + 128 * half:256 * ec + 128 * half + 128],
                              ksq[:, 512 * ck:512 * ck + 512],
                              start=(half == 0), stop=(half == 1))
                return inst
            mmord = [ksq[:, 512 * (4 * half + ec // 2):512 * (4 * half + ec // 2) + 512]
                     for half in range(2)]
            p.add("pe", mmo, reads=[mk.kvb[:, :]] + mmord, writes=[pso[:, :]],
                  tag=f"a{l}o")
            p.add("dve", lambda v, ec=ec, nq0=nq0, pso=pso: v.tensor_tensor(
                      b0d[ec][:, nq0:nq0 + 512], b0d[ec][:, nq0:nq0 + 512],
                      pso[:, :], ALU.add),
                  reads=[pso[:, :], b0d[ec][:, nq0:nq0 + 512]],
                  writes=[b0d[ec][:, nq0:nq0 + 512]], tag=f"a{l}sm2")
        mk.emit_ln(b0w(nq0), hbf3(par), [hb(c, par) for c in range(CH)],
                   b0w(nq0), EPS_IN,
                   gamma8=mk.lnAg[:, CH * l:CH * l + CH],
                   beta8=mk.lnAb[:, CH * l:CH * l + CH],
                   tag=f"lnA{l}_{seq}_{nq}")

    q_A(0)
    q_A(1)
    q_B(0)
    q_A(2)
    q_B(1)
    q_A(3)
    q_B(2)
    q_B(3)
    return False


def build_mlp(mk, seq, scope, b0d, b0w, hbf3, hbfc):
    nc, p = mk.nc, mk.p
    ub = scope.enter_context(nc.sbuf_tensor(f"ub_{seq}", [128, 4 * 512], BF16))
    fsb = scope.enter_context(nc.sbuf_tensor(f"fsb_{seq}", [128, CH * 1024], F32))

    mbf = scope.enter_context(nc.sbuf_tensor(f"mbf_{seq}", [128, CH * 1024], BF16))
    ws = mk.ws

    for nq in range(2):
        nq0 = 1024 * nq
        mbfc = [mbf[:, 1024 * c:1024 * (c + 1)] for c in range(CH)]
        mbf3 = mbf[:, :].rearrange("p (c n) -> p c n", n=1024)
        p.add("act", lambda a, nq0=nq0: a.copy(mbf3, b0w(nq0, 1024)),
              reads=[b0w(nq0, 1024)], writes=[mbf3], tag="mh")
        for b in range(8):
            wsl1 = ws[b % 2]
            wsl2 = ws[2 + b % 2]
            p.dma("pool", f"ws{b % 2}", [(wsl1[:, :], mk.d_W1[b, :, :])], tag="w1l")
            p.dma("pool", f"ws{2 + b % 2}",
                  [(wsl2[:, :], mk.d_W2[:, 4096 * b:4096 * b + 4096])], tag="w2l")
            for nh in range(2):
                o = 512 * nh
                for msub in range(4):
                    psq = mk.ps[msub % 2]

                    def mm1(t, wsl1=wsl1, msub=msub, psq=psq, o=o):
                        for kc in range(CH):
                            inst = MM(t, psq[:, :],
                                      wsl1[:, 512 * kc + 128 * msub:512 * kc + 128 * msub + 128],
                                      mbfc[kc][:, o:o + 512],
                                      start=(kc == 0), stop=(kc == CH - 1))
                        return inst
                    p.add("pe", mm1,
                          reads=[wsl1[:, :]] + [mbfc[kc][:, o:o + 512] for kc in range(CH)],
                          writes=[psq[:, :]], tag="w1mm")
                    hidx = 4 * b + msub
                    p.add("act", lambda a, msub=msub, psq=psq, hidx=hidx: a.activation(
                              ub[:, 512 * msub:512 * msub + 512], psq[:, :], AF.Gelu,
                              bias=mk.b1c[:, hidx:hidx + 1], scale=1.0),
                          reads=[psq[:, :], mk.b1c[:, hidx:hidx + 1]],
                          writes=[ub[:, 512 * msub:512 * msub + 512]], tag="w1ev")
                for mc in range(CH):
                    psq = mk.ps[4 + (mc % 2)]

                    def mm2(t, wsl2=wsl2, mc=mc, psq=psq):
                        for kci in range(4):
                            inst = MM(t, psq[:, :],
                                      wsl2[:, 1024 * kci + 128 * mc:1024 * kci + 128 * mc + 128],
                                      ub[:, 512 * kci:512 * kci + 512],
                                      start=(kci == 0), stop=(kci == 3))
                        return inst
                    p.add("pe", mm2, reads=[wsl2[:, :], ub[:, :]], writes=[psq[:, :]],
                          tag="w2mm")
                    fdst = fsb[:, 1024 * mc + 512 * nh:1024 * mc + 512 * nh + 512]
                    if b == 0:
                        p.add("act", lambda a, psq=psq, fdst=fdst: a.copy(fdst, psq[:, :]),
                              reads=[psq[:, :]], writes=[fdst], tag="f0")
                    else:
                        p.add("dve", lambda v, psq=psq, fdst=fdst: v.tensor_tensor(
                                  fdst, fdst, psq[:, :], ALU.add),
                              reads=[psq[:, :], fdst], writes=[fdst], tag="facc")
        for nh in range(2):
            o = 512 * nh
            fsb_3 = fsb[:, :].rearrange("p (c n) -> p c n", n=1024)
            fw3 = fsb_3[:, :, o:o + 512]
            for c in range(CH):
                fdst = fsb[:, 1024 * c + o:1024 * c + o + 512]
                p.add("dve", lambda v, fdst=fdst, c=c, nq0=nq0, o=o: v.scalar_tensor_tensor(
                          fdst, fdst, mk.b2c[:, c:c + 1], b0d[c][:, nq0 + o:nq0 + o + 512],
                          ALU.add, ALU.add),
                      reads=[fdst, mk.b2c[:, c:c + 1], b0d[c][:, nq0 + o:nq0 + o + 512]],
                      writes=[fdst], tag="fb2")
            fwin = [fsb[:, 1024 * c + o:1024 * c + o + 512] for c in range(CH)]
            mk.emit_ln(fw3, hbf3(1), hbfc[8:16], fw3, EPS_IN,
                       gamma8=mk.ln2g[:, :], beta8=mk.ln2b[:, :],
                       tag=f"ln2_{seq}_{nq}_{nh}")
            mk.emit_ln(fw3, hbf3(1), hbfc[8:16], fw3, EPS_OUT,
                       tag=f"lnp_{seq}_{nq}_{nh}")
            p.dma("sp", "xo", [(b0d[c][:, nq0 + o:nq0 + o + 512],
                                mk.d_x[seq, 128 * c:128 * (c + 1), nq0 + o:nq0 + o + 512])
                               for c in range(CH)], tag="xol")
            for c in range(CH):
                p.add("dve", lambda v, c=c, nq0=nq0, o=o, fdst=fwin[c]: v.tensor_tensor(
                          b0d[c][:, nq0 + o:nq0 + o + 512],
                          b0d[c][:, nq0 + o:nq0 + o + 512], fdst, ALU.add),
                      reads=[fwin[c], b0d[c][:, nq0 + o:nq0 + o + 512]],
                      writes=[b0d[c][:, nq0 + o:nq0 + o + 512]], tag="x2")
            x3win = [hbfc[c] for c in range(CH)]
            mk.emit_ln(b0w(nq0 + o), hbf3(1), hbfc[8:16], hbf3(0), EPS_OUT,
                       bf_out=True, scratch3=fw3, tag=f"ln3_{seq}_{nq}_{nh}")
            p.dma("sp", "x3sp", [(mk.d_x3sp[128 * c:128 * (c + 1), nq0 + o:nq0 + o + 512],
                                  x3win[c]) for c in range(CH)], tag="x3sp")
        p.dma("sp", "x2sp", [(mk.d_x2sp[128 * c:128 * (c + 1), nq0:nq0 + 1024],
                              b0d[c][:, nq0:nq0 + 1024]) for c in range(CH)], tag="x2sp")


def build_ffn(mk, seq, scope):
    nc, p = mk.nc, mk.p
    x3w = scope.enter_context(nc.sbuf_tensor(f"x3w_{seq}", [128, CH * 512], BF16))
    Ab = scope.enter_context(nc.sbuf_tensor(f"A_{seq}", [128, 32 * 512], BF16))
    Bb = scope.enter_context(nc.sbuf_tensor(f"B_{seq}", [128, 32 * 512], BF16))
    sil = scope.enter_context(nc.sbuf_tensor(f"sil_{seq}", [128, 512], F32))
    u2 = scope.enter_context(nc.sbuf_tensor(f"u2_{seq}", [128, 512], F32))
    f3 = scope.enter_context(nc.sbuf_tensor(f"f3_{seq}", [128, CH * 512], F32))
    x2c = scope.enter_context(nc.sbuf_tensor(f"x2c_{seq}", [128, CH * 512], F32))
    ws = mk.ws
    x3wc = [x3w[:, 512 * c:512 * c + 512] for c in range(CH)]

    for tc in range(4):
        t0 = 512 * tc
        src3 = mk.d_x3sp.rearrange("(c p) n -> p c n", p=128)
        p.dma("sp", "x3ld", [(x3w[:, :].rearrange("p (c n) -> p c n", n=512),
                              src3[:, :, t0:t0 + 512])], tag="x3ld")
        for b in range(8):
            p.dma("pool", f"ws{b % 2}", [(ws[b % 2][:, :], mk.d_f1[b, :, :])], tag="f1l")
            p.dma("pool", f"ws{2 + b % 2}", [(ws[2 + b % 2][:, :], mk.d_f2[b, :, :])],
                  tag="f2l")
            for msub in range(4):
                hidx = 4 * b + msub
                psq = mk.ps[msub % 2]

                def mmf1(t, b=b, msub=msub, psq=psq):
                    wsl = ws[b % 2]
                    for kc in range(CH):
                        inst = MM(t, psq[:, :],
                                  wsl[:, 512 * kc + 128 * msub:512 * kc + 128 * msub + 128],
                                  x3wc[kc], start=(kc == 0), stop=(kc == CH - 1))
                    return inst
                p.add("pe", mmf1, reads=[ws[b % 2][:, :]] + x3wc, writes=[psq[:, :]],
                      tag="f1mm")
                p.add("act", lambda a, psq=psq, hidx=hidx: a.activation(
                          sil[:, :], psq[:, :], AF.Silu, bias=0.0,
                          scale=mk.s1c[:, hidx:hidx + 1]),
                      reads=[psq[:, :], mk.s1c[:, hidx:hidx + 1]], writes=[sil[:, :]],
                      tag="f1ev")
                ps2q = mk.ps[2 + msub % 2]

                def mmf2(t, b=b, msub=msub, ps2q=ps2q):
                    wsl = ws[2 + b % 2]
                    for kc in range(CH):
                        inst = MM(t, ps2q[:, :],
                                  wsl[:, 512 * kc + 128 * msub:512 * kc + 128 * msub + 128],
                                  x3wc[kc], start=(kc == 0), stop=(kc == CH - 1))
                    return inst
                p.add("pe", mmf2, reads=[ws[2 + b % 2][:, :]] + x3wc, writes=[ps2q[:, :]],
                      tag="f2mm")
                p.add("act", lambda a, ps2q=ps2q, hidx=hidx: a.activation(
                          u2[:, :], ps2q[:, :], AF.Identity, bias=0.0,
                          scale=mk.s2c[:, hidx:hidx + 1]),
                      reads=[ps2q[:, :], mk.s2c[:, hidx:hidx + 1]], writes=[u2[:, :]],
                      tag="f2ev")
                p.add("dve", lambda v, hidx=hidx: v.tensor_tensor(
                          Ab[:, 512 * hidx:512 * hidx + 512], sil[:, :], u2[:, :],
                          ALU.mult),
                      reads=[sil[:, :], u2[:, :]],
                      writes=[Ab[:, 512 * hidx:512 * hidx + 512]], tag="f12m")
        for g in range(32):
            p.dma("pool", f"ws{4 + g % 2}", [(ws[4 + g % 2][:, :], mk.d_h1[g, :, :])],
                  tag="h1l")
            psq = mk.ps[4 + g % 2]

            def mmh1(t, g=g, psq=psq):
                wsl = ws[4 + g % 2]
                for kc in range(32):
                    inst = MM(t, psq[:, :], wsl[:, 128 * kc:128 * kc + 128],
                              Ab[:, 512 * kc:512 * kc + 512],
                              start=(kc == 0), stop=(kc == 31))
                return inst
            p.add("pe", mmh1, reads=[ws[4 + g % 2][:, :], Ab[:, :]], writes=[psq[:, :]],
                  tag="h1mm")
            p.add("act", lambda a, g=g, psq=psq: a.activation(
                      Bb[:, 512 * g:512 * g + 512], psq[:, :], AF.Gelu, bias=0.0,
                      scale=mk.h1sc[:, g:g + 1]),
                  reads=[psq[:, :], mk.h1sc[:, g:g + 1]],
                  writes=[Bb[:, 512 * g:512 * g + 512]], tag="h1ev")
        for g in range(32):
            p.dma("pool", f"ws{g % 2}", [(ws[g % 2][:, :], mk.d_h2[g, :, :])], tag="h2l")
            psq = mk.ps[g % 2]

            def mmh2(t, g=g, psq=psq):
                wsl = ws[g % 2]
                for kc in range(32):
                    inst = MM(t, psq[:, :], wsl[:, 128 * kc:128 * kc + 128],
                              Bb[:, 512 * kc:512 * kc + 512],
                              start=(kc == 0), stop=(kc == 31))
                return inst
            p.add("pe", mmh2, reads=[ws[g % 2][:, :], Bb[:, :]], writes=[psq[:, :]],
                  tag="h2mm")
            p.add("act", lambda a, g=g, psq=psq: a.activation(
                      Ab[:, 512 * g:512 * g + 512], psq[:, :], AF.Gelu, bias=0.0,
                      scale=mk.h2sc[:, g:g + 1]),
                  reads=[psq[:, :], mk.h2sc[:, g:g + 1]],
                  writes=[Ab[:, 512 * g:512 * g + 512]], tag="h2ev")
        for g in range(CH):
            p.dma("pool", f"ws{2 + g % 2}", [(ws[2 + g % 2][:, :], mk.d_w3[g, :, :])],
                  tag="w3l")
            psq = mk.ps[2 + g % 2]

            def mmw3(t, g=g, psq=psq):
                wsl = ws[2 + g % 2]
                for kc in range(32):
                    inst = MM(t, psq[:, :], wsl[:, 128 * kc:128 * kc + 128],
                              Ab[:, 512 * kc:512 * kc + 512],
                              start=(kc == 0), stop=(kc == 31))
                return inst
            p.add("pe", mmw3, reads=[ws[2 + g % 2][:, :], Ab[:, :]], writes=[psq[:, :]],
                  tag="w3mm")
            p.add("act", lambda a, g=g, psq=psq: a.activation(
                      f3[:, 512 * g:512 * g + 512], psq[:, :], AF.Identity, bias=0.0,
                      scale=mk.s3c[:, g:g + 1]),
                  reads=[psq[:, :], mk.s3c[:, g:g + 1]],
                  writes=[f3[:, 512 * g:512 * g + 512]], tag="w3ev")
        bfw = [Bb[:, 512 * c:512 * c + 512] for c in range(CH)]
        f3_3 = f3[:, :].rearrange("p (c n) -> p c n", n=512)
        bf3w = Bb[:, 0:CH * 512].rearrange("p (c n) -> p c n", n=512)
        f3w = [f3[:, 512 * c:512 * c + 512] for c in range(CH)]
        mk.emit_ln(f3_3, bf3w, bfw, f3_3, EPS_OUT, tag=f"lnf_{seq}_{tc}")
        src2 = mk.d_x2sp.rearrange("(c p) n -> p c n", p=128)
        p.dma("sp", "x2ld", [(x2c[:, :].rearrange("p (c n) -> p c n", n=512),
                              src2[:, :, t0:t0 + 512])], tag="x2ld")
        for c in range(CH):
            p.add("dve", lambda v, c=c: v.tensor_tensor(
                      x2c[:, 512 * c:512 * c + 512], x2c[:, 512 * c:512 * c + 512],
                      f3w[c], ALU.add),
                  reads=[f3w[c], x2c[:, 512 * c:512 * c + 512]],
                  writes=[x2c[:, 512 * c:512 * c + 512]], tag="yadd")
        dsty = mk.d_y[seq, :, t0:t0 + 512].rearrange("(c p) n -> p c n", p=128)
        p.dma("sp", "yout", [(dsty, x2c[:, :].rearrange("p (c n) -> p c n", n=512))],
              tag="yst")



def build_program(stop_after=None):
    from contextlib import ExitStack
    nc = bass.Bass("TRN2", target_bir_lowering=False)
    mk = MK(nc)
    p = mk.p
    mk.load_constants()

    stopped = False
    for seq in range(NSEQ):
        if stopped:
            break
        with ExitStack() as seq_scope:
            stopped = build_seq(mk, seq, seq_scope, stop_after)
        p.barrier(exclude_sems=("consts_c", "consts_p"))

    p.final_wait()
    p.emit()
    return nc


def build_seq(mk, seq, seq_scope, stop_after):
    from contextlib import ExitStack
    nc, p = mk.nc, mk.p

    def dump(chunks):
        pairs = [(mk.d_y[seq, 128 * c:128 * (c + 1), :], chunks[c][:, 0:T])
                 for c in range(CH)]
        p.dma("sp", "yout", pairs, tag="dbg_out")
        return True

    main_scope = ExitStack()
    seq_scope.enter_context(main_scope)
    big0 = main_scope.enter_context(
        nc.sbuf_tensor(f"big0_{seq}", [128, CH * PADW], F32))
    b0 = [big0[:, PADW * c:PADW * (c + 1)] for c in range(CH)]
    b0d = [big0[:, PADW * c + 2:PADW * c + 2 + T] for c in range(CH)]
    big0_3 = big0[:, :].rearrange("p (c n) -> p c n", n=PADW)

    def b0w(nq0, w=512):
        return big0_3[:, :, 2 + nq0:2 + nq0 + w]
    hbf = main_scope.enter_context(
        nc.sbuf_tensor(f"hbf_{seq}", [128, 2 * CH * 512], BF16))
    hbfc = [hbf[:, 512 * c:512 * (c + 1)] for c in range(2 * CH)]
    hbf3a = hbf[:, 0:CH * 512].rearrange("p (c n) -> p c n", n=512)
    hbf3b = hbf[:, CH * 512:2 * CH * 512].rearrange("p (c n) -> p c n", n=512)

    def hbf3(par):
        return hbf3a if par % 2 == 0 else hbf3b

    build_p0(mk, seq, b0, b0d, b0w, hbf3, hbfc)
    if stop_after == "ln1" and seq == 0:
        return dump(b0d)

    with ExitStack() as conv_scope:
        build_conv(mk, seq, conv_scope, b0, b0d, b0w, hbf3, hbfc)
    p.barrier(exclude_sems=("consts_c", "consts_p"))
    if stop_after == "conv" and seq == 0:
        return dump(b0d)

    with ExitStack() as attn_scope:
        ae = attn_scope.enter_context
        cosT = ae(nc.sbuf_tensor(f"cosT_{seq}", [128, 8192], BF16))
        sinT = ae(nc.sbuf_tensor(f"sinT_{seq}", [128, 8192], BF16))
        cosc = [cosT[:, 2048 * i:2048 * (i + 1)] for i in range(4)]
        sinc = [sinT[:, 2048 * i:2048 * (i + 1)] for i in range(4)]
        p.dma("pool", "tbl",
              [(cosT[:, :], mk.d_cos[:, :]), (sinT[:, :], mk.d_sin[:, :])],
              tag="tables")
        ksc = ae(nc.sbuf_tensor(f"ksc_{seq}", [128, 2 * CH * 512], BF16))
        tmp1 = ae(nc.sbuf_tensor(f"tmp1_{seq}", [128, 3072], BF16))
        krt = ae(nc.sbuf_tensor(f"krt_{seq}", [128, 1024], BF16))
        vt = ae(nc.sbuf_tensor(f"vt_{seq}", [128, 1024], BF16))
        bufs = dict(cosc=cosc, sinc=sinc, ksc=ksc, tmp1=tmp1,
                    krt=krt, vt=vt)
        for l in range(NL):
            build_attn_layer(mk, seq, l, b0d, b0w, hbf3, hbfc, bufs)
            if stop_after == f"attn{l}" and seq == 0:
                return dump(b0d)
    p.barrier(exclude_sems=("consts_c", "consts_p"))

    with ExitStack() as mlp_scope:
        build_mlp(mk, seq, mlp_scope, b0d, b0w, hbf3, hbfc)
    p.barrier(exclude_sems=("consts_c", "consts_p"))
    if stop_after == "mlp" and seq == 0:
        return dump(b0d)

    main_scope.close()
    p.barrier(exclude_sems=("consts_c", "consts_p"))

    with ExitStack() as ffn_scope:
        build_ffn(mk, seq, ffn_scope)
    return False



def kernel(**inputs):
    prep = prepare_inputs(inputs)
    x = np.asarray(inputs["L0_attn_x"], np.float32)
    nc = build_program(STOP_AFTER)
    in_maps = []
    for core in range(8):
        m = dict(prep)
        m["x"] = np.ascontiguousarray(x[2 * core:2 * core + 2].transpose(0, 2, 1))
        in_maps.append(m)
    res = run_bass_kernel_spmd(nc, in_maps, core_ids=list(range(8)))
    out = np.empty((16, 2048, 1024), np.float32)
    for core in range(8):
        y = np.asarray(res.results[core]["y"])
        out[2 * core:2 * core + 2] = y.transpose(0, 2, 1)
    return out


# version 10
# speedup vs baseline: 1.1499x; 1.0095x over previous
import numpy as np
import ml_dtypes

import concourse.bass as bass
import concourse.mybir as mybir
from concourse.bass_utils import run_bass_kernel_spmd

F32 = mybir.dt.float32
BF16 = mybir.dt.bfloat16
AF = mybir.ActivationFunctionType
ALU = mybir.AluOpType
AX = mybir.AxisListType

C = 1024
CH = 8
T = 2048
NL = 4
NH = 16
NSEQ = 2
PADW = 2052
EPS_OUT = 1e-5
EPS_IN = 1e-12

STOP_AFTER = None



class Prog:
    ENG = {"pe": "tensor", "act": "scalar", "dve": "vector", "pool": "gpsimd",
           "sp": "sync"}

    def __init__(self, nc):
        self.nc = nc
        self.esem = {e: nc.alloc_semaphore("esem_" + e)
                     for e in ("pe", "act", "dve", "pool")}
        self.tasks = {e: [] for e in self.ENG}
        self.ccnt = {e: 0 for e in ("pe", "act", "dve", "pool")}
        self.dsem = {}
        self.res = {}
        self.waited = {e: {} for e in self.ENG}

    def _iv(self, ap):
        t = ap.tensor
        if isinstance(t, bass.PSumTensorHandle):
            return (t.name, 0, 1 << 40)
        off = ap.offset if isinstance(ap.offset, int) else 0
        span = 1
        for step, count in ap.ap:
            span += abs(step) * (count - 1)
        return (t.name, off, off + span)

    def _collect(self, engine, reads, writes):
        deps = {}

        def want(dep):
            kind, key, val = dep
            k = (kind, key)
            if deps.get(k, -1) < val:
                deps[k] = val

        ivs_r = [self._iv(a) for a in reads]
        ivs_w = [self._iv(a) for a in writes]
        for name, lo, hi in ivs_r:
            ent = self.res.get(name)
            if ent:
                for a, b, dep in ent["w"]:
                    if a < hi and lo < b:
                        want(dep)
        for name, lo, hi in ivs_w:
            ent = self.res.get(name)
            if ent:
                for a, b, dep in ent["w"]:
                    if a < hi and lo < b:
                        want(dep)
                for a, b, dep in ent["r"]:
                    if a < hi and lo < b:
                        want(dep)
        return deps, ivs_r, ivs_w

    def _register(self, ivs_r, ivs_w, token):
        for name, lo, hi in ivs_w:
            ent = self.res.setdefault(name, {"w": [], "r": []})
            ent["w"] = [(a, b, d) for (a, b, d) in ent["w"] if not (lo <= a and b <= hi)]
            ent["r"] = [(a, b, d) for (a, b, d) in ent["r"] if not (lo <= a and b <= hi)]
            ent["w"].append((lo, hi, token))
        for name, lo, hi in ivs_r:
            self.res.setdefault(name, {"w": [], "r": []})["r"].append((lo, hi, token))

    def _mk_waits(self, engine, deps):
        waits = []
        for (kind, key), val in sorted(deps.items(), key=lambda kv: str(kv[0])):
            if val <= 0:
                continue
            sem = self.esem[key] if kind == "E" else self.dsem[key][0]
            prev = self.waited[engine].get((kind, key), -1)
            if val > prev:
                waits.append((sem, val))
                self.waited[engine][(kind, key)] = val
        return waits

    def add(self, engine, fn, reads=(), writes=(), tag=""):
        deps, ivs_r, ivs_w = self._collect(engine, reads, writes)
        waits = self._mk_waits(engine, deps)
        self.ccnt[engine] += 1
        token = ("E", engine, self.ccnt[engine])
        self.tasks[engine].append((waits, fn, tag, False))
        self._register(ivs_r, ivs_w, token)
        return token

    def dma(self, queue, semname, pairs, tag=""):
        if semname not in self.dsem:
            self.dsem[semname] = [self.nc.alloc_semaphore("dsem_" + semname), 0]
        ent = self.dsem[semname]
        prev_cum = ent[1]
        reads = [pp[1] for pp in pairs]
        writes = [pp[0] for pp in pairs]
        deps, ivs_r, ivs_w = self._collect(queue, reads, writes)
        if prev_cum > 0 and deps.get(("D", semname), -1) < prev_cum:
            deps[("D", semname)] = prev_cum
        waits = self._mk_waits(queue, deps)
        ent[1] += 16 * len(pairs)
        token = ("D", semname, ent[1])
        sem = ent[0]

        def fn(eng, pairs=pairs, sem=sem):
            for (o, i) in pairs:
                eng.dma_start(out=o, in_=i).then_inc(sem, 16)
            return None

        self.tasks[queue].append((waits, fn, tag, True))
        self._register(ivs_r, ivs_w, token)
        return token

    def barrier(self, exclude_sems=()):
        targets = {("E", e): self.ccnt[e] for e in self.esem if self.ccnt[e] > 0}
        targets.update({("D", n): ent[1] for n, ent in self.dsem.items()
                        if ent[1] > 0 and n not in exclude_sems})
        for e in self.ENG:
            waits = self._mk_waits(e, targets)
            if waits:
                def fn(eng, waits=waits):
                    return None
                self.tasks[e].append((waits, fn, "barrier", True))
        self.res = {}

    def final_wait(self, engine="sp"):
        targets = {("D", n): ent[1] for n, ent in self.dsem.items() if ent[1] > 0}
        targets.update({("E", e): self.ccnt[e] for e in self.esem if self.ccnt[e] > 0})
        waits = self._mk_waits(engine, targets)

        def fn(eng):
            return None
        self.tasks[engine].append((waits, fn, "final", True))

    def emit(self):
        with self.nc.Block() as block:
            for e, attr in self.ENG.items():
                dec = getattr(block, attr)
                tasks = self.tasks[e]
                esem = self.esem.get(e)

                def body(eng, tasks=tasks, esem=esem, ename=e):
                    for (waits, fn, tag, is_dma) in tasks:
                        for (sem, val) in waits:
                            eng.wait_ge(sem, val)
                        last = fn(eng)
                        if not is_dma:
                            assert last is not None, \
                                f"compute task {tag} on {ename} returned None"
                            last.then_inc(esem, 1)
                dec(body)



def _tile_km(w, granule):
    K, M = w.shape
    kc = K // 128
    nb = M // granule
    out = w.reshape(kc, 128, nb, granule).transpose(2, 1, 0, 3)
    return np.ascontiguousarray(out.reshape(nb, 128, kc * granule))


def _col_pack(v, nchunk):
    return np.ascontiguousarray(np.asarray(v, np.float32).reshape(nchunk, 128).T)


def _perm_order():
    r = np.arange(512)
    return np.concatenate([2 * r, 2 * r + 1])


def prepare_inputs(inputs):
    f = np.float32
    d = {}
    perm = _perm_order()

    j = np.arange(512, dtype=np.float64)
    inv = 1.0 / np.power(10000.0, 2.0 * j / 1024.0)
    pos = np.arange(T, dtype=np.float64)
    ang = pos[None, :] * inv[:, None]
    def _tbl(a):
        a = a.reshape(4, 128, 4, 512).transpose(1, 2, 0, 3)
        return np.ascontiguousarray(a.reshape(128, 4 * 4 * 512)).astype(f)
    d["cosT"] = _tbl(np.cos(ang))
    d["sinT"] = _tbl(np.sin(ang))

    d["eye128"] = np.eye(128, dtype=f)
    d["eps_col"] = np.broadcast_to(np.array([[EPS_OUT, EPS_IN]], f), (128, 2)).copy()
    d["ones_col"] = np.ones((128, 1), dtype=f)
    d["ones_row"] = np.ones((1, 128), dtype=f)

    ebc = np.zeros((16, CH * 128), dtype=f)
    for ci in range(CH):
        for m in range(128):
            ebc[4 * (ci % 4) + m // 32, 128 * ci + m] = 1.0
    d["Ebc"] = ebc

    Wq = np.asarray(inputs["L0_attn_Wq"], dtype=f)[:, :, perm]
    Wk = np.asarray(inputs["L0_attn_Wk"], dtype=f)[:, :, perm]
    Wv = np.asarray(inputs["L0_attn_Wv"], dtype=f)
    d["wq_t"] = np.stack([_tile_km(Wq[l], 512) for l in range(NL)])
    d["wk_t"] = np.stack([_tile_km(Wk[l], 512) for l in range(NL)])
    d["wv_t"] = np.stack([_tile_km(Wv[l], 512) for l in range(NL)])

    bq = np.asarray(inputs["L0_attn_bq"], dtype=f)[:, perm]
    bk = np.asarray(inputs["L0_attn_bk"], dtype=f)[:, perm]
    d["bq_col"] = np.ascontiguousarray(
        np.stack([_col_pack(bq[l], CH) for l in range(NL)], axis=1).reshape(128, NL * CH))
    d["bk_col"] = np.ascontiguousarray(
        np.stack([_col_pack(bk[l], CH) for l in range(NL)], axis=1).reshape(128, NL * CH))

    bv = np.asarray(inputs["L0_attn_bv"], dtype=f)
    bvr = np.zeros((64, NL * C), dtype=f)
    for l in range(NL):
        bvr[0, C * l:C * (l + 1)] = bv[l]
        bvr[32, C * l:C * (l + 1)] = bv[l]
    d["bv_rows"] = bvr

    d["lnAg_col"] = np.ascontiguousarray(
        np.stack([_col_pack(np.asarray(inputs["L0_attn_lnA_g"], f)[l], CH)
                  for l in range(NL)], axis=1).reshape(128, NL * CH))
    d["lnAb_col"] = np.ascontiguousarray(
        np.stack([_col_pack(np.asarray(inputs["L0_attn_lnA_b"], f)[l], CH)
                  for l in range(NL)], axis=1).reshape(128, NL * CH))
    d["ln0g_col"] = _col_pack(inputs["L0_attn_ln0_g"], CH)
    d["ln0b_col"] = _col_pack(inputs["L0_attn_ln0_b"], CH)
    d["ln2g_col"] = _col_pack(inputs["L0_attn_ln2_g"], CH)
    d["ln2b_col"] = _col_pack(inputs["L0_attn_ln2_b"], CH)

    w1r = np.asarray(inputs["L0_attn_conv1_w"], f).reshape(C, 4, 5)
    w2r = np.asarray(inputs["L0_attn_conv2_w"], f)
    dia1 = np.zeros((CH, 128, 20 * 128), dtype=f)
    dia2 = np.zeros((CH, 128, 20 * 128), dtype=f)
    idx = np.arange(128)
    for c in range(CH):
        for jj in range(4):
            for t5 in range(5):
                col = 128 * (5 * jj + t5)
                dia1[c, idx, col + idx] = w1r[128 * c + idx, jj, t5]
                dia2[c, idx, col + idx] = w2r[128 * c + idx, jj, t5]
    d["conv1d"] = dia1
    d["conv2d"] = dia2
    d["cb1_col"] = np.ascontiguousarray(
        np.asarray(inputs["L0_attn_conv1_b"], f).reshape(CH, 128, 4)
        .transpose(1, 0, 2).reshape(128, CH * 4))
    d["cb2_col"] = _col_pack(inputs["L0_attn_conv2_b"], CH)

    d["W1_t"] = _tile_km(np.asarray(inputs["L0_attn_W1"], f), 512)
    d["W2_t"] = _tile_km(np.asarray(inputs["L0_attn_W2"], f), 1024)[0]
    d["b1_col"] = _col_pack(inputs["L0_attn_b1"], 32)
    d["b2_col"] = _col_pack(inputs["L0_attn_b2"], CH)

    d["f1_t"] = _tile_km(np.ascontiguousarray(np.asarray(inputs["L0_ffn_w1_w"], f).T), 512)
    d["f2_t"] = _tile_km(np.ascontiguousarray(np.asarray(inputs["L0_ffn_w2_w"], f).T), 512)
    d["h1_t"] = _tile_km(np.ascontiguousarray(np.asarray(inputs["L0_ffn_h1_w"], f).T), 128)
    d["h2_t"] = _tile_km(np.ascontiguousarray(np.asarray(inputs["L0_ffn_h2_w"], f).T), 128)
    d["w3_t"] = _tile_km(np.ascontiguousarray(np.asarray(inputs["L0_ffn_w3_w"], f).T), 128)
    d["s1_col"] = _col_pack(inputs["L0_ffn_w1_s"], 32)
    d["s2_col"] = _col_pack(inputs["L0_ffn_w2_s"], 32)
    d["h1s_col"] = _col_pack(inputs["L0_ffn_h1_s"], 32)
    d["h2s_col"] = _col_pack(inputs["L0_ffn_h2_s"], 32)
    d["s3_col"] = _col_pack(inputs["L0_ffn_w3_s"], CH)
    return d



class MK:
    def __init__(self, nc):
        self.nc = nc
        self.p = Prog(nc)
        self._declare_dram()
        self._alloc_top()

    def _declare_dram(self):
        nc = self.nc

        def D(n, sh, dt, kind="ExternalInput"):
            return nc.dram_tensor(n, sh, dt, kind=kind).ap()

        self.d_x = D("x", [NSEQ, C, T], F32)
        self.d_cos = D("cosT", [128, 8192], F32)
        self.d_sin = D("sinT", [128, 8192], F32)
        self.d_eye = D("eye128", [128, 128], F32)
        self.d_eps = D("eps_col", [128, 2], F32)
        self.d_ones_col = D("ones_col", [128, 1], F32)
        self.d_ones_row = D("ones_row", [1, 128], F32)
        self.d_ebc = D("Ebc", [16, CH * 128], F32)
        self.d_wq = D("wq_t", [NL, 2, 128, 4096], F32)
        self.d_wk = D("wk_t", [NL, 2, 128, 4096], F32)
        self.d_wv = D("wv_t", [NL, 2, 128, 4096], F32)
        self.d_bq = D("bq_col", [128, NL * CH], F32)
        self.d_bk = D("bk_col", [128, NL * CH], F32)
        self.d_bvr = D("bv_rows", [64, NL * C], F32)
        self.d_lnAg = D("lnAg_col", [128, NL * CH], F32)
        self.d_lnAb = D("lnAb_col", [128, NL * CH], F32)
        self.d_ln0g = D("ln0g_col", [128, CH], F32)
        self.d_ln0b = D("ln0b_col", [128, CH], F32)
        self.d_ln2g = D("ln2g_col", [128, CH], F32)
        self.d_ln2b = D("ln2b_col", [128, CH], F32)
        self.d_c1d = D("conv1d", [CH, 128, 2560], F32)
        self.d_c2d = D("conv2d", [CH, 128, 2560], F32)
        self.d_cb1 = D("cb1_col", [128, CH * 4], F32)
        self.d_cb2 = D("cb2_col", [128, CH], F32)
        self.d_W1 = D("W1_t", [8, 128, 4096], F32)
        self.d_W2 = D("W2_t", [128, 32768], F32)
        self.d_b1 = D("b1_col", [128, 32], F32)
        self.d_b2 = D("b2_col", [128, CH], F32)
        self.d_f1 = D("f1_t", [8, 128, 4096], F32)
        self.d_f2 = D("f2_t", [8, 128, 4096], F32)
        self.d_h1 = D("h1_t", [32, 128, 4096], F32)
        self.d_h2 = D("h2_t", [32, 128, 4096], F32)
        self.d_w3 = D("w3_t", [8, 128, 4096], F32)
        self.d_s1 = D("s1_col", [128, 32], F32)
        self.d_s2 = D("s2_col", [128, 32], F32)
        self.d_h1s = D("h1s_col", [128, 32], F32)
        self.d_h2s = D("h2s_col", [128, 32], F32)
        self.d_s3 = D("s3_col", [128, CH], F32)
        self.d_y = D("y", [NSEQ, C, T], F32, "ExternalOutput")
        self.d_x2sp = D("x2_spill", [C, T], F32, "Internal")
        self.d_x3sp = D("x3_spill", [C, T], BF16, "Internal")

    def _alloc_top(self):
        nc = self.nc
        A = nc.alloc_sbuf_tensor
        self.ws = [A(f"ws{i}", [128, 4096], BF16) for i in range(6)]
        self.eye = A("eye", [128, 128], BF16)
        self.epsc = A("epsc", [128, 2], F32)
        self.ones_col = A("onescol", [128, 1], BF16)
        self.ones_row = A("onesrow", [1, 128], F32)
        self.ebc = A("ebc", [16, CH * 128], BF16)
        self.bvr = A("bvr", [64, C], BF16)
        self.bq = A("bqc", [128, NL * CH], F32)
        self.bk = A("bkc", [128, NL * CH], F32)
        self.lnAg = A("lnAg", [128, NL * CH], F32)
        self.lnAb = A("lnAb", [128, NL * CH], F32)
        self.ln0g = A("ln0g", [128, CH], F32)
        self.ln0b = A("ln0b", [128, CH], F32)
        self.ln2g = A("ln2g", [128, CH], F32)
        self.ln2b = A("ln2b", [128, CH], F32)
        self.cb1 = A("cb1", [128, CH * 4], F32)
        self.cb2 = A("cb2", [128, CH], F32)
        self.b1c = A("b1c", [128, 32], F32)
        self.b2c = A("b2c", [128, CH], F32)
        self.s1c = A("s1c", [128, 32], F32)
        self.s2c = A("s2c", [128, 32], F32)
        self.h1sc = A("h1sc", [128, 32], F32)
        self.h2sc = A("h2sc", [128, 32], F32)
        self.s3c = A("s3c", [128, CH], F32)
        self.m_row = A("m_row", [1, 512], F32)
        self.r_row = A("r_row", [1, 512], F32)
        self.km_tmp = A("km_tmp", [128, 1], F32)
        self.km_acc = A("km_acc", [128, CH], F32)
        self.kmp = A("kmp", [128, 32], F32)
        self.km_mat = A("km_mat", [128, CH * 16], BF16)
        self.krs = A("krs", [128, 1024], BF16)
        self.z_bf = A("z_bf", [16, 1024], BF16)
        self.kvb = A("kvb", [128, 16 * 128], BF16)
        P = nc.alloc_psum_tensor
        self.ps = [P(f"ps{i}", [128, 512], F32) for i in range(6)]
        self.pt = P("pt", [128, 512], BF16)
        self.kvp = P("kvp", [128, 512], F32)

    def load_constants(self):
        p = self.p
        p.dma("pool", "consts_c", [
            (self.eye[:, :], self.d_eye),
            (self.ones_col[:, :], self.d_ones_col),
            (self.ebc[:, :], self.d_ebc),
        ], tag="consts_cast")
        p.dma("sp", "consts_p", [
            (self.ones_row[:, :], self.d_ones_row),
            (self.epsc[:, :], self.d_eps),
            (self.bq[:, :], self.d_bq), (self.bk[:, :], self.d_bk),
            (self.lnAg[:, :], self.d_lnAg), (self.lnAb[:, :], self.d_lnAb),
            (self.ln0g[:, :], self.d_ln0g), (self.ln0b[:, :], self.d_ln0b),
            (self.ln2g[:, :], self.d_ln2g), (self.ln2b[:, :], self.d_ln2b),
            (self.cb1[:, :], self.d_cb1), (self.cb2[:, :], self.d_cb2),
            (self.b1c[:, :], self.d_b1), (self.b2c[:, :], self.d_b2),
            (self.s1c[:, :], self.d_s1), (self.s2c[:, :], self.d_s2),
            (self.h1sc[:, :], self.d_h1s), (self.h2sc[:, :], self.d_h2s),
            (self.s3c[:, :], self.d_s3),
        ], tag="consts_plain")

    def emit_ln(self, src3, bf3, bfs, dst3, eps, gamma8=None, beta8=None,
                dst_bf3=None, bf_out=False, scratch3=None, cast=True,
                tag="ln"):
        p = self.p
        ps_st, ps_m, ps_r = self.ps[2], self.ps[3], self.ps[4]
        if cast:
            p.add("act", lambda a: a.copy(bf3, src3),
                  reads=[src3], writes=[bf3], tag=tag + "_c")

        def mm_sum(t):
            for c in range(CH):
                inst = t.matmul(ps_st[0:1, :], self.ones_col[:, :], bfs[c],
                                start=(c == 0), stop=False, skip_group_check=True)
            return inst
        p.add("pe", mm_sum, reads=[bf3, self.ones_col[:, :]],
              writes=[ps_st[:, :]], tag=tag + "_mmsum")
        p.add("dve", lambda v: v.tensor_tensor(bf3, bf3, bf3, ALU.mult),
              reads=[bf3], writes=[bf3], tag=tag + "_sq")

        def mm_sq(t):
            for c in range(CH):
                inst = t.matmul(ps_st[64:65, :], self.ones_col[:, :], bfs[c],
                                start=(c == 0), stop=(c == CH - 1),
                                tile_position=(0, 64), skip_group_check=True)
            return inst
        p.add("pe", mm_sq, reads=[bf3, self.ones_col[:, :]],
              writes=[ps_st[:, :]], tag=tag + "_mmsq")

        p.add("dve", lambda v: v.tensor_scalar_mul(self.m_row[:, :], ps_st[0:1, :], 1.0 / C),
              reads=[ps_st[:, :]], writes=[self.m_row[:, :]], tag=tag + "_m")
        p.add("dve", lambda v: v.tensor_tensor(self.r_row[:, :], self.m_row[:, :],
                                               self.m_row[:, :], ALU.mult),
              reads=[self.m_row[:, :]], writes=[self.r_row[:, :]], tag=tag + "_m2")
        p.add("dve", lambda v: v.scalar_tensor_tensor(self.r_row[:, :], ps_st[64:65, :],
                                                      1.0 / C, self.r_row[:, :],
                                                      ALU.mult, ALU.subtract),
              reads=[ps_st[:, :], self.r_row[:, :]], writes=[self.r_row[:, :]],
              tag=tag + "_var")
        ecol = self.epsc[0:1, 0:1] if eps == EPS_OUT else self.epsc[0:1, 1:2]
        p.add("act", lambda a, ecol=ecol: a.activation(self.r_row[:, :], self.r_row[:, :],
                                                       AF.Sqrt, bias=ecol, scale=1.0),
              reads=[self.r_row[:, :], ecol], writes=[self.r_row[:, :]], tag=tag + "_sd")
        p.add("dve", lambda v: v.reciprocal(self.r_row[:, :], self.r_row[:, :]),
              reads=[self.r_row[:, :]], writes=[self.r_row[:, :]], tag=tag + "_rstd")
        p.add("pe", lambda t: t.matmul(ps_m[:, :], self.ones_row[:, :], self.m_row[:, :],
                                       start=True, stop=True, skip_group_check=True),
              reads=[self.m_row[:, :], self.ones_row[:, :]], writes=[ps_m[:, :]],
              tag=tag + "_bm")
        p.add("pe", lambda t: t.matmul(ps_r[:, :], self.ones_row[:, :], self.r_row[:, :],
                                       start=True, stop=True, skip_group_check=True),
              reads=[self.r_row[:, :], self.ones_row[:, :]], writes=[ps_r[:, :]],
              tag=tag + "_br")

        mb3 = ps_m[:, :].rearrange("p (c n) -> p c n", c=1).broadcast_to([128, CH, 512])
        rb3 = ps_r[:, :].rearrange("p (c n) -> p c n", c=1).broadcast_to([128, CH, 512])
        if bf_out:
            p.add("dve", lambda v: v.tensor_tensor(scratch3, src3, mb3, ALU.subtract),
                  reads=[src3, ps_m[:, :]], writes=[scratch3], tag=tag + "_a1")
            p.add("dve", lambda v: v.tensor_tensor(dst3, scratch3, rb3, ALU.mult),
                  reads=[scratch3, ps_r[:, :]], writes=[dst3], tag=tag + "_a2")
        else:
            p.add("dve", lambda v: v.tensor_tensor(dst3, src3, mb3, ALU.subtract),
                  reads=[src3, ps_m[:, :]], writes=[dst3], tag=tag + "_a1")
            p.add("dve", lambda v: v.tensor_tensor(dst3, dst3, rb3, ALU.mult),
                  reads=[dst3, ps_r[:, :]], writes=[dst3], tag=tag + "_a2")
            if gamma8 is not None:
                g3 = gamma8.rearrange("p (c n) -> p c n", n=1).broadcast_to([128, CH, 512])
                b3 = beta8.rearrange("p (c n) -> p c n", n=1).broadcast_to([128, CH, 512])
                p.add("pool", lambda g: g.tensor_tensor(dst3, dst3, g3, ALU.mult),
                      reads=[dst3, gamma8], writes=[dst3], tag=tag + "_a2g")
                p.add("dve", lambda v: v.tensor_tensor(dst3, dst3, b3, ALU.add),
                      reads=[dst3, beta8], writes=[dst3], tag=tag + "_a3")
            if dst_bf3 is not None:
                p.add("act", lambda a: a.copy(dst_bf3, dst3),
                      reads=[dst3], writes=[dst_bf3], tag=tag + "_a4")


def MM(t, out, lhsT, rhs, start, stop, tp=None):
    return t.matmul(out, lhsT, rhs, start=start, stop=stop,
                    tile_position=tp, skip_group_check=True)



def build_p0(mk, seq, b0, b0d, b0w, hbf3, hbfc):
    p = mk.p
    for c in range(CH):
        p.add("pool", lambda g, c=c: g.memset(b0[c][:, 0:2], 0.0),
              writes=[b0[c][:, 0:2]], tag="padz")
        p.add("pool", lambda g, c=c: g.memset(b0[c][:, 2 + T:PADW], 0.0),
              writes=[b0[c][:, 2 + T:PADW]], tag="padz")
    p.dma("sp", "xin", [(b0d[c], mk.d_x[seq, 128 * c:128 * (c + 1), :])
                        for c in range(CH)], tag="xload")
    for nq in range(4):
        nq0 = 512 * nq
        p.dma("pool", "xbf", [(hbfc[c], mk.d_x[seq, 128 * c:128 * (c + 1), nq0:nq0 + 512])
                              for c in range(CH)], tag="xbfload")
        mk.emit_ln(b0w(nq0), hbf3(0), hbfc[0:CH], b0w(nq0), EPS_OUT,
                   cast=False, tag=f"ln1_{seq}_{nq}")


def build_conv(mk, seq, scope, b0, b0d, b0w, hbf3, hbfc):
    nc, p = mk.nc, mk.p
    xbfc = scope.enter_context(nc.sbuf_tensor(f"xbfc_{seq}", [128, 8 * 264], BF16))
    hjb = [scope.enter_context(nc.sbuf_tensor(f"hjb{i}_{seq}", [128, 4 * PADW], BF16))
           for i in range(2)]

    for c in range(CH):
        s1 = mk.ws[c % 2]
        s2 = mk.ws[2 + c % 2]
        p.dma("pool", f"ws{c % 2}", [(s1[:, 0:2560], mk.d_c1d[c, :, :])], tag="c1d")
        p.dma("pool", f"ws{2 + c % 2}", [(s2[:, 0:2560], mk.d_c2d[c, :, :])], tag="c2d")
        hj = hjb[c % 2]
        for jj in range(4):
            p.add("pool", lambda g, jj=jj, hj=hj: g.memset(hj[:, PADW * jj:PADW * jj + 2], 0.0),
                  writes=[hj[:, PADW * jj:PADW * jj + 2]], tag="hjz")
            p.add("pool", lambda g, jj=jj, hj=hj: g.memset(hj[:, PADW * jj + 2 + T:PADW * (jj + 1)], 0.0),
                  writes=[hj[:, PADW * jj + 2 + T:PADW * (jj + 1)]], tag="hjz")
        for tk in range(8):
            w = xbfc[:, 264 * tk:264 * tk + 260]
            p.add("act", lambda a, c=c, tk=tk, w=w: a.copy(
                      w, b0[c][:, 256 * tk:256 * tk + 260]),
                  reads=[b0[c][:, 256 * tk:256 * tk + 260]], writes=[w], tag="xc")
            for jj in range(4):
                psq = mk.ps[(tk * 4 + jj) % 2]

                def mm1(t, s1=s1, jj=jj, tk=tk, psq=psq):
                    for t5 in range(5):
                        inst = MM(t, psq[:, 0:256], s1[:, 128 * (5 * jj + t5):128 * (5 * jj + t5 + 1)],
                                  xbfc[:, 264 * tk + t5:264 * tk + t5 + 256],
                                  start=(t5 == 0), stop=(t5 == 4))
                    return inst
                p.add("pe", mm1, reads=[s1[:, 0:2560], w], writes=[psq[:, :]], tag="c1mm")
                p.add("act", lambda a, hj=hj, jj=jj, tk=tk, psq=psq, c=c: a.activation(
                          hj[:, PADW * jj + 2 + 256 * tk:PADW * jj + 2 + 256 * tk + 256],
                          psq[:, 0:256], AF.Gelu,
                          bias=mk.cb1[:, 4 * c + jj:4 * c + jj + 1], scale=1.0),
                      reads=[psq[:, :], mk.cb1[:, 4 * c + jj:4 * c + jj + 1]],
                      writes=[hj[:, PADW * jj + 2 + 256 * tk:PADW * jj + 2 + 256 * tk + 256]],
                      tag="c1ev")
        for tk in range(8):
            psq = mk.ps[2 + tk % 2]

            def mm2(t, s2=s2, tk=tk, psq=psq, hj=hj):
                first = True
                for jj in range(4):
                    for t5 in range(5):
                        inst = MM(t, psq[:, 0:256], s2[:, 128 * (5 * jj + t5):128 * (5 * jj + t5 + 1)],
                                  hj[:, PADW * jj + 256 * tk + t5:PADW * jj + 256 * tk + t5 + 256],
                                  start=first, stop=(jj == 3 and t5 == 4))
                        first = False
                return inst
            p.add("pe", mm2, reads=[s2[:, 0:2560], hj[:, :]], writes=[psq[:, :]], tag="c2mm")
            sl = slice(256 * tk, 256 * tk + 256)
            p.add("dve", lambda v, c=c, psq=psq, sl=sl: v.scalar_tensor_tensor(
                      b0d[c][:, sl], psq[:, 0:256], mk.cb2[:, c:c + 1], b0d[c][:, sl],
                      ALU.add, ALU.add),
                  reads=[psq[:, :], mk.cb2[:, c:c + 1], b0d[c][:, sl]],
                  writes=[b0d[c][:, sl]], tag="c2ev")
    for nq in range(4):
        nq0 = 512 * nq
        mk.emit_ln(b0w(nq0), hbf3(0), hbfc[0:CH], b0w(nq0), EPS_IN,
                   gamma8=mk.ln0g[:, :], beta8=mk.ln0b[:, :],
                   tag=f"ln0_{seq}_{nq}")


def build_attn_layer(mk, seq, l, b0d, b0w, hbf3, hbfc2, bufs, upto=None):
    p = mk.p
    cosc, sinc = bufs["cosc"], bufs["sinc"]
    ksc = bufs["ksc"]
    tmp1 = bufs["tmp1"]
    krt, vt = bufs["krt"], bufs["vt"]
    ws = mk.ws

    p.dma("pool", "bvl", [(mk.bvr[:, :], mk.d_bvr[:, C * l:C * (l + 1)])], tag=f"bv{l}")
    for b in range(2):
        p.dma("pool", f"ws{b}", [(ws[b][:, :], mk.d_wk[l, b, :, :])], tag=f"wk{l}")
        p.dma("pool", f"ws{2 + b}", [(ws[2 + b][:, :], mk.d_wv[l, b, :, :])], tag=f"wv{l}")
        p.dma("pool", f"ws{4 + b}", [(ws[4 + b][:, :], mk.d_wq[l, b, :, :])], tag=f"wq{l}")

    def hb(c, par):
        return hbfc2[(par % 2) * CH + c]

    def kscv(par):
        off = (par % 2) * CH * 512
        return ksc[:, off:off + CH * 512]

    def cast_h(nq0, tag, par):
        p.add("act", lambda a, nq0=nq0, par=par: a.copy(hbf3(par), b0w(nq0)),
              reads=[b0w(nq0)], writes=[hbf3(par)], tag=tag)

    def proj(wslots, biascol, dst, tag, par, accum_nq=None):
        for mc in range(CH):
            wsl = wslots[0] if mc < 4 else wslots[1]
            mloc = mc % 4
            psq = mk.ps[mc % 2]

            def mmp(t, wsl=wsl, mloc=mloc, psq=psq, par=par):
                for kc in range(CH):
                    inst = MM(t, psq[:, :], wsl[:, 512 * kc + 128 * mloc:512 * kc + 128 * mloc + 128],
                              hb(kc, par), start=(kc == 0), stop=(kc == CH - 1))
                return inst
            p.add("pe", mmp, reads=[wsl[:, :]] + [hb(kc, par) for kc in range(CH)],
                  writes=[psq[:, :]], tag=tag + "mm")
            if accum_nq is None:
                p.add("act", lambda a, mc=mc, psq=psq: a.activation(
                          dst[:, 512 * mc:512 * mc + 512], psq[:, :], AF.Gelu,
                          bias=biascol[:, CH * l + mc:CH * l + mc + 1], scale=1.0),
                      reads=[psq[:, :], biascol[:, CH * l + mc:CH * l + mc + 1]],
                      writes=[dst[:, 512 * mc:512 * mc + 512]], tag=tag + "ev")
            else:
                kcol = mk.kmp[:, 4 * mc + accum_nq:4 * mc + accum_nq + 1]
                p.add("act", lambda a, mc=mc, psq=psq, kcol=kcol: a.activation(
                          dst[:, 512 * mc:512 * mc + 512], psq[:, :], AF.Gelu,
                          bias=biascol[:, CH * l + mc:CH * l + mc + 1], scale=1.0,
                          accum_out=kcol),
                      reads=[psq[:, :], biascol[:, CH * l + mc:CH * l + mc + 1]],
                      writes=[dst[:, 512 * mc:512 * mc + 512], kcol], tag=tag + "ev")
        p.add("dve", lambda v: v.tensor_scalar_add(
                  dst[:, 0:CH * 512], dst[:, 0:CH * 512], 0.21),
              reads=[dst[:, 0:CH * 512]], writes=[dst[:, 0:CH * 512]], tag=tag + "21")

    def rope(buf, nq, tag):
        cw = cosc[nq]
        sw = sinc[nq]
        se = buf[:, 0:2048]
        so = buf[:, 2048:4096]
        te = tmp1[:, 0:2048]
        toh = tmp1[:, 2048:3072]
        p.add("dve", lambda v: v.tensor_tensor(te, so, sw, ALU.mult),
              reads=[so, sw], writes=[te], tag=tag + "1")
        for hf in range(2):
            seh = buf[:, 1024 * hf:1024 * hf + 1024]
            swh = sw[:, 1024 * hf:1024 * hf + 1024]
            p.add("dve", lambda v, seh=seh, swh=swh: v.tensor_tensor(
                      toh, seh, swh, ALU.mult),
                  reads=[seh, swh], writes=[toh], tag=tag + f"2{hf}")
            soh = buf[:, 2048 + 1024 * hf:2048 + 1024 * hf + 1024]
            cwh = cw[:, 1024 * hf:1024 * hf + 1024]
            p.add("dve", lambda v, soh=soh, cwh=cwh: v.tensor_tensor(
                      soh, soh, cwh, ALU.mult),
                  reads=[soh, cwh], writes=[soh], tag=tag + f"5{hf}")
            p.add("pool", lambda g, soh=soh: g.tensor_tensor(
                      soh, soh, toh, ALU.add),
                  reads=[soh, toh], writes=[soh], tag=tag + f"6{hf}")
        p.add("dve", lambda v: v.tensor_tensor(se, se, cw, ALU.mult),
              reads=[se, cw], writes=[se], tag=tag + "3")
        p.add("dve", lambda v: v.tensor_tensor(se, se, te, ALU.subtract),
              reads=[se, te], writes=[se], tag=tag + "4")

    def kv_A(nq):
        nq0 = 512 * nq
        par = nq % 2
        ksq = kscv(par)
        cast_h(nq0, f"a{l}kc", par)
        proj((ws[0], ws[1]), mk.bk, ksq, f"a{l}k", par, accum_nq=nq)
        rope(ksq, nq, f"a{l}rk")

    def kv_B(nq):
        nq0 = 512 * nq
        par = nq % 2
        ksq = kscv(par)
        for g in range(4):
            for half in range(2):
                def tr(t, half=half, g=g, ksq=ksq):
                    for pp in range(4):
                        ci = 4 * half + pp
                        inst = t.transpose(mk.pt[:, 128 * pp:128 * pp + 128],
                                           ksq[:, 512 * ci + 128 * g:512 * ci + 128 * g + 128],
                                           mk.eye[:, :])
                    return inst
                trd = [ksq[:, 512 * (4 * half + pp) + 128 * g:512 * (4 * half + pp) + 128 * g + 128]
                       for pp in range(4)]
                p.add("pe", tr, reads=trd + [mk.eye[:, :]], writes=[mk.pt[:, :]],
                      tag=f"a{l}tr")
                p.add("act", lambda a, half=half: a.copy(
                          krt[:, 512 * half:512 * half + 512], mk.pt[:, :]),
                      reads=[mk.pt[:, :]], writes=[krt[:, 512 * half:512 * half + 512]],
                      tag=f"a{l}trev")
            for vh in range(2):
                wsl = ws[2 + vh]
                psq = mk.ps[3 + vh]

                def mmv(t, wsl=wsl, g=g, psq=psq, par=par):
                    for kc in range(CH):
                        inst = MM(t, psq[:, :], hb(kc, par)[:, 128 * g:128 * g + 128],
                                  wsl[:, 512 * kc:512 * kc + 512],
                                  start=(kc == 0), stop=(kc == CH - 1))
                    return inst
                p.add("pe", mmv, reads=[wsl[:, :]] + [hb(kc, par) for kc in range(CH)],
                      writes=[psq[:, :]], tag=f"a{l}vmm")
                p.add("act", lambda a, vh=vh, psq=psq: a.copy(
                          vt[:, 512 * vh:512 * vh + 512], psq[:, :]),
                      reads=[psq[:, :]], writes=[vt[:, 512 * vh:512 * vh + 512]],
                      tag=f"a{l}vev")

            def mmkv(t, nq=nq, g=g):
                for h in range(NH):
                    rb = (32 * h) % 128
                    for half in range(2):
                        inst = MM(t, mk.kvp[rb:rb + 32, (h // 4) * 128 + 64 * half:
                                            (h // 4) * 128 + 64 * half + 64],
                                  krt[:, 512 * half + 32 * h:512 * half + 32 * h + 32],
                                  vt[:, 64 * h:64 * h + 64],
                                  start=(nq == 0 and g == 0 and h < 4 and half == 0),
                                  stop=False, tp=(0, rb))
                return inst
            p.add("pe", mmkv, reads=[krt[:, :], vt[:, :]], writes=[mk.kvp[:, :]],
                  tag=f"a{l}kv")

            def mmks(t, nq=nq, g=g):
                st = (nq == 0 and g == 0)
                MM(t, mk.ps[5][0:1, :], mk.ones_col[:, :], krt[:, 0:512],
                   start=st, stop=False)
                return MM(t, mk.ps[5][32:33, :], mk.ones_col[:, :], krt[:, 512:1024],
                          start=st, stop=False, tp=(0, 32))
            p.add("pe", mmks, reads=[krt[:, :], mk.ones_col[:, :]],
                  writes=[mk.ps[5][:, :]], tag=f"a{l}ks")

    kv_A(0)
    kv_A(1)
    kv_B(0)
    kv_A(2)
    kv_B(1)
    kv_A(3)
    kv_B(2)
    kv_B(3)

    def ksev(a):
        a.copy(mk.krs[0:1, 0:512], mk.ps[5][0:1, :])
        return a.copy(mk.krs[32:33, 0:512], mk.ps[5][32:33, :])
    p.add("act", ksev, reads=[mk.ps[5][:, :]], writes=[mk.krs[:, :]], tag=f"a{l}ksev")
    p.dma("sp", "krsmv", [(mk.krs[0:1, 512:1024], mk.krs[32:33, 0:512])],
          tag=f"a{l}krsmv")

    def mmbv(t):
        for h in range(NH):
            rb = (32 * h) % 128
            for half in range(2):
                inst = MM(t, mk.kvp[rb:rb + 32, (h // 4) * 128 + 64 * half:
                                    (h // 4) * 128 + 64 * half + 64],
                          mk.krs[0:1, 512 * half + 32 * h:512 * half + 32 * h + 32],
                          mk.bvr[0:1, 64 * h:64 * h + 64],
                          start=False, stop=(h == NH - 1 and half == 1),
                          tp=(0, rb))
        return inst
    p.add("pe", mmbv, reads=[mk.krs[:, :], mk.bvr[:, :]], writes=[mk.kvp[:, :]],
          tag=f"a{l}bv")
    p.add("pool", lambda g: g.memset(mk.kvb[:, :], 0.0),
          writes=[mk.kvb[:, :]], tag=f"a{l}kvz")
    for Mp in range(CH):
        for half in range(2):
            base = 256 * Mp + 128 * half
            for hh in range(2):
                h = 2 * Mp + hh
                rb = (32 * h) % 128
                p.add("dve", lambda v, base=base, hh=hh, h=h, rb=rb, half=half:
                      v.tensor_scalar_mul(
                          mk.kvb[rb:rb + 32, base + 64 * hh:base + 64 * hh + 64],
                          mk.kvp[rb:rb + 32, (h // 4) * 128 + 64 * half:
                                 (h // 4) * 128 + 64 * half + 64], 1.0 / T),
                      reads=[mk.kvp[:, :]],
                      writes=[mk.kvb[rb:rb + 32, base + 64 * hh:base + 64 * hh + 64]],
                      tag=f"a{l}kvev")
    for c in range(CH):
        p.add("dve", lambda v, c=c: v.reduce_sum(
                  mk.km_acc[:, c:c + 1], mk.kmp[:, 4 * c:4 * c + 4], axis=AX.X),
              reads=[mk.kmp[:, 4 * c:4 * c + 4]], writes=[mk.km_acc[:, c:c + 1]],
              tag=f"a{l}kmr")
    p.add("pool", lambda g: g.memset(mk.km_mat[:, :], 0.0),
          writes=[mk.km_mat[:, :]], tag=f"a{l}kmz")
    for c in range(CH):
        cc = c % 4
        for q4 in range(4):
            h = 4 * cc + q4
            p.add("dve", lambda v, c=c, q4=q4, h=h: v.tensor_scalar(
                      mk.km_mat[32 * q4:32 * q4 + 32, 16 * c + h:16 * c + h + 1],
                      mk.km_acc[32 * q4:32 * q4 + 32, c:c + 1], 1.0 / T, 0.21,
                      ALU.mult, ALU.add),
                  reads=[mk.km_acc[:, c:c + 1]],
                  writes=[mk.km_mat[:, 16 * c:16 * c + 16]], tag=f"a{l}kmb")

    def zv(par, bf):
        off = (par % 2) * 512
        return mk.z_bf[:, off:off + 512]

    def q_A(nq):
        nq0 = 512 * nq
        par = nq % 2
        ksq = kscv(par)
        cast_h(nq0, f"a{l}qc", par)
        proj((ws[4], ws[5]), mk.bq, ksq, f"a{l}q", par)

        def mmz(t, ksq=ksq):
            for c in range(CH):
                inst = MM(t, mk.ps[5][0:16, :], mk.km_mat[:, 16 * c:16 * c + 16],
                          ksq[:, 512 * c:512 * c + 512],
                          start=(c == 0), stop=(c == CH - 1))
            return inst
        p.add("pe", mmz, reads=[mk.km_mat[:, :]] + [ksq[:, 512 * c:512 * c + 512] for c in range(CH)],
              writes=[mk.ps[5][:, :]], tag=f"a{l}z")
        zb = zv(par, True)
        p.add("dve", lambda v: v.tensor_scalar_add(mk.ps[5][0:16, :], mk.ps[5][0:16, :], 1e-6),
              reads=[mk.ps[5][:, :]], writes=[mk.ps[5][:, :]], tag=f"a{l}zc1")
        p.add("dve", lambda v: v.reciprocal(mk.ps[5][0:16, :], mk.ps[5][0:16, :]),
              reads=[mk.ps[5][:, :]], writes=[mk.ps[5][:, :]], tag=f"a{l}zc2")
        p.add("dve", lambda v, zb=zb: v.tensor_copy(out=zb, in_=mk.ps[5][0:16, :]),
              reads=[mk.ps[5][:, :]], writes=[zb], tag=f"a{l}zc3")
        rope(ksq, nq, f"a{l}rq")
        for ci in range(CH):
            zbig = mk.ps[2] if ci % 2 == 0 else mk.ps[5]
            p.add("pe", lambda t, ci=ci, zbig=zbig, zb=zb: MM(
                      t, zbig[:, :], mk.ebc[:, 128 * ci:128 * ci + 128], zb,
                      start=True, stop=True),
                  reads=[mk.ebc[:, :], zb], writes=[zbig[:, :]], tag=f"a{l}zbg")
            qsl = ksq[:, 512 * ci:512 * ci + 512]
            p.add("dve", lambda v, qsl=qsl, zbig=zbig: v.tensor_tensor(
                      qsl, qsl, zbig[:, :], ALU.mult),
                  reads=[qsl, zbig[:, :]], writes=[qsl], tag=f"a{l}zap")

    def q_B(nq):
        nq0 = 512 * nq
        par = nq % 2
        ksq = kscv(par)
        for ec in range(CH):
            pso = mk.ps[3 + ec % 2]

            def mmo(t, ec=ec, pso=pso, ksq=ksq):
                for half in range(2):
                    ck = 4 * half + ec // 2
                    inst = MM(t, pso[:, :],
                              mk.kvb[:, 256 * ec + 128 * half:256 * ec + 128 * half + 128],
                              ksq[:, 512 * ck:512 * ck + 512],
                              start=(half == 0), stop=(half == 1))
                return inst
            mmord = [ksq[:, 512 * (4 * half + ec // 2):512 * (4 * half + ec // 2) + 512]
                     for half in range(2)]
            p.add("pe", mmo, reads=[mk.kvb[:, :]] + mmord, writes=[pso[:, :]],
                  tag=f"a{l}o")
            p.add("dve", lambda v, ec=ec, nq0=nq0, pso=pso: v.tensor_tensor(
                      b0d[ec][:, nq0:nq0 + 512], b0d[ec][:, nq0:nq0 + 512],
                      pso[:, :], ALU.add),
                  reads=[pso[:, :], b0d[ec][:, nq0:nq0 + 512]],
                  writes=[b0d[ec][:, nq0:nq0 + 512]], tag=f"a{l}sm2")
        mk.emit_ln(b0w(nq0), hbf3(par), [hb(c, par) for c in range(CH)],
                   b0w(nq0), EPS_IN,
                   gamma8=mk.lnAg[:, CH * l:CH * l + CH],
                   beta8=mk.lnAb[:, CH * l:CH * l + CH],
                   tag=f"lnA{l}_{seq}_{nq}")

    q_A(0)
    q_A(1)
    q_B(0)
    q_A(2)
    q_B(1)
    q_A(3)
    q_B(2)
    q_B(3)
    return False


def build_mlp(mk, seq, scope, b0d, b0w, hbf3, hbfc):
    nc, p = mk.nc, mk.p
    ub = scope.enter_context(nc.sbuf_tensor(f"ub_{seq}", [128, 4 * 512], BF16))
    fsb = scope.enter_context(nc.sbuf_tensor(f"fsb_{seq}", [128, CH * 1024], F32))

    mbf = scope.enter_context(nc.sbuf_tensor(f"mbf_{seq}", [128, CH * 1024], BF16))
    ws = mk.ws

    for nq in range(2):
        nq0 = 1024 * nq
        mbfc = [mbf[:, 1024 * c:1024 * (c + 1)] for c in range(CH)]
        mbf3 = mbf[:, :].rearrange("p (c n) -> p c n", n=1024)
        p.add("act", lambda a, nq0=nq0: a.copy(mbf3, b0w(nq0, 1024)),
              reads=[b0w(nq0, 1024)], writes=[mbf3], tag="mh")
        for b in range(8):
            wsl1 = ws[b % 2]
            wsl2 = ws[2 + b % 2]
            p.dma("pool", f"ws{b % 2}", [(wsl1[:, :], mk.d_W1[b, :, :])], tag="w1l")
            p.dma("pool", f"ws{2 + b % 2}",
                  [(wsl2[:, :], mk.d_W2[:, 4096 * b:4096 * b + 4096])], tag="w2l")
            for nh in range(2):
                o = 512 * nh
                for msub in range(4):
                    psq = mk.ps[msub % 2]

                    def mm1(t, wsl1=wsl1, msub=msub, psq=psq, o=o):
                        for kc in range(CH):
                            inst = MM(t, psq[:, :],
                                      wsl1[:, 512 * kc + 128 * msub:512 * kc + 128 * msub + 128],
                                      mbfc[kc][:, o:o + 512],
                                      start=(kc == 0), stop=(kc == CH - 1))
                        return inst
                    p.add("pe", mm1,
                          reads=[wsl1[:, :]] + [mbfc[kc][:, o:o + 512] for kc in range(CH)],
                          writes=[psq[:, :]], tag="w1mm")
                    hidx = 4 * b + msub
                    p.add("act", lambda a, msub=msub, psq=psq, hidx=hidx: a.activation(
                              ub[:, 512 * msub:512 * msub + 512], psq[:, :], AF.Gelu,
                              bias=mk.b1c[:, hidx:hidx + 1], scale=1.0),
                          reads=[psq[:, :], mk.b1c[:, hidx:hidx + 1]],
                          writes=[ub[:, 512 * msub:512 * msub + 512]], tag="w1ev")
                for mc in range(CH):
                    psq = mk.ps[4 + (mc % 2)]

                    def mm2(t, wsl2=wsl2, mc=mc, psq=psq):
                        for kci in range(4):
                            inst = MM(t, psq[:, :],
                                      wsl2[:, 1024 * kci + 128 * mc:1024 * kci + 128 * mc + 128],
                                      ub[:, 512 * kci:512 * kci + 512],
                                      start=(kci == 0), stop=(kci == 3))
                        return inst
                    p.add("pe", mm2, reads=[wsl2[:, :], ub[:, :]], writes=[psq[:, :]],
                          tag="w2mm")
                    fdst = fsb[:, 1024 * mc + 512 * nh:1024 * mc + 512 * nh + 512]
                    if b == 0:
                        p.add("act", lambda a, psq=psq, fdst=fdst: a.copy(fdst, psq[:, :]),
                              reads=[psq[:, :]], writes=[fdst], tag="f0")
                    else:
                        p.add("dve", lambda v, psq=psq, fdst=fdst: v.tensor_tensor(
                                  fdst, fdst, psq[:, :], ALU.add),
                              reads=[psq[:, :], fdst], writes=[fdst], tag="facc")
        for nh in range(2):
            o = 512 * nh
            fsb_3 = fsb[:, :].rearrange("p (c n) -> p c n", n=1024)
            fw3 = fsb_3[:, :, o:o + 512]
            for c in range(CH):
                fdst = fsb[:, 1024 * c + o:1024 * c + o + 512]
                p.add("dve", lambda v, fdst=fdst, c=c, nq0=nq0, o=o: v.scalar_tensor_tensor(
                          fdst, fdst, mk.b2c[:, c:c + 1], b0d[c][:, nq0 + o:nq0 + o + 512],
                          ALU.add, ALU.add),
                      reads=[fdst, mk.b2c[:, c:c + 1], b0d[c][:, nq0 + o:nq0 + o + 512]],
                      writes=[fdst], tag="fb2")
            fwin = [fsb[:, 1024 * c + o:1024 * c + o + 512] for c in range(CH)]
            mk.emit_ln(fw3, hbf3(1), hbfc[8:16], fw3, EPS_IN,
                       gamma8=mk.ln2g[:, :], beta8=mk.ln2b[:, :],
                       tag=f"ln2_{seq}_{nq}_{nh}")
            mk.emit_ln(fw3, hbf3(1), hbfc[8:16], fw3, EPS_OUT,
                       tag=f"lnp_{seq}_{nq}_{nh}")
            p.dma("sp", "xo", [(b0d[c][:, nq0 + o:nq0 + o + 512],
                                mk.d_x[seq, 128 * c:128 * (c + 1), nq0 + o:nq0 + o + 512])
                               for c in range(CH)], tag="xol")
            for c in range(CH):
                p.add("dve", lambda v, c=c, nq0=nq0, o=o, fdst=fwin[c]: v.tensor_tensor(
                          b0d[c][:, nq0 + o:nq0 + o + 512],
                          b0d[c][:, nq0 + o:nq0 + o + 512], fdst, ALU.add),
                      reads=[fwin[c], b0d[c][:, nq0 + o:nq0 + o + 512]],
                      writes=[b0d[c][:, nq0 + o:nq0 + o + 512]], tag="x2")
            x3win = [hbfc[c] for c in range(CH)]
            mk.emit_ln(b0w(nq0 + o), hbf3(1), hbfc[8:16], hbf3(0), EPS_OUT,
                       bf_out=True, scratch3=fw3, tag=f"ln3_{seq}_{nq}_{nh}")
            p.dma("sp", "x3sp", [(mk.d_x3sp[128 * c:128 * (c + 1), nq0 + o:nq0 + o + 512],
                                  x3win[c]) for c in range(CH)], tag="x3sp")
        p.dma("sp", "x2sp", [(mk.d_x2sp[128 * c:128 * (c + 1), nq0:nq0 + 1024],
                              b0d[c][:, nq0:nq0 + 1024]) for c in range(CH)], tag="x2sp")


def build_ffn(mk, seq, scope):
    nc, p = mk.nc, mk.p
    x3w = scope.enter_context(nc.sbuf_tensor(f"x3w_{seq}", [128, CH * 512], BF16))
    Ab = scope.enter_context(nc.sbuf_tensor(f"A_{seq}", [128, 32 * 512], BF16))
    Bb = scope.enter_context(nc.sbuf_tensor(f"B_{seq}", [128, 32 * 512], BF16))
    sil = scope.enter_context(nc.sbuf_tensor(f"sil_{seq}", [128, 512], F32))
    u2 = scope.enter_context(nc.sbuf_tensor(f"u2_{seq}", [128, 512], F32))
    f3 = scope.enter_context(nc.sbuf_tensor(f"f3_{seq}", [128, CH * 512], F32))
    x2c = scope.enter_context(nc.sbuf_tensor(f"x2c_{seq}", [128, CH * 512], F32))
    ws = mk.ws
    x3wc = [x3w[:, 512 * c:512 * c + 512] for c in range(CH)]

    for tc in range(4):
        t0 = 512 * tc
        src3 = mk.d_x3sp.rearrange("(c p) n -> p c n", p=128)
        p.dma("sp", "x3ld", [(x3w[:, :].rearrange("p (c n) -> p c n", n=512),
                              src3[:, :, t0:t0 + 512])], tag="x3ld")
        for b in range(8):
            p.dma("pool", f"ws{b % 2}", [(ws[b % 2][:, :], mk.d_f1[b, :, :])], tag="f1l")
            p.dma("pool", f"ws{2 + b % 2}", [(ws[2 + b % 2][:, :], mk.d_f2[b, :, :])],
                  tag="f2l")
            for msub in range(4):
                hidx = 4 * b + msub
                psq = mk.ps[msub % 2]

                def mmf1(t, b=b, msub=msub, psq=psq):
                    wsl = ws[b % 2]
                    for kc in range(CH):
                        inst = MM(t, psq[:, :],
                                  wsl[:, 512 * kc + 128 * msub:512 * kc + 128 * msub + 128],
                                  x3wc[kc], start=(kc == 0), stop=(kc == CH - 1))
                    return inst
                p.add("pe", mmf1, reads=[ws[b % 2][:, :]] + x3wc, writes=[psq[:, :]],
                      tag="f1mm")
                p.add("act", lambda a, psq=psq, hidx=hidx: a.activation(
                          sil[:, :], psq[:, :], AF.Silu, bias=0.0,
                          scale=mk.s1c[:, hidx:hidx + 1]),
                      reads=[psq[:, :], mk.s1c[:, hidx:hidx + 1]], writes=[sil[:, :]],
                      tag="f1ev")
                ps2q = mk.ps[2 + msub % 2]

                def mmf2(t, b=b, msub=msub, ps2q=ps2q):
                    wsl = ws[2 + b % 2]
                    for kc in range(CH):
                        inst = MM(t, ps2q[:, :],
                                  wsl[:, 512 * kc + 128 * msub:512 * kc + 128 * msub + 128],
                                  x3wc[kc], start=(kc == 0), stop=(kc == CH - 1))
                    return inst
                p.add("pe", mmf2, reads=[ws[2 + b % 2][:, :]] + x3wc, writes=[ps2q[:, :]],
                      tag="f2mm")
                p.add("act", lambda a, ps2q=ps2q, hidx=hidx: a.activation(
                          u2[:, :], ps2q[:, :], AF.Identity, bias=0.0,
                          scale=mk.s2c[:, hidx:hidx + 1]),
                      reads=[ps2q[:, :], mk.s2c[:, hidx:hidx + 1]], writes=[u2[:, :]],
                      tag="f2ev")
                p.add("dve", lambda v, hidx=hidx: v.tensor_tensor(
                          Ab[:, 512 * hidx:512 * hidx + 512], sil[:, :], u2[:, :],
                          ALU.mult),
                      reads=[sil[:, :], u2[:, :]],
                      writes=[Ab[:, 512 * hidx:512 * hidx + 512]], tag="f12m")
        for g in range(32):
            p.dma("pool", f"ws{4 + g % 2}", [(ws[4 + g % 2][:, :], mk.d_h1[g, :, :])],
                  tag="h1l")
            psq = mk.ps[4 + g % 2]

            def mmh1(t, g=g, psq=psq):
                wsl = ws[4 + g % 2]
                for kc in range(32):
                    inst = MM(t, psq[:, :], wsl[:, 128 * kc:128 * kc + 128],
                              Ab[:, 512 * kc:512 * kc + 512],
                              start=(kc == 0), stop=(kc == 31))
                return inst
            p.add("pe", mmh1, reads=[ws[4 + g % 2][:, :], Ab[:, :]], writes=[psq[:, :]],
                  tag="h1mm")
            p.add("act", lambda a, g=g, psq=psq: a.activation(
                      Bb[:, 512 * g:512 * g + 512], psq[:, :], AF.Gelu, bias=0.0,
                      scale=mk.h1sc[:, g:g + 1]),
                  reads=[psq[:, :], mk.h1sc[:, g:g + 1]],
                  writes=[Bb[:, 512 * g:512 * g + 512]], tag="h1ev")
        for g in range(32):
            p.dma("pool", f"ws{g % 2}", [(ws[g % 2][:, :], mk.d_h2[g, :, :])], tag="h2l")
            psq = mk.ps[g % 2]

            def mmh2(t, g=g, psq=psq):
                wsl = ws[g % 2]
                for kc in range(32):
                    inst = MM(t, psq[:, :], wsl[:, 128 * kc:128 * kc + 128],
                              Bb[:, 512 * kc:512 * kc + 512],
                              start=(kc == 0), stop=(kc == 31))
                return inst
            p.add("pe", mmh2, reads=[ws[g % 2][:, :], Bb[:, :]], writes=[psq[:, :]],
                  tag="h2mm")
            p.add("act", lambda a, g=g, psq=psq: a.activation(
                      Ab[:, 512 * g:512 * g + 512], psq[:, :], AF.Gelu, bias=0.0,
                      scale=mk.h2sc[:, g:g + 1]),
                  reads=[psq[:, :], mk.h2sc[:, g:g + 1]],
                  writes=[Ab[:, 512 * g:512 * g + 512]], tag="h2ev")
        for g in range(CH):
            p.dma("pool", f"ws{2 + g % 2}", [(ws[2 + g % 2][:, :], mk.d_w3[g, :, :])],
                  tag="w3l")
            psq = mk.ps[2 + g % 2]

            def mmw3(t, g=g, psq=psq):
                wsl = ws[2 + g % 2]
                for kc in range(32):
                    inst = MM(t, psq[:, :], wsl[:, 128 * kc:128 * kc + 128],
                              Ab[:, 512 * kc:512 * kc + 512],
                              start=(kc == 0), stop=(kc == 31))
                return inst
            p.add("pe", mmw3, reads=[ws[2 + g % 2][:, :], Ab[:, :]], writes=[psq[:, :]],
                  tag="w3mm")
            p.add("act", lambda a, g=g, psq=psq: a.activation(
                      f3[:, 512 * g:512 * g + 512], psq[:, :], AF.Identity, bias=0.0,
                      scale=mk.s3c[:, g:g + 1]),
                  reads=[psq[:, :], mk.s3c[:, g:g + 1]],
                  writes=[f3[:, 512 * g:512 * g + 512]], tag="w3ev")
        bfw = [Bb[:, 512 * c:512 * c + 512] for c in range(CH)]
        f3_3 = f3[:, :].rearrange("p (c n) -> p c n", n=512)
        bf3w = Bb[:, 0:CH * 512].rearrange("p (c n) -> p c n", n=512)
        f3w = [f3[:, 512 * c:512 * c + 512] for c in range(CH)]
        mk.emit_ln(f3_3, bf3w, bfw, f3_3, EPS_OUT, tag=f"lnf_{seq}_{tc}")
        src2 = mk.d_x2sp.rearrange("(c p) n -> p c n", p=128)
        p.dma("sp", "x2ld", [(x2c[:, :].rearrange("p (c n) -> p c n", n=512),
                              src2[:, :, t0:t0 + 512])], tag="x2ld")
        for c in range(CH):
            p.add("dve", lambda v, c=c: v.tensor_tensor(
                      x2c[:, 512 * c:512 * c + 512], x2c[:, 512 * c:512 * c + 512],
                      f3w[c], ALU.add),
                  reads=[f3w[c], x2c[:, 512 * c:512 * c + 512]],
                  writes=[x2c[:, 512 * c:512 * c + 512]], tag="yadd")
        dsty = mk.d_y[seq, :, t0:t0 + 512].rearrange("(c p) n -> p c n", p=128)
        p.dma("sp", "yout", [(dsty, x2c[:, :].rearrange("p (c n) -> p c n", n=512))],
              tag="yst")



def build_program(stop_after=None):
    from contextlib import ExitStack
    nc = bass.Bass("TRN2", target_bir_lowering=False)
    mk = MK(nc)
    p = mk.p
    mk.load_constants()

    stopped = False
    for seq in range(NSEQ):
        if stopped:
            break
        with ExitStack() as seq_scope:
            stopped = build_seq(mk, seq, seq_scope, stop_after)
        p.barrier(exclude_sems=("consts_c", "consts_p"))

    p.final_wait()
    p.emit()
    return nc


def build_seq(mk, seq, seq_scope, stop_after):
    from contextlib import ExitStack
    nc, p = mk.nc, mk.p

    def dump(chunks):
        pairs = [(mk.d_y[seq, 128 * c:128 * (c + 1), :], chunks[c][:, 0:T])
                 for c in range(CH)]
        p.dma("sp", "yout", pairs, tag="dbg_out")
        return True

    main_scope = ExitStack()
    seq_scope.enter_context(main_scope)
    big0 = main_scope.enter_context(
        nc.sbuf_tensor(f"big0_{seq}", [128, CH * PADW], F32))
    b0 = [big0[:, PADW * c:PADW * (c + 1)] for c in range(CH)]
    b0d = [big0[:, PADW * c + 2:PADW * c + 2 + T] for c in range(CH)]
    big0_3 = big0[:, :].rearrange("p (c n) -> p c n", n=PADW)

    def b0w(nq0, w=512):
        return big0_3[:, :, 2 + nq0:2 + nq0 + w]
    hbf = main_scope.enter_context(
        nc.sbuf_tensor(f"hbf_{seq}", [128, 2 * CH * 512], BF16))
    hbfc = [hbf[:, 512 * c:512 * (c + 1)] for c in range(2 * CH)]
    hbf3a = hbf[:, 0:CH * 512].rearrange("p (c n) -> p c n", n=512)
    hbf3b = hbf[:, CH * 512:2 * CH * 512].rearrange("p (c n) -> p c n", n=512)

    def hbf3(par):
        return hbf3a if par % 2 == 0 else hbf3b

    build_p0(mk, seq, b0, b0d, b0w, hbf3, hbfc)
    if stop_after == "ln1" and seq == 0:
        return dump(b0d)

    with ExitStack() as conv_scope:
        build_conv(mk, seq, conv_scope, b0, b0d, b0w, hbf3, hbfc)
    p.barrier(exclude_sems=("consts_c", "consts_p"))
    if stop_after == "conv" and seq == 0:
        return dump(b0d)

    with ExitStack() as attn_scope:
        ae = attn_scope.enter_context
        cosT = ae(nc.sbuf_tensor(f"cosT_{seq}", [128, 8192], BF16))
        sinT = ae(nc.sbuf_tensor(f"sinT_{seq}", [128, 8192], BF16))
        cosc = [cosT[:, 2048 * i:2048 * (i + 1)] for i in range(4)]
        sinc = [sinT[:, 2048 * i:2048 * (i + 1)] for i in range(4)]
        p.dma("pool", "tbl",
              [(cosT[:, :], mk.d_cos[:, :]), (sinT[:, :], mk.d_sin[:, :])],
              tag="tables")
        ksc = ae(nc.sbuf_tensor(f"ksc_{seq}", [128, 2 * CH * 512], BF16))
        tmp1 = ae(nc.sbuf_tensor(f"tmp1_{seq}", [128, 3072], BF16))
        krt = ae(nc.sbuf_tensor(f"krt_{seq}", [128, 1024], BF16))
        vt = ae(nc.sbuf_tensor(f"vt_{seq}", [128, 1024], BF16))
        bufs = dict(cosc=cosc, sinc=sinc, ksc=ksc, tmp1=tmp1,
                    krt=krt, vt=vt)
        for l in range(NL):
            build_attn_layer(mk, seq, l, b0d, b0w, hbf3, hbfc, bufs)
            if stop_after == f"attn{l}" and seq == 0:
                return dump(b0d)
    p.barrier(exclude_sems=("consts_c", "consts_p"))

    with ExitStack() as mlp_scope:
        build_mlp(mk, seq, mlp_scope, b0d, b0w, hbf3, hbfc)
    p.barrier(exclude_sems=("consts_c", "consts_p"))
    if stop_after == "mlp" and seq == 0:
        return dump(b0d)

    main_scope.close()
    p.barrier(exclude_sems=("consts_c", "consts_p"))

    with ExitStack() as ffn_scope:
        build_ffn(mk, seq, ffn_scope)
    return False



def kernel(**inputs):
    prep = prepare_inputs(inputs)
    x = np.asarray(inputs["L0_attn_x"], np.float32)
    nc = build_program(STOP_AFTER)
    in_maps = []
    for core in range(8):
        m = dict(prep)
        m["x"] = np.ascontiguousarray(x[2 * core:2 * core + 2].transpose(0, 2, 1))
        in_maps.append(m)
    res = run_bass_kernel_spmd(nc, in_maps, core_ids=list(range(8)))
    out = np.empty((16, 2048, 1024), np.float32)
    for core in range(8):
        y = np.asarray(res.results[core]["y"])
        out[2 * core:2 * core + 2] = y.transpose(0, 2, 1)
    return out
